# version 04
# baseline (speedup 1.0000x reference)
import numpy as np
import ml_dtypes
import concourse.bass as bass
import concourse.mybir as mybir
from concourse.bass_utils import run_bass_kernel_spmd
from contextlib import ExitStack

F32 = mybir.dt.float32
BF16 = mybir.dt.bfloat16
AF = mybir.ActivationFunctionType
ALU = mybir.AluOpType
AX = mybir.AxisListType

NCORES = 8
B = 2
S = 4096
D = 4096
NT = S // 128
TOK = 1024
NTILE = TOK // 128
BF = ml_dtypes.bfloat16
NPAIR = 8

EPS = 1e-5


class Buf:
    __slots__ = ("ap", "wtok", "rtoks", "frozen")

    def __init__(self, ap):
        self.ap = ap
        self.wtok = None
        self.rtoks = []
        self.frozen = False


class Prog:
    ENGS = ("sync", "vector", "scalar", "tensor", "gpsimd")

    NDMA = 8

    def __init__(self, nc):
        self.nc = nc
        self.qs = {e: [] for e in self.ENGS}
        self.sem_names = (["v", "s", "t", "g", "y"]
                          + [f"d{q}{i}" for q in "SPA" for i in range(self.NDMA)])
        self.sem_cnt = {n: 0 for n in self.sem_names}
        self.sems = {}
        self.wait_cache = {e: {} for e in self.ENGS}
        self.eng_sem = {"sync": "y", "gpsimd": "g", "vector": "v",
                        "scalar": "s", "tensor": "t"}
        self.dma_sem = {"sync": "S", "gpsimd": "P", "scalar": "A"}
        self.dma_idx = {"S": 0, "P": 0, "A": 0}
        self.barrier_toks = None

    def _emit_wait(self, eng_name, tok):
        if tok is None:
            return
        sem, val = tok
        if eng_name == "tensor" and sem == "t":
            return
        wc = self.wait_cache[eng_name]
        if wc.get(sem, 0) >= val:
            return
        wc[sem] = val
        self.qs[eng_name].append(("wait", sem, val))

    def _next(self, sem, step):
        self.sem_cnt[sem] += step
        return (sem, self.sem_cnt[sem])

    def op(self, eng_name, build, reads=(), writes=(), inc=True, dma=False,
           extra_waits=()):
        reads = [b for b in reads if b is not None]
        writes = [b for b in writes if b is not None]
        for b in reads:
            self._emit_wait(eng_name, b.wtok)
        for b in writes:
            self._emit_wait(eng_name, b.wtok)
            for t in b.rtoks:
                self._emit_wait(eng_name, t)
        for t in extra_waits:
            self._emit_wait(eng_name, t)
        if self.barrier_toks is not None:
            for b in writes:
                if b.wtok is None:
                    for t in self.barrier_toks:
                        self._emit_wait(eng_name, t)
                    break
        tok = None
        if inc:
            if dma:
                q = self.dma_sem[eng_name]
                i = self.dma_idx[q]
                self.dma_idx[q] = (i + 1) % self.NDMA
                tok = self._next(f"d{q}{i}", 16)
            else:
                tok = self._next(self.eng_sem[eng_name], 1)
        self.qs[eng_name].append(("inst", build, tok))
        if tok is not None:
            for b in reads:
                if not b.frozen:
                    b.rtoks.append(tok)
            for b in writes:
                b.rtoks = []
                b.wtok = tok
        return tok

    def load(self, dst, src_ap, queue="sync", dst_ap=None, src_buf=None):
        ap = dst.ap[:] if dst_ap is None else dst_ap
        return self.op(queue, lambda e: e.dma_start(out=ap, in_=src_ap),
                       reads=[src_buf] if src_buf is not None else (),
                       writes=[dst], dma=True)

    def store(self, dst_ap, src, queue="scalar", src_ap=None, dst_buf=None):
        ap = src.ap[:] if src_ap is None else src_ap
        return self.op(queue, lambda e: e.dma_start(out=dst_ap, in_=ap),
                       reads=[src],
                       writes=[dst_buf] if dst_buf is not None else (), dma=True)

    def all_tokens(self):
        return [(n, self.sem_cnt[n]) for n in self.sem_names if self.sem_cnt[n] > 0]

    def final_wait(self):
        toks = self.all_tokens()
        self.op("sync", lambda e: e.nop(), extra_waits=toks, inc=False)

    def finalize(self, block):
        def mk(eng_name):
            def body(eng):
                for item in self.qs[eng_name]:
                    if item[0] == "wait":
                        eng.wait_ge(self.sems[item[1]], item[2])
                    else:
                        inst = item[1](eng)
                        tok = item[2]
                        if tok is not None:
                            step = 16 if tok[0].startswith("d") else 1
                            inst.then_inc(self.sems[tok[0]], step)
            return body
        for e, deco in (("sync", block.sync), ("vector", block.vector),
                        ("scalar", block.scalar), ("tensor", block.tensor),
                        ("gpsimd", block.gpsimd)):
            if self.qs[e]:
                deco(mk(e))


class Env:

    def __init__(self, name):
        self.nc = bass.Bass("TRN2", target_bir_lowering=False)
        self.p = Prog(self.nc)
        self.ctx = ExitStack()
        self.scopes = []
        self._ins = {}
        self._outs = {}
        self.name = name

    def push_scope(self):
        self.scopes.append(ExitStack())

    def pop_scope(self):
        self.scopes.pop().close()
        self.p.barrier_toks = list(self.p.all_tokens())

    def dram_in(self, name, shape, dtype=F32):
        h = self.nc.dram_tensor(name, list(shape), dtype, kind="ExternalInput")
        self._ins[name] = h
        return h.ap()

    def dram_tmp(self, name, shape, dtype=F32):
        h = self.nc.dram_tensor(name, list(shape), dtype, kind="Internal")
        return h.ap()

    def dram_out(self, name, shape, dtype=F32):
        h = self.nc.dram_tensor(name, list(shape), dtype, kind="ExternalOutput")
        self._outs[name] = h
        return h.ap()

    def sbuf(self, name, shape, dtype):
        stack = self.scopes[-1] if self.scopes else self.ctx
        h = stack.enter_context(self.nc.sbuf_tensor(name, list(shape), dtype))
        return Buf(h)

    def psum(self, name, shape, dtype=F32):
        stack = self.scopes[-1] if self.scopes else self.ctx
        h = stack.enter_context(self.nc.psum_tensor(name, list(shape), dtype))
        return Buf(h)

    def build(self, emit_fn):
        with self.ctx:
            for n in self.p.sem_names:
                self.p.sems[n] = self.ctx.enter_context(self.nc.semaphore(n))
            block = self.ctx.enter_context(self.nc.Block())
            emit_fn(self)
            self.p.final_wait()
            self.p.finalize(block)
        return self.nc


def block_lhsT(w, kc=128, mc=128):
    K, M = w.shape
    return np.ascontiguousarray(
        w.reshape(K // kc, kc, M // mc, mc).transpose(0, 2, 1, 3))


def block_rhs(w, kc=128, nb=512):
    K, N = w.shape
    return np.ascontiguousarray(
        w.reshape(K // kc, kc, N // nb, nb).transpose(0, 2, 1, 3))


def rope_tables(positions, dim, base=10000.0):
    half = dim // 2
    inv = 1.0 / base ** (np.arange(half, dtype=np.float64) / half)
    ang = inv[:, None] * np.asarray(positions, np.float64)[None, :]
    cos = np.cos(ang)
    sin = np.sin(ang)
    cos2 = np.concatenate([cos, cos], 0).astype(np.float32)
    sin2 = np.concatenate([sin, sin], 0).astype(np.float32)
    return cos2, sin2


def yarn_tables(positions):
    DR = 64
    BASE, FACTOR, BFAST, BSLOW, OLD_CTX = 10000.0, 40.0, 32.0, 1.0, 4096.0
    freqs = 1.0 / BASE ** (np.arange(0, DR, 2, dtype=np.float64) / DR)
    wavelengths = 2.0 * np.pi / freqs
    ramp = np.clip((wavelengths / OLD_CTX - BSLOW) / (BFAST - BSLOW), 0.0, 1.0)
    scale = 1.0 - ramp + ramp * FACTOR
    inv = freqs / scale
    ang = inv[:, None] * np.asarray(positions, np.float64)[None, :]
    cos = np.cos(ang)
    sin = np.sin(ang)
    cos2 = np.concatenate([cos, cos], 0).astype(np.float32)
    sin2 = np.concatenate([sin, sin], 0).astype(np.float32)
    return cos2, sin2


def alibi_slopes(n_heads=32):
    import math
    closest = 2 ** int(math.floor(math.log2(n_heads)))
    base = 2.0 ** (-(2.0 ** (-(math.log2(closest) - 3))))
    slopes = [base ** i for i in range(1, closest + 1)]
    if closest != n_heads:
        eb = 2.0 ** (-(2.0 ** (-(math.log2(2 * closest) - 3))))
        n_extra = min(closest, n_heads - closest)
        slopes += [eb ** i for i in range(1, 2 * n_extra + 1, 2)]
    return np.array(slopes, np.float32)


class Pools:

    def __init__(self, env, n_pf=6, n_pt=2, ln=True, wstream=True, rope=True):
        self.env = env
        p = env.p
        self.pf = [env.psum(f"psf{i}", [128, 512], F32) for i in range(n_pf)]
        self.pt = [env.psum(f"pst{i}", [128, 128], BF16) for i in range(n_pt)]
        self._tp_i = 0
        self.ones1 = env.sbuf("sc_ones1", [128, 1], BF16)
        env.p.op("vector", lambda e: e.memset(self.ones1.ap[:], 1.0), writes=[self.ones1])
        if ln or n_pt:
            self.ident = env.sbuf("sc_ident", [128, 128], BF16)
            ident_in = env.dram_in("c_ident", [128, 128], BF16)
            env.p.load(self.ident, ident_in)
        if ln:
            self.mu = env.sbuf("sc_mu", [128, 1], F32)
            self.sq = env.sbuf("sc_sq", [128, 1], F32)
            self.rstd = env.sbuf("sc_rstd", [128, 1], F32)
            self.nmur = env.sbuf("sc_nmur", [128, 1], F32)
            self.dump = env.sbuf("sc_dump", [128, 4096], F32)
            self.eps = env.sbuf("sc_eps", [128, 1], F32)
            env.p.op("vector", lambda e: e.memset(self.eps.ap[:], EPS), writes=[self.eps])
        if wstream:
            self.wb = [env.sbuf(f"sc_wb{i}", [128, 512], BF16) for i in range(4)]
        self._wb_i = 0
        if rope:
            self.rope_t1 = env.sbuf("sc_ropet1", [128, 512], BF16)
            self.rope_t2 = env.sbuf("sc_ropet2", [128, 512], BF16)
            self.rope_tb = env.sbuf("sc_ropetb", [128, 512], F32)

    def next_wb(self):
        b = self.wb[self._wb_i % len(self.wb)]
        self._wb_i += 1
        return b

    def next_tp(self):
        b = self.pt[self._tp_i % len(self.pt)]
        self._tp_i += 1
        return b


def emit_ln(env, po, x_buf, xn_buf):
    p = env.p
    inv = 1.0 / D
    mu, sq, rstd, nmur, dump, eps = po.mu, po.sq, po.rstd, po.nmur, po.dump, po.eps
    p.op("vector", lambda e: e.reduce_sum(mu.ap[:], x_buf.ap[:], axis=AX.X),
         reads=[x_buf], writes=[mu])
    p.op("scalar", lambda e: e.activation(out=dump.ap[:], in_=x_buf.ap[:],
                                          func=AF.Square, accum_out=sq.ap[:]),
         reads=[x_buf], writes=[sq, dump])
    p.op("vector", lambda e: e.tensor_scalar_mul(mu.ap[:], mu.ap[:], inv),
         reads=[mu], writes=[mu])
    p.op("vector", lambda e: e.tensor_scalar_mul(sq.ap[:], sq.ap[:], inv),
         reads=[sq], writes=[sq])
    p.op("vector", lambda e: e.tensor_mul(rstd.ap[:], mu.ap[:], mu.ap[:]),
         reads=[mu], writes=[rstd])
    p.op("vector", lambda e: e.tensor_sub(rstd.ap[:], sq.ap[:], rstd.ap[:]),
         reads=[sq, rstd], writes=[rstd])
    p.op("scalar", lambda e: e.activation(out=rstd.ap[:], in_=rstd.ap[:],
                                          func=AF.Ln, bias=eps.ap[:]),
         reads=[rstd, eps], writes=[rstd])
    p.op("scalar", lambda e: e.activation(out=rstd.ap[:], in_=rstd.ap[:],
                                          func=AF.Exp, scale=-0.5),
         reads=[rstd], writes=[rstd])
    p.op("scalar", lambda e: e.mul(nmur.ap[:], mu.ap[:], -1.0),
         reads=[mu], writes=[nmur])
    p.op("scalar", lambda e: e.mul(nmur.ap[:], nmur.ap[:], rstd.ap[:]),
         reads=[nmur, rstd], writes=[nmur])
    p.op("scalar", lambda e: e.activation(out=xn_buf.ap[:], in_=x_buf.ap[:],
                                          func=AF.Identity, bias=nmur.ap[:],
                                          scale=rstd.ap[:]),
         reads=[x_buf, nmur, rstd], writes=[xn_buf])


def emit_tp128(env, po, src_buf, src_ap, dst_buf, dst_ap):
    p = env.p
    ps = po.next_tp()
    p.op("tensor", lambda e: e.transpose(ps.ap[:], src_ap, po.ident.ap[:]),
         reads=[src_buf, po.ident], writes=[ps])
    p.op("vector", lambda e: e.tensor_copy(out=dst_ap, in_=ps.ap[:]),
         reads=[ps], writes=[dst_buf])


def emit_ln_tp_phase(env, po, x_dram, xnT, xnT_out=None, x_tiles_keep=None):
    p = env.p
    xw = env.sbuf("lt_x", [128, D], F32)
    xnw = env.sbuf("lt_xn", [128, D], BF16)
    for tile in range(NTILE):
        xbuf = x_tiles_keep[tile] if x_tiles_keep is not None else xw
        p.load(xbuf, x_dram[tile, :, :])
        emit_ln(env, po, xbuf, xnw)
        for c in range(D // 128):
            emit_tp128(env, po, xnw, xnw.ap[:, c * 128:(c + 1) * 128],
                       xnT[c], xnT[c].ap[:, tile * 128:(tile + 1) * 128])
    if xnT_out is not None:
        for c in range(D // 128):
            p.store(xnT_out[c, :, :], xnT[c])


def emit_proj_feat(env, po, w_dram, xnT, out_chunks, ndm=32, evac="copy",
                   bias=None, rope=None, out_store=None, scale=1.0, oc_base=0):
    p = env.p
    noc = len(out_chunks)
    ntb = TOK // 512
    for oc in range(noc):
        pss = [po.pf[oc % 2 * 2], po.pf[oc % 2 * 2 + 1]][:ntb] if ntb <= 2 else None
        for dm in range(ndm):
            wb = po.next_wb()
            p.load(wb, w_dram[dm, oc_base + oc, :, :], queue="gpsimd",
                   dst_ap=wb.ap[:, 0:128])
            for tb in range(ntb):
                def mk(e, wb=wb, dm=dm, tb=tb, oc=oc, ps=pss[tb]):
                    return e.matmul(ps.ap[:], wb.ap[:, 0:128],
                                    xnT[dm].ap[:, tb * 512:(tb + 1) * 512],
                                    start=(dm == 0), stop=(dm == ndm - 1))
                p.op("tensor", mk, reads=[wb, xnT[dm]], writes=[pss[tb]])
        ob = out_chunks[oc]
        for tb in range(ntb):
            ps = pss[tb]
            cols = slice(tb * 512, (tb + 1) * 512)
            if evac == "copy":
                if bias is not None:
                    bbuf, bap = bias(oc)
                    p.op("scalar", lambda e, ps=ps, ob=ob, cols=cols, bap=bap:
                         e.activation(out=ob.ap[:, cols], in_=ps.ap[:],
                                      func=AF.Identity, bias=bap, scale=scale),
                         reads=[ps, bbuf], writes=[ob])
                elif scale != 1.0:
                    p.op("scalar", lambda e, ps=ps, ob=ob, cols=cols:
                         e.activation(out=ob.ap[:, cols], in_=ps.ap[:],
                                      func=AF.Identity, scale=scale),
                         reads=[ps], writes=[ob])
                else:
                    p.op("scalar", lambda e, ps=ps, ob=ob, cols=cols:
                         e.copy(out=ob.ap[:, cols], in_=ps.ap[:]),
                         reads=[ps], writes=[ob])
            elif evac == "rope":
                cos_b, sin_b = rope
                emit_rope_evac(env, po, ps, ob, cols, cos_b, sin_b, half=64,
                               bias=None if bias is None else bias(oc))
        if out_store is not None:
            p.store(out_store[oc, :, :], ob)


def emit_rope_evac(env, po, ps, ob, cols, cos_b, sin_b, half=64, bias=None):
    p = env.p
    t1, t2 = po.rope_t1, po.rope_t2
    src = ps
    if bias is not None:
        bbuf, bap = bias
        tb = po.rope_tb
        p.op("scalar", lambda e: e.activation(out=tb.ap[:, 0:512], in_=ps.ap[:],
                                              func=AF.Identity, bias=bap),
             reads=[ps, bbuf], writes=[tb])
        src = tb
    h = half
    sa = src.ap[0:h, 0:512]
    sb_ = src.ap[h:2 * h, 0:512]
    ca = cos_b.ap[0:h, cols]
    sn = sin_b.ap[h:2 * h, cols]
    cb = cos_b.ap[h:2 * h, cols]
    snb = sin_b.ap[0:h, cols]
    p.op("vector", lambda e: e.tensor_tensor(out=t1.ap[0:h, 0:512], in0=sa, in1=ca, op=ALU.mult),
         reads=[src, cos_b], writes=[t1])
    p.op("vector", lambda e: e.tensor_tensor(out=t2.ap[0:h, 0:512], in0=sb_, in1=sn, op=ALU.mult),
         reads=[src, sin_b], writes=[t2])
    p.op("vector", lambda e: e.tensor_tensor(out=ob.ap[0:h, cols], in0=t1.ap[0:h, 0:512],
                                             in1=t2.ap[0:h, 0:512], op=ALU.subtract),
         reads=[t1, t2], writes=[ob])
    p.op("vector", lambda e: e.tensor_tensor(out=t1.ap[0:h, 0:512], in0=sb_, in1=cb, op=ALU.mult),
         reads=[src, cos_b], writes=[t1])
    p.op("vector", lambda e: e.tensor_tensor(out=t2.ap[0:h, 0:512], in0=sa, in1=snb, op=ALU.mult),
         reads=[src, sin_b], writes=[t2])
    p.op("vector", lambda e: e.tensor_tensor(out=ob.ap[h:2 * h, cols], in0=t1.ap[0:h, 0:512],
                                             in1=t2.ap[0:h, 0:512], op=ALU.add),
         reads=[t1, t2], writes=[ob])


def emit_proj_tok(env, po, w_dram, inT, ncolblk, ndm=32, colw=512,
                  out_tiles=None, out_store=None, evac_fn=None, npsum=6,
                  inT_cols=None):
    p = env.p
    if inT_cols is None:
        def inT_cols(dm, ti):
            return inT[dm].ap[:, ti * 128:(ti + 1) * 128]
    for cb in range(ncolblk):
        for grp in range(0, NTILE, npsum):
            tiles = list(range(grp, min(grp + npsum, NTILE)))
            for dm in range(ndm):
                wb = po.next_wb()
                p.load(wb, w_dram[dm, cb, :, 0:colw], queue="gpsimd",
                       dst_ap=wb.ap[:, 0:colw])
                for ti in tiles:
                    ps = po.pf[ti % npsum]
                    def mk(e, wb=wb, dm=dm, ti=ti, ps=ps):
                        return e.matmul(ps.ap[:, 0:colw], inT_cols(dm, ti),
                                        wb.ap[:, 0:colw],
                                        start=(dm == 0), stop=(dm == ndm - 1))
                    p.op("tensor", mk, reads=[wb, inT[dm]], writes=[ps])
            for ti in tiles:
                ps = po.pf[ti % npsum]
                if evac_fn is not None:
                    evac_fn(cb, ti, ps)
                else:
                    ob = out_tiles[ti]
                    cols = slice(cb * colw, (cb + 1) * colw)
                    p.op("scalar", lambda e, ps=ps, ob=ob, cols=cols:
                         e.copy(out=ob.ap[:, cols], in_=ps.ap[:, 0:colw]),
                         reads=[ps], writes=[ob])
    if out_store is not None:
        for ti in range(NTILE):
            p.store(out_store[ti, :, :], out_tiles[ti])


SQ128 = float(np.sqrt(128.0))


def emit_qproj_pairs(env, po, wq_all, xnf_dram, qT_own, ndm=32, seq=S,
                     evac_fn=None):
    p = env.p
    nh = (ndm + 1) // 2
    xt = env.sbuf("qp_x", [128, nh * 512], BF16)
    ntb = seq // 512
    for tb in range(ntb):
      for half in range(2):
        d0, d1 = half * nh, min(ndm, (half + 1) * nh)
        if d0 >= d1:
            continue
        p.load(xt, xnf_dram[d0:d1, :, tb * 512:(tb + 1) * 512]
               .rearrange("a p m -> p a m"), queue="sync",
               dst_ap=xt.ap[:, 0:(d1 - d0) * 512].rearrange("p (a m) -> p a m", a=d1 - d0))
        for dm in range(d0, d1):
            for j in range(NPAIR):
                c0 = (dm * NPAIR + j) * 128
                def mk(e, dm=dm, j=j, c0=c0, d0=d0):
                    return e.matmul(po.pf[j].ap[:], wq_all.ap[:, c0:c0 + 128],
                                    xt.ap[:, (dm - d0) * 512:(dm - d0 + 1) * 512],
                                    start=(dm == 0), stop=(dm == ndm - 1))
                p.op("tensor", mk, reads=[wq_all, xt], writes=[po.pf[j]])
      for j in range(NPAIR):
            ps = po.pf[j]
            cols = slice(tb * 512, (tb + 1) * 512)
            if evac_fn is not None:
                evac_fn(j, tb, ps, qT_own[j], cols)
            else:
                p.op("scalar", lambda e, ps=ps, j=j, cols=cols:
                     e.copy(out=qT_own[j].ap[:, cols], in_=ps.ap[:]),
                     reads=[ps], writes=[qT_own[j]])


def emit_attn_pairs(env, po, qT_own, kT_getter, v_getter, ctx_out, seq=S,
                    exp_evac=None, score_extra=None, ctx_sb=None):
    p = env.p
    nqb = seq // 512
    pts = [env.sbuf(f"at_p{i}", [128, 512], BF16) for i in range(4)]
    zrow = env.sbuf("at_z", [1, 512], F32)
    onesr = env.sbuf("at_or", [1, 128], F32)
    bz = env.sbuf("at_bz", [128, 512], F32)
    ctxs = env.sbuf("at_ctx", [128, 512], BF16)
    p.op("vector", lambda e: e.memset(onesr.ap[:], 1.0), writes=[onesr])
    psZ, psO, psB = po.pf[4], po.pf[5], po.pf[6]
    for j in range(NPAIR):
        kbuf, kc0 = kT_getter(j)
        vbuf, vc0 = v_getter(j)
        for qblk in range(nqb):
            nkt = 4 * qblk + 4
            for kt in range(nkt):
                psS = po.pf[kt % 4]
                def mm_s(e, j=j, qblk=qblk, kt=kt, psS=psS, kbuf=kbuf, kc0=kc0):
                    return e.matmul(psS.ap[:], kbuf.ap[:, kc0 + kt * 128:kc0 + (kt + 1) * 128],
                                    qT_own[j].ap[:, qblk * 512:(qblk + 1) * 512],
                                    start=True, stop=(score_extra is None))
                p.op("tensor", mm_s, reads=[kbuf, qT_own[j]], writes=[psS])
                if score_extra is not None:
                    score_extra(j, qblk, kt, psS)
                ptile = pts[kt % 4]
                exp_evac(j, qblk, kt, psS, ptile)
                def mm_z(e, kt=kt, nkt=nkt, ptile=ptile):
                    return e.matmul(psZ.ap[0:1, :], po.ones1.ap[:],
                                    ptile.ap[:], start=(kt == 0), stop=(kt == nkt - 1))
                p.op("tensor", mm_z, reads=[po.ones1, ptile], writes=[psZ])
                def mm_o(e, kt=kt, nkt=nkt, ptile=ptile, vbuf=vbuf, vc0=vc0):
                    return e.matmul(psO.ap[:], vbuf.ap[:, vc0 + kt * 128:vc0 + (kt + 1) * 128],
                                    ptile.ap[:], start=(kt == 0), stop=(kt == nkt - 1))
                p.op("tensor", mm_o, reads=[vbuf, ptile], writes=[psO])
            p.op("vector", lambda e: e.tensor_copy(out=zrow.ap[:], in_=psZ.ap[0:1, :]),
                 reads=[psZ], writes=[zrow])
            p.op("vector", lambda e: e.reciprocal(out=zrow.ap[:], in_=zrow.ap[:]),
                 reads=[zrow], writes=[zrow])
            p.op("tensor", lambda e: e.matmul(psB.ap[:], onesr.ap[:], zrow.ap[:],
                                              start=True, stop=True),
                 reads=[onesr, zrow], writes=[psB])
            p.op("vector", lambda e: e.tensor_copy(out=bz.ap[:], in_=psB.ap[:]),
                 reads=[psB], writes=[bz])
            p.op("vector", lambda e: e.tensor_tensor(out=ctxs.ap[:], in0=psO.ap[:],
                                                     in1=bz.ap[:], op=ALU.mult),
                 reads=[psO, bz], writes=[ctxs])
            if ctx_sb is not None:
                p.op("vector", lambda e, j=j, qblk=qblk:
                     e.tensor_copy(out=ctx_sb[j].ap[:, qblk * 512:(qblk + 1) * 512],
                                   in_=ctxs.ap[:]),
                     reads=[ctxs], writes=[ctx_sb[j]])
            if ctx_out is not None:
                p.store(ctx_out[j, :, qblk * 512:(qblk + 1) * 512], ctxs)


def build_attn0(env):
    po = Pools(env, n_pf=8, n_pt=0, ln=False, wstream=False, rope=False)
    p = env.p
    xnf = env.dram_in("xnf", [32, 128, S], BF16)
    kTf = env.dram_in("kTf", [1, 128, S], BF16)
    vf = env.dram_in("vf", [128, NT * 128], BF16)
    wq = env.dram_in("wq", [128, 32 * NPAIR * 128])
    biasKQ = env.dram_in("biasKQ", [128, NPAIR * NT * 8])
    qoffm = env.dram_in("qoffm", [128, 5 * 512])
    slopes = env.dram_in("slopes", [128, NPAIR])
    ctx_out = env.dram_out("ctx", [NPAIR, 128, S], BF16)

    wq_all = env.sbuf("wq_all", [128, 32 * NPAIR * 128], BF16)
    p.load(wq_all, wq, queue="gpsimd")
    wq_all.frozen = True
    kT = env.sbuf("kT", [128, S], BF16)
    p.load(kT, kTf[0, :, :])
    kT.frozen = True
    v_all = env.sbuf("v_all", [128, NT * 128], BF16)
    p.load(v_all, vf)
    v_all.frozen = True
    bkq = env.sbuf("bkq", [128, NPAIR * NT * 8], F32)
    p.load(bkq, biasKQ)
    bkq.frozen = True
    qoff = env.sbuf("qoff", [128, 5 * 512], F32)
    p.load(qoff, qoffm)
    qoff.frozen = True
    slp = env.sbuf("slp", [128, NPAIR], F32)
    p.load(slp, slopes)
    slp.frozen = True

    qT_own = [env.sbuf(f"qT{j}", [128, S], BF16) for j in range(NPAIR)]
    emit_qproj_pairs(env, po, wq_all, xnf, qT_own)
    for b in qT_own:
        b.frozen = True

    tmp2 = [env.sbuf(f"at_tmp{i}", [128, 512], F32) for i in range(2)]

    def exp_evac(j, qblk, kt, psS, ptile):
        tmp = tmp2[kt % 2]
        r = kt - 4 * qblk
        var = (r + 1) if 0 <= r <= 3 else 0
        qsl = qoff.ap[:, var * 512:(var + 1) * 512]
        p.op("vector", lambda e, tmp=tmp: e.scalar_tensor_tensor(
            out=tmp.ap[:], in0=qsl, scalar=slp.ap[:, j:j + 1],
            in1=psS.ap[:], op0=ALU.mult, op1=ALU.add),
            reads=[qoff, slp, psS], writes=[tmp])
        col = (j * NT + kt) * 8 + qblk
        p.op("scalar", lambda e, col=col, tmp=tmp: e.activation(
            out=ptile.ap[:], in_=tmp.ap[:], func=AF.Exp,
            bias=bkq.ap[:, col:col + 1], scale=1.0 / SQ128),
            reads=[tmp, bkq], writes=[ptile])

    emit_attn_pairs(env, po, qT_own, lambda j: (kT, 0), lambda j: (v_all, 0),
                    ctx_out, exp_evac=exp_evac)


def emit_wo_res(env, po, ctxT, wo_dram, x_in_dram, x_out_dram, bias_dram=None,
                x_keep=None):
    p = env.p
    xsl = [env.sbuf(f"wr_x{i}", [128, 512], F32) for i in range(2)]
    xo = [env.sbuf(f"wr_o{i}", [128, 512], F32) for i in range(2)]
    bias_b = None
    if bias_dram is not None:
        bias_b = env.sbuf("wr_b", [128, D], F32)
        p.load(bias_b, bias_dram)
        bias_b.frozen = True

    def evac(cb, ti, ps):
        xs = xsl[ti % 2]
        xob = xo[ti % 2]
        p.load(xs, x_in_dram[ti, :, cb * 512:(cb + 1) * 512], queue="sync")
        if bias_b is not None:
            p.op("vector", lambda e, xs=xs, cb=cb: e.tensor_tensor(
                out=xs.ap[:], in0=xs.ap[:],
                in1=bias_b.ap[:, cb * 512:(cb + 1) * 512], op=ALU.add),
                reads=[xs, bias_b], writes=[xs])
        p.op("vector", lambda e, ps=ps, xs=xs, xob=xob: e.scalar_tensor_tensor(
            out=xob.ap[:], in0=ps.ap[:], scalar=1.0, in1=xs.ap[:],
            op0=ALU.mult, op1=ALU.add),
            reads=[ps, xs], writes=[xob])
        if x_keep is not None:
            p.op("vector", lambda e, xob=xob, ti=ti, cb=cb: e.tensor_copy(
                out=x_keep[ti].ap[:, cb * 512:(cb + 1) * 512], in_=xob.ap[:]),
                reads=[xob], writes=[x_keep[ti]])
        p.store(x_out_dram[ti, :, cb * 512:(cb + 1) * 512], xob)

    emit_proj_tok(env, po, wo_dram, ctxT, ncolblk=8, evac_fn=evac, npsum=8)


def mk_build_wo_res(with_bias):
    def build(env):
        po = Pools(env, n_pf=8, n_pt=0, ln=False, wstream=True, rope=False)
        p = env.p
        ctx_in = env.dram_in("ctxT", [32, 128, TOK], BF16)
        wo = env.dram_in("wo", [32, 8, 128, 512])
        x_in = env.dram_in("x", [NTILE, 128, D])
        bias = env.dram_in("bias", [128, D]) if with_bias else None
        x_out = env.dram_out("xo", [NTILE, 128, D])
        ctxT = [env.sbuf(f"ctxT{c}", [128, TOK], BF16) for c in range(32)]
        for c in range(32):
            p.load(ctxT[c], ctx_in[c, :, :])
            ctxT[c].frozen = True
        emit_wo_res(env, po, ctxT, wo, x_in, x_out, bias_dram=bias)
    return build


build_wo_res = mk_build_wo_res(False)
build_wo_res_bias = mk_build_wo_res(True)


def emit_ffn0(env, po, x_in_dram, x_out_dram, ab_bufs, prr, x_tiles=None):
    p = env.p
    A_rhs, B_rhs, WqR, keysT = (ab_bufs["A"], ab_bufs["B"], ab_bufs["WqR"],
                                ab_bufs["keysT"])
    xw = env.sbuf("f0_x", [128, D], F32)
    xnw = env.sbuf("f0_xn", [128, D], BF16)
    xnTt = env.sbuf("f0_xnT", [128, D], BF16)
    z_bf = env.sbuf("f0_z", [128, 512], BF16)
    qTf = env.sbuf("f0_qT", [16, 128], F32)
    scT = env.sbuf("f0_scT", [64, 128], F32)
    sc = env.sbuf("f0_sc", [128, 64], F32)
    m8a = env.sbuf("f0_m8a", [128, 8], F32)
    m8b = env.sbuf("f0_m8b", [128, 8], F32)
    ind8 = env.sbuf("f0_ind8", [128, 64], F32)
    scm = env.sbuf("f0_scm", [128, 64], F32)
    kept = env.sbuf("f0_kept", [128, 64], F32)
    negm1 = env.sbuf("f0_negm1", [128, 1], F32)
    gexp = env.sbuf("f0_gexp", [128, 64], F32)
    zs = env.sbuf("f0_zs", [128, 1], F32)
    gsc = env.sbuf("f0_gsc", [128, 1], F32)
    gate_bf = env.sbuf("f0_gate", [128, 64], BF16)
    zg = env.sbuf("f0_zg", [128, 512], BF16)
    zgTt = env.sbuf("f0_zgT", [128, 512], BF16)
    xsl = env.sbuf("f0_xsl", [128, 512], F32)
    xob = env.sbuf("f0_xo", [128, 512], F32)
    psZz, psQ, psSc, psD = po.pf[0], po.pf[1], po.pf[2], po.pf[3]

    for ti in range(NTILE):
        if x_tiles is not None:
            xbuf = x_tiles[ti]
        else:
            xbuf = xw
            p.load(xbuf, x_in_dram[ti, :, :], queue="sync")
        emit_ln(env, po, xbuf, xnw)
        for c in range(32):
            emit_tp128(env, po, xnw, xnw.ap[:, c * 128:(c + 1) * 128],
                       xnTt, xnTt.ap[:, c * 128:(c + 1) * 128])
        for dm in range(32):
            p.op("tensor", lambda e, dm=dm: e.matmul(
                psZz.ap[:], xnTt.ap[:, dm * 128:(dm + 1) * 128],
                A_rhs.ap[:, dm * 512:(dm + 1) * 512],
                start=(dm == 0), stop=(dm == 31)),
                reads=[xnTt, A_rhs], writes=[psZz])
        p.op("scalar", lambda e: e.copy(out=z_bf.ap[:], in_=psZz.ap[:]),
             reads=[psZz], writes=[z_bf])
        for dm in range(32):
            p.op("tensor", lambda e, dm=dm: e.matmul(
                psQ.ap[0:16, 0:128], WqR.ap[:, dm * 16:(dm + 1) * 16],
                xnTt.ap[:, dm * 128:(dm + 1) * 128],
                start=(dm == 0), stop=(dm == 31)),
                reads=[xnTt, WqR], writes=[psQ])
        p.op("vector", lambda e: e.tensor_copy(out=qTf.ap[:], in_=psQ.ap[0:16, 0:128]),
             reads=[psQ], writes=[qTf])
        p.op("tensor", lambda e: e.matmul(psSc.ap[0:64, 0:128], keysT.ap[:],
                                          qTf.ap[:], start=True, stop=True),
             reads=[keysT, qTf], writes=[psSc])
        p.op("vector", lambda e: e.tensor_copy(out=scT.ap[:], in_=psSc.ap[0:64, 0:128]),
             reads=[psSc], writes=[scT])
        p.op("tensor", lambda e: e.transpose(prr.ap[0:128, 0:64], scT.ap[:],
                                             po.identf.ap[0:64, 0:64]),
             reads=[scT, po.identf], writes=[prr])
        p.op("vector", lambda e: e.tensor_copy(out=sc.ap[:], in_=prr.ap[0:128, 0:64]),
             reads=[prr], writes=[sc])
        p.op("vector", lambda e: e.max(out=m8a.ap[:], in_=sc.ap[:]),
             reads=[sc], writes=[m8a])
        p.op("vector", lambda e: e.tensor_scalar(
            out=ind8.ap[:], in0=sc.ap[:], scalar1=m8a.ap[:, 7:8], scalar2=None,
            op0=ALU.is_ge), reads=[sc, m8a], writes=[ind8])
        p.op("vector", lambda e: e.scalar_tensor_tensor(
            out=scm.ap[:], in0=ind8.ap[:], scalar=-1e30, in1=sc.ap[:],
            op0=ALU.mult, op1=ALU.add), reads=[ind8, sc], writes=[scm])
        p.op("vector", lambda e: e.max(out=m8b.ap[:], in_=scm.ap[:]),
             reads=[scm], writes=[m8b])
        p.op("vector", lambda e: e.tensor_scalar(
            out=kept.ap[:], in0=sc.ap[:], scalar1=m8b.ap[:, 7:8], scalar2=None,
            op0=ALU.is_ge), reads=[sc, m8b], writes=[kept])
        p.op("scalar", lambda e: e.mul(negm1.ap[:], m8a.ap[:, 0:1], -1.0),
             reads=[m8a], writes=[negm1])
        p.op("scalar", lambda e: e.activation(out=gexp.ap[:], in_=sc.ap[:],
                                              func=AF.Exp, bias=negm1.ap[:]),
             reads=[sc, negm1], writes=[gexp])
        p.op("vector", lambda e: e.tensor_tensor(out=gexp.ap[:], in0=gexp.ap[:],
                                                 in1=kept.ap[:], op=ALU.mult),
             reads=[gexp, kept], writes=[gexp])
        p.op("vector", lambda e: e.reduce_sum(zs.ap[:], gexp.ap[:], axis=AX.X),
             reads=[gexp], writes=[zs])
        p.op("vector", lambda e: e.reciprocal(out=gsc.ap[:], in_=zs.ap[:]),
             reads=[zs], writes=[gsc])
        p.op("vector", lambda e: e.tensor_scalar_mul(gsc.ap[:], gsc.ap[:], 2.0),
             reads=[gsc], writes=[gsc])
        p.op("vector", lambda e: e.tensor_scalar_mul(gexp.ap[:], gexp.ap[:], gsc.ap[:]),
             reads=[gexp, gsc], writes=[gexp])
        p.op("vector", lambda e: e.tensor_copy(out=gate_bf.ap[:], in_=gexp.ap[:]),
             reads=[gexp], writes=[gate_bf])
        if hasattr(env, "dbg_gate"):
            p.store(env.dbg_gate[ti, :, :], gexp)
            p.store(env.dbg_z[ti, :, :], z_bf)
            p.store(env.dbg_sc[ti, :, :], sc)
        p.op("vector", lambda e: e.tensor_tensor(
            out=zg.ap[:].rearrange("p (a r) -> p a r", a=64),
            in0=z_bf.ap[:].rearrange("p (a r) -> p a r", a=64),
            in1=gate_bf.ap[:].unsqueeze(2).to_broadcast([128, 64, 8]),
            op=ALU.mult), reads=[z_bf, gate_bf], writes=[zg])
        if hasattr(env, "dbg_zg"):
            p.store(env.dbg_zg[ti, :, :], zg)
        for i in range(4):
            emit_tp128(env, po, zg, zg.ap[:, i * 128:(i + 1) * 128],
                       zgTt, zgTt.ap[:, i * 128:(i + 1) * 128])
        if hasattr(env, "dbg_zgT"):
            p.store(env.dbg_zgT[ti, :, :], zgTt)
        for cb in range(8):
            for i in range(4):
                p.op("tensor", lambda e, i=i, cb=cb: e.matmul(
                    psD.ap[:], zgTt.ap[:, i * 128:(i + 1) * 128],
                    B_rhs.ap[:, (i * 8 + cb) * 512:(i * 8 + cb + 1) * 512],
                    start=(i == 0), stop=(i == 3)),
                    reads=[zgTt, B_rhs], writes=[psD])
            if x_tiles is not None:
                p.op("vector", lambda e, ti=ti, cb=cb: e.scalar_tensor_tensor(
                    out=xob.ap[:], in0=psD.ap[:], scalar=1.0,
                    in1=x_tiles[ti].ap[:, cb * 512:(cb + 1) * 512],
                    op0=ALU.mult, op1=ALU.add),
                    reads=[psD, x_tiles[ti]], writes=[xob])
            else:
                p.load(xsl, x_in_dram[ti, :, cb * 512:(cb + 1) * 512], queue="sync")
                p.op("vector", lambda e: e.scalar_tensor_tensor(
                    out=xob.ap[:], in0=psD.ap[:], scalar=1.0, in1=xsl.ap[:],
                    op0=ALU.mult, op1=ALU.add),
                    reads=[psD, xsl], writes=[xob])
            p.store(x_out_dram[ti, :, cb * 512:(cb + 1) * 512], xob)


def build_ffn0(env):
    po = Pools(env, n_pf=4, n_pt=2, ln=True, wstream=False, rope=False)
    po.identf = env.sbuf("sc_identf", [128, 128], F32)
    idf_in = env.dram_in("c_identf", [128, 128], F32)
    env.p.load(po.identf, idf_in)
    po.identf.frozen = True
    prr = env.psum("psrr", [128, 64], F32)
    p = env.p
    x_in = env.dram_in("x", [NTILE, 128, D])
    a_in = env.dram_in("Arhs", [128, 32 * 512])
    b_in = env.dram_in("Brhs", [128, 32 * 512])
    wqr_in = env.dram_in("WqR", [128, 32 * 16])
    keys_in = env.dram_in("keysT", [16, 64])
    x_out = env.dram_out("xo", [NTILE, 128, D])
    A = env.sbuf("A_rhs", [128, 32 * 512], BF16)
    Bb = env.sbuf("B_rhs", [128, 32 * 512], BF16)
    Wq = env.sbuf("WqR_b", [128, 32 * 16], BF16)
    ks = env.sbuf("keysT_b", [16, 64], F32)
    p.load(A, a_in, queue="gpsimd")
    p.load(Bb, b_in, queue="gpsimd")
    p.load(Wq, wqr_in, queue="gpsimd")
    p.load(ks, keys_in, queue="sync")
    for b in (A, Bb, Wq, ks):
        b.frozen = True
    emit_ffn0(env, po, x_in, x_out,
              {"A": A, "B": Bb, "WqR": Wq, "keysT": ks}, prr)


def build_pre0(env):
    po = Pools(env)
    x_in = env.dram_in("x", [NTILE, 128, D])
    wk = env.dram_in("wk", [32, 1, 128, 128])
    wv = env.dram_in("wv", [32, 1, 128, 128])
    xnT_out = env.dram_out("xnT", [32, 128, TOK], BF16)
    kT_out = env.dram_out("kT", [1, 128, TOK], BF16)
    v_out = env.dram_out("v", [NTILE, 128, 128], BF16)
    xnT = [env.sbuf(f"xnT{c}", [128, TOK], BF16) for c in range(32)]
    emit_ln_tp_phase(env, po, x_in, xnT, xnT_out=xnT_out)
    for c in xnT:
        c.frozen = True
    kT = [env.sbuf("kT0", [128, TOK], BF16)]
    emit_proj_feat(env, po, wk, xnT, kT, out_store=kT_out)
    vt = [env.sbuf(f"vt{t}", [128, 128], BF16) for t in range(NTILE)]
    emit_proj_tok(env, po, wv, xnT, ncolblk=1, colw=128,
                  out_tiles=vt, out_store=v_out)


_IDENT = np.eye(128, dtype=BF)
_IDENTF = np.eye(128, dtype=np.float32)


_HOOK = None


def run_launch(build_fn, in_maps, name="l", trace=False):
    env = Env(name)
    nc = env.build(build_fn)
    names = set(env._ins)
    for m in in_maps:
        if "c_ident" in names:
            m["c_ident"] = _IDENT
        if "c_identf" in names:
            m["c_identf"] = _IDENTF
    res = run_bass_kernel_spmd(nc, in_maps, core_ids=list(range(NCORES)),
                               trace=trace)
    if _HOOK is not None:
        _HOOK(name, res.results if not trace else res.results)
    return res.results if not trace else res


MBIG = 2.33e10


def _pm(a):
    a = np.asarray(a)
    if a.ndim == 2:
        return np.ascontiguousarray(a)
    if a.ndim == 3:
        assert a.shape[1] == 128, a.shape
        return np.ascontiguousarray(a.transpose(1, 0, 2).reshape(a.shape[1], -1))
    raise ValueError(a.shape)


def _row(v):
    return np.broadcast_to(np.asarray(v, np.float32)[None, :], (128, len(v))).copy()


def _gather_batch(res, key, b, axis=2):
    return np.concatenate([np.asarray(res[4 * b + i][key]) for i in range(4)],
                          axis=axis)


def _ctx_for_core(res, c):
    b = c // 4
    tc = c % 4
    ctx = np.stack([np.asarray(res[4 * b + j // 8]["ctx"][j % 8])
                    for j in range(32)])
    return np.ascontiguousarray(ctx[:, :, tc * 1024:(tc + 1) * 1024])


def kernel(**I):
    perm = np.asarray(I["L1_ffn_perm"]).astype(np.int64)
    x = np.asarray(I["L0_attn_a_hidden_states"])
    xp = np.ascontiguousarray(x[:, :, perm])
    shards = [np.ascontiguousarray(
        xp[c // 4, (c % 4) * TOK:(c % 4 + 1) * TOK].reshape(NTILE, 128, D))
        for c in range(NCORES)]

    slopes = alibi_slopes(32)
    cosS, sinS = rope_tables(np.arange(S), 128)
    cosyS, sinyS = yarn_tables(np.arange(S))
    qoffm = np.zeros((5, 128, 512), np.float32)
    qoffm[0] = np.broadcast_to(np.arange(512, dtype=np.float32)[None, :], (128, 512))
    for r in range(4):
        qoffm[r + 1] = qoffm[0] + MBIG * (
            np.arange(512)[None, :] < 128 * r + np.arange(128)[:, None])
    qoffm_pm = _pm(qoffm)
    mdiag = np.where(np.arange(128)[:, None] < np.arange(128)[None, :],
                     -428.0, 0.0).astype(np.float32)
    mvar = np.zeros((4, 128, 512), np.float32)
    for r in range(4):
        mvar[r] = -1e9 * (np.arange(512)[None, :] < 128 * r + np.arange(128)[:, None])
    mvar_pm = _pm(mvar)
    biasKQ = {}
    for h0 in (0, 8, 16, 24):
        t = np.zeros((128, 8 * 32 * 8), np.float32)
        p_ = np.arange(128, dtype=np.float32)
        for j in range(8):
            sl = slopes[h0 + j]
            for kt in range(32):
                for qb_ in range(8):
                    t[:, (j * 32 + kt) * 8 + qb_] = sl * (kt * 128 + p_ - qb_ * 512)
        biasKQ[h0] = t
    slp = {h0: _row(-slopes[h0:h0 + 8] * SQ128) for h0 in (0, 8, 16, 24)}
    percore_pos = [np.arange((c % 4) * TOK, (c % 4 + 1) * TOK) for c in range(NCORES)]

    Wq0 = np.asarray(I["L0_attn_a_Wq"])[perm, :]
    Wkv0 = np.asarray(I["L0_attn_a_Wkv"])[perm, :]
    Wo0 = np.asarray(I["L0_attn_a_Wo"])[:, perm]
    A_p = np.asarray(I["L0_ffn_A_w"])[:, perm]
    B_p = np.asarray(I["L0_ffn_B_w"])[perm, :]
    WqR_p = np.asarray(I["L0_ffn_Wq_w"])[:, perm]
    keys0 = np.asarray(I["L0_ffn_keys"])
    Wqb = np.asarray(I["L0_attn_b_Wq"])[:, perm]
    Wkb = np.asarray(I["L0_attn_b_Wk"])[:, perm]
    Wvb = np.asarray(I["L0_attn_b_Wv"])[:, perm]
    Wob = np.asarray(I["L0_attn_b_Wo"])[perm, :]
    hp1 = np.asarray(I["L0_attn_b_hash_proj1"])
    hb1 = np.asarray(I["L0_attn_b_hash_bias1"])
    hp2 = np.asarray(I["L0_attn_b_hash_proj2"])
    hb2 = np.asarray(I["L0_attn_b_hash_bias2"])
    qkvw = np.asarray(I["L1_attn_a_qkv_w"])[perm, :]
    qkvb = np.asarray(I["L1_attn_a_qkv_b"])
    outw = np.asarray(I["L1_attn_a_out_w"])[:, perm]
    outb = np.asarray(I["L1_attn_a_out_b"])[perm]
    wqd = np.asarray(I["L1_attn_b_w_q_down"])[perm, :]
    wqun = np.asarray(I["L1_attn_b_w_q_up_nope"])
    wqur = np.asarray(I["L1_attn_b_w_q_up_rope"])
    wkvd = np.asarray(I["L1_attn_b_w_kv_down"])[perm, :]
    wkrp = np.asarray(I["L1_attn_b_w_k_rope"])[perm, :]
    wuk = np.asarray(I["L1_attn_b_w_uk"])
    wuv = np.asarray(I["L1_attn_b_w_uv"])
    wout1b = np.asarray(I["L1_attn_b_w_out"])[:, perm]
    up = np.asarray(I["L1_ffn_up"])
    down = np.asarray(I["L1_ffn_down"])

    wk0_blk = block_lhsT(Wkv0[:, :128])
    wv0_rhs = block_rhs(Wkv0[:, 128:], nb=128)
    wq0_4 = Wq0.reshape(32, 128, 32, 128).transpose(0, 2, 1, 3)
    wo0_blk = block_rhs(Wo0)
    Arhs = _pm(A_p.T.reshape(32, 128, 512))
    Brhs = np.ascontiguousarray(
        B_p.T.reshape(4, 128, 8, 512).transpose(1, 0, 2, 3)).reshape(128, -1)
    WqRb = _pm(WqR_p.T.reshape(32, 128, 16))
    keysT = np.ascontiguousarray(keys0.T)
    wkb_blk = block_lhsT(Wkb.T)
    wvb_rhs = block_rhs(Wvb.T)
    wqb_4 = Wqb.T.reshape(32, 128, 32, 128).transpose(0, 2, 1, 3)
    wob_blk = block_rhs(Wob.T)
    hp1_pm = _pm(hp1)
    hp2_pm = _pm(hp2)
    hb1_pm = np.ascontiguousarray(hb1[:, 0, :].T)
    hb2_pm = np.ascontiguousarray(hb2[:, 0, :].T)
    wk1_blk = block_lhsT(qkvw[:, D:2 * D])
    wv1_rhs = block_rhs(qkvw[:, 2 * D:])
    wq1_4 = qkvw[:, :D].reshape(32, 128, 32, 128).transpose(0, 2, 1, 3)
    kb1 = np.ascontiguousarray(qkvb[D:2 * D].reshape(32, 128).T)
    vb1_row = _row(qkvb[2 * D:])
    qb1 = np.ascontiguousarray(qkvb[:D].reshape(32, 128).T)
    wo1_blk = block_rhs(outw)
    ob1_row = _row(outb)
    up_pm = _pm(up.reshape(16, 2, 128, 8, 128).transpose(2, 0, 1, 3, 4)
                .reshape(128, -1))
    dn_pm = _pm(down.reshape(16, 8, 128, 256).transpose(2, 0, 1, 3)
                .reshape(128, -1))
    wcq_blk = block_lhsT(wqd)
    wckv_blk = block_lhsT(wkvd)
    wkr_blk = block_lhsT(wkrp, mc=64)
    wqn_blk = block_lhsT(wqun)
    wqr_blk = block_lhsT(wqur, mc=64)
    wuk4 = wuk.reshape(32, 128, 4, 128).transpose(0, 2, 3, 1)
    wuv4 = wuv.reshape(32, 128, 4, 128).transpose(0, 2, 3, 1)
    wo1b_blk = block_rhs(wout1b)

    def wq_slice(w4, h0):
        return np.ascontiguousarray(
            w4[:, h0:h0 + 8].reshape(-1, 128, 128).transpose(1, 0, 2)
        ).reshape(128, -1)

    in1 = [{"x": shards[c], "wk": wk0_blk, "wv": wv0_rhs} for c in range(NCORES)]
    r1 = run_launch(build_pre0, in1, "pre0")
    in2 = []
    for c in range(NCORES):
        b, h0 = c // 4, 8 * (c % 4)
        in2.append({"xnf": _gather_batch(r1, "xnT", b),
                    "kTf": _gather_batch(r1, "kT", b),
                    "vf": _pm(_gather_batch(r1, "v", b, axis=0)),
                    "wq": wq_slice(wq0_4, h0),
                    "biasKQ": biasKQ[h0], "qoffm": qoffm_pm, "slopes": slp[h0]})
    r2 = run_launch(build_attn0, in2, "attn0")
    in3 = [{"ctxT": _ctx_for_core(r2, c), "wo": wo0_blk, "x": shards[c]}
           for c in range(NCORES)]
    r3 = run_launch(build_wo_res, in3, "wores0")
    in4 = [{"x": np.asarray(r3[c]["xo"]), "Arhs": Arhs, "Brhs": Brhs,
            "WqR": WqRb, "keysT": keysT} for c in range(NCORES)]
    r4 = run_launch(build_ffn0, in4, "ffn0a")

    in5 = []
    for c in range(NCORES):
        pos = percore_pos[c]
        in5.append({"x": np.asarray(r4[c]["xo"]), "wk": wkb_blk, "wv": wvb_rhs,
                    "cosk": cosS[:, pos], "sink": sinS[:, pos],
                    "hp1": hp1_pm, "hb1": hb1_pm, "hp2": hp2_pm, "hb2": hb2_pm})
    r5 = run_launch(build_pre_l0b, in5, "prel0b")
    in6 = []
    for c in range(NCORES):
        b, h0 = c // 4, 8 * (c % 4)
        g0 = h0 // 4
        vfull = _gather_batch(r5, "v", b, axis=0)
        in6.append({"xnf": _gather_batch(r5, "xnT", b),
                    "wq": np.ascontiguousarray(
                        wqb_4[:, h0:h0 + 8].transpose(0, 2, 1, 3)
                        .reshape(32, 128, -1)),
                    "cosq": cosS.astype(BF), "sinq": sinS.astype(BF),
                    "hp1": hp1_pm[:, h0 * 128:(h0 + 8) * 128],
                    "hb1": hb1_pm[:, h0:h0 + 8],
                    "hp2": hp2_pm[:, h0 * 128:(h0 + 8) * 128],
                    "hb2": hb2_pm[:, h0:h0 + 8],
                    "khf": _gather_batch(r5, "kh", b)[h0:h0 + 8],
                    "krf": _gather_batch(r5, "kT", b)[g0:g0 + 2],
                    "vf": _pm(vfull[:, :, g0 * 128:(g0 + 2) * 128]),
                    "mdiag": mdiag})
    r6 = run_launch(build_attn_l0b, in6, "attnl0b")
    in7 = [{"ctxT": _ctx_for_core(r6, c), "wo": wob_blk,
            "x": np.asarray(r4[c]["xo"])} for c in range(NCORES)]
    r7 = run_launch(build_wo_res, in7, "woresl0b")
    in8 = [{"x": np.asarray(r7[c]["xo"]), "Arhs": Arhs, "Brhs": Brhs,
            "WqR": WqRb, "keysT": keysT} for c in range(NCORES)]
    r8 = run_launch(build_ffn0, in8, "ffn0b")

    in9 = []
    for c in range(NCORES):
        pos = percore_pos[c]
        in9.append({"x": np.asarray(r8[c]["xo"]), "wk": wk1_blk, "wv": wv1_rhs,
                    "kb": kb1, "vbrow": vb1_row,
                    "cosk": cosS[:, pos], "sink": sinS[:, pos]})
    r9 = run_launch(build_pre_l1a, in9, "prel1a")
    in10 = []
    for c in range(NCORES):
        b, h0 = c // 4, 8 * (c % 4)
        vfull = _gather_batch(r9, "v", b, axis=0)
        vf = np.stack([_pm(vfull[:, :, (h0 + j) * 128:(h0 + j + 1) * 128])
                       for j in range(8)])
        in10.append({"xnf": _gather_batch(r9, "xnT", b),
                     "wq": wq_slice(wq1_4, h0),
                     "qb": qb1[:, h0:h0 + 8],
                     "cosq": cosS.astype(BF), "sinq": sinS.astype(BF),
                     "kTf": _gather_batch(r9, "kT", b)[h0:h0 + 8],
                     "vf": vf, "mvar": mvar_pm})
    r10 = run_launch(build_attn_l1a, in10, "attnl1a")
    in11 = [{"ctxT": _ctx_for_core(r10, c), "wo": wo1_blk,
             "x": np.asarray(r8[c]["xo"]), "bias": ob1_row}
            for c in range(NCORES)]
    r11 = run_launch(build_wo_res_bias, in11, "woresl1a")
    in12 = [{"x": np.asarray(r11[c]["xo"]), "up": up_pm, "dn": dn_pm}
            for c in range(NCORES)]
    r12 = run_launch(build_ffn1, in12, "ffn1a")

    in13 = []
    for c in range(NCORES):
        pos = percore_pos[c]
        cosy, siny = cosyS[:, pos], sinyS[:, pos]
        in13.append({"x": np.asarray(r12[c]["xo"]), "wcq": wcq_blk,
                     "wckv": wckv_blk, "wkr": wkr_blk,
                     "cosy": cosy, "siny": siny})
    r13 = run_launch(build_pre_l1b, in13, "prel1b")
    in14 = []
    for c in range(NCORES):
        b, h0 = c // 4, 8 * (c % 4)
        in14.append({"cqf": _gather_batch(r13, "cqT", b),
                     "ckvf": _gather_batch(r13, "ckvT", b),
                     "krf": _gather_batch(r13, "krT", b, axis=1),
                     "wqn": np.ascontiguousarray(
                         wqn_blk[:, h0:h0 + 8].transpose(2, 0, 1, 3)).reshape(128, -1),
                     "wqr": np.ascontiguousarray(
                         wqr_blk[:, h0:h0 + 8].transpose(2, 0, 1, 3)).reshape(128, -1),
                     "wuk": np.ascontiguousarray(
                         wuk4[h0:h0 + 8].transpose(2, 1, 0, 3)).reshape(128, -1),
                     "wuv": np.ascontiguousarray(
                         wuv4[h0:h0 + 8].transpose(2, 1, 0, 3)).reshape(128, -1),
                     "cosy": cosyS, "siny": sinyS, "mvar": mvar_pm})
    r14 = run_launch(build_attn_l1b, in14, "attnl1b")
    in15 = [{"ctxT": _ctx_for_core(r14, c), "wo": wo1b_blk,
             "x": np.asarray(r12[c]["xo"])} for c in range(NCORES)]
    r15 = run_launch(build_wo_res, in15, "woresl1b")
    in16 = [{"x": np.asarray(r15[c]["xo"]), "up": up_pm, "dn": dn_pm}
            for c in range(NCORES)]
    r16 = run_launch(build_ffn1, in16, "ffn1b")

    out = np.empty((B, S, D), np.float32)
    for c in range(NCORES):
        out[c // 4, (c % 4) * TOK:(c % 4 + 1) * TOK, :] = \
            np.asarray(r16[c]["xo"]).reshape(TOK, D)
    res = np.empty_like(out)
    res[:, :, perm] = out
    return res


def emit_hash(env, po, src_chunks, out_chunks, hp1, hb1, hp2, hb2, nh, seq,
              psA, psB_, out_dram=None, out_dram_bufs=None):
    p = env.p
    t1 = env.sbuf("hh_t1", [128, 512], BF16)
    x1h = env.sbuf("hh_x1", [128, 512], BF16)
    t2 = env.sbuf("hh_t2", [128, 512], BF16)
    t3 = env.sbuf("hh_t3", [128, 512], BF16)
    stg = env.sbuf("hh_stg", [128, 512], BF16) if out_dram is not None else None
    for h in range(nh):
        src = src_chunks(h)
        for tb in range(seq // 512):
            cols = slice(tb * 512, (tb + 1) * 512)
            p.op("tensor", lambda e, h=h, cols=cols, src=src: e.matmul(
                psA.ap[:], hp1.ap[:, h * 128:(h + 1) * 128], src.ap[:, cols],
                start=True, stop=True), reads=[hp1, src], writes=[psA])
            p.op("scalar", lambda e, h=h: e.activation(
                out=t1.ap[:], in_=psA.ap[:], func=AF.Silu,
                bias=hb1.ap[:, h:h + 1]), reads=[psA, hb1], writes=[t1])
            p.op("vector", lambda e, cols=cols, src=src: e.tensor_tensor(
                out=x1h.ap[:], in0=t1.ap[:], in1=src.ap[:, cols], op=ALU.add),
                reads=[t1, src], writes=[x1h])
            p.op("tensor", lambda e, h=h: e.matmul(
                psB_.ap[:], hp2.ap[:, h * 128:(h + 1) * 128], x1h.ap[:],
                start=True, stop=True), reads=[hp2, x1h], writes=[psB_])
            p.op("scalar", lambda e, h=h: e.activation(
                out=t2.ap[:], in_=psB_.ap[:], func=AF.Identity,
                bias=hb2.ap[:, h:h + 1]), reads=[psB_, hb2], writes=[t2])
            p.op("vector", lambda e: e.tensor_tensor(
                out=t3.ap[:], in0=t2.ap[:], in1=x1h.ap[:], op=ALU.add),
                reads=[t2, x1h], writes=[t3])
            if out_dram is None:
                p.op("scalar", lambda e, h=h, cols=cols: e.activation(
                    out=out_chunks[h].ap[:, cols], in_=t3.ap[:], func=AF.Sign),
                    reads=[t3], writes=[out_chunks[h]])
            else:
                p.op("scalar", lambda e: e.activation(
                    out=stg.ap[:], in_=t3.ap[:], func=AF.Sign),
                    reads=[t3], writes=[stg])
                p.store(out_dram[h, :, cols], stg,
                        dst_buf=None if out_dram_bufs is None else out_dram_bufs[h])


def build_pre_l0b(env):
    po = Pools(env, n_pf=4, n_pt=2, ln=True, wstream=True, rope=True)
    p = env.p
    x_in = env.dram_in("x", [NTILE, 128, D])
    wk = env.dram_in("wk", [32, 8, 128, 128])
    wv = env.dram_in("wv", [32, 2, 128, 512])
    cosk = env.dram_in("cosk", [128, TOK])
    sink = env.dram_in("sink", [128, TOK])
    hp1_in = env.dram_in("hp1", [128, 32 * 128])
    hb1_in = env.dram_in("hb1", [128, 32])
    hp2_in = env.dram_in("hp2", [128, 32 * 128])
    hb2_in = env.dram_in("hb2", [128, 32])
    xnT_out = env.dram_out("xnT", [32, 128, TOK], BF16)
    kT_out = env.dram_out("kT", [8, 128, TOK], BF16)
    kh_out = env.dram_out("kh", [32, 128, TOK], BF16)
    v_out = env.dram_out("v", [NTILE, 128, 1024], BF16)

    xnT = [env.sbuf(f"xnT{c}", [128, TOK], BF16) for c in range(32)]
    emit_ln_tp_phase(env, po, x_in, xnT, xnT_out=xnT_out)
    for c in xnT:
        c.frozen = True
    cos_b = env.sbuf("cos_b", [128, TOK], F32)
    sin_b = env.sbuf("sin_b", [128, TOK], F32)
    p.load(cos_b, cosk)
    p.load(sin_b, sink)
    cos_b.frozen = True
    sin_b.frozen = True
    kT = [env.sbuf(f"kTc{cgrp}", [128, TOK], BF16) for cgrp in range(8)]
    emit_proj_feat(env, po, wk, xnT, kT, evac="rope", rope=(cos_b, sin_b),
                   out_store=kT_out)
    for b in kT:
        b.frozen = True
    vt = [env.sbuf(f"vt{t}", [128, 1024], BF16) for t in range(NTILE)]
    emit_proj_tok(env, po, wv, xnT, ncolblk=2, colw=512, out_tiles=vt,
                  out_store=v_out, npsum=2)
    hp1 = env.sbuf("hp1b", [128, 32 * 128], BF16)
    hp2 = env.sbuf("hp2b", [128, 32 * 128], BF16)
    hb1 = env.sbuf("hb1b", [128, 32], F32)
    hb2 = env.sbuf("hb2b", [128, 32], F32)
    p.load(hp1, hp1_in, queue="gpsimd")
    p.load(hp2, hp2_in, queue="gpsimd")
    p.load(hb1, hb1_in)
    p.load(hb2, hb2_in)
    for b in (hp1, hp2, hb1, hb2):
        b.frozen = True
    emit_hash(env, po, lambda h: kT[h // 4], None, hp1, hb1, hp2, hb2, 32, TOK,
              po.pf[2], po.pf[3], out_dram=kh_out)


def build_attn_l0b(env):
    po = Pools(env, n_pf=0, n_pt=2, ln=False, wstream=False, rope=True)
    p = env.p
    xnf = env.dram_in("xnf", [32, 128, S], BF16)
    wq_in = env.dram_in("wq", [32, 128, NPAIR * 128])
    cosq = env.dram_in("cosq", [128, S], BF16)
    sinq = env.dram_in("sinq", [128, S], BF16)
    hp1_in = env.dram_in("hp1", [128, NPAIR * 128])
    hb1_in = env.dram_in("hb1", [128, NPAIR])
    hp2_in = env.dram_in("hp2", [128, NPAIR * 128])
    hb2_in = env.dram_in("hb2", [128, NPAIR])
    khf = env.dram_in("khf", [NPAIR, 128, S], BF16)
    krf = env.dram_in("krf", [2, 128, S], BF16)
    vf = env.dram_in("vf", [128, NT * 256], BF16)
    mdiag_in = env.dram_in("mdiag", [128, 128])
    idf_in = env.dram_in("c_identf", [128, 128], F32)
    ctx_out = env.dram_out("ctx", [NPAIR, 128, S], BF16)

    v_all = env.sbuf("v_all", [128, NT * 256], BF16)
    p.load(v_all, vf)
    identf = env.sbuf("identf_b", [128, 128], F32)
    p.load(identf, idf_in)
    identf.frozen = True
    mdiag = env.sbuf("mdiag_b", [128, 128], F32)
    p.load(mdiag, mdiag_in)
    causal01 = env.sbuf("caus01", [128, 128], BF16)
    p.op("vector", lambda e: e.tensor_scalar(
        out=causal01.ap[:], in0=mdiag.ap[:], scalar1=-1.0, scalar2=None,
        op0=ALU.is_ge), reads=[mdiag], writes=[causal01])
    for b in (v_all, mdiag, causal01):
        b.frozen = True

    qrd = env.dram_tmp("qrd", [NPAIR, 128, S], BF16)
    qhd = env.dram_tmp("qhd", [NPAIR, 128, S], BF16)
    qrd_b = [Buf(None) for _ in range(NPAIR)]
    qhd_b = [Buf(None) for _ in range(NPAIR)]
    env.push_scope()
    pfA = [env.psum(f"pa{i}", [128, 512], F32) for i in range(4)]
    qrT = [env.sbuf(f"qrT{j}", [128, S], BF16) for j in range(NPAIR)]
    cos_b = env.sbuf("cosb", [128, S], BF16)
    sin_b = env.sbuf("sinb", [128, S], BF16)
    p.load(cos_b, cosq)
    p.load(sin_b, sinq)
    hp1 = env.sbuf("hp1b", [128, NPAIR * 128], BF16)
    hp2 = env.sbuf("hp2b", [128, NPAIR * 128], BF16)
    hb1 = env.sbuf("hb1b", [128, NPAIR], F32)
    hb2 = env.sbuf("hb2b", [128, NPAIR], F32)
    p.load(hp1, hp1_in, queue="gpsimd")
    p.load(hp2, hp2_in, queue="gpsimd")
    p.load(hb1, hb1_in)
    p.load(hb2, hb2_in)
    for b in (cos_b, sin_b, hp1, hp2, hb1, hb2):
        b.frozen = True
    xtl = [env.sbuf(f"qp_xl{i}", [128, 8 * 512], BF16) for i in range(2)]
    xth = [env.sbuf(f"qp_xh{i}", [128, 8 * 512], BF16) for i in range(2)]
    wqb = [env.sbuf(f"wqb{i}", [128, NPAIR * 128], BF16) for i in range(2)]
    for grp in range(2):
        pj = range(grp * 4, grp * 4 + 4)
        for tb in range(8):
            for half in range(2):
                d0 = half * 16
                par = (tb * 2 + half) % 2
                xl, xh = xtl[par], xth[par]
                p.load(xl, xnf[d0:d0 + 8, :, tb * 512:(tb + 1) * 512]
                       .rearrange("a p m -> p a m"), queue="sync",
                       dst_ap=xl.ap[:].rearrange("p (a m) -> p a m", a=8))
                p.load(xh, xnf[d0 + 8:d0 + 16, :, tb * 512:(tb + 1) * 512]
                       .rearrange("a p m -> p a m"), queue="scalar",
                       dst_ap=xh.ap[:].rearrange("p (a m) -> p a m", a=8))
                for dm in range(d0, d0 + 16):
                    wb = wqb[dm % 2]
                    p.load(wb, wq_in[dm, :, :], queue="gpsimd")
                    src = xl if dm - d0 < 8 else xh
                    off = (dm - d0) % 8
                    for j in pj:
                        def mk(e, j=j, dm=dm, off=off, wb=wb, src=src):
                            return e.matmul(
                                pfA[j % 4].ap[:], wb.ap[:, j * 128:(j + 1) * 128],
                                src.ap[:, off * 512:(off + 1) * 512],
                                start=(dm == 0), stop=(dm == 31))
                        p.op("tensor", mk, reads=[wb, src], writes=[pfA[j % 4]])
            for j in pj:
                cols = slice(tb * 512, (tb + 1) * 512)
                emit_rope_evac(env, po, pfA[j % 4], qrT[j], cols, cos_b, sin_b)
    for b in qrT:
        b.frozen = True
    emit_hash(env, po, lambda j: qrT[j], None, hp1, hb1, hp2, hb2, NPAIR, S,
              pfA[0], pfA[1], out_dram=qhd, out_dram_bufs=qhd_b)
    for j in range(NPAIR):
        p.store(qrd[j, :, :], qrT[j], dst_buf=qrd_b[j])
    env.pop_scope()

    env.push_scope()
    pfC = [env.psum(f"pc{i}", [128, 512], F32) for i in range(6)]
    psSim2, psSc2 = pfC[0:2], pfC[2:4]
    psO, psZ = pfC[4], pfC[5]
    kh_p = [env.sbuf(f"kh_p{i}", [128, S], BF16) for i in range(2)]
    kr_p = [env.sbuf(f"kr_p{i}", [128, S], BF16) for i in range(2)]
    qh_p = [env.sbuf(f"qh_p{i}", [128, S], BF16) for i in range(2)]
    qr_p = [env.sbuf(f"qr_p{i}", [128, S], BF16) for i in range(2)]
    sim2 = [env.sbuf(f"sim{i}", [128, S], BF16) for i in range(2)]
    pm2 = [env.sbuf(f"pmb{i}", [128, S], BF16) for i in range(2)]
    junk = env.sbuf("junkb", [128, S], BF16)
    onesb = env.sbuf("onesb", [128, S], BF16)
    p.op("vector", lambda e: e.memset(onesb.ap[:], 1.0), writes=[onesb])
    onesb.frozen = True
    scanb = env.sbuf("scanb", [128, S], F32)
    e1 = env.sbuf("e1b", [128, S], BF16)
    tieok = env.sbuf("tieokb", [128, S], BF16)
    kept = env.sbuf("keptb", [128, S], BF16)
    ptb = [env.sbuf(f"ptb{i}", [128, 128], BF16) for i in range(2)]
    sv = [{n: env.sbuf(f"sv_{n}{i}", [128, 1], F32)
           for n in ("tbm0", "tbm1", "acc", "s2", "bias", "bias3", "tval",
                     "rr", "zc", "rzc")}
          for i in range(2)]
    zrow = env.sbuf("zrowb", [1, 128], F32)
    bz = env.sbuf("bzb", [128, 128], F32)
    onesr = env.sbuf("onesrb", [1, 128], F32)
    p.op("vector", lambda e: e.memset(onesr.ap[:], 1.0), writes=[onesr])
    onesr.frozen = True
    ctxs2 = [env.sbuf(f"ctxsb{i}", [128, 128], BF16) for i in range(2)]
    junkV = env.sbuf("junkV", [128, 768], BF16)
    STEPS = (128, 64, 32, 16, 8, 4, 2, 1)
    cA = env.sbuf("cstA", [128, 8], F32)
    cH = env.sbuf("cstH", [128, 8], F32)
    cB = env.sbuf("cstB", [128, 26], F32)
    cC = env.sbuf("cstC", [128, 26], F32)
    cT = env.sbuf("cstT", [128, 2], F32)
    for i, s in enumerate(STEPS):
        p.op("vector", lambda e, i=i, s=s: e.memset(
            cA.ap[:, i:i + 1], -2.0 * s + 130.5), writes=[cA])
        p.op("vector", lambda e, i=i, s=s: e.memset(
            cH.ap[:, i:i + 1], s / 2.0), writes=[cH])
    for qi in range(26):
        extq = (qi + 7) * 128
        p.op("vector", lambda e, qi=qi, extq=extq: e.memset(
            cB.ap[:, qi:qi + 1], float(extq) - 1639.5), writes=[cB])
        p.op("vector", lambda e, qi=qi, extq=extq: e.memset(
            cC.ap[:, qi:qi + 1], 820.0 - extq / 2.0), writes=[cC])
    p.op("vector", lambda e: e.memset(cT.ap[:, 0:1], -130.0), writes=[cT])
    p.op("vector", lambda e: e.memset(cT.ap[:, 1:2], -0.5), writes=[cT])
    for b in (cA, cH, cB, cC, cT):
        b.frozen = True

    def emit_ss(j, pj, qt):
        par = qt % 2
        simb, pmb = sim2[par], pm2[par]
        psA, psB2 = psSim2[par], psSc2[par]
        ext = (qt + 1) * 128
        nch = (ext + 511) // 512
        for ch in range(nch):
            n = min(512, ext - ch * 512)
            p.op("tensor", lambda e, qt=qt, ch=ch, n=n, psA=psA: e.matmul(
                psA.ap[:, 0:n], qh_p[pj].ap[:, qt * 128:(qt + 1) * 128],
                kh_p[pj].ap[:, ch * 512:ch * 512 + n], start=True, stop=True),
                reads=[qh_p[pj], kh_p[pj]], writes=[psA])
            p.op("scalar", lambda e, ch=ch, n=n, psA=psA, simb=simb: e.copy(
                out=simb.ap[:, ch * 512:ch * 512 + n], in_=psA.ap[:, 0:n]),
                reads=[psA], writes=[simb])
        p.op("vector", lambda e, qt=qt, simb=simb: e.tensor_tensor(
            out=simb.ap[:, qt * 128:(qt + 1) * 128],
            in0=simb.ap[:, qt * 128:(qt + 1) * 128], in1=mdiag.ap[:],
            op=ALU.add), reads=[simb, mdiag], writes=[simb])
        krb = kr_p[(j // 4) % 2]
        for ch in range(nch):
            n = min(512, ext - ch * 512)
            p.op("tensor", lambda e, qt=qt, ch=ch, n=n, psB2=psB2, krb=krb: e.matmul(
                psB2.ap[:, 0:n], qr_p[pj].ap[:, qt * 128:(qt + 1) * 128],
                krb.ap[:, ch * 512:ch * 512 + n], start=True, stop=True),
                reads=[qr_p[pj], krb], writes=[psB2])
            p.op("scalar", lambda e, ch=ch, n=n, psB2=psB2, pmb=pmb: e.activation(
                out=pmb.ap[:, ch * 512:ch * 512 + n], in_=psB2.ap[:, 0:n],
                func=AF.Exp, scale=1.0 / SQ128),
                reads=[psB2], writes=[pmb])

    def emit_search(qt):
        par = qt % 2
        simb = sim2[par]
        v = sv[par]
        ext = (qt + 1) * 128
        qi = qt - 6
        p.op("scalar", lambda e, v=v: e.mul(v["tbm0"].ap[:], cT.ap[:, 1:2], 0.0),
             reads=[cT], writes=[v["tbm0"]])
        cur, nxt = "tbm0", "tbm1"
        for i, step in enumerate(STEPS):
            p.op("scalar", lambda e, v=v, i=i, cur=cur: e.activation(
                out=v["bias"].ap[:], in_=v[cur].ap[:], func=AF.Identity,
                scale=-2.0, bias=cA.ap[:, i:i + 1]),
                reads=[v[cur], cA], writes=[v["bias"]])
            p.op("scalar", lambda e, v=v, i=i, cur=cur: e.activation(
                out=v["bias3"].ap[:], in_=v[cur].ap[:], func=AF.Identity,
                scale=1.0, bias=cH.ap[:, i:i + 1]),
                reads=[v[cur], cH], writes=[v["bias3"]])
            p.op("scalar", lambda e, v=v, ext=ext, simb=simb: e.activation(
                out=junk.ap[:, 0:ext], in_=simb.ap[:, 0:ext], func=AF.Sign,
                bias=v["bias"].ap[:], accum_out=v["acc"].ap[:]),
                reads=[simb, v["bias"]], writes=[junk, v["acc"]])
            p.op("scalar", lambda e, v=v, qi=qi: e.activation(
                out=v["s2"].ap[:], in_=v["acc"].ap[:], func=AF.Sign,
                bias=cB.ap[:, qi:qi + 1]),
                reads=[v["acc"], cB], writes=[v["s2"]])
            p.op("scalar", lambda e, v=v, step=step, nxt=nxt: e.activation(
                out=v[nxt].ap[:], in_=v["s2"].ap[:], func=AF.Identity,
                scale=step / 2.0, bias=v["bias3"].ap[:]),
                reads=[v["s2"], v["bias3"]], writes=[v[nxt]])
            cur, nxt = nxt, cur
        p.op("scalar", lambda e, v=v, cur=cur: e.activation(
            out=v["tval"].ap[:], in_=v[cur].ap[:], func=AF.Identity,
            scale=2.0, bias=cT.ap[:, 0:1]),
            reads=[v[cur], cT], writes=[v["tval"]])
        p.op("scalar", lambda e, v=v: e.activation(
            out=v["bias"].ap[:], in_=v["tval"].ap[:], func=AF.Identity,
            scale=-1.0, bias=cT.ap[:, 1:2]),
            reads=[v["tval"], cT], writes=[v["bias"]])
        p.op("scalar", lambda e, v=v, ext=ext, simb=simb: e.activation(
            out=junk.ap[:, 0:ext], in_=simb.ap[:, 0:ext], func=AF.Sign,
            bias=v["bias"].ap[:], accum_out=v["acc"].ap[:]),
            reads=[simb, v["bias"]], writes=[junk, v["acc"]])
        p.op("scalar", lambda e, v=v, qi=qi: e.activation(
            out=v["rr"].ap[:], in_=v["acc"].ap[:], func=AF.Identity,
            scale=-0.5, bias=cC.ap[:, qi:qi + 1]),
            reads=[v["acc"], cC], writes=[v["rr"]])

    def emit_post(j, qt):
        par = qt % 2
        simb, pmb = sim2[par], pm2[par]
        psB2, psSimb = psSc2[par], psSim2[par]
        v = sv[par]
        ctxs = ctxs2[par]
        ext = (qt + 1) * 128
        if qt >= 6:
            p.op("vector", lambda e: e.tensor_scalar(
                out=e1.ap[:, 0:ext], in0=simb.ap[:, 0:ext],
                scalar1=v["tval"].ap[:], scalar2=None, op0=ALU.is_equal),
                reads=[simb, v["tval"]], writes=[e1])
            p.op("vector", lambda e: e.tensor_tensor_scan(
                out=scanb.ap[:, 0:ext], data0=onesb.ap[:, 0:ext],
                data1=e1.ap[:, 0:ext], initial=0.0, op0=ALU.mult, op1=ALU.add),
                reads=[onesb, e1], writes=[scanb])
            p.op("vector", lambda e: e.scalar_tensor_tensor(
                out=tieok.ap[:, 0:ext], in0=scanb.ap[:, 0:ext],
                scalar=v["rr"].ap[:], in1=e1.ap[:, 0:ext],
                op0=ALU.is_le, op1=ALU.mult),
                reads=[scanb, v["rr"], e1], writes=[tieok])
            p.op("vector", lambda e: e.scalar_tensor_tensor(
                out=kept.ap[:, 0:ext], in0=simb.ap[:, 0:ext],
                scalar=v["tval"].ap[:], in1=tieok.ap[:, 0:ext],
                op0=ALU.is_gt, op1=ALU.max),
                reads=[simb, v["tval"], tieok], writes=[kept])
            p.op("vector", lambda e: e.scalar_tensor_tensor(
                out=pmb.ap[:, 0:ext], in0=pmb.ap[:, 0:ext], scalar=1.0,
                in1=kept.ap[:, 0:ext], op0=ALU.mult, op1=ALU.mult,
                accum_out=v["zc"].ap[:]),
                reads=[pmb, kept], writes=[pmb, v["zc"]])
        else:
            p.op("vector", lambda e: e.tensor_tensor(
                out=pmb.ap[:, qt * 128:ext], in0=pmb.ap[:, qt * 128:ext],
                in1=causal01.ap[:], op=ALU.mult),
                reads=[pmb, causal01], writes=[pmb])
            p.op("vector", lambda e: e.tensor_scalar(
                out=junkV.ap[:, 0:ext], in0=pmb.ap[:, 0:ext], scalar1=1.0,
                scalar2=0.0, op0=ALU.mult, op1=ALU.add,
                accum_out=v["zc"].ap[:]),
                reads=[pmb], writes=[junkV, v["zc"]])
        p.op("vector", lambda e: e.reciprocal(
            out=v["rzc"].ap[:], in_=v["zc"].ap[:]),
            reads=[v["zc"]], writes=[v["rzc"]])
        p.op("tensor", lambda e: e.matmul(
            psSimb.ap[0:1, 0:128], v["rzc"].ap[:], identf.ap[:],
            start=True, stop=True), reads=[v["rzc"], identf], writes=[psSimb])
        p.op("vector", lambda e: e.tensor_copy(
            out=zrow.ap[:], in_=psSimb.ap[0:1, 0:128]),
            reads=[psSimb], writes=[zrow])
        p.op("tensor", lambda e: e.matmul(
            psB2.ap[:, 0:128], onesr.ap[:], zrow.ap[:], start=True, stop=True),
            reads=[onesr, zrow], writes=[psB2])
        p.op("vector", lambda e: e.tensor_copy(
            out=bz.ap[:], in_=psB2.ap[:, 0:128]), reads=[psB2], writes=[bz])
        for kk in range(qt + 1):
            pt = po.next_tp()
            pb = ptb[kk % 2]
            p.op("tensor", lambda e, pt=pt, kk=kk: e.transpose(
                pt.ap[:], pmb.ap[:, kk * 128:(kk + 1) * 128], po.ident.ap[:]),
                reads=[pmb, po.ident], writes=[pt])
            if kk % 2 == 0:
                p.op("vector", lambda e, pt=pt, pb=pb: e.tensor_copy(
                    out=pb.ap[:], in_=pt.ap[:]), reads=[pt], writes=[pb])
            else:
                p.op("scalar", lambda e, pt=pt, pb=pb: e.copy(
                    out=pb.ap[:], in_=pt.ap[:]), reads=[pt], writes=[pb])
            vc0 = kk * 256 + (j // 4) * 128
            p.op("tensor", lambda e, pb=pb, kk=kk, vc0=vc0: e.matmul(
                psO.ap[:, 0:128], v_all.ap[:, vc0:vc0 + 128], pb.ap[:],
                start=(kk == 0), stop=(kk == qt)),
                reads=[v_all, pb], writes=[psO])
        p.op("vector", lambda e: e.tensor_tensor(
            out=ctxs.ap[:], in0=psO.ap[:, 0:128], in1=bz.ap[:], op=ALU.mult),
            reads=[psO, bz], writes=[ctxs])
        p.store(ctx_out[j, :, qt * 128:(qt + 1) * 128], ctxs, queue="sync")

    for j in range(NPAIR):
        pj = j % 2
        p.load(kh_p[pj], khf[j, :, :], queue="sync")
        if j % 4 == 0:
            p.load(kr_p[(j // 4) % 2], krf[j // 4, :, :], queue="scalar")
        p.load(qh_p[pj], qhd[j, :, :], src_buf=qhd_b[j], queue="sync")
        p.load(qr_p[pj], qrd[j, :, :], src_buf=qrd_b[j], queue="scalar")
        emit_ss(j, pj, 0)
        for qt in range(NT):
            if qt >= 1:
                emit_post(j, qt - 1)
            if qt + 1 < NT:
                emit_ss(j, pj, qt + 1)
            if qt >= 6:
                emit_search(qt)
        emit_post(j, NT - 1)
    env.pop_scope()


def build_pre_l1a(env):
    po = Pools(env, n_pf=4, n_pt=2, ln=True, wstream=True, rope=True)
    p = env.p
    x_in = env.dram_in("x", [NTILE, 128, D])
    wk = env.dram_in("wk", [32, 32, 128, 128])
    wv = env.dram_in("wv", [32, 8, 128, 512])
    kb_in = env.dram_in("kb", [128, 32])
    vb_in = env.dram_in("vbrow", [128, D])
    cosk = env.dram_in("cosk", [128, TOK])
    sink = env.dram_in("sink", [128, TOK])
    xnT_out = env.dram_out("xnT", [32, 128, TOK], BF16)
    kT_out = env.dram_out("kT", [32, 128, TOK], BF16)
    v_out = env.dram_out("v", [NTILE, 128, D], BF16)

    xnT = [env.sbuf(f"xnT{c}", [128, TOK], BF16) for c in range(32)]
    emit_ln_tp_phase(env, po, x_in, xnT, xnT_out=xnT_out)
    for c in xnT:
        c.frozen = True
    cos_b = env.sbuf("cos_b", [128, TOK], F32)
    sin_b = env.sbuf("sin_b", [128, TOK], F32)
    p.load(cos_b, cosk)
    p.load(sin_b, sink)
    kb = env.sbuf("kb_b", [128, 32], F32)
    p.load(kb, kb_in)
    vb = env.sbuf("vb_b", [128, D], F32)
    p.load(vb, vb_in)
    for b in (cos_b, sin_b, kb, vb):
        b.frozen = True
    kc2 = [env.sbuf(f"kc{i}", [128, TOK], BF16) for i in range(2)]
    for oc in range(32):
        kc = kc2[oc % 2]
        emit_proj_feat(env, po, wk, xnT, [kc], evac="rope",
                       rope=(cos_b, sin_b), oc_base=oc,
                       bias=lambda _oc, oc=oc: (kb, kb.ap[:, oc:oc + 1]))
        p.store(kT_out[oc, :, :], kc)
    vt = [env.sbuf(f"vt{t}", [128, 512], BF16) for t in range(2)]

    def evac_v(cb, ti, ps):
        vtb = vt[ti % 2]
        p.op("vector", lambda e, ps=ps, cb=cb, vtb=vtb: e.scalar_tensor_tensor(
            out=vtb.ap[:], in0=ps.ap[:], scalar=1.0,
            in1=vb.ap[:, cb * 512:(cb + 1) * 512], op0=ALU.mult, op1=ALU.add),
            reads=[ps, vb], writes=[vtb])
        p.store(v_out[ti, :, cb * 512:(cb + 1) * 512], vtb)

    emit_proj_tok(env, po, wv, xnT, ncolblk=8, evac_fn=evac_v, npsum=4)


def build_attn_l1a(env):
    po = Pools(env, n_pf=8, n_pt=0, ln=False, wstream=False, rope=True)
    p = env.p
    xnf = env.dram_in("xnf", [32, 128, S], BF16)
    wq = env.dram_in("wq", [128, 32 * NPAIR * 128])
    qb_in = env.dram_in("qb", [128, NPAIR])
    cosq = env.dram_in("cosq", [128, S], BF16)
    sinq = env.dram_in("sinq", [128, S], BF16)
    kTf = env.dram_in("kTf", [NPAIR, 128, S], BF16)
    vf = env.dram_in("vf", [NPAIR, 128, NT * 128], BF16)
    mvar = env.dram_in("mvar", [128, 4 * 512])
    ctx_out = env.dram_out("ctx", [NPAIR, 128, S], BF16)

    wq_all = env.sbuf("wq_all", [128, 32 * NPAIR * 128], BF16)
    p.load(wq_all, wq, queue="gpsimd")
    qb = env.sbuf("qb_b", [128, NPAIR], F32)
    p.load(qb, qb_in)
    cos_b = env.sbuf("cosb", [128, S], BF16)
    sin_b = env.sbuf("sinb", [128, S], BF16)
    p.load(cos_b, cosq)
    p.load(sin_b, sinq)
    mv = env.sbuf("mv_b", [128, 4 * 512], F32)
    p.load(mv, mvar)
    for b in (wq_all, qb, cos_b, sin_b, mv):
        b.frozen = True

    qT_own = [env.sbuf(f"qT{j}", [128, S], BF16) for j in range(NPAIR)]

    def evac_rope(j, tb, ps, dst, cols):
        emit_rope_evac(env, po, ps, dst, cols, cos_b, sin_b,
                       bias=(qb, qb.ap[:, j:j + 1]))

    emit_qproj_pairs(env, po, wq_all, xnf, qT_own, evac_fn=evac_rope)
    for b in qT_own:
        b.frozen = True

    k_p = env.sbuf("k_p", [128, S], BF16)
    v_p = env.sbuf("v_p", [128, NT * 128], BF16)
    tmp2 = [env.sbuf(f"at_tmp{i}", [128, 512], F32) for i in range(2)]
    cur = [-1]

    def kget(j):
        if cur[0] != j:
            p.load(k_p, kTf[j, :, :])
            p.load(v_p, vf[j, :, :])
            cur[0] = j
        return (k_p, 0)

    def exp_evac(j, qblk, kt, psS, ptile):
        tmp = tmp2[kt % 2]
        r = kt - 4 * qblk
        if 0 <= r <= 3:
            p.op("vector", lambda e, psS=psS, r=r, tmp=tmp: e.scalar_tensor_tensor(
                out=tmp.ap[:], in0=mv.ap[:, r * 512:(r + 1) * 512], scalar=1.0,
                in1=psS.ap[:], op0=ALU.mult, op1=ALU.add),
                reads=[mv, psS], writes=[tmp])
            p.op("scalar", lambda e, ptile=ptile, tmp=tmp: e.activation(
                out=ptile.ap[:], in_=tmp.ap[:], func=AF.Exp, scale=1.0 / SQ128),
                reads=[tmp], writes=[ptile])
        else:
            p.op("scalar", lambda e, psS=psS, ptile=ptile: e.activation(
                out=ptile.ap[:], in_=psS.ap[:], func=AF.Exp, scale=1.0 / SQ128),
                reads=[psS], writes=[ptile])

    emit_attn_pairs(env, po, qT_own, kget, lambda j: (v_p, 0), ctx_out,
                    exp_evac=exp_evac)


def build_ffn1(env):
    po = Pools(env, n_pf=4, n_pt=2, ln=True, wstream=False, rope=False)
    p = env.p
    x_in = env.dram_in("x", [NTILE, 128, D])
    up_in = env.dram_in("up", [128, 16 * 2 * 8 * 128])
    dn_in = env.dram_in("dn", [128, 16 * 8 * 256])
    x_out = env.dram_out("xo", [NTILE, 128, D])
    up_all = env.sbuf("up_all", [128, 16 * 2 * 8 * 128], BF16)
    dn_all = env.sbuf("dn_all", [128, 16 * 8 * 256], BF16)
    p.load(up_all, up_in, queue="gpsimd")
    p.load(dn_all, dn_in, queue="gpsimd")
    up_all.frozen = True
    dn_all.frozen = True
    xw = env.sbuf("f1_x", [128, D], F32)
    xnw = env.sbuf("f1_xn", [128, D], BF16)
    xnTt = env.sbuf("f1_xnT", [128, D], BF16)
    gT = env.sbuf("f1_gT", [128, 8 * 128], BF16)
    xob = env.sbuf("f1_xo", [128, 256], F32)
    psG, psD = po.pf[0], po.pf[1]
    for ti in range(NTILE):
        p.load(xw, x_in[ti, :, :], queue="sync")
        emit_ln(env, po, xw, xnw)
        for c in range(32):
            emit_tp128(env, po, xnw, xnw.ap[:, c * 128:(c + 1) * 128],
                       xnTt, xnTt.ap[:, c * 128:(c + 1) * 128])
        for h in range(16):
            for mc in range(8):
                for dc in range(2):
                    c0 = ((h * 2 + dc) * 8 + mc) * 128
                    p.op("tensor", lambda e, h=h, mc=mc, dc=dc, c0=c0: e.matmul(
                        psG.ap[:, 0:128], up_all.ap[:, c0:c0 + 128],
                        xnTt.ap[:, (2 * h + dc) * 128:(2 * h + dc + 1) * 128],
                        start=(dc == 0), stop=(dc == 1)),
                        reads=[up_all, xnTt], writes=[psG])
                p.op("scalar", lambda e, mc=mc: e.activation(
                    out=gT.ap[:, mc * 128:(mc + 1) * 128], in_=psG.ap[:, 0:128],
                    func=AF.Gelu), reads=[psG], writes=[gT])
            for mc in range(8):
                c0 = (h * 8 + mc) * 256
                p.op("tensor", lambda e, h=h, mc=mc, c0=c0: e.matmul(
                    psD.ap[:, 0:256], gT.ap[:, mc * 128:(mc + 1) * 128],
                    dn_all.ap[:, c0:c0 + 256],
                    start=(mc == 0), stop=(mc == 7)),
                    reads=[gT, dn_all], writes=[psD])
            p.op("vector", lambda e, h=h: e.scalar_tensor_tensor(
                out=xob.ap[:], in0=psD.ap[:, 0:256], scalar=1.0,
                in1=xw.ap[:, h * 256:(h + 1) * 256], op0=ALU.mult, op1=ALU.add),
                reads=[psD, xw], writes=[xob])
            p.store(x_out[ti, :, h * 256:(h + 1) * 256], xob)


def build_pre_l1b(env):
    po = Pools(env, n_pf=4, n_pt=2, ln=True, wstream=True, rope=True)
    p = env.p
    x_in = env.dram_in("x", [NTILE, 128, D])
    wcq = env.dram_in("wcq", [32, 12, 128, 128])
    wckv = env.dram_in("wckv", [32, 4, 128, 128])
    wkr = env.dram_in("wkr", [32, 1, 128, 64])
    cosy = env.dram_in("cosy", [64, TOK])
    siny = env.dram_in("siny", [64, TOK])
    xnT_out = env.dram_out("xnT", [32, 128, TOK], BF16)
    cq_out = env.dram_out("cqT", [12, 128, TOK], BF16)
    ckv_out = env.dram_out("ckvT", [4, 128, TOK], BF16)
    kr_out = env.dram_out("krT", [64, TOK], BF16)

    xnT = [env.sbuf(f"xnT{c}", [128, TOK], BF16) for c in range(32)]
    emit_ln_tp_phase(env, po, x_in, xnT, xnT_out=xnT_out)
    for c in xnT:
        c.frozen = True
    cqT = [env.sbuf(f"cqT{i}", [128, TOK], BF16) for i in range(12)]
    emit_proj_feat(env, po, wcq, xnT, cqT, out_store=cq_out)
    ckvT = [env.sbuf(f"ckvT{i}", [128, TOK], BF16) for i in range(4)]
    emit_proj_feat(env, po, wckv, xnT, ckvT, out_store=ckv_out)
    cos_b = env.sbuf("cosy_b", [64, TOK], F32)
    sin_b = env.sbuf("siny_b", [64, TOK], F32)
    p.load(cos_b, cosy)
    p.load(sin_b, siny)
    cos_b.frozen = True
    sin_b.frozen = True
    krc = env.sbuf("krc", [64, TOK], BF16)
    ps = po.pf[0]
    for tb in range(TOK // 512):
        for dm in range(32):
            wb = po.next_wb()
            p.load(wb, wkr[dm, 0, :, 0:64], queue="gpsimd", dst_ap=wb.ap[:, 0:64])
            p.op("tensor", lambda e, dm=dm, tb=tb, wb=wb: e.matmul(
                ps.ap[0:64, :], wb.ap[:, 0:64],
                xnT[dm].ap[:, tb * 512:(tb + 1) * 512],
                start=(dm == 0), stop=(dm == 31)),
                reads=[wb, xnT[dm]], writes=[ps])
        emit_rope_evac64(env, po, ps, krc, slice(tb * 512, (tb + 1) * 512),
                         cos_b, sin_b, 0)
    p.store(kr_out, krc)


def emit_rope_evac64(env, po, ps, ob, cols, cos_b, sin_b, row0, bias=None):
    p = env.p
    t1, t2 = po.rope_t1, po.rope_t2
    src = ps
    sr = row0
    if bias is not None:
        bbuf, bap = bias
        tb = po.rope_tb
        p.op("scalar", lambda e: e.activation(out=tb.ap[sr:sr + 64, 0:512],
                                              in_=ps.ap[sr:sr + 64, 0:512],
                                              func=AF.Identity, bias=bap),
             reads=[ps, bbuf], writes=[tb])
        src = tb
    ncol = cols.stop - cols.start
    sa = src.ap[sr:sr + 32, 0:ncol]
    sb_ = src.ap[sr + 32:sr + 64, 0:ncol]
    ca = cos_b.ap[row0:row0 + 32, cols]
    sn = sin_b.ap[row0 + 32:row0 + 64, cols]
    cb = cos_b.ap[row0 + 32:row0 + 64, cols]
    snb = sin_b.ap[row0:row0 + 32, cols]
    p.op("vector", lambda e: e.tensor_tensor(out=t1.ap[0:32, 0:ncol], in0=sa, in1=ca, op=ALU.mult),
         reads=[src, cos_b], writes=[t1])
    p.op("vector", lambda e: e.tensor_tensor(out=t2.ap[0:32, 0:ncol], in0=sb_, in1=sn, op=ALU.mult),
         reads=[src, sin_b], writes=[t2])
    p.op("vector", lambda e: e.tensor_tensor(out=ob.ap[row0:row0 + 32, cols], in0=t1.ap[0:32, 0:ncol],
                                             in1=t2.ap[0:32, 0:ncol], op=ALU.subtract),
         reads=[t1, t2], writes=[ob])
    p.op("vector", lambda e: e.tensor_tensor(out=t1.ap[0:32, 0:ncol], in0=sb_, in1=cb, op=ALU.mult),
         reads=[src, cos_b], writes=[t1])
    p.op("vector", lambda e: e.tensor_tensor(out=t2.ap[0:32, 0:ncol], in0=sa, in1=snb, op=ALU.mult),
         reads=[src, sin_b], writes=[t2])
    p.op("vector", lambda e: e.tensor_tensor(out=ob.ap[row0 + 32:row0 + 64, cols], in0=t1.ap[0:32, 0:ncol],
                                             in1=t2.ap[0:32, 0:ncol], op=ALU.add),
         reads=[t1, t2], writes=[ob])


def build_attn_l1b(env):
    po = Pools(env, n_pf=7, n_pt=0, ln=False, wstream=False, rope=True)
    p = env.p
    cqf = env.dram_in("cqf", [12, 128, S], BF16)
    ckvf = env.dram_in("ckvf", [4, 128, S], BF16)
    krf = env.dram_in("krf", [64, S], BF16)
    wqn = env.dram_in("wqn", [128, 12 * NPAIR * 128])
    wqr = env.dram_in("wqr", [128, 12 * NPAIR * 64])
    wuk = env.dram_in("wuk", [128, 4 * NPAIR * 128])
    wuv = env.dram_in("wuv", [128, 4 * NPAIR * 128])
    cosy = env.dram_in("cosy", [64, S])
    siny = env.dram_in("siny", [64, S])
    mvar = env.dram_in("mvar", [128, 4 * 512])
    ctx_out = env.dram_out("ctx", [NPAIR, 128, S], BF16)

    ckvT = [env.sbuf(f"ckvT{i}", [128, S], BF16) for i in range(4)]
    for i in range(4):
        p.load(ckvT[i], ckvf[i, :, :])
        ckvT[i].frozen = True
    krT = env.sbuf("krT_b", [64, S], BF16)
    p.load(krT, krf)
    krT.frozen = True
    w_qn = env.sbuf("wqn_b", [128, 12 * NPAIR * 128], BF16)
    w_qr = env.sbuf("wqr_b", [128, 12 * NPAIR * 64], BF16)
    w_uk = env.sbuf("wuk_b", [128, 4 * NPAIR * 128], BF16)
    w_uv = env.sbuf("wuv_b", [128, 4 * NPAIR * 128], BF16)
    p.load(w_qn, wqn, queue="gpsimd")
    p.load(w_qr, wqr, queue="gpsimd")
    p.load(w_uk, wuk, queue="gpsimd")
    p.load(w_uv, wuv, queue="gpsimd")
    cos_b = env.sbuf("cosy_b", [64, S], F32)
    sin_b = env.sbuf("siny_b", [64, S], F32)
    p.load(cos_b, cosy)
    p.load(sin_b, siny)
    mv = env.sbuf("mv_b", [128, 4 * 512], F32)
    p.load(mv, mvar)
    for b in (w_qn, w_qr, w_uk, w_uv, cos_b, sin_b, mv):
        b.frozen = True

    cqt = env.sbuf("cq_x", [128, 12 * 512], BF16)
    qnT = env.sbuf("qnT", [128, S], BF16)
    qroT = env.sbuf("qroT", [64, S], BF16)
    k_p = env.sbuf("k_p", [128, S], BF16)
    v_p = env.sbuf("v_p", [128, NT * 128], BF16)
    tmp2 = [env.sbuf(f"at_tmp{i}", [128, 512], F32) for i in range(2)]
    SQ192 = float(np.sqrt(192.0))

    for j in range(NPAIR):
        for tb in range(8):
            p.load(cqt, cqf[:, :, tb * 512:(tb + 1) * 512]
                   .rearrange("a p m -> p a m"), queue="sync",
                   dst_ap=cqt.ap[:].rearrange("p (a m) -> p a m", a=12))
            for dm in range(12):
                c0 = (dm * NPAIR + j) * 128
                p.op("tensor", lambda e, dm=dm, c0=c0: e.matmul(
                    po.pf[0].ap[:], w_qn.ap[:, c0:c0 + 128],
                    cqt.ap[:, dm * 512:(dm + 1) * 512],
                    start=(dm == 0), stop=(dm == 11)),
                    reads=[w_qn, cqt], writes=[po.pf[0]])
                c1 = (dm * NPAIR + j) * 64
                p.op("tensor", lambda e, dm=dm, c1=c1: e.matmul(
                    po.pf[1].ap[0:64, :], w_qr.ap[:, c1:c1 + 64],
                    cqt.ap[:, dm * 512:(dm + 1) * 512],
                    start=(dm == 0), stop=(dm == 11)),
                    reads=[w_qr, cqt], writes=[po.pf[1]])
            cols = slice(tb * 512, (tb + 1) * 512)
            p.op("scalar", lambda e, cols=cols: e.copy(
                out=qnT.ap[:, cols], in_=po.pf[0].ap[:]),
                reads=[po.pf[0]], writes=[qnT])
            emit_rope_evac64(env, po, po.pf[1], qroT, cols, cos_b, sin_b, 0)
        for tb in range(8):
            for i in range(4):
                c0 = (i * NPAIR + j) * 128
                p.op("tensor", lambda e, i=i, c0=c0, tb=tb: e.matmul(
                    po.pf[2].ap[:], w_uk.ap[:, c0:c0 + 128],
                    ckvT[i].ap[:, tb * 512:(tb + 1) * 512],
                    start=(i == 0), stop=(i == 3)),
                    reads=[w_uk, ckvT[i]], writes=[po.pf[2]])
            p.op("scalar", lambda e, tb=tb: e.copy(
                out=k_p.ap[:, tb * 512:(tb + 1) * 512], in_=po.pf[2].ap[:]),
                reads=[po.pf[2]], writes=[k_p])
        for kk in range(NT):
            for i in range(4):
                c0 = (i * NPAIR + j) * 128
                p.op("tensor", lambda e, i=i, c0=c0, kk=kk: e.matmul(
                    po.pf[2].ap[:, 0:128],
                    ckvT[i].ap[:, kk * 128:(kk + 1) * 128],
                    w_uv.ap[:, c0:c0 + 128],
                    start=(i == 0), stop=(i == 3)),
                    reads=[w_uv, ckvT[i]], writes=[po.pf[2]])
            p.op("scalar", lambda e, kk=kk: e.copy(
                out=v_p.ap[:, kk * 128:(kk + 1) * 128], in_=po.pf[2].ap[:, 0:128]),
                reads=[po.pf[2]], writes=[v_p])
        def score_extra(jj, qblk, kt, psS):
            p.op("tensor", lambda e, qblk=qblk, kt=kt, psS=psS: e.matmul(
                psS.ap[:], krT.ap[:, kt * 128:(kt + 1) * 128],
                qroT.ap[:, qblk * 512:(qblk + 1) * 512],
                start=False, stop=True),
                reads=[krT, qroT], writes=[psS])

        def exp_evac(jj, qblk, kt, psS, ptile):
            tmp = tmp2[kt % 2]
            r = kt - 4 * qblk
            if 0 <= r <= 3:
                p.op("vector", lambda e, psS=psS, r=r, tmp=tmp: e.scalar_tensor_tensor(
                    out=tmp.ap[:], in0=mv.ap[:, r * 512:(r + 1) * 512],
                    scalar=1.0, in1=psS.ap[:], op0=ALU.mult, op1=ALU.add),
                    reads=[mv, psS], writes=[tmp])
                p.op("scalar", lambda e, ptile=ptile, tmp=tmp: e.activation(
                    out=ptile.ap[:], in_=tmp.ap[:], func=AF.Exp, scale=1.0 / SQ192),
                    reads=[tmp], writes=[ptile])
            else:
                p.op("scalar", lambda e, psS=psS, ptile=ptile: e.activation(
                    out=ptile.ap[:], in_=psS.ap[:], func=AF.Exp, scale=1.0 / SQ192),
                    reads=[psS], writes=[ptile])

        emit_attn_pairs_single(env, po, j, qnT, k_p, v_p, ctx_out,
                               exp_evac=exp_evac, score_extra=score_extra)


def emit_attn_pairs_single(env, po, j, qT, kbuf, vbuf, ctx_out,
                           exp_evac, score_extra=None, seq=S):
    p = env.p
    nqb = seq // 512
    if not hasattr(env, "_aps_bufs"):
        pts = [env.sbuf(f"as_p{i}", [128, 512], BF16) for i in range(4)]
        zrow = env.sbuf("as_z", [1, 512], F32)
        onesr = env.sbuf("as_or", [1, 128], F32)
        bz = env.sbuf("as_bz", [128, 512], F32)
        ctxs = env.sbuf("as_ctx", [128, 512], BF16)
        p.op("vector", lambda e: e.memset(onesr.ap[:], 1.0), writes=[onesr])
        onesr.frozen = True
        env._aps_bufs = (pts, zrow, onesr, bz, ctxs)
    pts, zrow, onesr, bz, ctxs = env._aps_bufs
    psZ, psO, psB = po.pf[4], po.pf[5], po.pf[6]
    for qblk in range(nqb):
        nkt = 4 * qblk + 4
        for kt in range(nkt):
            psS = po.pf[kt % 4]
            def mm_s(e, qblk=qblk, kt=kt, psS=psS):
                return e.matmul(psS.ap[:], kbuf.ap[:, kt * 128:(kt + 1) * 128],
                                qT.ap[:, qblk * 512:(qblk + 1) * 512],
                                start=True, stop=(score_extra is None))
            p.op("tensor", mm_s, reads=[kbuf, qT], writes=[psS])
            if score_extra is not None:
                score_extra(j, qblk, kt, psS)
            ptile = pts[kt % 4]
            exp_evac(j, qblk, kt, psS, ptile)
            p.op("tensor", lambda e, kt=kt, nkt=nkt, ptile=ptile: e.matmul(
                psZ.ap[0:1, :], po.ones1.ap[:], ptile.ap[:],
                start=(kt == 0), stop=(kt == nkt - 1)),
                reads=[po.ones1, ptile], writes=[psZ])
            p.op("tensor", lambda e, kt=kt, nkt=nkt, ptile=ptile: e.matmul(
                psO.ap[:], vbuf.ap[:, kt * 128:(kt + 1) * 128], ptile.ap[:],
                start=(kt == 0), stop=(kt == nkt - 1)),
                reads=[vbuf, ptile], writes=[psO])
        p.op("vector", lambda e: e.tensor_copy(out=zrow.ap[:], in_=psZ.ap[0:1, :]),
             reads=[psZ], writes=[zrow])
        p.op("vector", lambda e: e.reciprocal(out=zrow.ap[:], in_=zrow.ap[:]),
             reads=[zrow], writes=[zrow])
        p.op("tensor", lambda e: e.matmul(psB.ap[:], onesr.ap[:], zrow.ap[:],
                                          start=True, stop=True),
             reads=[onesr, zrow], writes=[psB])
        p.op("vector", lambda e: e.tensor_copy(out=bz.ap[:], in_=psB.ap[:]),
             reads=[psB], writes=[bz])
        p.op("vector", lambda e: e.tensor_tensor(out=ctxs.ap[:], in0=psO.ap[:],
                                                 in1=bz.ap[:], op=ALU.mult),
             reads=[psO, bz], writes=[ctxs])
        p.store(ctx_out[j, :, qblk * 512:(qblk + 1) * 512], ctxs)


# version 05
# speedup vs baseline: 1.3393x; 1.3393x over previous
import numpy as np
import ml_dtypes
import concourse.bass as bass
import concourse.mybir as mybir
from concourse.bass_utils import run_bass_kernel_spmd
from contextlib import ExitStack

F32 = mybir.dt.float32
BF16 = mybir.dt.bfloat16
AF = mybir.ActivationFunctionType
ALU = mybir.AluOpType
AX = mybir.AxisListType

NCORES = 8
B = 2
S = 4096
D = 4096
NT = S // 128
TOK = 1024
NTILE = TOK // 128
BF = ml_dtypes.bfloat16
NPAIR = 8

EPS = 1e-5


class Buf:
    __slots__ = ("ap", "wtok", "rtoks", "frozen")

    def __init__(self, ap):
        self.ap = ap
        self.wtok = None
        self.rtoks = []
        self.frozen = False


class Prog:
    ENGS = ("sync", "vector", "scalar", "tensor", "gpsimd")

    NDMA = 8

    def __init__(self, nc):
        self.nc = nc
        self.qs = {e: [] for e in self.ENGS}
        self.sem_names = (["v", "s", "t", "g", "y"]
                          + [f"d{q}{i}" for q in "SPA" for i in range(self.NDMA)])
        self.sem_cnt = {n: 0 for n in self.sem_names}
        self.sems = {}
        self.wait_cache = {e: {} for e in self.ENGS}
        self.eng_sem = {"sync": "y", "gpsimd": "g", "vector": "v",
                        "scalar": "s", "tensor": "t"}
        self.dma_sem = {"sync": "S", "gpsimd": "P", "scalar": "A"}
        self.dma_idx = {"S": 0, "P": 0, "A": 0}
        self.barrier_toks = None

    def _emit_wait(self, eng_name, tok):
        if tok is None:
            return
        sem, val = tok
        if eng_name == "tensor" and sem == "t":
            return
        wc = self.wait_cache[eng_name]
        if wc.get(sem, 0) >= val:
            return
        wc[sem] = val
        self.qs[eng_name].append(("wait", sem, val))

    def _next(self, sem, step):
        self.sem_cnt[sem] += step
        return (sem, self.sem_cnt[sem])

    def op(self, eng_name, build, reads=(), writes=(), inc=True, dma=False,
           extra_waits=()):
        reads = [b for b in reads if b is not None]
        writes = [b for b in writes if b is not None]
        for b in reads:
            self._emit_wait(eng_name, b.wtok)
        for b in writes:
            self._emit_wait(eng_name, b.wtok)
            for t in b.rtoks:
                self._emit_wait(eng_name, t)
        for t in extra_waits:
            self._emit_wait(eng_name, t)
        if self.barrier_toks is not None:
            for b in writes:
                if b.wtok is None:
                    for t in self.barrier_toks:
                        self._emit_wait(eng_name, t)
                    break
        tok = None
        if inc:
            if dma:
                q = self.dma_sem[eng_name]
                i = self.dma_idx[q]
                self.dma_idx[q] = (i + 1) % self.NDMA
                tok = self._next(f"d{q}{i}", 16)
            else:
                tok = self._next(self.eng_sem[eng_name], 1)
        self.qs[eng_name].append(("inst", build, tok))
        if tok is not None:
            for b in reads:
                if not b.frozen:
                    b.rtoks.append(tok)
            for b in writes:
                b.rtoks = []
                b.wtok = tok
        return tok

    def load(self, dst, src_ap, queue="sync", dst_ap=None, src_buf=None):
        ap = dst.ap[:] if dst_ap is None else dst_ap
        return self.op(queue, lambda e: e.dma_start(out=ap, in_=src_ap),
                       reads=[src_buf] if src_buf is not None else (),
                       writes=[dst], dma=True)

    def store(self, dst_ap, src, queue="scalar", src_ap=None, dst_buf=None):
        ap = src.ap[:] if src_ap is None else src_ap
        return self.op(queue, lambda e: e.dma_start(out=dst_ap, in_=ap),
                       reads=[src],
                       writes=[dst_buf] if dst_buf is not None else (), dma=True)

    def all_tokens(self):
        return [(n, self.sem_cnt[n]) for n in self.sem_names if self.sem_cnt[n] > 0]

    def final_wait(self):
        toks = self.all_tokens()
        self.op("sync", lambda e: e.nop(), extra_waits=toks, inc=False)

    def finalize(self, block):
        def mk(eng_name):
            def body(eng):
                for item in self.qs[eng_name]:
                    if item[0] == "wait":
                        eng.wait_ge(self.sems[item[1]], item[2])
                    else:
                        inst = item[1](eng)
                        tok = item[2]
                        if tok is not None:
                            step = 16 if tok[0].startswith("d") else 1
                            inst.then_inc(self.sems[tok[0]], step)
            return body
        for e, deco in (("sync", block.sync), ("vector", block.vector),
                        ("scalar", block.scalar), ("tensor", block.tensor),
                        ("gpsimd", block.gpsimd)):
            if self.qs[e]:
                deco(mk(e))


class Env:

    def __init__(self, name):
        self.nc = bass.Bass("TRN2", target_bir_lowering=False)
        self.p = Prog(self.nc)
        self.ctx = ExitStack()
        self.scopes = []
        self._ins = {}
        self._outs = {}
        self.name = name

    def push_scope(self):
        self.scopes.append(ExitStack())

    def pop_scope(self):
        self.scopes.pop().close()
        self.p.barrier_toks = list(self.p.all_tokens())

    def dram_in(self, name, shape, dtype=F32):
        h = self.nc.dram_tensor(name, list(shape), dtype, kind="ExternalInput")
        self._ins[name] = h
        return h.ap()

    def dram_tmp(self, name, shape, dtype=F32):
        h = self.nc.dram_tensor(name, list(shape), dtype, kind="Internal")
        return h.ap()

    def dram_out(self, name, shape, dtype=F32):
        h = self.nc.dram_tensor(name, list(shape), dtype, kind="ExternalOutput")
        self._outs[name] = h
        return h.ap()

    def sbuf(self, name, shape, dtype):
        stack = self.scopes[-1] if self.scopes else self.ctx
        h = stack.enter_context(self.nc.sbuf_tensor(name, list(shape), dtype))
        return Buf(h)

    def psum(self, name, shape, dtype=F32):
        stack = self.scopes[-1] if self.scopes else self.ctx
        h = stack.enter_context(self.nc.psum_tensor(name, list(shape), dtype))
        return Buf(h)

    def build(self, emit_fn):
        with self.ctx:
            for n in self.p.sem_names:
                self.p.sems[n] = self.ctx.enter_context(self.nc.semaphore(n))
            block = self.ctx.enter_context(self.nc.Block())
            emit_fn(self)
            self.p.final_wait()
            self.p.finalize(block)
        return self.nc


def block_lhsT(w, kc=128, mc=128):
    K, M = w.shape
    return np.ascontiguousarray(
        w.reshape(K // kc, kc, M // mc, mc).transpose(0, 2, 1, 3))


def block_rhs(w, kc=128, nb=512):
    K, N = w.shape
    return np.ascontiguousarray(
        w.reshape(K // kc, kc, N // nb, nb).transpose(0, 2, 1, 3))


def rope_tables(positions, dim, base=10000.0):
    half = dim // 2
    inv = 1.0 / base ** (np.arange(half, dtype=np.float64) / half)
    ang = inv[:, None] * np.asarray(positions, np.float64)[None, :]
    cos = np.cos(ang)
    sin = np.sin(ang)
    cos2 = np.concatenate([cos, cos], 0).astype(np.float32)
    sin2 = np.concatenate([sin, sin], 0).astype(np.float32)
    return cos2, sin2


def yarn_tables(positions):
    DR = 64
    BASE, FACTOR, BFAST, BSLOW, OLD_CTX = 10000.0, 40.0, 32.0, 1.0, 4096.0
    freqs = 1.0 / BASE ** (np.arange(0, DR, 2, dtype=np.float64) / DR)
    wavelengths = 2.0 * np.pi / freqs
    ramp = np.clip((wavelengths / OLD_CTX - BSLOW) / (BFAST - BSLOW), 0.0, 1.0)
    scale = 1.0 - ramp + ramp * FACTOR
    inv = freqs / scale
    ang = inv[:, None] * np.asarray(positions, np.float64)[None, :]
    cos = np.cos(ang)
    sin = np.sin(ang)
    cos2 = np.concatenate([cos, cos], 0).astype(np.float32)
    sin2 = np.concatenate([sin, sin], 0).astype(np.float32)
    return cos2, sin2


def alibi_slopes(n_heads=32):
    import math
    closest = 2 ** int(math.floor(math.log2(n_heads)))
    base = 2.0 ** (-(2.0 ** (-(math.log2(closest) - 3))))
    slopes = [base ** i for i in range(1, closest + 1)]
    if closest != n_heads:
        eb = 2.0 ** (-(2.0 ** (-(math.log2(2 * closest) - 3))))
        n_extra = min(closest, n_heads - closest)
        slopes += [eb ** i for i in range(1, 2 * n_extra + 1, 2)]
    return np.array(slopes, np.float32)


class Pools:

    def __init__(self, env, n_pf=6, n_pt=2, ln=True, wstream=True, rope=True):
        self.env = env
        p = env.p
        self.pf = [env.psum(f"psf{i}", [128, 512], F32) for i in range(n_pf)]
        self.pt = [env.psum(f"pst{i}", [128, 128], BF16) for i in range(n_pt)]
        self._tp_i = 0
        self.ones1 = env.sbuf("sc_ones1", [128, 1], BF16)
        env.p.op("vector", lambda e: e.memset(self.ones1.ap[:], 1.0), writes=[self.ones1])
        if ln or n_pt:
            self.ident = env.sbuf("sc_ident", [128, 128], BF16)
            ident_in = env.dram_in("c_ident", [128, 128], BF16)
            env.p.load(self.ident, ident_in)
        if ln:
            self.mu = env.sbuf("sc_mu", [128, 1], F32)
            self.sq = env.sbuf("sc_sq", [128, 1], F32)
            self.rstd = env.sbuf("sc_rstd", [128, 1], F32)
            self.nmur = env.sbuf("sc_nmur", [128, 1], F32)
            self.dump = env.sbuf("sc_dump", [128, 4096], F32)
            self.eps = env.sbuf("sc_eps", [128, 1], F32)
            env.p.op("vector", lambda e: e.memset(self.eps.ap[:], EPS), writes=[self.eps])
        if wstream:
            self.wb = [env.sbuf(f"sc_wb{i}", [128, 512], BF16) for i in range(4)]
        self._wb_i = 0
        if rope:
            self.rope_t1 = env.sbuf("sc_ropet1", [128, 512], BF16)
            self.rope_t2 = env.sbuf("sc_ropet2", [128, 512], BF16)
            self.rope_tb = env.sbuf("sc_ropetb", [128, 512], F32)

    def next_wb(self):
        b = self.wb[self._wb_i % len(self.wb)]
        self._wb_i += 1
        return b

    def next_tp(self):
        b = self.pt[self._tp_i % len(self.pt)]
        self._tp_i += 1
        return b


def emit_ln(env, po, x_buf, xn_buf):
    p = env.p
    inv = 1.0 / D
    mu, sq, rstd, nmur, dump, eps = po.mu, po.sq, po.rstd, po.nmur, po.dump, po.eps
    p.op("vector", lambda e: e.reduce_sum(mu.ap[:], x_buf.ap[:], axis=AX.X),
         reads=[x_buf], writes=[mu])
    p.op("scalar", lambda e: e.activation(out=dump.ap[:], in_=x_buf.ap[:],
                                          func=AF.Square, accum_out=sq.ap[:]),
         reads=[x_buf], writes=[sq, dump])
    p.op("vector", lambda e: e.tensor_scalar_mul(mu.ap[:], mu.ap[:], inv),
         reads=[mu], writes=[mu])
    p.op("vector", lambda e: e.tensor_scalar_mul(sq.ap[:], sq.ap[:], inv),
         reads=[sq], writes=[sq])
    p.op("vector", lambda e: e.tensor_mul(rstd.ap[:], mu.ap[:], mu.ap[:]),
         reads=[mu], writes=[rstd])
    p.op("vector", lambda e: e.tensor_sub(rstd.ap[:], sq.ap[:], rstd.ap[:]),
         reads=[sq, rstd], writes=[rstd])
    p.op("scalar", lambda e: e.activation(out=rstd.ap[:], in_=rstd.ap[:],
                                          func=AF.Ln, bias=eps.ap[:]),
         reads=[rstd, eps], writes=[rstd])
    p.op("scalar", lambda e: e.activation(out=rstd.ap[:], in_=rstd.ap[:],
                                          func=AF.Exp, scale=-0.5),
         reads=[rstd], writes=[rstd])
    p.op("scalar", lambda e: e.mul(nmur.ap[:], mu.ap[:], -1.0),
         reads=[mu], writes=[nmur])
    p.op("scalar", lambda e: e.mul(nmur.ap[:], nmur.ap[:], rstd.ap[:]),
         reads=[nmur, rstd], writes=[nmur])
    p.op("scalar", lambda e: e.activation(out=xn_buf.ap[:], in_=x_buf.ap[:],
                                          func=AF.Identity, bias=nmur.ap[:],
                                          scale=rstd.ap[:]),
         reads=[x_buf, nmur, rstd], writes=[xn_buf])


def emit_tp128(env, po, src_buf, src_ap, dst_buf, dst_ap):
    p = env.p
    ps = po.next_tp()
    p.op("tensor", lambda e: e.transpose(ps.ap[:], src_ap, po.ident.ap[:]),
         reads=[src_buf, po.ident], writes=[ps])
    p.op("vector", lambda e: e.tensor_copy(out=dst_ap, in_=ps.ap[:]),
         reads=[ps], writes=[dst_buf])


def emit_ln_tp_phase(env, po, x_dram, xnT, xnT_out=None, x_tiles_keep=None):
    p = env.p
    xw = env.sbuf("lt_x", [128, D], F32)
    xnw = env.sbuf("lt_xn", [128, D], BF16)
    for tile in range(NTILE):
        xbuf = x_tiles_keep[tile] if x_tiles_keep is not None else xw
        p.load(xbuf, x_dram[tile, :, :])
        emit_ln(env, po, xbuf, xnw)
        for c in range(D // 128):
            emit_tp128(env, po, xnw, xnw.ap[:, c * 128:(c + 1) * 128],
                       xnT[c], xnT[c].ap[:, tile * 128:(tile + 1) * 128])
    if xnT_out is not None:
        for c in range(D // 128):
            p.store(xnT_out[c, :, :], xnT[c])


def emit_proj_feat(env, po, w_dram, xnT, out_chunks, ndm=32, evac="copy",
                   bias=None, rope=None, out_store=None, scale=1.0, oc_base=0):
    p = env.p
    noc = len(out_chunks)
    ntb = TOK // 512
    for oc in range(noc):
        pss = [po.pf[oc % 2 * 2], po.pf[oc % 2 * 2 + 1]][:ntb] if ntb <= 2 else None
        for dm in range(ndm):
            wb = po.next_wb()
            p.load(wb, w_dram[dm, oc_base + oc, :, :], queue="gpsimd",
                   dst_ap=wb.ap[:, 0:128])
            for tb in range(ntb):
                def mk(e, wb=wb, dm=dm, tb=tb, oc=oc, ps=pss[tb]):
                    return e.matmul(ps.ap[:], wb.ap[:, 0:128],
                                    xnT[dm].ap[:, tb * 512:(tb + 1) * 512],
                                    start=(dm == 0), stop=(dm == ndm - 1))
                p.op("tensor", mk, reads=[wb, xnT[dm]], writes=[pss[tb]])
        ob = out_chunks[oc]
        for tb in range(ntb):
            ps = pss[tb]
            cols = slice(tb * 512, (tb + 1) * 512)
            if evac == "copy":
                if bias is not None:
                    bbuf, bap = bias(oc)
                    p.op("scalar", lambda e, ps=ps, ob=ob, cols=cols, bap=bap:
                         e.activation(out=ob.ap[:, cols], in_=ps.ap[:],
                                      func=AF.Identity, bias=bap, scale=scale),
                         reads=[ps, bbuf], writes=[ob])
                elif scale != 1.0:
                    p.op("scalar", lambda e, ps=ps, ob=ob, cols=cols:
                         e.activation(out=ob.ap[:, cols], in_=ps.ap[:],
                                      func=AF.Identity, scale=scale),
                         reads=[ps], writes=[ob])
                else:
                    p.op("scalar", lambda e, ps=ps, ob=ob, cols=cols:
                         e.copy(out=ob.ap[:, cols], in_=ps.ap[:]),
                         reads=[ps], writes=[ob])
            elif evac == "rope":
                cos_b, sin_b = rope
                emit_rope_evac(env, po, ps, ob, cols, cos_b, sin_b, half=64,
                               bias=None if bias is None else bias(oc))
        if out_store is not None:
            p.store(out_store[oc, :, :], ob)


def emit_rope_evac(env, po, ps, ob, cols, cos_b, sin_b, half=64, bias=None):
    p = env.p
    t1, t2 = po.rope_t1, po.rope_t2
    src = ps
    if bias is not None:
        bbuf, bap = bias
        tb = po.rope_tb
        p.op("scalar", lambda e: e.activation(out=tb.ap[:, 0:512], in_=ps.ap[:],
                                              func=AF.Identity, bias=bap),
             reads=[ps, bbuf], writes=[tb])
        src = tb
    h = half
    sa = src.ap[0:h, 0:512]
    sb_ = src.ap[h:2 * h, 0:512]
    ca = cos_b.ap[0:h, cols]
    sn = sin_b.ap[h:2 * h, cols]
    cb = cos_b.ap[h:2 * h, cols]
    snb = sin_b.ap[0:h, cols]
    p.op("vector", lambda e: e.tensor_tensor(out=t1.ap[0:h, 0:512], in0=sa, in1=ca, op=ALU.mult),
         reads=[src, cos_b], writes=[t1])
    p.op("vector", lambda e: e.tensor_tensor(out=t2.ap[0:h, 0:512], in0=sb_, in1=sn, op=ALU.mult),
         reads=[src, sin_b], writes=[t2])
    p.op("vector", lambda e: e.tensor_tensor(out=ob.ap[0:h, cols], in0=t1.ap[0:h, 0:512],
                                             in1=t2.ap[0:h, 0:512], op=ALU.subtract),
         reads=[t1, t2], writes=[ob])
    p.op("vector", lambda e: e.tensor_tensor(out=t1.ap[0:h, 0:512], in0=sb_, in1=cb, op=ALU.mult),
         reads=[src, cos_b], writes=[t1])
    p.op("vector", lambda e: e.tensor_tensor(out=t2.ap[0:h, 0:512], in0=sa, in1=snb, op=ALU.mult),
         reads=[src, sin_b], writes=[t2])
    p.op("vector", lambda e: e.tensor_tensor(out=ob.ap[h:2 * h, cols], in0=t1.ap[0:h, 0:512],
                                             in1=t2.ap[0:h, 0:512], op=ALU.add),
         reads=[t1, t2], writes=[ob])


def emit_proj_tok(env, po, w_dram, inT, ncolblk, ndm=32, colw=512,
                  out_tiles=None, out_store=None, evac_fn=None, npsum=6,
                  inT_cols=None):
    p = env.p
    if inT_cols is None:
        def inT_cols(dm, ti):
            return inT[dm].ap[:, ti * 128:(ti + 1) * 128]
    for cb in range(ncolblk):
        for grp in range(0, NTILE, npsum):
            tiles = list(range(grp, min(grp + npsum, NTILE)))
            for dm in range(ndm):
                wb = po.next_wb()
                p.load(wb, w_dram[dm, cb, :, 0:colw], queue="gpsimd",
                       dst_ap=wb.ap[:, 0:colw])
                for ti in tiles:
                    ps = po.pf[ti % npsum]
                    def mk(e, wb=wb, dm=dm, ti=ti, ps=ps):
                        return e.matmul(ps.ap[:, 0:colw], inT_cols(dm, ti),
                                        wb.ap[:, 0:colw],
                                        start=(dm == 0), stop=(dm == ndm - 1))
                    p.op("tensor", mk, reads=[wb, inT[dm]], writes=[ps])
            for ti in tiles:
                ps = po.pf[ti % npsum]
                if evac_fn is not None:
                    evac_fn(cb, ti, ps)
                else:
                    ob = out_tiles[ti]
                    cols = slice(cb * colw, (cb + 1) * colw)
                    p.op("scalar", lambda e, ps=ps, ob=ob, cols=cols:
                         e.copy(out=ob.ap[:, cols], in_=ps.ap[:, 0:colw]),
                         reads=[ps], writes=[ob])
    if out_store is not None:
        for ti in range(NTILE):
            p.store(out_store[ti, :, :], out_tiles[ti])


SQ128 = float(np.sqrt(128.0))


def emit_qproj_pairs(env, po, wq_all, xnf_dram, qT_own, ndm=32, seq=S,
                     evac_fn=None):
    p = env.p
    nh = (ndm + 1) // 2
    xt = env.sbuf("qp_x", [128, nh * 512], BF16)
    ntb = seq // 512
    for tb in range(ntb):
      for half in range(2):
        d0, d1 = half * nh, min(ndm, (half + 1) * nh)
        if d0 >= d1:
            continue
        p.load(xt, xnf_dram[d0:d1, :, tb * 512:(tb + 1) * 512]
               .rearrange("a p m -> p a m"), queue="sync",
               dst_ap=xt.ap[:, 0:(d1 - d0) * 512].rearrange("p (a m) -> p a m", a=d1 - d0))
        for dm in range(d0, d1):
            for j in range(NPAIR):
                c0 = (dm * NPAIR + j) * 128
                def mk(e, dm=dm, j=j, c0=c0, d0=d0):
                    return e.matmul(po.pf[j].ap[:], wq_all.ap[:, c0:c0 + 128],
                                    xt.ap[:, (dm - d0) * 512:(dm - d0 + 1) * 512],
                                    start=(dm == 0), stop=(dm == ndm - 1))
                p.op("tensor", mk, reads=[wq_all, xt], writes=[po.pf[j]])
      for j in range(NPAIR):
            ps = po.pf[j]
            cols = slice(tb * 512, (tb + 1) * 512)
            if evac_fn is not None:
                evac_fn(j, tb, ps, qT_own[j], cols)
            else:
                p.op("scalar", lambda e, ps=ps, j=j, cols=cols:
                     e.copy(out=qT_own[j].ap[:, cols], in_=ps.ap[:]),
                     reads=[ps], writes=[qT_own[j]])


def emit_attn_pairs(env, po, qT_own, kT_getter, v_getter, ctx_out, seq=S,
                    exp_evac=None, score_extra=None, ctx_sb=None):
    p = env.p
    nqb = seq // 512
    pts = [env.sbuf(f"at_p{i}", [128, 512], BF16) for i in range(4)]
    zrow = env.sbuf("at_z", [1, 512], F32)
    onesr = env.sbuf("at_or", [1, 128], F32)
    bz = env.sbuf("at_bz", [128, 512], F32)
    ctxs = env.sbuf("at_ctx", [128, 512], BF16)
    p.op("vector", lambda e: e.memset(onesr.ap[:], 1.0), writes=[onesr])
    psZ, psO, psB = po.pf[4], po.pf[5], po.pf[6]
    for j in range(NPAIR):
        kbuf, kc0 = kT_getter(j)
        vbuf, vc0 = v_getter(j)
        for qblk in range(nqb):
            nkt = 4 * qblk + 4
            for kt in range(nkt):
                psS = po.pf[kt % 4]
                def mm_s(e, j=j, qblk=qblk, kt=kt, psS=psS, kbuf=kbuf, kc0=kc0):
                    return e.matmul(psS.ap[:], kbuf.ap[:, kc0 + kt * 128:kc0 + (kt + 1) * 128],
                                    qT_own[j].ap[:, qblk * 512:(qblk + 1) * 512],
                                    start=True, stop=(score_extra is None))
                p.op("tensor", mm_s, reads=[kbuf, qT_own[j]], writes=[psS])
                if score_extra is not None:
                    score_extra(j, qblk, kt, psS)
                ptile = pts[kt % 4]
                exp_evac(j, qblk, kt, psS, ptile)
                def mm_z(e, kt=kt, nkt=nkt, ptile=ptile):
                    return e.matmul(psZ.ap[0:1, :], po.ones1.ap[:],
                                    ptile.ap[:], start=(kt == 0), stop=(kt == nkt - 1))
                p.op("tensor", mm_z, reads=[po.ones1, ptile], writes=[psZ])
                def mm_o(e, kt=kt, nkt=nkt, ptile=ptile, vbuf=vbuf, vc0=vc0):
                    return e.matmul(psO.ap[:], vbuf.ap[:, vc0 + kt * 128:vc0 + (kt + 1) * 128],
                                    ptile.ap[:], start=(kt == 0), stop=(kt == nkt - 1))
                p.op("tensor", mm_o, reads=[vbuf, ptile], writes=[psO])
            p.op("vector", lambda e: e.tensor_copy(out=zrow.ap[:], in_=psZ.ap[0:1, :]),
                 reads=[psZ], writes=[zrow])
            p.op("vector", lambda e: e.reciprocal(out=zrow.ap[:], in_=zrow.ap[:]),
                 reads=[zrow], writes=[zrow])
            p.op("tensor", lambda e: e.matmul(psB.ap[:], onesr.ap[:], zrow.ap[:],
                                              start=True, stop=True),
                 reads=[onesr, zrow], writes=[psB])
            p.op("vector", lambda e: e.tensor_copy(out=bz.ap[:], in_=psB.ap[:]),
                 reads=[psB], writes=[bz])
            p.op("vector", lambda e: e.tensor_tensor(out=ctxs.ap[:], in0=psO.ap[:],
                                                     in1=bz.ap[:], op=ALU.mult),
                 reads=[psO, bz], writes=[ctxs])
            if ctx_sb is not None:
                p.op("vector", lambda e, j=j, qblk=qblk:
                     e.tensor_copy(out=ctx_sb[j].ap[:, qblk * 512:(qblk + 1) * 512],
                                   in_=ctxs.ap[:]),
                     reads=[ctxs], writes=[ctx_sb[j]])
            if ctx_out is not None:
                p.store(ctx_out[j, :, qblk * 512:(qblk + 1) * 512], ctxs)


def build_attn0(env):
    po = Pools(env, n_pf=8, n_pt=0, ln=False, wstream=False, rope=False)
    p = env.p
    xnf = env.dram_in("xnf", [32, 128, S], BF16)
    kTf = env.dram_in("kTf", [1, 128, S], BF16)
    vf = env.dram_in("vf", [128, NT * 128], BF16)
    wq = env.dram_in("wq", [128, 32 * NPAIR * 128])
    biasKQ = env.dram_in("biasKQ", [128, NPAIR * NT * 8])
    qoffm = env.dram_in("qoffm", [128, 5 * 512])
    slopes = env.dram_in("slopes", [128, NPAIR])
    ctx_out = env.dram_out("ctx", [NPAIR, 128, S], BF16)

    wq_all = env.sbuf("wq_all", [128, 32 * NPAIR * 128], BF16)
    p.load(wq_all, wq, queue="gpsimd")
    wq_all.frozen = True
    kT = env.sbuf("kT", [128, S], BF16)
    p.load(kT, kTf[0, :, :])
    kT.frozen = True
    v_all = env.sbuf("v_all", [128, NT * 128], BF16)
    p.load(v_all, vf)
    v_all.frozen = True
    bkq = env.sbuf("bkq", [128, NPAIR * NT * 8], F32)
    p.load(bkq, biasKQ)
    bkq.frozen = True
    qoff = env.sbuf("qoff", [128, 5 * 512], F32)
    p.load(qoff, qoffm)
    qoff.frozen = True
    slp = env.sbuf("slp", [128, NPAIR], F32)
    p.load(slp, slopes)
    slp.frozen = True

    qT_own = [env.sbuf(f"qT{j}", [128, S], BF16) for j in range(NPAIR)]
    emit_qproj_pairs(env, po, wq_all, xnf, qT_own)
    for b in qT_own:
        b.frozen = True

    tmp2 = [env.sbuf(f"at_tmp{i}", [128, 512], F32) for i in range(2)]

    def exp_evac(j, qblk, kt, psS, ptile):
        tmp = tmp2[kt % 2]
        r = kt - 4 * qblk
        var = (r + 1) if 0 <= r <= 3 else 0
        qsl = qoff.ap[:, var * 512:(var + 1) * 512]
        p.op("vector", lambda e, tmp=tmp: e.scalar_tensor_tensor(
            out=tmp.ap[:], in0=qsl, scalar=slp.ap[:, j:j + 1],
            in1=psS.ap[:], op0=ALU.mult, op1=ALU.add),
            reads=[qoff, slp, psS], writes=[tmp])
        col = (j * NT + kt) * 8 + qblk
        p.op("scalar", lambda e, col=col, tmp=tmp: e.activation(
            out=ptile.ap[:], in_=tmp.ap[:], func=AF.Exp,
            bias=bkq.ap[:, col:col + 1], scale=1.0 / SQ128),
            reads=[tmp, bkq], writes=[ptile])

    emit_attn_pairs(env, po, qT_own, lambda j: (kT, 0), lambda j: (v_all, 0),
                    ctx_out, exp_evac=exp_evac)


def emit_wo_res(env, po, ctxT, wo_dram, x_in_dram, x_out_dram, bias_dram=None,
                x_keep=None):
    p = env.p
    xsl = [env.sbuf(f"wr_x{i}", [128, 512], F32) for i in range(2)]
    xo = [env.sbuf(f"wr_o{i}", [128, 512], F32) for i in range(2)]
    bias_b = None
    if bias_dram is not None:
        bias_b = env.sbuf("wr_b", [128, D], F32)
        p.load(bias_b, bias_dram)
        bias_b.frozen = True

    def evac(cb, ti, ps):
        xs = xsl[ti % 2]
        xob = xo[ti % 2]
        p.load(xs, x_in_dram[ti, :, cb * 512:(cb + 1) * 512], queue="sync")
        if bias_b is not None:
            p.op("vector", lambda e, xs=xs, cb=cb: e.tensor_tensor(
                out=xs.ap[:], in0=xs.ap[:],
                in1=bias_b.ap[:, cb * 512:(cb + 1) * 512], op=ALU.add),
                reads=[xs, bias_b], writes=[xs])
        p.op("vector", lambda e, ps=ps, xs=xs, xob=xob: e.scalar_tensor_tensor(
            out=xob.ap[:], in0=ps.ap[:], scalar=1.0, in1=xs.ap[:],
            op0=ALU.mult, op1=ALU.add),
            reads=[ps, xs], writes=[xob])
        if x_keep is not None:
            p.op("vector", lambda e, xob=xob, ti=ti, cb=cb: e.tensor_copy(
                out=x_keep[ti].ap[:, cb * 512:(cb + 1) * 512], in_=xob.ap[:]),
                reads=[xob], writes=[x_keep[ti]])
        p.store(x_out_dram[ti, :, cb * 512:(cb + 1) * 512], xob)

    emit_proj_tok(env, po, wo_dram, ctxT, ncolblk=8, evac_fn=evac, npsum=8)


def mk_build_wo_res(with_bias):
    def build(env):
        po = Pools(env, n_pf=8, n_pt=0, ln=False, wstream=True, rope=False)
        p = env.p
        ctx_in = env.dram_in("ctxT", [32, 128, TOK], BF16)
        wo = env.dram_in("wo", [32, 8, 128, 512])
        x_in = env.dram_in("x", [NTILE, 128, D])
        bias = env.dram_in("bias", [128, D]) if with_bias else None
        x_out = env.dram_out("xo", [NTILE, 128, D])
        ctxT = [env.sbuf(f"ctxT{c}", [128, TOK], BF16) for c in range(32)]
        for c in range(32):
            p.load(ctxT[c], ctx_in[c, :, :])
            ctxT[c].frozen = True
        emit_wo_res(env, po, ctxT, wo, x_in, x_out, bias_dram=bias)
    return build


build_wo_res = mk_build_wo_res(False)
build_wo_res_bias = mk_build_wo_res(True)


def emit_ffn0(env, po, x_in_dram, x_out_dram, ab_bufs, prr, x_tiles=None):
    p = env.p
    A_rhs, B_rhs, WqR, keysT = (ab_bufs["A"], ab_bufs["B"], ab_bufs["WqR"],
                                ab_bufs["keysT"])
    xw = env.sbuf("f0_x", [128, D], F32)
    xnw = env.sbuf("f0_xn", [128, D], BF16)
    xnTt = env.sbuf("f0_xnT", [128, D], BF16)
    z_bf = env.sbuf("f0_z", [128, 512], BF16)
    qTf = env.sbuf("f0_qT", [16, 128], F32)
    scT = env.sbuf("f0_scT", [64, 128], F32)
    sc = env.sbuf("f0_sc", [128, 64], F32)
    m8a = env.sbuf("f0_m8a", [128, 8], F32)
    m8b = env.sbuf("f0_m8b", [128, 8], F32)
    ind8 = env.sbuf("f0_ind8", [128, 64], F32)
    scm = env.sbuf("f0_scm", [128, 64], F32)
    kept = env.sbuf("f0_kept", [128, 64], F32)
    negm1 = env.sbuf("f0_negm1", [128, 1], F32)
    gexp = env.sbuf("f0_gexp", [128, 64], F32)
    zs = env.sbuf("f0_zs", [128, 1], F32)
    gsc = env.sbuf("f0_gsc", [128, 1], F32)
    gate_bf = env.sbuf("f0_gate", [128, 64], BF16)
    zg = env.sbuf("f0_zg", [128, 512], BF16)
    zgTt = env.sbuf("f0_zgT", [128, 512], BF16)
    xsl = env.sbuf("f0_xsl", [128, 512], F32)
    xob = env.sbuf("f0_xo", [128, 512], F32)
    psZz, psQ, psSc, psD = po.pf[0], po.pf[1], po.pf[2], po.pf[3]

    for ti in range(NTILE):
        if x_tiles is not None:
            xbuf = x_tiles[ti]
        else:
            xbuf = xw
            p.load(xbuf, x_in_dram[ti, :, :], queue="sync")
        emit_ln(env, po, xbuf, xnw)
        for c in range(32):
            emit_tp128(env, po, xnw, xnw.ap[:, c * 128:(c + 1) * 128],
                       xnTt, xnTt.ap[:, c * 128:(c + 1) * 128])
        for dm in range(32):
            p.op("tensor", lambda e, dm=dm: e.matmul(
                psZz.ap[:], xnTt.ap[:, dm * 128:(dm + 1) * 128],
                A_rhs.ap[:, dm * 512:(dm + 1) * 512],
                start=(dm == 0), stop=(dm == 31)),
                reads=[xnTt, A_rhs], writes=[psZz])
        p.op("scalar", lambda e: e.copy(out=z_bf.ap[:], in_=psZz.ap[:]),
             reads=[psZz], writes=[z_bf])
        for dm in range(32):
            p.op("tensor", lambda e, dm=dm: e.matmul(
                psQ.ap[0:16, 0:128], WqR.ap[:, dm * 16:(dm + 1) * 16],
                xnTt.ap[:, dm * 128:(dm + 1) * 128],
                start=(dm == 0), stop=(dm == 31)),
                reads=[xnTt, WqR], writes=[psQ])
        p.op("vector", lambda e: e.tensor_copy(out=qTf.ap[:], in_=psQ.ap[0:16, 0:128]),
             reads=[psQ], writes=[qTf])
        p.op("tensor", lambda e: e.matmul(psSc.ap[0:64, 0:128], keysT.ap[:],
                                          qTf.ap[:], start=True, stop=True),
             reads=[keysT, qTf], writes=[psSc])
        p.op("vector", lambda e: e.tensor_copy(out=scT.ap[:], in_=psSc.ap[0:64, 0:128]),
             reads=[psSc], writes=[scT])
        p.op("tensor", lambda e: e.transpose(prr.ap[0:128, 0:64], scT.ap[:],
                                             po.identf.ap[0:64, 0:64]),
             reads=[scT, po.identf], writes=[prr])
        p.op("vector", lambda e: e.tensor_copy(out=sc.ap[:], in_=prr.ap[0:128, 0:64]),
             reads=[prr], writes=[sc])
        p.op("vector", lambda e: e.max(out=m8a.ap[:], in_=sc.ap[:]),
             reads=[sc], writes=[m8a])
        p.op("vector", lambda e: e.tensor_scalar(
            out=ind8.ap[:], in0=sc.ap[:], scalar1=m8a.ap[:, 7:8], scalar2=None,
            op0=ALU.is_ge), reads=[sc, m8a], writes=[ind8])
        p.op("vector", lambda e: e.scalar_tensor_tensor(
            out=scm.ap[:], in0=ind8.ap[:], scalar=-1e30, in1=sc.ap[:],
            op0=ALU.mult, op1=ALU.add), reads=[ind8, sc], writes=[scm])
        p.op("vector", lambda e: e.max(out=m8b.ap[:], in_=scm.ap[:]),
             reads=[scm], writes=[m8b])
        p.op("vector", lambda e: e.tensor_scalar(
            out=kept.ap[:], in0=sc.ap[:], scalar1=m8b.ap[:, 7:8], scalar2=None,
            op0=ALU.is_ge), reads=[sc, m8b], writes=[kept])
        p.op("scalar", lambda e: e.mul(negm1.ap[:], m8a.ap[:, 0:1], -1.0),
             reads=[m8a], writes=[negm1])
        p.op("scalar", lambda e: e.activation(out=gexp.ap[:], in_=sc.ap[:],
                                              func=AF.Exp, bias=negm1.ap[:]),
             reads=[sc, negm1], writes=[gexp])
        p.op("vector", lambda e: e.tensor_tensor(out=gexp.ap[:], in0=gexp.ap[:],
                                                 in1=kept.ap[:], op=ALU.mult),
             reads=[gexp, kept], writes=[gexp])
        p.op("vector", lambda e: e.reduce_sum(zs.ap[:], gexp.ap[:], axis=AX.X),
             reads=[gexp], writes=[zs])
        p.op("vector", lambda e: e.reciprocal(out=gsc.ap[:], in_=zs.ap[:]),
             reads=[zs], writes=[gsc])
        p.op("vector", lambda e: e.tensor_scalar_mul(gsc.ap[:], gsc.ap[:], 2.0),
             reads=[gsc], writes=[gsc])
        p.op("vector", lambda e: e.tensor_scalar_mul(gexp.ap[:], gexp.ap[:], gsc.ap[:]),
             reads=[gexp, gsc], writes=[gexp])
        p.op("vector", lambda e: e.tensor_copy(out=gate_bf.ap[:], in_=gexp.ap[:]),
             reads=[gexp], writes=[gate_bf])
        if hasattr(env, "dbg_gate"):
            p.store(env.dbg_gate[ti, :, :], gexp)
            p.store(env.dbg_z[ti, :, :], z_bf)
            p.store(env.dbg_sc[ti, :, :], sc)
        p.op("vector", lambda e: e.tensor_tensor(
            out=zg.ap[:].rearrange("p (a r) -> p a r", a=64),
            in0=z_bf.ap[:].rearrange("p (a r) -> p a r", a=64),
            in1=gate_bf.ap[:].unsqueeze(2).to_broadcast([128, 64, 8]),
            op=ALU.mult), reads=[z_bf, gate_bf], writes=[zg])
        if hasattr(env, "dbg_zg"):
            p.store(env.dbg_zg[ti, :, :], zg)
        for i in range(4):
            emit_tp128(env, po, zg, zg.ap[:, i * 128:(i + 1) * 128],
                       zgTt, zgTt.ap[:, i * 128:(i + 1) * 128])
        if hasattr(env, "dbg_zgT"):
            p.store(env.dbg_zgT[ti, :, :], zgTt)
        for cb in range(8):
            for i in range(4):
                p.op("tensor", lambda e, i=i, cb=cb: e.matmul(
                    psD.ap[:], zgTt.ap[:, i * 128:(i + 1) * 128],
                    B_rhs.ap[:, (i * 8 + cb) * 512:(i * 8 + cb + 1) * 512],
                    start=(i == 0), stop=(i == 3)),
                    reads=[zgTt, B_rhs], writes=[psD])
            if x_tiles is not None:
                p.op("vector", lambda e, ti=ti, cb=cb: e.scalar_tensor_tensor(
                    out=xob.ap[:], in0=psD.ap[:], scalar=1.0,
                    in1=x_tiles[ti].ap[:, cb * 512:(cb + 1) * 512],
                    op0=ALU.mult, op1=ALU.add),
                    reads=[psD, x_tiles[ti]], writes=[xob])
            else:
                p.load(xsl, x_in_dram[ti, :, cb * 512:(cb + 1) * 512], queue="sync")
                p.op("vector", lambda e: e.scalar_tensor_tensor(
                    out=xob.ap[:], in0=psD.ap[:], scalar=1.0, in1=xsl.ap[:],
                    op0=ALU.mult, op1=ALU.add),
                    reads=[psD, xsl], writes=[xob])
            p.store(x_out_dram[ti, :, cb * 512:(cb + 1) * 512], xob)


def build_ffn0(env):
    po = Pools(env, n_pf=4, n_pt=2, ln=True, wstream=False, rope=False)
    po.identf = env.sbuf("sc_identf", [128, 128], F32)
    idf_in = env.dram_in("c_identf", [128, 128], F32)
    env.p.load(po.identf, idf_in)
    po.identf.frozen = True
    prr = env.psum("psrr", [128, 64], F32)
    p = env.p
    x_in = env.dram_in("x", [NTILE, 128, D])
    a_in = env.dram_in("Arhs", [128, 32 * 512])
    b_in = env.dram_in("Brhs", [128, 32 * 512])
    wqr_in = env.dram_in("WqR", [128, 32 * 16])
    keys_in = env.dram_in("keysT", [16, 64])
    x_out = env.dram_out("xo", [NTILE, 128, D])
    A = env.sbuf("A_rhs", [128, 32 * 512], BF16)
    Bb = env.sbuf("B_rhs", [128, 32 * 512], BF16)
    Wq = env.sbuf("WqR_b", [128, 32 * 16], BF16)
    ks = env.sbuf("keysT_b", [16, 64], F32)
    p.load(A, a_in, queue="gpsimd")
    p.load(Bb, b_in, queue="gpsimd")
    p.load(Wq, wqr_in, queue="gpsimd")
    p.load(ks, keys_in, queue="sync")
    for b in (A, Bb, Wq, ks):
        b.frozen = True
    emit_ffn0(env, po, x_in, x_out,
              {"A": A, "B": Bb, "WqR": Wq, "keysT": ks}, prr)


def build_pre0(env):
    po = Pools(env)
    x_in = env.dram_in("x", [NTILE, 128, D])
    wk = env.dram_in("wk", [32, 1, 128, 128])
    wv = env.dram_in("wv", [32, 1, 128, 128])
    xnT_out = env.dram_out("xnT", [32, 128, TOK], BF16)
    kT_out = env.dram_out("kT", [1, 128, TOK], BF16)
    v_out = env.dram_out("v", [NTILE, 128, 128], BF16)
    xnT = [env.sbuf(f"xnT{c}", [128, TOK], BF16) for c in range(32)]
    emit_ln_tp_phase(env, po, x_in, xnT, xnT_out=xnT_out)
    for c in xnT:
        c.frozen = True
    kT = [env.sbuf("kT0", [128, TOK], BF16)]
    emit_proj_feat(env, po, wk, xnT, kT, out_store=kT_out)
    vt = [env.sbuf(f"vt{t}", [128, 128], BF16) for t in range(NTILE)]
    emit_proj_tok(env, po, wv, xnT, ncolblk=1, colw=128,
                  out_tiles=vt, out_store=v_out)


_IDENT = np.eye(128, dtype=BF)
_IDENTF = np.eye(128, dtype=np.float32)


_HOOK = None


def run_launch(build_fn, in_maps, name="l", trace=False):
    env = Env(name)
    nc = env.build(build_fn)
    names = set(env._ins)
    for m in in_maps:
        if "c_ident" in names:
            m["c_ident"] = _IDENT
        if "c_identf" in names:
            m["c_identf"] = _IDENTF
    res = run_bass_kernel_spmd(nc, in_maps, core_ids=list(range(NCORES)),
                               trace=trace)
    if _HOOK is not None:
        _HOOK(name, res.results if not trace else res.results)
    return res.results if not trace else res


MBIG = 2.33e10


def _pm(a):
    a = np.asarray(a)
    if a.ndim == 2:
        return np.ascontiguousarray(a)
    if a.ndim == 3:
        assert a.shape[1] == 128, a.shape
        return np.ascontiguousarray(a.transpose(1, 0, 2).reshape(a.shape[1], -1))
    raise ValueError(a.shape)


def _row(v):
    return np.broadcast_to(np.asarray(v, np.float32)[None, :], (128, len(v))).copy()


def _gather_batch(res, key, b, axis=2):
    return np.concatenate([np.asarray(res[4 * b + i][key]) for i in range(4)],
                          axis=axis)


def _ctx_for_core(res, c):
    b = c // 4
    tc = c % 4
    ctx = np.stack([np.asarray(res[4 * b + j // 8]["ctx"][j % 8])
                    for j in range(32)])
    return np.ascontiguousarray(ctx[:, :, tc * 1024:(tc + 1) * 1024])


def kernel(**I):
    perm = np.asarray(I["L1_ffn_perm"]).astype(np.int64)
    x = np.asarray(I["L0_attn_a_hidden_states"])
    xp = np.ascontiguousarray(x[:, :, perm])
    shards = [np.ascontiguousarray(
        xp[c // 4, (c % 4) * TOK:(c % 4 + 1) * TOK].reshape(NTILE, 128, D))
        for c in range(NCORES)]

    slopes = alibi_slopes(32)
    cosS, sinS = rope_tables(np.arange(S), 128)
    cosyS, sinyS = yarn_tables(np.arange(S))
    qoffm = np.zeros((5, 128, 512), np.float32)
    qoffm[0] = np.broadcast_to(np.arange(512, dtype=np.float32)[None, :], (128, 512))
    for r in range(4):
        qoffm[r + 1] = qoffm[0] + MBIG * (
            np.arange(512)[None, :] < 128 * r + np.arange(128)[:, None])
    qoffm_pm = _pm(qoffm)
    mdiag = np.where(np.arange(128)[:, None] < np.arange(128)[None, :],
                     -428.0, 0.0).astype(np.float32)
    mdiagw = np.zeros((128, 4 * 512), np.float32)
    for r_ in range(4):
        mdiagw[:, r_ * 512 + r_ * 128:r_ * 512 + (r_ + 1) * 128] = mdiag
    mvar = np.zeros((4, 128, 512), np.float32)
    for r in range(4):
        mvar[r] = -1e9 * (np.arange(512)[None, :] < 128 * r + np.arange(128)[:, None])
    mvar_pm = _pm(mvar)
    biasKQ = {}
    for h0 in (0, 8, 16, 24):
        t = np.zeros((128, 8 * 32 * 8), np.float32)
        p_ = np.arange(128, dtype=np.float32)
        for j in range(8):
            sl = slopes[h0 + j]
            for kt in range(32):
                for qb_ in range(8):
                    t[:, (j * 32 + kt) * 8 + qb_] = sl * (kt * 128 + p_ - qb_ * 512)
        biasKQ[h0] = t
    slp = {h0: _row(-slopes[h0:h0 + 8] * SQ128) for h0 in (0, 8, 16, 24)}
    percore_pos = [np.arange((c % 4) * TOK, (c % 4 + 1) * TOK) for c in range(NCORES)]

    Wq0 = np.asarray(I["L0_attn_a_Wq"])[perm, :]
    Wkv0 = np.asarray(I["L0_attn_a_Wkv"])[perm, :]
    Wo0 = np.asarray(I["L0_attn_a_Wo"])[:, perm]
    A_p = np.asarray(I["L0_ffn_A_w"])[:, perm]
    B_p = np.asarray(I["L0_ffn_B_w"])[perm, :]
    WqR_p = np.asarray(I["L0_ffn_Wq_w"])[:, perm]
    keys0 = np.asarray(I["L0_ffn_keys"])
    Wqb = np.asarray(I["L0_attn_b_Wq"])[:, perm]
    Wkb = np.asarray(I["L0_attn_b_Wk"])[:, perm]
    Wvb = np.asarray(I["L0_attn_b_Wv"])[:, perm]
    Wob = np.asarray(I["L0_attn_b_Wo"])[perm, :]
    hp1 = np.asarray(I["L0_attn_b_hash_proj1"])
    hb1 = np.asarray(I["L0_attn_b_hash_bias1"])
    hp2 = np.asarray(I["L0_attn_b_hash_proj2"])
    hb2 = np.asarray(I["L0_attn_b_hash_bias2"])
    qkvw = np.asarray(I["L1_attn_a_qkv_w"])[perm, :]
    qkvb = np.asarray(I["L1_attn_a_qkv_b"])
    outw = np.asarray(I["L1_attn_a_out_w"])[:, perm]
    outb = np.asarray(I["L1_attn_a_out_b"])[perm]
    wqd = np.asarray(I["L1_attn_b_w_q_down"])[perm, :]
    wqun = np.asarray(I["L1_attn_b_w_q_up_nope"])
    wqur = np.asarray(I["L1_attn_b_w_q_up_rope"])
    wkvd = np.asarray(I["L1_attn_b_w_kv_down"])[perm, :]
    wkrp = np.asarray(I["L1_attn_b_w_k_rope"])[perm, :]
    wuk = np.asarray(I["L1_attn_b_w_uk"])
    wuv = np.asarray(I["L1_attn_b_w_uv"])
    wout1b = np.asarray(I["L1_attn_b_w_out"])[:, perm]
    up = np.asarray(I["L1_ffn_up"])
    down = np.asarray(I["L1_ffn_down"])

    wk0_blk = block_lhsT(Wkv0[:, :128])
    wv0_rhs = block_rhs(Wkv0[:, 128:], nb=128)
    wq0_4 = Wq0.reshape(32, 128, 32, 128).transpose(0, 2, 1, 3)
    wo0_blk = block_rhs(Wo0)
    Arhs = _pm(A_p.T.reshape(32, 128, 512))
    Brhs = np.ascontiguousarray(
        B_p.T.reshape(4, 128, 8, 512).transpose(1, 0, 2, 3)).reshape(128, -1)
    WqRb = _pm(WqR_p.T.reshape(32, 128, 16))
    keysT = np.ascontiguousarray(keys0.T)
    wkb_blk = block_lhsT(Wkb.T)
    wvb_rhs = block_rhs(Wvb.T)
    wqb_4 = Wqb.T.reshape(32, 128, 32, 128).transpose(0, 2, 1, 3)
    wob_blk = block_rhs(Wob.T)
    hp1_pm = _pm(hp1)
    hp2_pm = _pm(hp2)
    hb1_pm = np.ascontiguousarray(hb1[:, 0, :].T)
    hb2_pm = np.ascontiguousarray(hb2[:, 0, :].T)
    wk1_blk = block_lhsT(qkvw[:, D:2 * D])
    wv1_rhs = block_rhs(qkvw[:, 2 * D:])
    wq1_4 = qkvw[:, :D].reshape(32, 128, 32, 128).transpose(0, 2, 1, 3)
    kb1 = np.ascontiguousarray(qkvb[D:2 * D].reshape(32, 128).T)
    vb1_row = _row(qkvb[2 * D:])
    qb1 = np.ascontiguousarray(qkvb[:D].reshape(32, 128).T)
    wo1_blk = block_rhs(outw)
    ob1_row = _row(outb)
    up_pm = _pm(up.reshape(16, 2, 128, 8, 128).transpose(2, 0, 1, 3, 4)
                .reshape(128, -1))
    dn_pm = _pm(down.reshape(16, 8, 128, 256).transpose(2, 0, 1, 3)
                .reshape(128, -1))
    wcq_blk = block_lhsT(wqd)
    wckv_blk = block_lhsT(wkvd)
    wkr_blk = block_lhsT(wkrp, mc=64)
    wqn_blk = block_lhsT(wqun)
    wqr_blk = block_lhsT(wqur, mc=64)
    wuk4 = wuk.reshape(32, 128, 4, 128).transpose(0, 2, 3, 1)
    wuv4 = wuv.reshape(32, 128, 4, 128).transpose(0, 2, 3, 1)
    wo1b_blk = block_rhs(wout1b)

    def wq_slice(w4, h0):
        return np.ascontiguousarray(
            w4[:, h0:h0 + 8].reshape(-1, 128, 128).transpose(1, 0, 2)
        ).reshape(128, -1)

    in1 = [{"x": shards[c], "wk": wk0_blk, "wv": wv0_rhs} for c in range(NCORES)]
    r1 = run_launch(build_pre0, in1, "pre0")
    in2 = []
    for c in range(NCORES):
        b, h0 = c // 4, 8 * (c % 4)
        in2.append({"xnf": _gather_batch(r1, "xnT", b),
                    "kTf": _gather_batch(r1, "kT", b),
                    "vf": _pm(_gather_batch(r1, "v", b, axis=0)),
                    "wq": wq_slice(wq0_4, h0),
                    "biasKQ": biasKQ[h0], "qoffm": qoffm_pm, "slopes": slp[h0]})
    r2 = run_launch(build_attn0, in2, "attn0")
    in3 = [{"ctxT": _ctx_for_core(r2, c), "wo": wo0_blk, "x": shards[c]}
           for c in range(NCORES)]
    r3 = run_launch(build_wo_res, in3, "wores0")
    in4 = [{"x": np.asarray(r3[c]["xo"]), "Arhs": Arhs, "Brhs": Brhs,
            "WqR": WqRb, "keysT": keysT} for c in range(NCORES)]
    r4 = run_launch(build_ffn0, in4, "ffn0a")

    in5 = []
    for c in range(NCORES):
        pos = percore_pos[c]
        in5.append({"x": np.asarray(r4[c]["xo"]), "wk": wkb_blk, "wv": wvb_rhs,
                    "cosk": cosS[:, pos], "sink": sinS[:, pos],
                    "hp1": hp1_pm, "hb1": hb1_pm, "hp2": hp2_pm, "hb2": hb2_pm})
    r5 = run_launch(build_pre_l0b, in5, "prel0b")
    in6 = []
    for c in range(NCORES):
        b, h0 = c // 4, 8 * (c % 4)
        g0 = h0 // 4
        vfull = _gather_batch(r5, "v", b, axis=0)
        in6.append({"xnf": _gather_batch(r5, "xnT", b),
                    "wq": np.ascontiguousarray(
                        wqb_4[:, h0:h0 + 8].transpose(0, 2, 1, 3)
                        .reshape(32, 128, -1)),
                    "cosq": cosS.astype(BF), "sinq": sinS.astype(BF),
                    "hp1": hp1_pm[:, h0 * 128:(h0 + 8) * 128],
                    "hb1": hb1_pm[:, h0:h0 + 8],
                    "hp2": hp2_pm[:, h0 * 128:(h0 + 8) * 128],
                    "hb2": hb2_pm[:, h0:h0 + 8],
                    "khf": _gather_batch(r5, "kh", b)[h0:h0 + 8],
                    "krf": _gather_batch(r5, "kT", b)[g0:g0 + 2],
                    "vf": _pm(vfull[:, :, g0 * 128:(g0 + 2) * 128]),
                    "mdiag": mdiag, "mdiagw": mdiagw})
    r6 = run_launch(build_attn_l0b, in6, "attnl0b")
    in7 = [{"ctxT": _ctx_for_core(r6, c), "wo": wob_blk,
            "x": np.asarray(r4[c]["xo"])} for c in range(NCORES)]
    r7 = run_launch(build_wo_res, in7, "woresl0b")
    in8 = [{"x": np.asarray(r7[c]["xo"]), "Arhs": Arhs, "Brhs": Brhs,
            "WqR": WqRb, "keysT": keysT} for c in range(NCORES)]
    r8 = run_launch(build_ffn0, in8, "ffn0b")

    in9 = []
    for c in range(NCORES):
        pos = percore_pos[c]
        in9.append({"x": np.asarray(r8[c]["xo"]), "wk": wk1_blk, "wv": wv1_rhs,
                    "kb": kb1, "vbrow": vb1_row,
                    "cosk": cosS[:, pos], "sink": sinS[:, pos]})
    r9 = run_launch(build_pre_l1a, in9, "prel1a")
    in10 = []
    for c in range(NCORES):
        b, h0 = c // 4, 8 * (c % 4)
        vfull = _gather_batch(r9, "v", b, axis=0)
        vf = np.stack([_pm(vfull[:, :, (h0 + j) * 128:(h0 + j + 1) * 128])
                       for j in range(8)])
        in10.append({"xnf": _gather_batch(r9, "xnT", b),
                     "wq": wq_slice(wq1_4, h0),
                     "qb": qb1[:, h0:h0 + 8],
                     "cosq": cosS.astype(BF), "sinq": sinS.astype(BF),
                     "kTf": _gather_batch(r9, "kT", b)[h0:h0 + 8],
                     "vf": vf, "mvar": mvar_pm})
    r10 = run_launch(build_attn_l1a, in10, "attnl1a")
    in11 = [{"ctxT": _ctx_for_core(r10, c), "wo": wo1_blk,
             "x": np.asarray(r8[c]["xo"]), "bias": ob1_row}
            for c in range(NCORES)]
    r11 = run_launch(build_wo_res_bias, in11, "woresl1a")
    in12 = [{"x": np.asarray(r11[c]["xo"]), "up": up_pm, "dn": dn_pm}
            for c in range(NCORES)]
    r12 = run_launch(build_ffn1, in12, "ffn1a")

    in13 = []
    for c in range(NCORES):
        pos = percore_pos[c]
        cosy, siny = cosyS[:, pos], sinyS[:, pos]
        in13.append({"x": np.asarray(r12[c]["xo"]), "wcq": wcq_blk,
                     "wckv": wckv_blk, "wkr": wkr_blk,
                     "cosy": cosy, "siny": siny})
    r13 = run_launch(build_pre_l1b, in13, "prel1b")
    in14 = []
    for c in range(NCORES):
        b, h0 = c // 4, 8 * (c % 4)
        in14.append({"cqf": _gather_batch(r13, "cqT", b),
                     "ckvf": _gather_batch(r13, "ckvT", b),
                     "krf": _gather_batch(r13, "krT", b, axis=1),
                     "wqn": np.ascontiguousarray(
                         wqn_blk[:, h0:h0 + 8].transpose(2, 0, 1, 3)).reshape(128, -1),
                     "wqr": np.ascontiguousarray(
                         wqr_blk[:, h0:h0 + 8].transpose(2, 0, 1, 3)).reshape(128, -1),
                     "wuk": np.ascontiguousarray(
                         wuk4[h0:h0 + 8].transpose(2, 1, 0, 3)).reshape(128, -1),
                     "wuv": np.ascontiguousarray(
                         wuv4[h0:h0 + 8].transpose(2, 1, 0, 3)).reshape(128, -1),
                     "cosy": cosyS, "siny": sinyS, "mvar": mvar_pm})
    r14 = run_launch(build_attn_l1b, in14, "attnl1b")
    in15 = [{"ctxT": _ctx_for_core(r14, c), "wo": wo1b_blk,
             "x": np.asarray(r12[c]["xo"])} for c in range(NCORES)]
    r15 = run_launch(build_wo_res, in15, "woresl1b")
    in16 = [{"x": np.asarray(r15[c]["xo"]), "up": up_pm, "dn": dn_pm}
            for c in range(NCORES)]
    r16 = run_launch(build_ffn1, in16, "ffn1b")

    out = np.empty((B, S, D), np.float32)
    for c in range(NCORES):
        out[c // 4, (c % 4) * TOK:(c % 4 + 1) * TOK, :] = \
            np.asarray(r16[c]["xo"]).reshape(TOK, D)
    res = np.empty_like(out)
    res[:, :, perm] = out
    return res


def emit_hash(env, po, src_chunks, out_chunks, hp1, hb1, hp2, hb2, nh, seq,
              psA, psB_, out_dram=None, out_dram_bufs=None):
    p = env.p
    t1 = env.sbuf("hh_t1", [128, 512], BF16)
    x1h = env.sbuf("hh_x1", [128, 512], BF16)
    t2 = env.sbuf("hh_t2", [128, 512], BF16)
    t3 = env.sbuf("hh_t3", [128, 512], BF16)
    stg = env.sbuf("hh_stg", [128, 512], BF16) if out_dram is not None else None
    for h in range(nh):
        src = src_chunks(h)
        for tb in range(seq // 512):
            cols = slice(tb * 512, (tb + 1) * 512)
            p.op("tensor", lambda e, h=h, cols=cols, src=src: e.matmul(
                psA.ap[:], hp1.ap[:, h * 128:(h + 1) * 128], src.ap[:, cols],
                start=True, stop=True), reads=[hp1, src], writes=[psA])
            p.op("scalar", lambda e, h=h: e.activation(
                out=t1.ap[:], in_=psA.ap[:], func=AF.Silu,
                bias=hb1.ap[:, h:h + 1]), reads=[psA, hb1], writes=[t1])
            p.op("vector", lambda e, cols=cols, src=src: e.tensor_tensor(
                out=x1h.ap[:], in0=t1.ap[:], in1=src.ap[:, cols], op=ALU.add),
                reads=[t1, src], writes=[x1h])
            p.op("tensor", lambda e, h=h: e.matmul(
                psB_.ap[:], hp2.ap[:, h * 128:(h + 1) * 128], x1h.ap[:],
                start=True, stop=True), reads=[hp2, x1h], writes=[psB_])
            p.op("scalar", lambda e, h=h: e.activation(
                out=t2.ap[:], in_=psB_.ap[:], func=AF.Identity,
                bias=hb2.ap[:, h:h + 1]), reads=[psB_, hb2], writes=[t2])
            p.op("vector", lambda e: e.tensor_tensor(
                out=t3.ap[:], in0=t2.ap[:], in1=x1h.ap[:], op=ALU.add),
                reads=[t2, x1h], writes=[t3])
            if out_dram is None:
                p.op("scalar", lambda e, h=h, cols=cols: e.activation(
                    out=out_chunks[h].ap[:, cols], in_=t3.ap[:], func=AF.Sign),
                    reads=[t3], writes=[out_chunks[h]])
            else:
                p.op("scalar", lambda e: e.activation(
                    out=stg.ap[:], in_=t3.ap[:], func=AF.Sign),
                    reads=[t3], writes=[stg])
                p.store(out_dram[h, :, cols], stg,
                        dst_buf=None if out_dram_bufs is None else out_dram_bufs[h])


def build_pre_l0b(env):
    po = Pools(env, n_pf=4, n_pt=2, ln=True, wstream=True, rope=True)
    p = env.p
    x_in = env.dram_in("x", [NTILE, 128, D])
    wk = env.dram_in("wk", [32, 8, 128, 128])
    wv = env.dram_in("wv", [32, 2, 128, 512])
    cosk = env.dram_in("cosk", [128, TOK])
    sink = env.dram_in("sink", [128, TOK])
    hp1_in = env.dram_in("hp1", [128, 32 * 128])
    hb1_in = env.dram_in("hb1", [128, 32])
    hp2_in = env.dram_in("hp2", [128, 32 * 128])
    hb2_in = env.dram_in("hb2", [128, 32])
    xnT_out = env.dram_out("xnT", [32, 128, TOK], BF16)
    kT_out = env.dram_out("kT", [8, 128, TOK], BF16)
    kh_out = env.dram_out("kh", [32, 128, TOK], BF16)
    v_out = env.dram_out("v", [NTILE, 128, 1024], BF16)

    xnT = [env.sbuf(f"xnT{c}", [128, TOK], BF16) for c in range(32)]
    emit_ln_tp_phase(env, po, x_in, xnT, xnT_out=xnT_out)
    for c in xnT:
        c.frozen = True
    cos_b = env.sbuf("cos_b", [128, TOK], F32)
    sin_b = env.sbuf("sin_b", [128, TOK], F32)
    p.load(cos_b, cosk)
    p.load(sin_b, sink)
    cos_b.frozen = True
    sin_b.frozen = True
    kT = [env.sbuf(f"kTc{cgrp}", [128, TOK], BF16) for cgrp in range(8)]
    emit_proj_feat(env, po, wk, xnT, kT, evac="rope", rope=(cos_b, sin_b),
                   out_store=kT_out)
    for b in kT:
        b.frozen = True
    vt = [env.sbuf(f"vt{t}", [128, 1024], BF16) for t in range(NTILE)]
    emit_proj_tok(env, po, wv, xnT, ncolblk=2, colw=512, out_tiles=vt,
                  out_store=v_out, npsum=2)
    hp1 = env.sbuf("hp1b", [128, 32 * 128], BF16)
    hp2 = env.sbuf("hp2b", [128, 32 * 128], BF16)
    hb1 = env.sbuf("hb1b", [128, 32], F32)
    hb2 = env.sbuf("hb2b", [128, 32], F32)
    p.load(hp1, hp1_in, queue="gpsimd")
    p.load(hp2, hp2_in, queue="gpsimd")
    p.load(hb1, hb1_in)
    p.load(hb2, hb2_in)
    for b in (hp1, hp2, hb1, hb2):
        b.frozen = True
    emit_hash(env, po, lambda h: kT[h // 4], None, hp1, hb1, hp2, hb2, 32, TOK,
              po.pf[2], po.pf[3], out_dram=kh_out)


def build_attn_l0b(env):
    po = Pools(env, n_pf=0, n_pt=2, ln=False, wstream=False, rope=True)
    p = env.p
    xnf = env.dram_in("xnf", [32, 128, S], BF16)
    wq_in = env.dram_in("wq", [32, 128, NPAIR * 128])
    cosq = env.dram_in("cosq", [128, S], BF16)
    sinq = env.dram_in("sinq", [128, S], BF16)
    hp1_in = env.dram_in("hp1", [128, NPAIR * 128])
    hb1_in = env.dram_in("hb1", [128, NPAIR])
    hp2_in = env.dram_in("hp2", [128, NPAIR * 128])
    hb2_in = env.dram_in("hb2", [128, NPAIR])
    khf = env.dram_in("khf", [NPAIR, 128, S], BF16)
    krf = env.dram_in("krf", [2, 128, S], BF16)
    vf = env.dram_in("vf", [128, NT * 256], BF16)
    mdiag_in = env.dram_in("mdiag", [128, 128])
    mdw_in = env.dram_in("mdiagw", [128, 4 * 512])
    idf_in = env.dram_in("c_identf", [128, 128], F32)
    ctx_out = env.dram_out("ctx", [NPAIR, 128, S], BF16)

    v_all = env.sbuf("v_all", [128, NT * 256], BF16)
    p.load(v_all, vf)
    identf = env.sbuf("identf_b", [128, 128], F32)
    p.load(identf, idf_in)
    identf.frozen = True
    mdiag = env.sbuf("mdiag_b", [128, 128], F32)
    p.load(mdiag, mdiag_in)
    causal01 = env.sbuf("caus01", [128, 128], BF16)
    p.op("vector", lambda e: e.tensor_scalar(
        out=causal01.ap[:], in0=mdiag.ap[:], scalar1=-1.0, scalar2=None,
        op0=ALU.is_ge), reads=[mdiag], writes=[causal01])
    for b in (v_all, mdiag, causal01):
        b.frozen = True

    qrd = env.dram_tmp("qrd", [NPAIR, 128, S], BF16)
    qhd = env.dram_tmp("qhd", [NPAIR, 128, S], BF16)
    qrd_b = [Buf(None) for _ in range(NPAIR)]
    qhd_b = [Buf(None) for _ in range(NPAIR)]
    env.push_scope()
    pfA = [env.psum(f"pa{i}", [128, 512], F32) for i in range(4)]
    qrT = [env.sbuf(f"qrT{j}", [128, S], BF16) for j in range(NPAIR)]
    cos_b = env.sbuf("cosb", [128, S], BF16)
    sin_b = env.sbuf("sinb", [128, S], BF16)
    p.load(cos_b, cosq)
    p.load(sin_b, sinq)
    hp1 = env.sbuf("hp1b", [128, NPAIR * 128], BF16)
    hp2 = env.sbuf("hp2b", [128, NPAIR * 128], BF16)
    hb1 = env.sbuf("hb1b", [128, NPAIR], F32)
    hb2 = env.sbuf("hb2b", [128, NPAIR], F32)
    p.load(hp1, hp1_in, queue="gpsimd")
    p.load(hp2, hp2_in, queue="gpsimd")
    p.load(hb1, hb1_in)
    p.load(hb2, hb2_in)
    for b in (cos_b, sin_b, hp1, hp2, hb1, hb2):
        b.frozen = True
    xtl = [env.sbuf(f"qp_xl{i}", [128, 8 * 512], BF16) for i in range(2)]
    xth = [env.sbuf(f"qp_xh{i}", [128, 8 * 512], BF16) for i in range(2)]
    wqb = [env.sbuf(f"wqb{i}", [128, NPAIR * 128], BF16) for i in range(2)]
    for grp in range(2):
        pj = range(grp * 4, grp * 4 + 4)
        for tb in range(8):
            for half in range(2):
                d0 = half * 16
                par = (tb * 2 + half) % 2
                xl, xh = xtl[par], xth[par]
                p.load(xl, xnf[d0:d0 + 8, :, tb * 512:(tb + 1) * 512]
                       .rearrange("a p m -> p a m"), queue="sync",
                       dst_ap=xl.ap[:].rearrange("p (a m) -> p a m", a=8))
                p.load(xh, xnf[d0 + 8:d0 + 16, :, tb * 512:(tb + 1) * 512]
                       .rearrange("a p m -> p a m"), queue="scalar",
                       dst_ap=xh.ap[:].rearrange("p (a m) -> p a m", a=8))
                for dm in range(d0, d0 + 16):
                    wb = wqb[dm % 2]
                    p.load(wb, wq_in[dm, :, :], queue="gpsimd")
                    src = xl if dm - d0 < 8 else xh
                    off = (dm - d0) % 8
                    for j in pj:
                        def mk(e, j=j, dm=dm, off=off, wb=wb, src=src):
                            return e.matmul(
                                pfA[j % 4].ap[:], wb.ap[:, j * 128:(j + 1) * 128],
                                src.ap[:, off * 512:(off + 1) * 512],
                                start=(dm == 0), stop=(dm == 31))
                        p.op("tensor", mk, reads=[wb, src], writes=[pfA[j % 4]])
            for j in pj:
                cols = slice(tb * 512, (tb + 1) * 512)
                emit_rope_evac(env, po, pfA[j % 4], qrT[j], cols, cos_b, sin_b)
    for b in qrT:
        b.frozen = True
    emit_hash(env, po, lambda j: qrT[j], None, hp1, hb1, hp2, hb2, NPAIR, S,
              pfA[0], pfA[1], out_dram=qhd, out_dram_bufs=qhd_b)
    for j in range(NPAIR):
        p.store(qrd[j, :, :], qrT[j], dst_buf=qrd_b[j])
    env.pop_scope()

    env.push_scope()
    pfC = [env.psum(f"pc{i}", [128, 512], F32) for i in range(6)]
    psSim2, psSc2 = pfC[0:2], pfC[2:4]
    psO, psZ = pfC[4], pfC[5]
    kh_p = [env.sbuf(f"kh_p{i}", [128, S], BF16) for i in range(2)]
    kr_p = [env.sbuf(f"kr_p{i}", [128, S], BF16) for i in range(2)]
    qh_p = [env.sbuf(f"qh_p{i}", [128, S], BF16) for i in range(2)]
    qr_p = [env.sbuf(f"qr_p{i}", [128, S], BF16) for i in range(2)]
    sim2 = [env.sbuf(f"sim{i}", [128, S], BF16) for i in range(2)]
    pm2 = [env.sbuf(f"pmb{i}", [128, S], BF16) for i in range(2)]
    junk = env.sbuf("junkb", [128, S], BF16)
    onesb = env.sbuf("onesb", [128, S], BF16)
    p.op("vector", lambda e: e.memset(onesb.ap[:], 1.0), writes=[onesb])
    onesb.frozen = True
    scanb = env.sbuf("scanb", [128, S], F32)
    e1 = env.sbuf("e1b", [128, S], BF16)
    tieok = env.sbuf("tieokb", [128, S], BF16)
    kept = env.sbuf("keptb", [128, S], BF16)
    ptb = [env.sbuf(f"ptb{i}", [128, 128], BF16) for i in range(2)]
    sv = [{n: env.sbuf(f"sv_{n}{i}", [128, 1], F32)
           for n in ("tbm0", "acc", "s2", "bias", "tval", "rr", "zc", "rzc")}
          for i in range(2)]
    zrow = env.sbuf("zrowb", [1, 128], F32)
    bz = env.sbuf("bzb", [128, 128], F32)
    onesr = env.sbuf("onesrb", [1, 128], F32)
    p.op("vector", lambda e: e.memset(onesr.ap[:], 1.0), writes=[onesr])
    onesr.frozen = True
    ctxs2 = [env.sbuf(f"ctxsb{i}", [128, 128], BF16) for i in range(2)]
    junkV = env.sbuf("junkV", [128, 768], BF16)
    STEPS = (128, 64, 32, 16, 8, 4, 2, 1)
    mdiagw = env.sbuf("mdiagw_b", [128, 4 * 512], F32)
    p.load(mdiagw, mdw_in)
    mdiagw.frozen = True

    def emit_ss(j, pj, qt):
        par = qt % 2
        simb, pmb = sim2[par], pm2[par]
        psA, psB2 = psSim2[par], psSc2[par]
        ext = (qt + 1) * 128
        nch = (ext + 511) // 512
        r = qt % 4
        for ch in range(nch):
            n = min(512, ext - ch * 512)
            p.op("tensor", lambda e, qt=qt, ch=ch, n=n, psA=psA: e.matmul(
                psA.ap[:, 0:n], qh_p[pj].ap[:, qt * 128:(qt + 1) * 128],
                kh_p[pj].ap[:, ch * 512:ch * 512 + n], start=True, stop=True),
                reads=[qh_p[pj], kh_p[pj]], writes=[psA])
            if ch == nch - 1:
                p.op("vector", lambda e, ch=ch, n=n, psA=psA, simb=simb, r=r:
                     e.scalar_tensor_tensor(
                         out=simb.ap[:, ch * 512:ch * 512 + n],
                         in0=psA.ap[:, 0:n], scalar=1.0,
                         in1=mdiagw.ap[:, r * 512:r * 512 + n],
                         op0=ALU.mult, op1=ALU.add),
                     reads=[psA, mdiagw], writes=[simb])
            else:
                p.op("scalar", lambda e, ch=ch, n=n, psA=psA, simb=simb: e.copy(
                    out=simb.ap[:, ch * 512:ch * 512 + n], in_=psA.ap[:, 0:n]),
                    reads=[psA], writes=[simb])
        krb = kr_p[(j // 4) % 2]
        for ch in range(nch):
            n = min(512, ext - ch * 512)
            p.op("tensor", lambda e, qt=qt, ch=ch, n=n, psB2=psB2, krb=krb: e.matmul(
                psB2.ap[:, 0:n], qr_p[pj].ap[:, qt * 128:(qt + 1) * 128],
                krb.ap[:, ch * 512:ch * 512 + n], start=True, stop=True),
                reads=[qr_p[pj], krb], writes=[psB2])
            p.op("scalar", lambda e, ch=ch, n=n, psB2=psB2, pmb=pmb: e.activation(
                out=pmb.ap[:, ch * 512:ch * 512 + n], in_=psB2.ap[:, 0:n],
                func=AF.Exp, scale=1.0 / SQ128),
                reads=[psB2], writes=[pmb])

    def emit_search(qt, dq):
        par = qt % 2
        simb = sim2[par]
        v = sv[par]
        ext = (qt + 1) * 128
        nsl = len(STEPS) + 1
        chunk = max(1, (len(dq) + nsl - 1) // nsl)

        def drain(k):
            for _ in range(k):
                if dq:
                    dq.pop(0)()
        p.op("vector", lambda e, v=v: e.memset(v["tbm0"].ap[:], 0.0),
             writes=[v["tbm0"]])
        for step in STEPS:
            p.op("vector", lambda e, v=v, step=step: e.tensor_scalar(
                out=v["bias"].ap[:], in0=v["tbm0"].ap[:], scalar1=-2.0,
                scalar2=float(-2 * step + 130.5), op0=ALU.mult, op1=ALU.add),
                reads=[v["tbm0"]], writes=[v["bias"]])
            p.op("scalar", lambda e, v=v, ext=ext, simb=simb: e.activation(
                out=junk.ap[:, 0:ext], in_=simb.ap[:, 0:ext], func=AF.Sign,
                bias=v["bias"].ap[:], accum_out=v["acc"].ap[:]),
                reads=[simb, v["bias"]], writes=[junk, v["acc"]])
            drain(chunk)
            p.op("vector", lambda e, v=v, ext=ext, step=step: e.tensor_scalar(
                out=v["s2"].ap[:], in0=v["acc"].ap[:],
                scalar1=float(1640 - ext), scalar2=float(step),
                op0=ALU.is_ge, op1=ALU.mult),
                reads=[v["acc"]], writes=[v["s2"]])
            p.op("vector", lambda e, v=v: e.tensor_tensor(
                out=v["tbm0"].ap[:], in0=v["tbm0"].ap[:], in1=v["s2"].ap[:],
                op=ALU.add), reads=[v["tbm0"], v["s2"]], writes=[v["tbm0"]])
        p.op("vector", lambda e, v=v: e.tensor_scalar(
            out=v["tval"].ap[:], in0=v["tbm0"].ap[:], scalar1=2.0,
            scalar2=-130.0, op0=ALU.mult, op1=ALU.add),
            reads=[v["tbm0"]], writes=[v["tval"]])
        p.op("vector", lambda e, v=v: e.tensor_scalar(
            out=v["bias"].ap[:], in0=v["tval"].ap[:], scalar1=-1.0,
            scalar2=-0.5, op0=ALU.mult, op1=ALU.add),
            reads=[v["tval"]], writes=[v["bias"]])
        p.op("scalar", lambda e, v=v, ext=ext, simb=simb: e.activation(
            out=junk.ap[:, 0:ext], in_=simb.ap[:, 0:ext], func=AF.Sign,
            bias=v["bias"].ap[:], accum_out=v["acc"].ap[:]),
            reads=[simb, v["bias"]], writes=[junk, v["acc"]])
        drain(len(dq))
        p.op("vector", lambda e, v=v, ext=ext: e.tensor_scalar(
            out=v["rr"].ap[:], in0=v["acc"].ap[:], scalar1=-0.5,
            scalar2=float(820 - ext // 2), op0=ALU.mult, op1=ALU.add),
            reads=[v["acc"]], writes=[v["rr"]])

    def emit_post(j, qt):
        par = qt % 2
        simb, pmb = sim2[par], pm2[par]
        psB2, psSimb = psSc2[par], psSim2[par]
        v = sv[par]
        ctxs = ctxs2[par]
        ext = (qt + 1) * 128
        dq = []
        if qt >= 6:
            dq.append(lambda: p.op("vector", lambda e: e.tensor_scalar(
                out=e1.ap[:, 0:ext], in0=simb.ap[:, 0:ext],
                scalar1=v["tval"].ap[:], scalar2=None, op0=ALU.is_equal),
                reads=[simb, v["tval"]], writes=[e1]))
            dq.append(lambda: p.op("vector", lambda e: e.tensor_tensor_scan(
                out=scanb.ap[:, 0:ext], data0=onesb.ap[:, 0:ext],
                data1=e1.ap[:, 0:ext], initial=0.0, op0=ALU.mult, op1=ALU.add),
                reads=[onesb, e1], writes=[scanb]))
            dq.append(lambda: p.op("vector", lambda e: e.scalar_tensor_tensor(
                out=tieok.ap[:, 0:ext], in0=scanb.ap[:, 0:ext],
                scalar=v["rr"].ap[:], in1=e1.ap[:, 0:ext],
                op0=ALU.is_le, op1=ALU.mult),
                reads=[scanb, v["rr"], e1], writes=[tieok]))
            dq.append(lambda: p.op("vector", lambda e: e.scalar_tensor_tensor(
                out=kept.ap[:, 0:ext], in0=simb.ap[:, 0:ext],
                scalar=v["tval"].ap[:], in1=tieok.ap[:, 0:ext],
                op0=ALU.is_gt, op1=ALU.max),
                reads=[simb, v["tval"], tieok], writes=[kept]))
            dq.append(lambda: p.op("vector", lambda e: e.scalar_tensor_tensor(
                out=pmb.ap[:, 0:ext], in0=pmb.ap[:, 0:ext], scalar=1.0,
                in1=kept.ap[:, 0:ext], op0=ALU.mult, op1=ALU.mult,
                accum_out=v["zc"].ap[:]),
                reads=[pmb, kept], writes=[pmb, v["zc"]]))
        else:
            dq.append(lambda: p.op("vector", lambda e: e.tensor_tensor(
                out=pmb.ap[:, qt * 128:ext], in0=pmb.ap[:, qt * 128:ext],
                in1=causal01.ap[:], op=ALU.mult),
                reads=[pmb, causal01], writes=[pmb]))
            dq.append(lambda: p.op("vector", lambda e: e.tensor_scalar(
                out=junkV.ap[:, 0:ext], in0=pmb.ap[:, 0:ext], scalar1=1.0,
                scalar2=0.0, op0=ALU.mult, op1=ALU.add,
                accum_out=v["zc"].ap[:]),
                reads=[pmb], writes=[junkV, v["zc"]]))
        dq.append(lambda: p.op("vector", lambda e: e.reciprocal(
            out=v["rzc"].ap[:], in_=v["zc"].ap[:]),
            reads=[v["zc"]], writes=[v["rzc"]]))
        dq.append(lambda: p.op("tensor", lambda e: e.matmul(
            psSimb.ap[0:1, 0:128], v["rzc"].ap[:], identf.ap[:],
            start=True, stop=True), reads=[v["rzc"], identf], writes=[psSimb]))
        dq.append(lambda: p.op("vector", lambda e: e.tensor_copy(
            out=zrow.ap[:], in_=psSimb.ap[0:1, 0:128]),
            reads=[psSimb], writes=[zrow]))
        dq.append(lambda: p.op("tensor", lambda e: e.matmul(
            psB2.ap[:, 0:128], onesr.ap[:], zrow.ap[:], start=True, stop=True),
            reads=[onesr, zrow], writes=[psB2]))
        dq.append(lambda: p.op("vector", lambda e: e.tensor_copy(
            out=bz.ap[:], in_=psB2.ap[:, 0:128]), reads=[psB2], writes=[bz]))
        for kk in range(qt + 1):
            pt = po.next_tp()
            pb = ptb[kk % 2]
            dq.append(lambda pt=pt, kk=kk: p.op(
                "tensor", lambda e: e.transpose(
                    pt.ap[:], pmb.ap[:, kk * 128:(kk + 1) * 128], po.ident.ap[:]),
                reads=[pmb, po.ident], writes=[pt]))
            dq.append(lambda pt=pt, pb=pb: p.op(
                "vector", lambda e: e.tensor_copy(out=pb.ap[:], in_=pt.ap[:]),
                reads=[pt], writes=[pb]))
            vc0 = kk * 256 + (j // 4) * 128
            dq.append(lambda pb=pb, kk=kk, vc0=vc0: p.op(
                "tensor", lambda e: e.matmul(
                    psO.ap[:, 0:128], v_all.ap[:, vc0:vc0 + 128], pb.ap[:],
                    start=(kk == 0), stop=(kk == qt)),
                reads=[v_all, pb], writes=[psO]))
        dq.append(lambda: p.op("vector", lambda e: e.tensor_tensor(
            out=ctxs.ap[:], in0=psO.ap[:, 0:128], in1=bz.ap[:], op=ALU.mult),
            reads=[psO, bz], writes=[ctxs]))
        dq.append(lambda: p.store(ctx_out[j, :, qt * 128:(qt + 1) * 128], ctxs,
                                  queue="sync"))
        return dq

    for j in range(NPAIR):
        pj = j % 2
        p.load(kh_p[pj], khf[j, :, :], queue="sync")
        if j % 4 == 0:
            p.load(kr_p[(j // 4) % 2], krf[j // 4, :, :], queue="scalar")
        p.load(qh_p[pj], qhd[j, :, :], src_buf=qhd_b[j], queue="sync")
        p.load(qr_p[pj], qrd[j, :, :], src_buf=qrd_b[j], queue="scalar")
        dq = []
        emit_ss(j, pj, 0)
        for qt in range(NT):
            if qt >= 6:
                emit_search(qt, dq)
            else:
                for f in dq:
                    f()
                dq = []
            dq = emit_post(j, qt)
            if qt + 1 < NT:
                emit_ss(j, pj, qt + 1)
        for f in dq:
            f()
    env.pop_scope()


def build_pre_l1a(env):
    po = Pools(env, n_pf=4, n_pt=2, ln=True, wstream=True, rope=True)
    p = env.p
    x_in = env.dram_in("x", [NTILE, 128, D])
    wk = env.dram_in("wk", [32, 32, 128, 128])
    wv = env.dram_in("wv", [32, 8, 128, 512])
    kb_in = env.dram_in("kb", [128, 32])
    vb_in = env.dram_in("vbrow", [128, D])
    cosk = env.dram_in("cosk", [128, TOK])
    sink = env.dram_in("sink", [128, TOK])
    xnT_out = env.dram_out("xnT", [32, 128, TOK], BF16)
    kT_out = env.dram_out("kT", [32, 128, TOK], BF16)
    v_out = env.dram_out("v", [NTILE, 128, D], BF16)

    xnT = [env.sbuf(f"xnT{c}", [128, TOK], BF16) for c in range(32)]
    emit_ln_tp_phase(env, po, x_in, xnT, xnT_out=xnT_out)
    for c in xnT:
        c.frozen = True
    cos_b = env.sbuf("cos_b", [128, TOK], F32)
    sin_b = env.sbuf("sin_b", [128, TOK], F32)
    p.load(cos_b, cosk)
    p.load(sin_b, sink)
    kb = env.sbuf("kb_b", [128, 32], F32)
    p.load(kb, kb_in)
    vb = env.sbuf("vb_b", [128, D], F32)
    p.load(vb, vb_in)
    for b in (cos_b, sin_b, kb, vb):
        b.frozen = True
    kc2 = [env.sbuf(f"kc{i}", [128, TOK], BF16) for i in range(2)]
    for oc in range(32):
        kc = kc2[oc % 2]
        emit_proj_feat(env, po, wk, xnT, [kc], evac="rope",
                       rope=(cos_b, sin_b), oc_base=oc,
                       bias=lambda _oc, oc=oc: (kb, kb.ap[:, oc:oc + 1]))
        p.store(kT_out[oc, :, :], kc)
    vt = [env.sbuf(f"vt{t}", [128, 512], BF16) for t in range(2)]

    def evac_v(cb, ti, ps):
        vtb = vt[ti % 2]
        p.op("vector", lambda e, ps=ps, cb=cb, vtb=vtb: e.scalar_tensor_tensor(
            out=vtb.ap[:], in0=ps.ap[:], scalar=1.0,
            in1=vb.ap[:, cb * 512:(cb + 1) * 512], op0=ALU.mult, op1=ALU.add),
            reads=[ps, vb], writes=[vtb])
        p.store(v_out[ti, :, cb * 512:(cb + 1) * 512], vtb)

    emit_proj_tok(env, po, wv, xnT, ncolblk=8, evac_fn=evac_v, npsum=4)


def build_attn_l1a(env):
    po = Pools(env, n_pf=8, n_pt=0, ln=False, wstream=False, rope=True)
    p = env.p
    xnf = env.dram_in("xnf", [32, 128, S], BF16)
    wq = env.dram_in("wq", [128, 32 * NPAIR * 128])
    qb_in = env.dram_in("qb", [128, NPAIR])
    cosq = env.dram_in("cosq", [128, S], BF16)
    sinq = env.dram_in("sinq", [128, S], BF16)
    kTf = env.dram_in("kTf", [NPAIR, 128, S], BF16)
    vf = env.dram_in("vf", [NPAIR, 128, NT * 128], BF16)
    mvar = env.dram_in("mvar", [128, 4 * 512])
    ctx_out = env.dram_out("ctx", [NPAIR, 128, S], BF16)

    wq_all = env.sbuf("wq_all", [128, 32 * NPAIR * 128], BF16)
    p.load(wq_all, wq, queue="gpsimd")
    qb = env.sbuf("qb_b", [128, NPAIR], F32)
    p.load(qb, qb_in)
    cos_b = env.sbuf("cosb", [128, S], BF16)
    sin_b = env.sbuf("sinb", [128, S], BF16)
    p.load(cos_b, cosq)
    p.load(sin_b, sinq)
    mv = env.sbuf("mv_b", [128, 4 * 512], F32)
    p.load(mv, mvar)
    for b in (wq_all, qb, cos_b, sin_b, mv):
        b.frozen = True

    qT_own = [env.sbuf(f"qT{j}", [128, S], BF16) for j in range(NPAIR)]

    def evac_rope(j, tb, ps, dst, cols):
        emit_rope_evac(env, po, ps, dst, cols, cos_b, sin_b,
                       bias=(qb, qb.ap[:, j:j + 1]))

    emit_qproj_pairs(env, po, wq_all, xnf, qT_own, evac_fn=evac_rope)
    for b in qT_own:
        b.frozen = True

    k_p = env.sbuf("k_p", [128, S], BF16)
    v_p = env.sbuf("v_p", [128, NT * 128], BF16)
    tmp2 = [env.sbuf(f"at_tmp{i}", [128, 512], F32) for i in range(2)]
    cur = [-1]

    def kget(j):
        if cur[0] != j:
            p.load(k_p, kTf[j, :, :])
            p.load(v_p, vf[j, :, :])
            cur[0] = j
        return (k_p, 0)

    def exp_evac(j, qblk, kt, psS, ptile):
        tmp = tmp2[kt % 2]
        r = kt - 4 * qblk
        if 0 <= r <= 3:
            p.op("vector", lambda e, psS=psS, r=r, tmp=tmp: e.scalar_tensor_tensor(
                out=tmp.ap[:], in0=mv.ap[:, r * 512:(r + 1) * 512], scalar=1.0,
                in1=psS.ap[:], op0=ALU.mult, op1=ALU.add),
                reads=[mv, psS], writes=[tmp])
            p.op("scalar", lambda e, ptile=ptile, tmp=tmp: e.activation(
                out=ptile.ap[:], in_=tmp.ap[:], func=AF.Exp, scale=1.0 / SQ128),
                reads=[tmp], writes=[ptile])
        else:
            p.op("scalar", lambda e, psS=psS, ptile=ptile: e.activation(
                out=ptile.ap[:], in_=psS.ap[:], func=AF.Exp, scale=1.0 / SQ128),
                reads=[psS], writes=[ptile])

    emit_attn_pairs(env, po, qT_own, kget, lambda j: (v_p, 0), ctx_out,
                    exp_evac=exp_evac)


def build_ffn1(env):
    po = Pools(env, n_pf=4, n_pt=2, ln=True, wstream=False, rope=False)
    p = env.p
    x_in = env.dram_in("x", [NTILE, 128, D])
    up_in = env.dram_in("up", [128, 16 * 2 * 8 * 128])
    dn_in = env.dram_in("dn", [128, 16 * 8 * 256])
    x_out = env.dram_out("xo", [NTILE, 128, D])
    up_all = env.sbuf("up_all", [128, 16 * 2 * 8 * 128], BF16)
    dn_all = env.sbuf("dn_all", [128, 16 * 8 * 256], BF16)
    p.load(up_all, up_in, queue="gpsimd")
    p.load(dn_all, dn_in, queue="gpsimd")
    up_all.frozen = True
    dn_all.frozen = True
    xw = env.sbuf("f1_x", [128, D], F32)
    xnw = env.sbuf("f1_xn", [128, D], BF16)
    xnTt = env.sbuf("f1_xnT", [128, D], BF16)
    gT = env.sbuf("f1_gT", [128, 8 * 128], BF16)
    xob = env.sbuf("f1_xo", [128, 256], F32)
    psG, psD = po.pf[0], po.pf[1]
    for ti in range(NTILE):
        p.load(xw, x_in[ti, :, :], queue="sync")
        emit_ln(env, po, xw, xnw)
        for c in range(32):
            emit_tp128(env, po, xnw, xnw.ap[:, c * 128:(c + 1) * 128],
                       xnTt, xnTt.ap[:, c * 128:(c + 1) * 128])
        for h in range(16):
            for mc in range(8):
                for dc in range(2):
                    c0 = ((h * 2 + dc) * 8 + mc) * 128
                    p.op("tensor", lambda e, h=h, mc=mc, dc=dc, c0=c0: e.matmul(
                        psG.ap[:, 0:128], up_all.ap[:, c0:c0 + 128],
                        xnTt.ap[:, (2 * h + dc) * 128:(2 * h + dc + 1) * 128],
                        start=(dc == 0), stop=(dc == 1)),
                        reads=[up_all, xnTt], writes=[psG])
                p.op("scalar", lambda e, mc=mc: e.activation(
                    out=gT.ap[:, mc * 128:(mc + 1) * 128], in_=psG.ap[:, 0:128],
                    func=AF.Gelu), reads=[psG], writes=[gT])
            for mc in range(8):
                c0 = (h * 8 + mc) * 256
                p.op("tensor", lambda e, h=h, mc=mc, c0=c0: e.matmul(
                    psD.ap[:, 0:256], gT.ap[:, mc * 128:(mc + 1) * 128],
                    dn_all.ap[:, c0:c0 + 256],
                    start=(mc == 0), stop=(mc == 7)),
                    reads=[gT, dn_all], writes=[psD])
            p.op("vector", lambda e, h=h: e.scalar_tensor_tensor(
                out=xob.ap[:], in0=psD.ap[:, 0:256], scalar=1.0,
                in1=xw.ap[:, h * 256:(h + 1) * 256], op0=ALU.mult, op1=ALU.add),
                reads=[psD, xw], writes=[xob])
            p.store(x_out[ti, :, h * 256:(h + 1) * 256], xob)


def build_pre_l1b(env):
    po = Pools(env, n_pf=4, n_pt=2, ln=True, wstream=True, rope=True)
    p = env.p
    x_in = env.dram_in("x", [NTILE, 128, D])
    wcq = env.dram_in("wcq", [32, 12, 128, 128])
    wckv = env.dram_in("wckv", [32, 4, 128, 128])
    wkr = env.dram_in("wkr", [32, 1, 128, 64])
    cosy = env.dram_in("cosy", [64, TOK])
    siny = env.dram_in("siny", [64, TOK])
    xnT_out = env.dram_out("xnT", [32, 128, TOK], BF16)
    cq_out = env.dram_out("cqT", [12, 128, TOK], BF16)
    ckv_out = env.dram_out("ckvT", [4, 128, TOK], BF16)
    kr_out = env.dram_out("krT", [64, TOK], BF16)

    xnT = [env.sbuf(f"xnT{c}", [128, TOK], BF16) for c in range(32)]
    emit_ln_tp_phase(env, po, x_in, xnT, xnT_out=xnT_out)
    for c in xnT:
        c.frozen = True
    cqT = [env.sbuf(f"cqT{i}", [128, TOK], BF16) for i in range(12)]
    emit_proj_feat(env, po, wcq, xnT, cqT, out_store=cq_out)
    ckvT = [env.sbuf(f"ckvT{i}", [128, TOK], BF16) for i in range(4)]
    emit_proj_feat(env, po, wckv, xnT, ckvT, out_store=ckv_out)
    cos_b = env.sbuf("cosy_b", [64, TOK], F32)
    sin_b = env.sbuf("siny_b", [64, TOK], F32)
    p.load(cos_b, cosy)
    p.load(sin_b, siny)
    cos_b.frozen = True
    sin_b.frozen = True
    krc = env.sbuf("krc", [64, TOK], BF16)
    ps = po.pf[0]
    for tb in range(TOK // 512):
        for dm in range(32):
            wb = po.next_wb()
            p.load(wb, wkr[dm, 0, :, 0:64], queue="gpsimd", dst_ap=wb.ap[:, 0:64])
            p.op("tensor", lambda e, dm=dm, tb=tb, wb=wb: e.matmul(
                ps.ap[0:64, :], wb.ap[:, 0:64],
                xnT[dm].ap[:, tb * 512:(tb + 1) * 512],
                start=(dm == 0), stop=(dm == 31)),
                reads=[wb, xnT[dm]], writes=[ps])
        emit_rope_evac64(env, po, ps, krc, slice(tb * 512, (tb + 1) * 512),
                         cos_b, sin_b, 0)
    p.store(kr_out, krc)


def emit_rope_evac64(env, po, ps, ob, cols, cos_b, sin_b, row0, bias=None):
    p = env.p
    t1, t2 = po.rope_t1, po.rope_t2
    src = ps
    sr = row0
    if bias is not None:
        bbuf, bap = bias
        tb = po.rope_tb
        p.op("scalar", lambda e: e.activation(out=tb.ap[sr:sr + 64, 0:512],
                                              in_=ps.ap[sr:sr + 64, 0:512],
                                              func=AF.Identity, bias=bap),
             reads=[ps, bbuf], writes=[tb])
        src = tb
    ncol = cols.stop - cols.start
    sa = src.ap[sr:sr + 32, 0:ncol]
    sb_ = src.ap[sr + 32:sr + 64, 0:ncol]
    ca = cos_b.ap[row0:row0 + 32, cols]
    sn = sin_b.ap[row0 + 32:row0 + 64, cols]
    cb = cos_b.ap[row0 + 32:row0 + 64, cols]
    snb = sin_b.ap[row0:row0 + 32, cols]
    p.op("vector", lambda e: e.tensor_tensor(out=t1.ap[0:32, 0:ncol], in0=sa, in1=ca, op=ALU.mult),
         reads=[src, cos_b], writes=[t1])
    p.op("vector", lambda e: e.tensor_tensor(out=t2.ap[0:32, 0:ncol], in0=sb_, in1=sn, op=ALU.mult),
         reads=[src, sin_b], writes=[t2])
    p.op("vector", lambda e: e.tensor_tensor(out=ob.ap[row0:row0 + 32, cols], in0=t1.ap[0:32, 0:ncol],
                                             in1=t2.ap[0:32, 0:ncol], op=ALU.subtract),
         reads=[t1, t2], writes=[ob])
    p.op("vector", lambda e: e.tensor_tensor(out=t1.ap[0:32, 0:ncol], in0=sb_, in1=cb, op=ALU.mult),
         reads=[src, cos_b], writes=[t1])
    p.op("vector", lambda e: e.tensor_tensor(out=t2.ap[0:32, 0:ncol], in0=sa, in1=snb, op=ALU.mult),
         reads=[src, sin_b], writes=[t2])
    p.op("vector", lambda e: e.tensor_tensor(out=ob.ap[row0 + 32:row0 + 64, cols], in0=t1.ap[0:32, 0:ncol],
                                             in1=t2.ap[0:32, 0:ncol], op=ALU.add),
         reads=[t1, t2], writes=[ob])


def build_attn_l1b(env):
    po = Pools(env, n_pf=7, n_pt=0, ln=False, wstream=False, rope=True)
    p = env.p
    cqf = env.dram_in("cqf", [12, 128, S], BF16)
    ckvf = env.dram_in("ckvf", [4, 128, S], BF16)
    krf = env.dram_in("krf", [64, S], BF16)
    wqn = env.dram_in("wqn", [128, 12 * NPAIR * 128])
    wqr = env.dram_in("wqr", [128, 12 * NPAIR * 64])
    wuk = env.dram_in("wuk", [128, 4 * NPAIR * 128])
    wuv = env.dram_in("wuv", [128, 4 * NPAIR * 128])
    cosy = env.dram_in("cosy", [64, S])
    siny = env.dram_in("siny", [64, S])
    mvar = env.dram_in("mvar", [128, 4 * 512])
    ctx_out = env.dram_out("ctx", [NPAIR, 128, S], BF16)

    ckvT = [env.sbuf(f"ckvT{i}", [128, S], BF16) for i in range(4)]
    for i in range(4):
        p.load(ckvT[i], ckvf[i, :, :])
        ckvT[i].frozen = True
    krT = env.sbuf("krT_b", [64, S], BF16)
    p.load(krT, krf)
    krT.frozen = True
    w_qn = env.sbuf("wqn_b", [128, 12 * NPAIR * 128], BF16)
    w_qr = env.sbuf("wqr_b", [128, 12 * NPAIR * 64], BF16)
    w_uk = env.sbuf("wuk_b", [128, 4 * NPAIR * 128], BF16)
    w_uv = env.sbuf("wuv_b", [128, 4 * NPAIR * 128], BF16)
    p.load(w_qn, wqn, queue="gpsimd")
    p.load(w_qr, wqr, queue="gpsimd")
    p.load(w_uk, wuk, queue="gpsimd")
    p.load(w_uv, wuv, queue="gpsimd")
    cos_b = env.sbuf("cosy_b", [64, S], F32)
    sin_b = env.sbuf("siny_b", [64, S], F32)
    p.load(cos_b, cosy)
    p.load(sin_b, siny)
    mv = env.sbuf("mv_b", [128, 4 * 512], F32)
    p.load(mv, mvar)
    for b in (w_qn, w_qr, w_uk, w_uv, cos_b, sin_b, mv):
        b.frozen = True

    cqt = env.sbuf("cq_x", [128, 12 * 512], BF16)
    qnT = env.sbuf("qnT", [128, S], BF16)
    qroT = env.sbuf("qroT", [64, S], BF16)
    k_p = env.sbuf("k_p", [128, S], BF16)
    v_p = env.sbuf("v_p", [128, NT * 128], BF16)
    tmp2 = [env.sbuf(f"at_tmp{i}", [128, 512], F32) for i in range(2)]
    SQ192 = float(np.sqrt(192.0))

    for j in range(NPAIR):
        for tb in range(8):
            p.load(cqt, cqf[:, :, tb * 512:(tb + 1) * 512]
                   .rearrange("a p m -> p a m"), queue="sync",
                   dst_ap=cqt.ap[:].rearrange("p (a m) -> p a m", a=12))
            for dm in range(12):
                c0 = (dm * NPAIR + j) * 128
                p.op("tensor", lambda e, dm=dm, c0=c0: e.matmul(
                    po.pf[0].ap[:], w_qn.ap[:, c0:c0 + 128],
                    cqt.ap[:, dm * 512:(dm + 1) * 512],
                    start=(dm == 0), stop=(dm == 11)),
                    reads=[w_qn, cqt], writes=[po.pf[0]])
                c1 = (dm * NPAIR + j) * 64
                p.op("tensor", lambda e, dm=dm, c1=c1: e.matmul(
                    po.pf[1].ap[0:64, :], w_qr.ap[:, c1:c1 + 64],
                    cqt.ap[:, dm * 512:(dm + 1) * 512],
                    start=(dm == 0), stop=(dm == 11)),
                    reads=[w_qr, cqt], writes=[po.pf[1]])
            cols = slice(tb * 512, (tb + 1) * 512)
            p.op("scalar", lambda e, cols=cols: e.copy(
                out=qnT.ap[:, cols], in_=po.pf[0].ap[:]),
                reads=[po.pf[0]], writes=[qnT])
            emit_rope_evac64(env, po, po.pf[1], qroT, cols, cos_b, sin_b, 0)
        for tb in range(8):
            for i in range(4):
                c0 = (i * NPAIR + j) * 128
                p.op("tensor", lambda e, i=i, c0=c0, tb=tb: e.matmul(
                    po.pf[2].ap[:], w_uk.ap[:, c0:c0 + 128],
                    ckvT[i].ap[:, tb * 512:(tb + 1) * 512],
                    start=(i == 0), stop=(i == 3)),
                    reads=[w_uk, ckvT[i]], writes=[po.pf[2]])
            p.op("scalar", lambda e, tb=tb: e.copy(
                out=k_p.ap[:, tb * 512:(tb + 1) * 512], in_=po.pf[2].ap[:]),
                reads=[po.pf[2]], writes=[k_p])
        for kk in range(NT):
            for i in range(4):
                c0 = (i * NPAIR + j) * 128
                p.op("tensor", lambda e, i=i, c0=c0, kk=kk: e.matmul(
                    po.pf[2].ap[:, 0:128],
                    ckvT[i].ap[:, kk * 128:(kk + 1) * 128],
                    w_uv.ap[:, c0:c0 + 128],
                    start=(i == 0), stop=(i == 3)),
                    reads=[w_uv, ckvT[i]], writes=[po.pf[2]])
            p.op("scalar", lambda e, kk=kk: e.copy(
                out=v_p.ap[:, kk * 128:(kk + 1) * 128], in_=po.pf[2].ap[:, 0:128]),
                reads=[po.pf[2]], writes=[v_p])
        def score_extra(jj, qblk, kt, psS):
            p.op("tensor", lambda e, qblk=qblk, kt=kt, psS=psS: e.matmul(
                psS.ap[:], krT.ap[:, kt * 128:(kt + 1) * 128],
                qroT.ap[:, qblk * 512:(qblk + 1) * 512],
                start=False, stop=True),
                reads=[krT, qroT], writes=[psS])

        def exp_evac(jj, qblk, kt, psS, ptile):
            tmp = tmp2[kt % 2]
            r = kt - 4 * qblk
            if 0 <= r <= 3:
                p.op("vector", lambda e, psS=psS, r=r, tmp=tmp: e.scalar_tensor_tensor(
                    out=tmp.ap[:], in0=mv.ap[:, r * 512:(r + 1) * 512],
                    scalar=1.0, in1=psS.ap[:], op0=ALU.mult, op1=ALU.add),
                    reads=[mv, psS], writes=[tmp])
                p.op("scalar", lambda e, ptile=ptile, tmp=tmp: e.activation(
                    out=ptile.ap[:], in_=tmp.ap[:], func=AF.Exp, scale=1.0 / SQ192),
                    reads=[tmp], writes=[ptile])
            else:
                p.op("scalar", lambda e, psS=psS, ptile=ptile: e.activation(
                    out=ptile.ap[:], in_=psS.ap[:], func=AF.Exp, scale=1.0 / SQ192),
                    reads=[psS], writes=[ptile])

        emit_attn_pairs_single(env, po, j, qnT, k_p, v_p, ctx_out,
                               exp_evac=exp_evac, score_extra=score_extra)


def emit_attn_pairs_single(env, po, j, qT, kbuf, vbuf, ctx_out,
                           exp_evac, score_extra=None, seq=S):
    p = env.p
    nqb = seq // 512
    if not hasattr(env, "_aps_bufs"):
        pts = [env.sbuf(f"as_p{i}", [128, 512], BF16) for i in range(4)]
        zrow = env.sbuf("as_z", [1, 512], F32)
        onesr = env.sbuf("as_or", [1, 128], F32)
        bz = env.sbuf("as_bz", [128, 512], F32)
        ctxs = env.sbuf("as_ctx", [128, 512], BF16)
        p.op("vector", lambda e: e.memset(onesr.ap[:], 1.0), writes=[onesr])
        onesr.frozen = True
        env._aps_bufs = (pts, zrow, onesr, bz, ctxs)
    pts, zrow, onesr, bz, ctxs = env._aps_bufs
    psZ, psO, psB = po.pf[4], po.pf[5], po.pf[6]
    for qblk in range(nqb):
        nkt = 4 * qblk + 4
        for kt in range(nkt):
            psS = po.pf[kt % 4]
            def mm_s(e, qblk=qblk, kt=kt, psS=psS):
                return e.matmul(psS.ap[:], kbuf.ap[:, kt * 128:(kt + 1) * 128],
                                qT.ap[:, qblk * 512:(qblk + 1) * 512],
                                start=True, stop=(score_extra is None))
            p.op("tensor", mm_s, reads=[kbuf, qT], writes=[psS])
            if score_extra is not None:
                score_extra(j, qblk, kt, psS)
            ptile = pts[kt % 4]
            exp_evac(j, qblk, kt, psS, ptile)
            p.op("tensor", lambda e, kt=kt, nkt=nkt, ptile=ptile: e.matmul(
                psZ.ap[0:1, :], po.ones1.ap[:], ptile.ap[:],
                start=(kt == 0), stop=(kt == nkt - 1)),
                reads=[po.ones1, ptile], writes=[psZ])
            p.op("tensor", lambda e, kt=kt, nkt=nkt, ptile=ptile: e.matmul(
                psO.ap[:], vbuf.ap[:, kt * 128:(kt + 1) * 128], ptile.ap[:],
                start=(kt == 0), stop=(kt == nkt - 1)),
                reads=[vbuf, ptile], writes=[psO])
        p.op("vector", lambda e: e.tensor_copy(out=zrow.ap[:], in_=psZ.ap[0:1, :]),
             reads=[psZ], writes=[zrow])
        p.op("vector", lambda e: e.reciprocal(out=zrow.ap[:], in_=zrow.ap[:]),
             reads=[zrow], writes=[zrow])
        p.op("tensor", lambda e: e.matmul(psB.ap[:], onesr.ap[:], zrow.ap[:],
                                          start=True, stop=True),
             reads=[onesr, zrow], writes=[psB])
        p.op("vector", lambda e: e.tensor_copy(out=bz.ap[:], in_=psB.ap[:]),
             reads=[psB], writes=[bz])
        p.op("vector", lambda e: e.tensor_tensor(out=ctxs.ap[:], in0=psO.ap[:],
                                                 in1=bz.ap[:], op=ALU.mult),
             reads=[psO, bz], writes=[ctxs])
        p.store(ctx_out[j, :, qblk * 512:(qblk + 1) * 512], ctxs)


# version 06 (final)
# speedup vs baseline: 1.3738x; 1.0258x over previous
import numpy as np
import ml_dtypes
import concourse.bass as bass
import concourse.mybir as mybir
from concourse.bass_utils import run_bass_kernel_spmd
from contextlib import ExitStack

F32 = mybir.dt.float32
BF16 = mybir.dt.bfloat16
AF = mybir.ActivationFunctionType
ALU = mybir.AluOpType
AX = mybir.AxisListType

NCORES = 8
B = 2
S = 4096
D = 4096
NT = S // 128
TOK = 1024
NTILE = TOK // 128
BF = ml_dtypes.bfloat16
NPAIR = 8

EPS = 1e-5


class Buf:
    __slots__ = ("ap", "wtok", "rtoks", "frozen")

    def __init__(self, ap):
        self.ap = ap
        self.wtok = None
        self.rtoks = []
        self.frozen = False


class Prog:
    ENGS = ("sync", "vector", "scalar", "tensor", "gpsimd")

    NDMA = 8

    def __init__(self, nc):
        self.nc = nc
        self.qs = {e: [] for e in self.ENGS}
        self.sem_names = (["v", "s", "t", "g", "y"]
                          + [f"d{q}{i}" for q in "SPA" for i in range(self.NDMA)])
        self.sem_cnt = {n: 0 for n in self.sem_names}
        self.sems = {}
        self.wait_cache = {e: {} for e in self.ENGS}
        self.eng_sem = {"sync": "y", "gpsimd": "g", "vector": "v",
                        "scalar": "s", "tensor": "t"}
        self.dma_sem = {"sync": "S", "gpsimd": "P", "scalar": "A"}
        self.dma_idx = {"S": 0, "P": 0, "A": 0}
        self.barrier_toks = None

    def _emit_wait(self, eng_name, tok):
        if tok is None:
            return
        sem, val = tok
        if eng_name == "tensor" and sem == "t":
            return
        wc = self.wait_cache[eng_name]
        if wc.get(sem, 0) >= val:
            return
        wc[sem] = val
        self.qs[eng_name].append(("wait", sem, val))

    def _next(self, sem, step):
        self.sem_cnt[sem] += step
        return (sem, self.sem_cnt[sem])

    def op(self, eng_name, build, reads=(), writes=(), inc=True, dma=False,
           extra_waits=()):
        reads = [b for b in reads if b is not None]
        writes = [b for b in writes if b is not None]
        for b in reads:
            self._emit_wait(eng_name, b.wtok)
        for b in writes:
            self._emit_wait(eng_name, b.wtok)
            for t in b.rtoks:
                self._emit_wait(eng_name, t)
        for t in extra_waits:
            self._emit_wait(eng_name, t)
        if self.barrier_toks is not None:
            for b in writes:
                if b.wtok is None:
                    for t in self.barrier_toks:
                        self._emit_wait(eng_name, t)
                    break
        tok = None
        if inc:
            if dma:
                q = self.dma_sem[eng_name]
                i = self.dma_idx[q]
                self.dma_idx[q] = (i + 1) % self.NDMA
                tok = self._next(f"d{q}{i}", 16)
            else:
                tok = self._next(self.eng_sem[eng_name], 1)
        self.qs[eng_name].append(("inst", build, tok))
        if tok is not None:
            for b in reads:
                if not b.frozen:
                    b.rtoks.append(tok)
            for b in writes:
                b.rtoks = []
                b.wtok = tok
        return tok

    def load(self, dst, src_ap, queue="sync", dst_ap=None, src_buf=None):
        ap = dst.ap[:] if dst_ap is None else dst_ap
        return self.op(queue, lambda e: e.dma_start(out=ap, in_=src_ap),
                       reads=[src_buf] if src_buf is not None else (),
                       writes=[dst], dma=True)

    def store(self, dst_ap, src, queue="scalar", src_ap=None, dst_buf=None):
        ap = src.ap[:] if src_ap is None else src_ap
        return self.op(queue, lambda e: e.dma_start(out=dst_ap, in_=ap),
                       reads=[src],
                       writes=[dst_buf] if dst_buf is not None else (), dma=True)

    def all_tokens(self):
        return [(n, self.sem_cnt[n]) for n in self.sem_names if self.sem_cnt[n] > 0]

    def final_wait(self):
        toks = self.all_tokens()
        self.op("sync", lambda e: e.nop(), extra_waits=toks, inc=False)

    def finalize(self, block):
        def mk(eng_name):
            def body(eng):
                for item in self.qs[eng_name]:
                    if item[0] == "wait":
                        eng.wait_ge(self.sems[item[1]], item[2])
                    else:
                        inst = item[1](eng)
                        tok = item[2]
                        if tok is not None:
                            step = 16 if tok[0].startswith("d") else 1
                            inst.then_inc(self.sems[tok[0]], step)
            return body
        for e, deco in (("sync", block.sync), ("vector", block.vector),
                        ("scalar", block.scalar), ("tensor", block.tensor),
                        ("gpsimd", block.gpsimd)):
            if self.qs[e]:
                deco(mk(e))


class Env:

    def __init__(self, name):
        self.nc = bass.Bass("TRN2", target_bir_lowering=False)
        self.p = Prog(self.nc)
        self.ctx = ExitStack()
        self.scopes = []
        self._ins = {}
        self._outs = {}
        self.name = name

    def push_scope(self):
        self.scopes.append(ExitStack())

    def pop_scope(self):
        self.scopes.pop().close()
        self.p.barrier_toks = list(self.p.all_tokens())

    def dram_in(self, name, shape, dtype=F32):
        h = self.nc.dram_tensor(name, list(shape), dtype, kind="ExternalInput")
        self._ins[name] = h
        return h.ap()

    def dram_tmp(self, name, shape, dtype=F32):
        h = self.nc.dram_tensor(name, list(shape), dtype, kind="Internal")
        return h.ap()

    def dram_out(self, name, shape, dtype=F32):
        h = self.nc.dram_tensor(name, list(shape), dtype, kind="ExternalOutput")
        self._outs[name] = h
        return h.ap()

    def sbuf(self, name, shape, dtype):
        stack = self.scopes[-1] if self.scopes else self.ctx
        h = stack.enter_context(self.nc.sbuf_tensor(name, list(shape), dtype))
        return Buf(h)

    def psum(self, name, shape, dtype=F32):
        stack = self.scopes[-1] if self.scopes else self.ctx
        h = stack.enter_context(self.nc.psum_tensor(name, list(shape), dtype))
        return Buf(h)

    def build(self, emit_fn):
        with self.ctx:
            for n in self.p.sem_names:
                self.p.sems[n] = self.ctx.enter_context(self.nc.semaphore(n))
            block = self.ctx.enter_context(self.nc.Block())
            emit_fn(self)
            self.p.final_wait()
            self.p.finalize(block)
        return self.nc


def block_lhsT(w, kc=128, mc=128):
    K, M = w.shape
    return np.ascontiguousarray(
        w.reshape(K // kc, kc, M // mc, mc).transpose(0, 2, 1, 3))


def block_rhs(w, kc=128, nb=512):
    K, N = w.shape
    return np.ascontiguousarray(
        w.reshape(K // kc, kc, N // nb, nb).transpose(0, 2, 1, 3))


def rope_tables(positions, dim, base=10000.0):
    half = dim // 2
    inv = 1.0 / base ** (np.arange(half, dtype=np.float64) / half)
    ang = inv[:, None] * np.asarray(positions, np.float64)[None, :]
    cos = np.cos(ang)
    sin = np.sin(ang)
    cos2 = np.concatenate([cos, cos], 0).astype(np.float32)
    sin2 = np.concatenate([sin, sin], 0).astype(np.float32)
    return cos2, sin2


def yarn_tables(positions):
    DR = 64
    BASE, FACTOR, BFAST, BSLOW, OLD_CTX = 10000.0, 40.0, 32.0, 1.0, 4096.0
    freqs = 1.0 / BASE ** (np.arange(0, DR, 2, dtype=np.float64) / DR)
    wavelengths = 2.0 * np.pi / freqs
    ramp = np.clip((wavelengths / OLD_CTX - BSLOW) / (BFAST - BSLOW), 0.0, 1.0)
    scale = 1.0 - ramp + ramp * FACTOR
    inv = freqs / scale
    ang = inv[:, None] * np.asarray(positions, np.float64)[None, :]
    cos = np.cos(ang)
    sin = np.sin(ang)
    cos2 = np.concatenate([cos, cos], 0).astype(np.float32)
    sin2 = np.concatenate([sin, sin], 0).astype(np.float32)
    return cos2, sin2


def alibi_slopes(n_heads=32):
    import math
    closest = 2 ** int(math.floor(math.log2(n_heads)))
    base = 2.0 ** (-(2.0 ** (-(math.log2(closest) - 3))))
    slopes = [base ** i for i in range(1, closest + 1)]
    if closest != n_heads:
        eb = 2.0 ** (-(2.0 ** (-(math.log2(2 * closest) - 3))))
        n_extra = min(closest, n_heads - closest)
        slopes += [eb ** i for i in range(1, 2 * n_extra + 1, 2)]
    return np.array(slopes, np.float32)


class Pools:

    def __init__(self, env, n_pf=6, n_pt=2, ln=True, wstream=True, rope=True):
        self.env = env
        p = env.p
        self.pf = [env.psum(f"psf{i}", [128, 512], F32) for i in range(n_pf)]
        self.pt = [env.psum(f"pst{i}", [128, 128], BF16) for i in range(n_pt)]
        self._tp_i = 0
        self.ones1 = env.sbuf("sc_ones1", [128, 1], BF16)
        env.p.op("vector", lambda e: e.memset(self.ones1.ap[:], 1.0), writes=[self.ones1])
        if ln or n_pt:
            self.ident = env.sbuf("sc_ident", [128, 128], BF16)
            ident_in = env.dram_in("c_ident", [128, 128], BF16)
            env.p.load(self.ident, ident_in)
        if ln:
            self.mu = env.sbuf("sc_mu", [128, 1], F32)
            self.sq = env.sbuf("sc_sq", [128, 1], F32)
            self.rstd = env.sbuf("sc_rstd", [128, 1], F32)
            self.nmur = env.sbuf("sc_nmur", [128, 1], F32)
            self.dump = env.sbuf("sc_dump", [128, 4096], F32)
            self.eps = env.sbuf("sc_eps", [128, 1], F32)
            env.p.op("vector", lambda e: e.memset(self.eps.ap[:], EPS), writes=[self.eps])
        if wstream:
            self.wb = [env.sbuf(f"sc_wb{i}", [128, 512], BF16) for i in range(4)]
        self._wb_i = 0
        if rope:
            self.rope_t1 = env.sbuf("sc_ropet1", [128, 512], BF16)
            self.rope_t2 = env.sbuf("sc_ropet2", [128, 512], BF16)
            self.rope_tb = env.sbuf("sc_ropetb", [128, 512], F32)

    def next_wb(self):
        b = self.wb[self._wb_i % len(self.wb)]
        self._wb_i += 1
        return b

    def next_tp(self):
        b = self.pt[self._tp_i % len(self.pt)]
        self._tp_i += 1
        return b


def emit_ln(env, po, x_buf, xn_buf):
    p = env.p
    inv = 1.0 / D
    mu, sq, rstd, nmur, dump, eps = po.mu, po.sq, po.rstd, po.nmur, po.dump, po.eps
    p.op("vector", lambda e: e.reduce_sum(mu.ap[:], x_buf.ap[:], axis=AX.X),
         reads=[x_buf], writes=[mu])
    p.op("scalar", lambda e: e.activation(out=dump.ap[:], in_=x_buf.ap[:],
                                          func=AF.Square, accum_out=sq.ap[:]),
         reads=[x_buf], writes=[sq, dump])
    p.op("vector", lambda e: e.tensor_scalar_mul(mu.ap[:], mu.ap[:], inv),
         reads=[mu], writes=[mu])
    p.op("vector", lambda e: e.tensor_scalar_mul(sq.ap[:], sq.ap[:], inv),
         reads=[sq], writes=[sq])
    p.op("vector", lambda e: e.tensor_mul(rstd.ap[:], mu.ap[:], mu.ap[:]),
         reads=[mu], writes=[rstd])
    p.op("vector", lambda e: e.tensor_sub(rstd.ap[:], sq.ap[:], rstd.ap[:]),
         reads=[sq, rstd], writes=[rstd])
    p.op("scalar", lambda e: e.activation(out=rstd.ap[:], in_=rstd.ap[:],
                                          func=AF.Ln, bias=eps.ap[:]),
         reads=[rstd, eps], writes=[rstd])
    p.op("scalar", lambda e: e.activation(out=rstd.ap[:], in_=rstd.ap[:],
                                          func=AF.Exp, scale=-0.5),
         reads=[rstd], writes=[rstd])
    p.op("scalar", lambda e: e.mul(nmur.ap[:], mu.ap[:], -1.0),
         reads=[mu], writes=[nmur])
    p.op("scalar", lambda e: e.mul(nmur.ap[:], nmur.ap[:], rstd.ap[:]),
         reads=[nmur, rstd], writes=[nmur])
    p.op("scalar", lambda e: e.activation(out=xn_buf.ap[:], in_=x_buf.ap[:],
                                          func=AF.Identity, bias=nmur.ap[:],
                                          scale=rstd.ap[:]),
         reads=[x_buf, nmur, rstd], writes=[xn_buf])


def emit_tp128(env, po, src_buf, src_ap, dst_buf, dst_ap):
    p = env.p
    ps = po.next_tp()
    p.op("tensor", lambda e: e.transpose(ps.ap[:], src_ap, po.ident.ap[:]),
         reads=[src_buf, po.ident], writes=[ps])
    p.op("vector", lambda e: e.tensor_copy(out=dst_ap, in_=ps.ap[:]),
         reads=[ps], writes=[dst_buf])


def emit_ln_tp_phase(env, po, x_dram, xnT, xnT_out=None, x_tiles_keep=None):
    p = env.p
    xw = env.sbuf("lt_x", [128, D], F32)
    xnw = env.sbuf("lt_xn", [128, D], BF16)
    for tile in range(NTILE):
        xbuf = x_tiles_keep[tile] if x_tiles_keep is not None else xw
        p.load(xbuf, x_dram[tile, :, :])
        emit_ln(env, po, xbuf, xnw)
        for c in range(D // 128):
            emit_tp128(env, po, xnw, xnw.ap[:, c * 128:(c + 1) * 128],
                       xnT[c], xnT[c].ap[:, tile * 128:(tile + 1) * 128])
    if xnT_out is not None:
        for c in range(D // 128):
            p.store(xnT_out[c, :, :], xnT[c])


def emit_proj_feat(env, po, w_dram, xnT, out_chunks, ndm=32, evac="copy",
                   bias=None, rope=None, out_store=None, scale=1.0, oc_base=0):
    p = env.p
    noc = len(out_chunks)
    ntb = TOK // 512
    for oc in range(noc):
        pss = [po.pf[oc % 2 * 2], po.pf[oc % 2 * 2 + 1]][:ntb] if ntb <= 2 else None
        for dm in range(ndm):
            wb = po.next_wb()
            p.load(wb, w_dram[dm, oc_base + oc, :, :], queue="gpsimd",
                   dst_ap=wb.ap[:, 0:128])
            for tb in range(ntb):
                def mk(e, wb=wb, dm=dm, tb=tb, oc=oc, ps=pss[tb]):
                    return e.matmul(ps.ap[:], wb.ap[:, 0:128],
                                    xnT[dm].ap[:, tb * 512:(tb + 1) * 512],
                                    start=(dm == 0), stop=(dm == ndm - 1))
                p.op("tensor", mk, reads=[wb, xnT[dm]], writes=[pss[tb]])
        ob = out_chunks[oc]
        for tb in range(ntb):
            ps = pss[tb]
            cols = slice(tb * 512, (tb + 1) * 512)
            if evac == "copy":
                if bias is not None:
                    bbuf, bap = bias(oc)
                    p.op("scalar", lambda e, ps=ps, ob=ob, cols=cols, bap=bap:
                         e.activation(out=ob.ap[:, cols], in_=ps.ap[:],
                                      func=AF.Identity, bias=bap, scale=scale),
                         reads=[ps, bbuf], writes=[ob])
                elif scale != 1.0:
                    p.op("scalar", lambda e, ps=ps, ob=ob, cols=cols:
                         e.activation(out=ob.ap[:, cols], in_=ps.ap[:],
                                      func=AF.Identity, scale=scale),
                         reads=[ps], writes=[ob])
                else:
                    p.op("scalar", lambda e, ps=ps, ob=ob, cols=cols:
                         e.copy(out=ob.ap[:, cols], in_=ps.ap[:]),
                         reads=[ps], writes=[ob])
            elif evac == "rope":
                cos_b, sin_b = rope
                emit_rope_evac(env, po, ps, ob, cols, cos_b, sin_b, half=64,
                               bias=None if bias is None else bias(oc))
        if out_store is not None:
            p.store(out_store[oc, :, :], ob)


def emit_rope_evac(env, po, ps, ob, cols, cos_b, sin_b, half=64, bias=None):
    p = env.p
    t1, t2 = po.rope_t1, po.rope_t2
    src = ps
    if bias is not None:
        bbuf, bap = bias
        tb = po.rope_tb
        p.op("scalar", lambda e: e.activation(out=tb.ap[:, 0:512], in_=ps.ap[:],
                                              func=AF.Identity, bias=bap),
             reads=[ps, bbuf], writes=[tb])
        src = tb
    h = half
    sa = src.ap[0:h, 0:512]
    sb_ = src.ap[h:2 * h, 0:512]
    ca = cos_b.ap[0:h, cols]
    sn = sin_b.ap[h:2 * h, cols]
    cb = cos_b.ap[h:2 * h, cols]
    snb = sin_b.ap[0:h, cols]
    p.op("vector", lambda e: e.tensor_tensor(out=t1.ap[0:h, 0:512], in0=sa, in1=ca, op=ALU.mult),
         reads=[src, cos_b], writes=[t1])
    p.op("vector", lambda e: e.tensor_tensor(out=t2.ap[0:h, 0:512], in0=sb_, in1=sn, op=ALU.mult),
         reads=[src, sin_b], writes=[t2])
    p.op("vector", lambda e: e.tensor_tensor(out=ob.ap[0:h, cols], in0=t1.ap[0:h, 0:512],
                                             in1=t2.ap[0:h, 0:512], op=ALU.subtract),
         reads=[t1, t2], writes=[ob])
    p.op("vector", lambda e: e.tensor_tensor(out=t1.ap[0:h, 0:512], in0=sb_, in1=cb, op=ALU.mult),
         reads=[src, cos_b], writes=[t1])
    p.op("vector", lambda e: e.tensor_tensor(out=t2.ap[0:h, 0:512], in0=sa, in1=snb, op=ALU.mult),
         reads=[src, sin_b], writes=[t2])
    p.op("vector", lambda e: e.tensor_tensor(out=ob.ap[h:2 * h, cols], in0=t1.ap[0:h, 0:512],
                                             in1=t2.ap[0:h, 0:512], op=ALU.add),
         reads=[t1, t2], writes=[ob])


def emit_proj_tok(env, po, w_dram, inT, ncolblk, ndm=32, colw=512,
                  out_tiles=None, out_store=None, evac_fn=None, npsum=6,
                  inT_cols=None):
    p = env.p
    if inT_cols is None:
        def inT_cols(dm, ti):
            return inT[dm].ap[:, ti * 128:(ti + 1) * 128]
    for cb in range(ncolblk):
        for grp in range(0, NTILE, npsum):
            tiles = list(range(grp, min(grp + npsum, NTILE)))
            for dm in range(ndm):
                wb = po.next_wb()
                p.load(wb, w_dram[dm, cb, :, 0:colw], queue="gpsimd",
                       dst_ap=wb.ap[:, 0:colw])
                for ti in tiles:
                    ps = po.pf[ti % npsum]
                    def mk(e, wb=wb, dm=dm, ti=ti, ps=ps):
                        return e.matmul(ps.ap[:, 0:colw], inT_cols(dm, ti),
                                        wb.ap[:, 0:colw],
                                        start=(dm == 0), stop=(dm == ndm - 1))
                    p.op("tensor", mk, reads=[wb, inT[dm]], writes=[ps])
            for ti in tiles:
                ps = po.pf[ti % npsum]
                if evac_fn is not None:
                    evac_fn(cb, ti, ps)
                else:
                    ob = out_tiles[ti]
                    cols = slice(cb * colw, (cb + 1) * colw)
                    p.op("scalar", lambda e, ps=ps, ob=ob, cols=cols:
                         e.copy(out=ob.ap[:, cols], in_=ps.ap[:, 0:colw]),
                         reads=[ps], writes=[ob])
    if out_store is not None:
        for ti in range(NTILE):
            p.store(out_store[ti, :, :], out_tiles[ti])


SQ128 = float(np.sqrt(128.0))


def emit_qproj_pairs(env, po, wq_all, xnf_dram, qT_own, ndm=32, seq=S,
                     evac_fn=None):
    p = env.p
    assert ndm == 32
    xql = [env.sbuf(f"qp_xl{i}", [128, 8 * 512], BF16) for i in range(2)]
    xqh = [env.sbuf(f"qp_xh{i}", [128, 8 * 512], BF16) for i in range(2)]
    ntb = seq // 512
    for tb in range(ntb):
      for half in range(2):
        d0 = half * 16
        par = (tb * 2 + half) % 2
        xl, xh = xql[par], xqh[par]
        p.load(xl, xnf_dram[d0:d0 + 8, :, tb * 512:(tb + 1) * 512]
               .rearrange("a p m -> p a m"), queue="sync",
               dst_ap=xl.ap[:].rearrange("p (a m) -> p a m", a=8))
        p.load(xh, xnf_dram[d0 + 8:d0 + 16, :, tb * 512:(tb + 1) * 512]
               .rearrange("a p m -> p a m"), queue="scalar",
               dst_ap=xh.ap[:].rearrange("p (a m) -> p a m", a=8))
        for dm in range(d0, d0 + 16):
            src = xl if dm - d0 < 8 else xh
            off = (dm - d0) % 8
            for j in range(NPAIR):
                c0 = (dm * NPAIR + j) * 128
                def mk(e, dm=dm, j=j, c0=c0, off=off, src=src):
                    return e.matmul(po.pf[j].ap[:], wq_all.ap[:, c0:c0 + 128],
                                    src.ap[:, off * 512:(off + 1) * 512],
                                    start=(dm == 0), stop=(dm == ndm - 1))
                p.op("tensor", mk, reads=[wq_all, src], writes=[po.pf[j]])
      for j in range(NPAIR):
            ps = po.pf[j]
            cols = slice(tb * 512, (tb + 1) * 512)
            if evac_fn is not None:
                evac_fn(j, tb, ps, qT_own[j], cols)
            else:
                p.op("scalar", lambda e, ps=ps, j=j, cols=cols:
                     e.copy(out=qT_own[j].ap[:, cols], in_=ps.ap[:]),
                     reads=[ps], writes=[qT_own[j]])


def emit_attn_pairs(env, po, qT_own, kT_getter, v_getter, ctx_out, seq=S,
                    exp_evac=None, score_extra=None, ctx_sb=None):
    p = env.p
    nqb = seq // 512
    pts = [env.sbuf(f"at_p{i}", [128, 512], BF16) for i in range(4)]
    zrow = env.sbuf("at_z", [1, 512], F32)
    onesr = env.sbuf("at_or", [1, 128], F32)
    bz = env.sbuf("at_bz", [128, 512], F32)
    ctxs = env.sbuf("at_ctx", [128, 512], BF16)
    p.op("vector", lambda e: e.memset(onesr.ap[:], 1.0), writes=[onesr])
    psZ, psO, psB = po.pf[4], po.pf[5], po.pf[6]
    for j in range(NPAIR):
        kbuf, kc0 = kT_getter(j)
        vbuf, vc0 = v_getter(j)
        for qblk in range(nqb):
            nkt = 4 * qblk + 4

            def zpv(kt, nkt=nkt, vbuf=vbuf, vc0=vc0):
                ptile = pts[kt % 4]
                def mm_z(e, kt=kt, ptile=ptile):
                    return e.matmul(psZ.ap[0:1, :], po.ones1.ap[:],
                                    ptile.ap[:], start=(kt == 0), stop=(kt == nkt - 1))
                p.op("tensor", mm_z, reads=[po.ones1, ptile], writes=[psZ])
                def mm_o(e, kt=kt, ptile=ptile):
                    return e.matmul(psO.ap[:], vbuf.ap[:, vc0 + kt * 128:vc0 + (kt + 1) * 128],
                                    ptile.ap[:], start=(kt == 0), stop=(kt == nkt - 1))
                p.op("tensor", mm_o, reads=[vbuf, ptile], writes=[psO])

            for kt in range(nkt):
                psS = po.pf[kt % 4]
                def mm_s(e, j=j, qblk=qblk, kt=kt, psS=psS, kbuf=kbuf, kc0=kc0):
                    return e.matmul(psS.ap[:], kbuf.ap[:, kc0 + kt * 128:kc0 + (kt + 1) * 128],
                                    qT_own[j].ap[:, qblk * 512:(qblk + 1) * 512],
                                    start=True, stop=(score_extra is None))
                p.op("tensor", mm_s, reads=[kbuf, qT_own[j]], writes=[psS])
                if score_extra is not None:
                    score_extra(j, qblk, kt, psS)
                exp_evac(j, qblk, kt, psS, pts[kt % 4])
                if kt >= 1:
                    zpv(kt - 1)
            zpv(nkt - 1)
            p.op("vector", lambda e: e.tensor_copy(out=zrow.ap[:], in_=psZ.ap[0:1, :]),
                 reads=[psZ], writes=[zrow])
            p.op("vector", lambda e: e.reciprocal(out=zrow.ap[:], in_=zrow.ap[:]),
                 reads=[zrow], writes=[zrow])
            p.op("tensor", lambda e: e.matmul(psB.ap[:], onesr.ap[:], zrow.ap[:],
                                              start=True, stop=True),
                 reads=[onesr, zrow], writes=[psB])
            p.op("vector", lambda e: e.tensor_copy(out=bz.ap[:], in_=psB.ap[:]),
                 reads=[psB], writes=[bz])
            p.op("vector", lambda e: e.tensor_tensor(out=ctxs.ap[:], in0=psO.ap[:],
                                                     in1=bz.ap[:], op=ALU.mult),
                 reads=[psO, bz], writes=[ctxs])
            if ctx_sb is not None:
                p.op("vector", lambda e, j=j, qblk=qblk:
                     e.tensor_copy(out=ctx_sb[j].ap[:, qblk * 512:(qblk + 1) * 512],
                                   in_=ctxs.ap[:]),
                     reads=[ctxs], writes=[ctx_sb[j]])
            if ctx_out is not None:
                p.store(ctx_out[j, :, qblk * 512:(qblk + 1) * 512], ctxs)


def build_attn0(env):
    po = Pools(env, n_pf=8, n_pt=0, ln=False, wstream=False, rope=False)
    p = env.p
    xnf = env.dram_in("xnf", [32, 128, S], BF16)
    kTf = env.dram_in("kTf", [1, 128, S], BF16)
    vf = env.dram_in("vf", [128, NT * 128], BF16)
    wq = env.dram_in("wq", [128, 32 * NPAIR * 128])
    biasKQ = env.dram_in("biasKQ", [128, NPAIR * NT * 8])
    qoffm = env.dram_in("qoffm", [128, 5 * 512])
    slopes = env.dram_in("slopes", [128, NPAIR])
    ctx_out = env.dram_out("ctx", [NPAIR, 128, S], BF16)

    wq_all = env.sbuf("wq_all", [128, 32 * NPAIR * 128], BF16)
    p.load(wq_all, wq, queue="gpsimd")
    wq_all.frozen = True
    kT = env.sbuf("kT", [128, S], BF16)
    p.load(kT, kTf[0, :, :])
    kT.frozen = True
    v_all = env.sbuf("v_all", [128, NT * 128], BF16)
    p.load(v_all, vf)
    v_all.frozen = True
    bkq = env.sbuf("bkq", [128, NPAIR * NT * 8], F32)
    p.load(bkq, biasKQ)
    bkq.frozen = True
    qoff = env.sbuf("qoff", [128, 5 * 512], F32)
    p.load(qoff, qoffm)
    qoff.frozen = True
    slp = env.sbuf("slp", [128, NPAIR], F32)
    p.load(slp, slopes)
    slp.frozen = True

    qT_own = [env.sbuf(f"qT{j}", [128, S], BF16) for j in range(NPAIR)]
    emit_qproj_pairs(env, po, wq_all, xnf, qT_own)
    for b in qT_own:
        b.frozen = True

    tmp2 = [env.sbuf(f"at_tmp{i}", [128, 512], F32) for i in range(2)]

    def exp_evac(j, qblk, kt, psS, ptile):
        tmp = tmp2[kt % 2]
        r = kt - 4 * qblk
        var = (r + 1) if 0 <= r <= 3 else 0
        qsl = qoff.ap[:, var * 512:(var + 1) * 512]
        p.op("vector", lambda e, tmp=tmp: e.scalar_tensor_tensor(
            out=tmp.ap[:], in0=qsl, scalar=slp.ap[:, j:j + 1],
            in1=psS.ap[:], op0=ALU.mult, op1=ALU.add),
            reads=[qoff, slp, psS], writes=[tmp])
        col = (j * NT + kt) * 8 + qblk
        p.op("scalar", lambda e, col=col, tmp=tmp: e.activation(
            out=ptile.ap[:], in_=tmp.ap[:], func=AF.Exp,
            bias=bkq.ap[:, col:col + 1], scale=1.0 / SQ128),
            reads=[tmp, bkq], writes=[ptile])

    emit_attn_pairs(env, po, qT_own, lambda j: (kT, 0), lambda j: (v_all, 0),
                    ctx_out, exp_evac=exp_evac)


def emit_wo_res(env, po, ctxT, wo_dram, x_in_dram, x_out_dram, bias_dram=None,
                x_keep=None):
    p = env.p
    xsl = [env.sbuf(f"wr_x{i}", [128, 512], F32) for i in range(2)]
    xo = [env.sbuf(f"wr_o{i}", [128, 512], F32) for i in range(2)]
    bias_b = None
    if bias_dram is not None:
        bias_b = env.sbuf("wr_b", [128, D], F32)
        p.load(bias_b, bias_dram)
        bias_b.frozen = True

    def evac(cb, ti, ps):
        xs = xsl[ti % 2]
        xob = xo[ti % 2]
        p.load(xs, x_in_dram[ti, :, cb * 512:(cb + 1) * 512], queue="sync")
        if bias_b is not None:
            p.op("vector", lambda e, xs=xs, cb=cb: e.tensor_tensor(
                out=xs.ap[:], in0=xs.ap[:],
                in1=bias_b.ap[:, cb * 512:(cb + 1) * 512], op=ALU.add),
                reads=[xs, bias_b], writes=[xs])
        p.op("vector", lambda e, ps=ps, xs=xs, xob=xob: e.scalar_tensor_tensor(
            out=xob.ap[:], in0=ps.ap[:], scalar=1.0, in1=xs.ap[:],
            op0=ALU.mult, op1=ALU.add),
            reads=[ps, xs], writes=[xob])
        if x_keep is not None:
            p.op("vector", lambda e, xob=xob, ti=ti, cb=cb: e.tensor_copy(
                out=x_keep[ti].ap[:, cb * 512:(cb + 1) * 512], in_=xob.ap[:]),
                reads=[xob], writes=[x_keep[ti]])
        p.store(x_out_dram[ti, :, cb * 512:(cb + 1) * 512], xob)

    emit_proj_tok(env, po, wo_dram, ctxT, ncolblk=8, evac_fn=evac, npsum=8)


def mk_build_wo_res(with_bias):
    def build(env):
        po = Pools(env, n_pf=8, n_pt=0, ln=False, wstream=True, rope=False)
        p = env.p
        ctx_in = env.dram_in("ctxT", [32, 128, TOK], BF16)
        wo = env.dram_in("wo", [32, 8, 128, 512])
        x_in = env.dram_in("x", [NTILE, 128, D])
        bias = env.dram_in("bias", [128, D]) if with_bias else None
        x_out = env.dram_out("xo", [NTILE, 128, D])
        ctxT = [env.sbuf(f"ctxT{c}", [128, TOK], BF16) for c in range(32)]
        for c in range(32):
            p.load(ctxT[c], ctx_in[c, :, :])
            ctxT[c].frozen = True
        emit_wo_res(env, po, ctxT, wo, x_in, x_out, bias_dram=bias)
    return build


build_wo_res = mk_build_wo_res(False)
build_wo_res_bias = mk_build_wo_res(True)


def emit_ffn0(env, po, x_in_dram, x_out_dram, ab_bufs, prr, x_tiles=None):
    p = env.p
    A_rhs, B_rhs, WqR, keysT = (ab_bufs["A"], ab_bufs["B"], ab_bufs["WqR"],
                                ab_bufs["keysT"])
    xw = env.sbuf("f0_x", [128, D], F32)
    xnw = env.sbuf("f0_xn", [128, D], BF16)
    xnTt = env.sbuf("f0_xnT", [128, D], BF16)
    z_bf = env.sbuf("f0_z", [128, 512], BF16)
    qTf = env.sbuf("f0_qT", [16, 128], F32)
    scT = env.sbuf("f0_scT", [64, 128], F32)
    sc = env.sbuf("f0_sc", [128, 64], F32)
    m8a = env.sbuf("f0_m8a", [128, 8], F32)
    m8b = env.sbuf("f0_m8b", [128, 8], F32)
    ind8 = env.sbuf("f0_ind8", [128, 64], F32)
    scm = env.sbuf("f0_scm", [128, 64], F32)
    kept = env.sbuf("f0_kept", [128, 64], F32)
    negm1 = env.sbuf("f0_negm1", [128, 1], F32)
    gexp = env.sbuf("f0_gexp", [128, 64], F32)
    zs = env.sbuf("f0_zs", [128, 1], F32)
    gsc = env.sbuf("f0_gsc", [128, 1], F32)
    gate_bf = env.sbuf("f0_gate", [128, 64], BF16)
    zg = env.sbuf("f0_zg", [128, 512], BF16)
    zgTt = env.sbuf("f0_zgT", [128, 512], BF16)
    xsl = env.sbuf("f0_xsl", [128, 512], F32)
    xob = env.sbuf("f0_xo", [128, 512], F32)
    psZz, psQ, psSc, psD = po.pf[0], po.pf[1], po.pf[2], po.pf[3]

    for ti in range(NTILE):
        if x_tiles is not None:
            xbuf = x_tiles[ti]
        else:
            xbuf = xw
            p.load(xbuf, x_in_dram[ti, :, :], queue="sync")
        emit_ln(env, po, xbuf, xnw)
        for c in range(32):
            emit_tp128(env, po, xnw, xnw.ap[:, c * 128:(c + 1) * 128],
                       xnTt, xnTt.ap[:, c * 128:(c + 1) * 128])
        for dm in range(32):
            p.op("tensor", lambda e, dm=dm: e.matmul(
                psZz.ap[:], xnTt.ap[:, dm * 128:(dm + 1) * 128],
                A_rhs.ap[:, dm * 512:(dm + 1) * 512],
                start=(dm == 0), stop=(dm == 31)),
                reads=[xnTt, A_rhs], writes=[psZz])
        p.op("scalar", lambda e: e.copy(out=z_bf.ap[:], in_=psZz.ap[:]),
             reads=[psZz], writes=[z_bf])
        for dm in range(32):
            p.op("tensor", lambda e, dm=dm: e.matmul(
                psQ.ap[0:16, 0:128], WqR.ap[:, dm * 16:(dm + 1) * 16],
                xnTt.ap[:, dm * 128:(dm + 1) * 128],
                start=(dm == 0), stop=(dm == 31)),
                reads=[xnTt, WqR], writes=[psQ])
        p.op("vector", lambda e: e.tensor_copy(out=qTf.ap[:], in_=psQ.ap[0:16, 0:128]),
             reads=[psQ], writes=[qTf])
        p.op("tensor", lambda e: e.matmul(psSc.ap[0:64, 0:128], keysT.ap[:],
                                          qTf.ap[:], start=True, stop=True),
             reads=[keysT, qTf], writes=[psSc])
        p.op("vector", lambda e: e.tensor_copy(out=scT.ap[:], in_=psSc.ap[0:64, 0:128]),
             reads=[psSc], writes=[scT])
        p.op("tensor", lambda e: e.transpose(prr.ap[0:128, 0:64], scT.ap[:],
                                             po.identf.ap[0:64, 0:64]),
             reads=[scT, po.identf], writes=[prr])
        p.op("vector", lambda e: e.tensor_copy(out=sc.ap[:], in_=prr.ap[0:128, 0:64]),
             reads=[prr], writes=[sc])
        p.op("vector", lambda e: e.max(out=m8a.ap[:], in_=sc.ap[:]),
             reads=[sc], writes=[m8a])
        p.op("vector", lambda e: e.tensor_scalar(
            out=ind8.ap[:], in0=sc.ap[:], scalar1=m8a.ap[:, 7:8], scalar2=None,
            op0=ALU.is_ge), reads=[sc, m8a], writes=[ind8])
        p.op("vector", lambda e: e.scalar_tensor_tensor(
            out=scm.ap[:], in0=ind8.ap[:], scalar=-1e30, in1=sc.ap[:],
            op0=ALU.mult, op1=ALU.add), reads=[ind8, sc], writes=[scm])
        p.op("vector", lambda e: e.max(out=m8b.ap[:], in_=scm.ap[:]),
             reads=[scm], writes=[m8b])
        p.op("vector", lambda e: e.tensor_scalar(
            out=kept.ap[:], in0=sc.ap[:], scalar1=m8b.ap[:, 7:8], scalar2=None,
            op0=ALU.is_ge), reads=[sc, m8b], writes=[kept])
        p.op("scalar", lambda e: e.mul(negm1.ap[:], m8a.ap[:, 0:1], -1.0),
             reads=[m8a], writes=[negm1])
        p.op("scalar", lambda e: e.activation(out=gexp.ap[:], in_=sc.ap[:],
                                              func=AF.Exp, bias=negm1.ap[:]),
             reads=[sc, negm1], writes=[gexp])
        p.op("vector", lambda e: e.tensor_tensor(out=gexp.ap[:], in0=gexp.ap[:],
                                                 in1=kept.ap[:], op=ALU.mult),
             reads=[gexp, kept], writes=[gexp])
        p.op("vector", lambda e: e.reduce_sum(zs.ap[:], gexp.ap[:], axis=AX.X),
             reads=[gexp], writes=[zs])
        p.op("vector", lambda e: e.reciprocal(out=gsc.ap[:], in_=zs.ap[:]),
             reads=[zs], writes=[gsc])
        p.op("vector", lambda e: e.tensor_scalar_mul(gsc.ap[:], gsc.ap[:], 2.0),
             reads=[gsc], writes=[gsc])
        p.op("vector", lambda e: e.tensor_scalar_mul(gexp.ap[:], gexp.ap[:], gsc.ap[:]),
             reads=[gexp, gsc], writes=[gexp])
        p.op("vector", lambda e: e.tensor_copy(out=gate_bf.ap[:], in_=gexp.ap[:]),
             reads=[gexp], writes=[gate_bf])
        if hasattr(env, "dbg_gate"):
            p.store(env.dbg_gate[ti, :, :], gexp)
            p.store(env.dbg_z[ti, :, :], z_bf)
            p.store(env.dbg_sc[ti, :, :], sc)
        p.op("vector", lambda e: e.tensor_tensor(
            out=zg.ap[:].rearrange("p (a r) -> p a r", a=64),
            in0=z_bf.ap[:].rearrange("p (a r) -> p a r", a=64),
            in1=gate_bf.ap[:].unsqueeze(2).to_broadcast([128, 64, 8]),
            op=ALU.mult), reads=[z_bf, gate_bf], writes=[zg])
        if hasattr(env, "dbg_zg"):
            p.store(env.dbg_zg[ti, :, :], zg)
        for i in range(4):
            emit_tp128(env, po, zg, zg.ap[:, i * 128:(i + 1) * 128],
                       zgTt, zgTt.ap[:, i * 128:(i + 1) * 128])
        if hasattr(env, "dbg_zgT"):
            p.store(env.dbg_zgT[ti, :, :], zgTt)
        for cb in range(8):
            for i in range(4):
                p.op("tensor", lambda e, i=i, cb=cb: e.matmul(
                    psD.ap[:], zgTt.ap[:, i * 128:(i + 1) * 128],
                    B_rhs.ap[:, (i * 8 + cb) * 512:(i * 8 + cb + 1) * 512],
                    start=(i == 0), stop=(i == 3)),
                    reads=[zgTt, B_rhs], writes=[psD])
            if x_tiles is not None:
                p.op("vector", lambda e, ti=ti, cb=cb: e.scalar_tensor_tensor(
                    out=xob.ap[:], in0=psD.ap[:], scalar=1.0,
                    in1=x_tiles[ti].ap[:, cb * 512:(cb + 1) * 512],
                    op0=ALU.mult, op1=ALU.add),
                    reads=[psD, x_tiles[ti]], writes=[xob])
            else:
                p.load(xsl, x_in_dram[ti, :, cb * 512:(cb + 1) * 512], queue="sync")
                p.op("vector", lambda e: e.scalar_tensor_tensor(
                    out=xob.ap[:], in0=psD.ap[:], scalar=1.0, in1=xsl.ap[:],
                    op0=ALU.mult, op1=ALU.add),
                    reads=[psD, xsl], writes=[xob])
            p.store(x_out_dram[ti, :, cb * 512:(cb + 1) * 512], xob)


def build_ffn0(env):
    po = Pools(env, n_pf=4, n_pt=2, ln=True, wstream=False, rope=False)
    po.identf = env.sbuf("sc_identf", [128, 128], F32)
    idf_in = env.dram_in("c_identf", [128, 128], F32)
    env.p.load(po.identf, idf_in)
    po.identf.frozen = True
    prr = env.psum("psrr", [128, 64], F32)
    p = env.p
    x_in = env.dram_in("x", [NTILE, 128, D])
    a_in = env.dram_in("Arhs", [128, 32 * 512])
    b_in = env.dram_in("Brhs", [128, 32 * 512])
    wqr_in = env.dram_in("WqR", [128, 32 * 16])
    keys_in = env.dram_in("keysT", [16, 64])
    x_out = env.dram_out("xo", [NTILE, 128, D])
    A = env.sbuf("A_rhs", [128, 32 * 512], BF16)
    Bb = env.sbuf("B_rhs", [128, 32 * 512], BF16)
    Wq = env.sbuf("WqR_b", [128, 32 * 16], BF16)
    ks = env.sbuf("keysT_b", [16, 64], F32)
    p.load(A, a_in, queue="gpsimd")
    p.load(Bb, b_in, queue="gpsimd")
    p.load(Wq, wqr_in, queue="gpsimd")
    p.load(ks, keys_in, queue="sync")
    for b in (A, Bb, Wq, ks):
        b.frozen = True
    emit_ffn0(env, po, x_in, x_out,
              {"A": A, "B": Bb, "WqR": Wq, "keysT": ks}, prr)


def build_pre0(env):
    po = Pools(env)
    x_in = env.dram_in("x", [NTILE, 128, D])
    wk = env.dram_in("wk", [32, 1, 128, 128])
    wv = env.dram_in("wv", [32, 1, 128, 128])
    xnT_out = env.dram_out("xnT", [32, 128, TOK], BF16)
    kT_out = env.dram_out("kT", [1, 128, TOK], BF16)
    v_out = env.dram_out("v", [NTILE, 128, 128], BF16)
    xnT = [env.sbuf(f"xnT{c}", [128, TOK], BF16) for c in range(32)]
    emit_ln_tp_phase(env, po, x_in, xnT, xnT_out=xnT_out)
    for c in xnT:
        c.frozen = True
    kT = [env.sbuf("kT0", [128, TOK], BF16)]
    emit_proj_feat(env, po, wk, xnT, kT, out_store=kT_out)
    vt = [env.sbuf(f"vt{t}", [128, 128], BF16) for t in range(NTILE)]
    emit_proj_tok(env, po, wv, xnT, ncolblk=1, colw=128,
                  out_tiles=vt, out_store=v_out)


_IDENT = np.eye(128, dtype=BF)
_IDENTF = np.eye(128, dtype=np.float32)


_HOOK = None


def run_launch(build_fn, in_maps, name="l", trace=False):
    env = Env(name)
    nc = env.build(build_fn)
    names = set(env._ins)
    for m in in_maps:
        if "c_ident" in names:
            m["c_ident"] = _IDENT
        if "c_identf" in names:
            m["c_identf"] = _IDENTF
    res = run_bass_kernel_spmd(nc, in_maps, core_ids=list(range(NCORES)),
                               trace=trace)
    if _HOOK is not None:
        _HOOK(name, res.results if not trace else res.results)
    return res.results if not trace else res


MBIG = 2.33e10


def _pm(a):
    a = np.asarray(a)
    if a.ndim == 2:
        return np.ascontiguousarray(a)
    if a.ndim == 3:
        assert a.shape[1] == 128, a.shape
        return np.ascontiguousarray(a.transpose(1, 0, 2).reshape(a.shape[1], -1))
    raise ValueError(a.shape)


def _row(v):
    return np.broadcast_to(np.asarray(v, np.float32)[None, :], (128, len(v))).copy()


def _gather_batch(res, key, b, axis=2):
    return np.concatenate([np.asarray(res[4 * b + i][key]) for i in range(4)],
                          axis=axis)


def _ctx_for_core(res, c):
    b = c // 4
    tc = c % 4
    ctx = np.stack([np.asarray(res[4 * b + j // 8]["ctx"][j % 8])
                    for j in range(32)])
    return np.ascontiguousarray(ctx[:, :, tc * 1024:(tc + 1) * 1024])


def kernel(**I):
    perm = np.asarray(I["L1_ffn_perm"]).astype(np.int64)
    x = np.asarray(I["L0_attn_a_hidden_states"])
    xp = np.ascontiguousarray(x[:, :, perm])
    shards = [np.ascontiguousarray(
        xp[c // 4, (c % 4) * TOK:(c % 4 + 1) * TOK].reshape(NTILE, 128, D))
        for c in range(NCORES)]

    slopes = alibi_slopes(32)
    cosS, sinS = rope_tables(np.arange(S), 128)
    cosyS, sinyS = yarn_tables(np.arange(S))
    qoffm = np.zeros((5, 128, 512), np.float32)
    qoffm[0] = np.broadcast_to(np.arange(512, dtype=np.float32)[None, :], (128, 512))
    for r in range(4):
        qoffm[r + 1] = qoffm[0] + MBIG * (
            np.arange(512)[None, :] < 128 * r + np.arange(128)[:, None])
    qoffm_pm = _pm(qoffm)
    mdiag = np.where(np.arange(128)[:, None] < np.arange(128)[None, :],
                     -428.0, 0.0).astype(np.float32)
    mdiagw = np.zeros((128, 4 * 512), np.float32)
    for r_ in range(4):
        mdiagw[:, r_ * 512 + r_ * 128:r_ * 512 + (r_ + 1) * 128] = mdiag
    mvar = np.zeros((4, 128, 512), np.float32)
    for r in range(4):
        mvar[r] = -1e9 * (np.arange(512)[None, :] < 128 * r + np.arange(128)[:, None])
    mvar_pm = _pm(mvar)
    biasKQ = {}
    for h0 in (0, 8, 16, 24):
        t = np.zeros((128, 8 * 32 * 8), np.float32)
        p_ = np.arange(128, dtype=np.float32)
        for j in range(8):
            sl = slopes[h0 + j]
            for kt in range(32):
                for qb_ in range(8):
                    t[:, (j * 32 + kt) * 8 + qb_] = sl * (kt * 128 + p_ - qb_ * 512)
        biasKQ[h0] = t
    slp = {h0: _row(-slopes[h0:h0 + 8] * SQ128) for h0 in (0, 8, 16, 24)}
    percore_pos = [np.arange((c % 4) * TOK, (c % 4 + 1) * TOK) for c in range(NCORES)]

    Wq0 = np.asarray(I["L0_attn_a_Wq"])[perm, :]
    Wkv0 = np.asarray(I["L0_attn_a_Wkv"])[perm, :]
    Wo0 = np.asarray(I["L0_attn_a_Wo"])[:, perm]
    A_p = np.asarray(I["L0_ffn_A_w"])[:, perm]
    B_p = np.asarray(I["L0_ffn_B_w"])[perm, :]
    WqR_p = np.asarray(I["L0_ffn_Wq_w"])[:, perm]
    keys0 = np.asarray(I["L0_ffn_keys"])
    Wqb = np.asarray(I["L0_attn_b_Wq"])[:, perm]
    Wkb = np.asarray(I["L0_attn_b_Wk"])[:, perm]
    Wvb = np.asarray(I["L0_attn_b_Wv"])[:, perm]
    Wob = np.asarray(I["L0_attn_b_Wo"])[perm, :]
    hp1 = np.asarray(I["L0_attn_b_hash_proj1"])
    hb1 = np.asarray(I["L0_attn_b_hash_bias1"])
    hp2 = np.asarray(I["L0_attn_b_hash_proj2"])
    hb2 = np.asarray(I["L0_attn_b_hash_bias2"])
    qkvw = np.asarray(I["L1_attn_a_qkv_w"])[perm, :]
    qkvb = np.asarray(I["L1_attn_a_qkv_b"])
    outw = np.asarray(I["L1_attn_a_out_w"])[:, perm]
    outb = np.asarray(I["L1_attn_a_out_b"])[perm]
    wqd = np.asarray(I["L1_attn_b_w_q_down"])[perm, :]
    wqun = np.asarray(I["L1_attn_b_w_q_up_nope"])
    wqur = np.asarray(I["L1_attn_b_w_q_up_rope"])
    wkvd = np.asarray(I["L1_attn_b_w_kv_down"])[perm, :]
    wkrp = np.asarray(I["L1_attn_b_w_k_rope"])[perm, :]
    wuk = np.asarray(I["L1_attn_b_w_uk"])
    wuv = np.asarray(I["L1_attn_b_w_uv"])
    wout1b = np.asarray(I["L1_attn_b_w_out"])[:, perm]
    up = np.asarray(I["L1_ffn_up"])
    down = np.asarray(I["L1_ffn_down"])

    wk0_blk = block_lhsT(Wkv0[:, :128])
    wv0_rhs = block_rhs(Wkv0[:, 128:], nb=128)
    wq0_4 = Wq0.reshape(32, 128, 32, 128).transpose(0, 2, 1, 3)
    wo0_blk = block_rhs(Wo0)
    Arhs = _pm(A_p.T.reshape(32, 128, 512))
    Brhs = np.ascontiguousarray(
        B_p.T.reshape(4, 128, 8, 512).transpose(1, 0, 2, 3)).reshape(128, -1)
    WqRb = _pm(WqR_p.T.reshape(32, 128, 16))
    keysT = np.ascontiguousarray(keys0.T)
    wkb_blk = block_lhsT(Wkb.T)
    wvb_rhs = block_rhs(Wvb.T)
    wqb_4 = Wqb.T.reshape(32, 128, 32, 128).transpose(0, 2, 1, 3)
    wob_blk = block_rhs(Wob.T)
    hp1_pm = _pm(hp1)
    hp2_pm = _pm(hp2)
    hb1_pm = np.ascontiguousarray(hb1[:, 0, :].T)
    hb2_pm = np.ascontiguousarray(hb2[:, 0, :].T)
    wk1_blk = block_lhsT(qkvw[:, D:2 * D])
    wv1_rhs = block_rhs(qkvw[:, 2 * D:])
    wq1_4 = qkvw[:, :D].reshape(32, 128, 32, 128).transpose(0, 2, 1, 3)
    kb1 = np.ascontiguousarray(qkvb[D:2 * D].reshape(32, 128).T)
    vb1_row = _row(qkvb[2 * D:])
    qb1 = np.ascontiguousarray(qkvb[:D].reshape(32, 128).T)
    wo1_blk = block_rhs(outw)
    ob1_row = _row(outb)
    up_pm = _pm(up.reshape(16, 2, 128, 8, 128).transpose(2, 0, 1, 3, 4)
                .reshape(128, -1))
    dn_pm = _pm(down.reshape(16, 8, 128, 256).transpose(2, 0, 1, 3)
                .reshape(128, -1))
    wcq_blk = block_lhsT(wqd)
    wckv_blk = block_lhsT(wkvd)
    wkr_blk = block_lhsT(wkrp, mc=64)
    wqn_blk = block_lhsT(wqun)
    wqr_blk = block_lhsT(wqur, mc=64)
    wuk4 = wuk.reshape(32, 128, 4, 128).transpose(0, 2, 3, 1)
    wuv4 = wuv.reshape(32, 128, 4, 128).transpose(0, 2, 3, 1)
    wo1b_blk = block_rhs(wout1b)

    def wq_slice(w4, h0):
        return np.ascontiguousarray(
            w4[:, h0:h0 + 8].reshape(-1, 128, 128).transpose(1, 0, 2)
        ).reshape(128, -1)

    in1 = [{"x": shards[c], "wk": wk0_blk, "wv": wv0_rhs} for c in range(NCORES)]
    r1 = run_launch(build_pre0, in1, "pre0")
    in2 = []
    for c in range(NCORES):
        b, h0 = c // 4, 8 * (c % 4)
        in2.append({"xnf": _gather_batch(r1, "xnT", b),
                    "kTf": _gather_batch(r1, "kT", b),
                    "vf": _pm(_gather_batch(r1, "v", b, axis=0)),
                    "wq": wq_slice(wq0_4, h0),
                    "biasKQ": biasKQ[h0], "qoffm": qoffm_pm, "slopes": slp[h0]})
    r2 = run_launch(build_attn0, in2, "attn0")
    in3 = [{"ctxT": _ctx_for_core(r2, c), "wo": wo0_blk, "x": shards[c]}
           for c in range(NCORES)]
    r3 = run_launch(build_wo_res, in3, "wores0")
    in4 = [{"x": np.asarray(r3[c]["xo"]), "Arhs": Arhs, "Brhs": Brhs,
            "WqR": WqRb, "keysT": keysT} for c in range(NCORES)]
    r4 = run_launch(build_ffn0, in4, "ffn0a")

    in5 = []
    for c in range(NCORES):
        pos = percore_pos[c]
        in5.append({"x": np.asarray(r4[c]["xo"]), "wk": wkb_blk, "wv": wvb_rhs,
                    "cosk": cosS[:, pos], "sink": sinS[:, pos],
                    "hp1": hp1_pm, "hb1": hb1_pm, "hp2": hp2_pm, "hb2": hb2_pm})
    r5 = run_launch(build_pre_l0b, in5, "prel0b")
    in6 = []
    for c in range(NCORES):
        b, h0 = c // 4, 8 * (c % 4)
        g0 = h0 // 4
        vfull = _gather_batch(r5, "v", b, axis=0)
        in6.append({"xnf": _gather_batch(r5, "xnT", b),
                    "wq": np.ascontiguousarray(
                        wqb_4[:, h0:h0 + 8].transpose(0, 2, 1, 3)
                        .reshape(32, 128, -1)),
                    "cosq": cosS.astype(BF), "sinq": sinS.astype(BF),
                    "hp1": hp1_pm[:, h0 * 128:(h0 + 8) * 128],
                    "hb1": hb1_pm[:, h0:h0 + 8],
                    "hp2": hp2_pm[:, h0 * 128:(h0 + 8) * 128],
                    "hb2": hb2_pm[:, h0:h0 + 8],
                    "khf": _gather_batch(r5, "kh", b)[h0:h0 + 8],
                    "krf": _gather_batch(r5, "kT", b)[g0:g0 + 2],
                    "vf": _pm(vfull[:, :, g0 * 128:(g0 + 2) * 128]),
                    "mdiag": mdiag, "mdiagw": mdiagw})
    r6 = run_launch(build_attn_l0b, in6, "attnl0b")
    in7 = [{"ctxT": _ctx_for_core(r6, c), "wo": wob_blk,
            "x": np.asarray(r4[c]["xo"])} for c in range(NCORES)]
    r7 = run_launch(build_wo_res, in7, "woresl0b")
    in8 = [{"x": np.asarray(r7[c]["xo"]), "Arhs": Arhs, "Brhs": Brhs,
            "WqR": WqRb, "keysT": keysT} for c in range(NCORES)]
    r8 = run_launch(build_ffn0, in8, "ffn0b")

    in9 = []
    for c in range(NCORES):
        pos = percore_pos[c]
        in9.append({"x": np.asarray(r8[c]["xo"]), "wk": wk1_blk, "wv": wv1_rhs,
                    "kb": kb1, "vbrow": vb1_row,
                    "cosk": cosS[:, pos], "sink": sinS[:, pos]})
    r9 = run_launch(build_pre_l1a, in9, "prel1a")
    in10 = []
    for c in range(NCORES):
        b, h0 = c // 4, 8 * (c % 4)
        vfull = _gather_batch(r9, "v", b, axis=0)
        vf = np.stack([_pm(vfull[:, :, (h0 + j) * 128:(h0 + j + 1) * 128])
                       for j in range(8)])
        in10.append({"xnf": _gather_batch(r9, "xnT", b),
                     "wq": wq_slice(wq1_4, h0),
                     "qb": qb1[:, h0:h0 + 8],
                     "cosq": cosS.astype(BF), "sinq": sinS.astype(BF),
                     "kTf": _gather_batch(r9, "kT", b)[h0:h0 + 8],
                     "vf": vf, "mvar": mvar_pm})
    r10 = run_launch(build_attn_l1a, in10, "attnl1a")
    in11 = [{"ctxT": _ctx_for_core(r10, c), "wo": wo1_blk,
             "x": np.asarray(r8[c]["xo"]), "bias": ob1_row}
            for c in range(NCORES)]
    r11 = run_launch(build_wo_res_bias, in11, "woresl1a")
    in12 = [{"x": np.asarray(r11[c]["xo"]), "up": up_pm, "dn": dn_pm}
            for c in range(NCORES)]
    r12 = run_launch(build_ffn1, in12, "ffn1a")

    in13 = []
    for c in range(NCORES):
        pos = percore_pos[c]
        cosy, siny = cosyS[:, pos], sinyS[:, pos]
        in13.append({"x": np.asarray(r12[c]["xo"]), "wcq": wcq_blk,
                     "wckv": wckv_blk, "wkr": wkr_blk,
                     "cosy": cosy, "siny": siny})
    r13 = run_launch(build_pre_l1b, in13, "prel1b")
    in14 = []
    for c in range(NCORES):
        b, h0 = c // 4, 8 * (c % 4)
        in14.append({"cqf": _gather_batch(r13, "cqT", b),
                     "ckvf": _gather_batch(r13, "ckvT", b),
                     "krf": _gather_batch(r13, "krT", b, axis=1),
                     "wqn": np.ascontiguousarray(
                         wqn_blk[:, h0:h0 + 8].transpose(2, 0, 1, 3)).reshape(128, -1),
                     "wqr": np.ascontiguousarray(
                         wqr_blk[:, h0:h0 + 8].transpose(2, 0, 1, 3)).reshape(128, -1),
                     "wuk": np.ascontiguousarray(
                         wuk4[h0:h0 + 8].transpose(2, 1, 0, 3)).reshape(128, -1),
                     "wuv": np.ascontiguousarray(
                         wuv4[h0:h0 + 8].transpose(2, 1, 0, 3)).reshape(128, -1),
                     "cosy": cosyS, "siny": sinyS, "mvar": mvar_pm})
    r14 = run_launch(build_attn_l1b, in14, "attnl1b")
    in15 = [{"ctxT": _ctx_for_core(r14, c), "wo": wo1b_blk,
             "x": np.asarray(r12[c]["xo"])} for c in range(NCORES)]
    r15 = run_launch(build_wo_res, in15, "woresl1b")
    in16 = [{"x": np.asarray(r15[c]["xo"]), "up": up_pm, "dn": dn_pm}
            for c in range(NCORES)]
    r16 = run_launch(build_ffn1, in16, "ffn1b")

    out = np.empty((B, S, D), np.float32)
    for c in range(NCORES):
        out[c // 4, (c % 4) * TOK:(c % 4 + 1) * TOK, :] = \
            np.asarray(r16[c]["xo"]).reshape(TOK, D)
    res = np.empty_like(out)
    res[:, :, perm] = out
    return res


def emit_hash(env, po, src_chunks, out_chunks, hp1, hb1, hp2, hb2, nh, seq,
              psA, psB_, out_dram=None, out_dram_bufs=None):
    p = env.p
    t1 = env.sbuf("hh_t1", [128, 512], BF16)
    x1h = env.sbuf("hh_x1", [128, 512], BF16)
    t2 = env.sbuf("hh_t2", [128, 512], BF16)
    t3 = env.sbuf("hh_t3", [128, 512], BF16)
    stg = env.sbuf("hh_stg", [128, 512], BF16) if out_dram is not None else None
    for h in range(nh):
        src = src_chunks(h)
        for tb in range(seq // 512):
            cols = slice(tb * 512, (tb + 1) * 512)
            p.op("tensor", lambda e, h=h, cols=cols, src=src: e.matmul(
                psA.ap[:], hp1.ap[:, h * 128:(h + 1) * 128], src.ap[:, cols],
                start=True, stop=True), reads=[hp1, src], writes=[psA])
            p.op("scalar", lambda e, h=h: e.activation(
                out=t1.ap[:], in_=psA.ap[:], func=AF.Silu,
                bias=hb1.ap[:, h:h + 1]), reads=[psA, hb1], writes=[t1])
            p.op("vector", lambda e, cols=cols, src=src: e.tensor_tensor(
                out=x1h.ap[:], in0=t1.ap[:], in1=src.ap[:, cols], op=ALU.add),
                reads=[t1, src], writes=[x1h])
            p.op("tensor", lambda e, h=h: e.matmul(
                psB_.ap[:], hp2.ap[:, h * 128:(h + 1) * 128], x1h.ap[:],
                start=True, stop=True), reads=[hp2, x1h], writes=[psB_])
            p.op("scalar", lambda e, h=h: e.activation(
                out=t2.ap[:], in_=psB_.ap[:], func=AF.Identity,
                bias=hb2.ap[:, h:h + 1]), reads=[psB_, hb2], writes=[t2])
            p.op("vector", lambda e: e.tensor_tensor(
                out=t3.ap[:], in0=t2.ap[:], in1=x1h.ap[:], op=ALU.add),
                reads=[t2, x1h], writes=[t3])
            if out_dram is None:
                p.op("scalar", lambda e, h=h, cols=cols: e.activation(
                    out=out_chunks[h].ap[:, cols], in_=t3.ap[:], func=AF.Sign),
                    reads=[t3], writes=[out_chunks[h]])
            else:
                p.op("scalar", lambda e: e.activation(
                    out=stg.ap[:], in_=t3.ap[:], func=AF.Sign),
                    reads=[t3], writes=[stg])
                p.store(out_dram[h, :, cols], stg,
                        dst_buf=None if out_dram_bufs is None else out_dram_bufs[h])


def build_pre_l0b(env):
    po = Pools(env, n_pf=4, n_pt=2, ln=True, wstream=True, rope=True)
    p = env.p
    x_in = env.dram_in("x", [NTILE, 128, D])
    wk = env.dram_in("wk", [32, 8, 128, 128])
    wv = env.dram_in("wv", [32, 2, 128, 512])
    cosk = env.dram_in("cosk", [128, TOK])
    sink = env.dram_in("sink", [128, TOK])
    hp1_in = env.dram_in("hp1", [128, 32 * 128])
    hb1_in = env.dram_in("hb1", [128, 32])
    hp2_in = env.dram_in("hp2", [128, 32 * 128])
    hb2_in = env.dram_in("hb2", [128, 32])
    xnT_out = env.dram_out("xnT", [32, 128, TOK], BF16)
    kT_out = env.dram_out("kT", [8, 128, TOK], BF16)
    kh_out = env.dram_out("kh", [32, 128, TOK], BF16)
    v_out = env.dram_out("v", [NTILE, 128, 1024], BF16)

    xnT = [env.sbuf(f"xnT{c}", [128, TOK], BF16) for c in range(32)]
    emit_ln_tp_phase(env, po, x_in, xnT, xnT_out=xnT_out)
    for c in xnT:
        c.frozen = True
    cos_b = env.sbuf("cos_b", [128, TOK], F32)
    sin_b = env.sbuf("sin_b", [128, TOK], F32)
    p.load(cos_b, cosk)
    p.load(sin_b, sink)
    cos_b.frozen = True
    sin_b.frozen = True
    kT = [env.sbuf(f"kTc{cgrp}", [128, TOK], BF16) for cgrp in range(8)]
    emit_proj_feat(env, po, wk, xnT, kT, evac="rope", rope=(cos_b, sin_b),
                   out_store=kT_out)
    for b in kT:
        b.frozen = True
    vt = [env.sbuf(f"vt{t}", [128, 1024], BF16) for t in range(NTILE)]
    emit_proj_tok(env, po, wv, xnT, ncolblk=2, colw=512, out_tiles=vt,
                  out_store=v_out, npsum=2)
    hp1 = env.sbuf("hp1b", [128, 32 * 128], BF16)
    hp2 = env.sbuf("hp2b", [128, 32 * 128], BF16)
    hb1 = env.sbuf("hb1b", [128, 32], F32)
    hb2 = env.sbuf("hb2b", [128, 32], F32)
    p.load(hp1, hp1_in, queue="gpsimd")
    p.load(hp2, hp2_in, queue="gpsimd")
    p.load(hb1, hb1_in)
    p.load(hb2, hb2_in)
    for b in (hp1, hp2, hb1, hb2):
        b.frozen = True
    emit_hash(env, po, lambda h: kT[h // 4], None, hp1, hb1, hp2, hb2, 32, TOK,
              po.pf[2], po.pf[3], out_dram=kh_out)


def build_attn_l0b(env):
    po = Pools(env, n_pf=0, n_pt=2, ln=False, wstream=False, rope=True)
    p = env.p
    xnf = env.dram_in("xnf", [32, 128, S], BF16)
    wq_in = env.dram_in("wq", [32, 128, NPAIR * 128])
    cosq = env.dram_in("cosq", [128, S], BF16)
    sinq = env.dram_in("sinq", [128, S], BF16)
    hp1_in = env.dram_in("hp1", [128, NPAIR * 128])
    hb1_in = env.dram_in("hb1", [128, NPAIR])
    hp2_in = env.dram_in("hp2", [128, NPAIR * 128])
    hb2_in = env.dram_in("hb2", [128, NPAIR])
    khf = env.dram_in("khf", [NPAIR, 128, S], BF16)
    krf = env.dram_in("krf", [2, 128, S], BF16)
    vf = env.dram_in("vf", [128, NT * 256], BF16)
    mdiag_in = env.dram_in("mdiag", [128, 128])
    mdw_in = env.dram_in("mdiagw", [128, 4 * 512])
    idf_in = env.dram_in("c_identf", [128, 128], F32)
    ctx_out = env.dram_out("ctx", [NPAIR, 128, S], BF16)

    v_all = env.sbuf("v_all", [128, NT * 256], BF16)
    p.load(v_all, vf)
    identf = env.sbuf("identf_b", [128, 128], F32)
    p.load(identf, idf_in)
    identf.frozen = True
    mdiag = env.sbuf("mdiag_b", [128, 128], F32)
    p.load(mdiag, mdiag_in)
    causal01 = env.sbuf("caus01", [128, 128], BF16)
    p.op("vector", lambda e: e.tensor_scalar(
        out=causal01.ap[:], in0=mdiag.ap[:], scalar1=-1.0, scalar2=None,
        op0=ALU.is_ge), reads=[mdiag], writes=[causal01])
    for b in (v_all, mdiag, causal01):
        b.frozen = True

    qrd = env.dram_tmp("qrd", [NPAIR, 128, S], BF16)
    qhd = env.dram_tmp("qhd", [NPAIR, 128, S], BF16)
    qrd_b = [Buf(None) for _ in range(NPAIR)]
    qhd_b = [Buf(None) for _ in range(NPAIR)]
    env.push_scope()
    pfA = [env.psum(f"pa{i}", [128, 512], F32) for i in range(4)]
    qrT = [env.sbuf(f"qrT{j}", [128, S], BF16) for j in range(NPAIR)]
    cos_b = env.sbuf("cosb", [128, S], BF16)
    sin_b = env.sbuf("sinb", [128, S], BF16)
    p.load(cos_b, cosq)
    p.load(sin_b, sinq)
    hp1 = env.sbuf("hp1b", [128, NPAIR * 128], BF16)
    hp2 = env.sbuf("hp2b", [128, NPAIR * 128], BF16)
    hb1 = env.sbuf("hb1b", [128, NPAIR], F32)
    hb2 = env.sbuf("hb2b", [128, NPAIR], F32)
    p.load(hp1, hp1_in, queue="gpsimd")
    p.load(hp2, hp2_in, queue="gpsimd")
    p.load(hb1, hb1_in)
    p.load(hb2, hb2_in)
    for b in (cos_b, sin_b, hp1, hp2, hb1, hb2):
        b.frozen = True
    xtl = [env.sbuf(f"qp_xl{i}", [128, 8 * 512], BF16) for i in range(2)]
    xth = [env.sbuf(f"qp_xh{i}", [128, 8 * 512], BF16) for i in range(2)]
    wqb = [env.sbuf(f"wqb{i}", [128, NPAIR * 128], BF16) for i in range(2)]
    for grp in range(2):
        pj = range(grp * 4, grp * 4 + 4)
        for tb in range(8):
            for half in range(2):
                d0 = half * 16
                par = (tb * 2 + half) % 2
                xl, xh = xtl[par], xth[par]
                p.load(xl, xnf[d0:d0 + 8, :, tb * 512:(tb + 1) * 512]
                       .rearrange("a p m -> p a m"), queue="sync",
                       dst_ap=xl.ap[:].rearrange("p (a m) -> p a m", a=8))
                p.load(xh, xnf[d0 + 8:d0 + 16, :, tb * 512:(tb + 1) * 512]
                       .rearrange("a p m -> p a m"), queue="scalar",
                       dst_ap=xh.ap[:].rearrange("p (a m) -> p a m", a=8))
                for dm in range(d0, d0 + 16):
                    wb = wqb[dm % 2]
                    p.load(wb, wq_in[dm, :, :], queue="gpsimd")
                    src = xl if dm - d0 < 8 else xh
                    off = (dm - d0) % 8
                    for j in pj:
                        def mk(e, j=j, dm=dm, off=off, wb=wb, src=src):
                            return e.matmul(
                                pfA[j % 4].ap[:], wb.ap[:, j * 128:(j + 1) * 128],
                                src.ap[:, off * 512:(off + 1) * 512],
                                start=(dm == 0), stop=(dm == 31))
                        p.op("tensor", mk, reads=[wb, src], writes=[pfA[j % 4]])
            for j in pj:
                cols = slice(tb * 512, (tb + 1) * 512)
                emit_rope_evac(env, po, pfA[j % 4], qrT[j], cols, cos_b, sin_b)
    for b in qrT:
        b.frozen = True
    emit_hash(env, po, lambda j: qrT[j], None, hp1, hb1, hp2, hb2, NPAIR, S,
              pfA[0], pfA[1], out_dram=qhd, out_dram_bufs=qhd_b)
    for j in range(NPAIR):
        p.store(qrd[j, :, :], qrT[j], dst_buf=qrd_b[j])
    env.pop_scope()

    env.push_scope()
    pfC = [env.psum(f"pc{i}", [128, 512], F32) for i in range(6)]
    psSim2, psSc2 = pfC[0:2], pfC[2:4]
    psO, psZ = pfC[4], pfC[5]
    kh_p = [env.sbuf(f"kh_p{i}", [128, S], BF16) for i in range(2)]
    kr_p = [env.sbuf(f"kr_p{i}", [128, S], BF16) for i in range(2)]
    qh_p = [env.sbuf(f"qh_p{i}", [128, S], BF16) for i in range(2)]
    qr_p = [env.sbuf(f"qr_p{i}", [128, S], BF16) for i in range(2)]
    sim2 = [env.sbuf(f"sim{i}", [128, S], BF16) for i in range(2)]
    pm2 = [env.sbuf(f"pmb{i}", [128, S], BF16) for i in range(2)]
    junk = env.sbuf("junkb", [128, S], BF16)
    onesb = env.sbuf("onesb", [128, S], BF16)
    p.op("vector", lambda e: e.memset(onesb.ap[:], 1.0), writes=[onesb])
    onesb.frozen = True
    scanb = env.sbuf("scanb", [128, S], F32)
    e1 = env.sbuf("e1b", [128, S], BF16)
    tieok = env.sbuf("tieokb", [128, S], BF16)
    kept = env.sbuf("keptb", [128, S], BF16)
    ptb = [env.sbuf(f"ptb{i}", [128, 128], BF16) for i in range(2)]
    sv = [{n: env.sbuf(f"sv_{n}{i}", [128, 1], F32)
           for n in ("tbm0", "acc", "s2", "bias", "tval", "rr", "zc", "rzc")}
          for i in range(2)]
    zrow = env.sbuf("zrowb", [1, 128], F32)
    bz = env.sbuf("bzb", [128, 128], F32)
    onesr = env.sbuf("onesrb", [1, 128], F32)
    p.op("vector", lambda e: e.memset(onesr.ap[:], 1.0), writes=[onesr])
    onesr.frozen = True
    ctxs2 = [env.sbuf(f"ctxsb{i}", [128, 128], BF16) for i in range(2)]
    junkV = env.sbuf("junkV", [128, 768], BF16)
    STEPS = (128, 64, 32, 16, 8, 4, 2, 1)
    mdiagw = env.sbuf("mdiagw_b", [128, 4 * 512], F32)
    p.load(mdiagw, mdw_in)
    mdiagw.frozen = True

    def emit_ss(j, pj, qt):
        par = qt % 2
        simb, pmb = sim2[par], pm2[par]
        psA, psB2 = psSim2[par], psSc2[par]
        ext = (qt + 1) * 128
        nch = (ext + 511) // 512
        r = qt % 4
        for ch in range(nch):
            n = min(512, ext - ch * 512)
            p.op("tensor", lambda e, qt=qt, ch=ch, n=n, psA=psA: e.matmul(
                psA.ap[:, 0:n], qh_p[pj].ap[:, qt * 128:(qt + 1) * 128],
                kh_p[pj].ap[:, ch * 512:ch * 512 + n], start=True, stop=True),
                reads=[qh_p[pj], kh_p[pj]], writes=[psA])
            if ch == nch - 1:
                p.op("vector", lambda e, ch=ch, n=n, psA=psA, simb=simb, r=r:
                     e.scalar_tensor_tensor(
                         out=simb.ap[:, ch * 512:ch * 512 + n],
                         in0=psA.ap[:, 0:n], scalar=1.0,
                         in1=mdiagw.ap[:, r * 512:r * 512 + n],
                         op0=ALU.mult, op1=ALU.add),
                     reads=[psA, mdiagw], writes=[simb])
            else:
                p.op("scalar", lambda e, ch=ch, n=n, psA=psA, simb=simb: e.copy(
                    out=simb.ap[:, ch * 512:ch * 512 + n], in_=psA.ap[:, 0:n]),
                    reads=[psA], writes=[simb])
        krb = kr_p[(j // 4) % 2]
        for ch in range(nch):
            n = min(512, ext - ch * 512)
            p.op("tensor", lambda e, qt=qt, ch=ch, n=n, psB2=psB2, krb=krb: e.matmul(
                psB2.ap[:, 0:n], qr_p[pj].ap[:, qt * 128:(qt + 1) * 128],
                krb.ap[:, ch * 512:ch * 512 + n], start=True, stop=True),
                reads=[qr_p[pj], krb], writes=[psB2])
            p.op("scalar", lambda e, ch=ch, n=n, psB2=psB2, pmb=pmb: e.activation(
                out=pmb.ap[:, ch * 512:ch * 512 + n], in_=psB2.ap[:, 0:n],
                func=AF.Exp, scale=1.0 / SQ128),
                reads=[psB2], writes=[pmb])

    def emit_search(qt, dq):
        par = qt % 2
        simb = sim2[par]
        v = sv[par]
        ext = (qt + 1) * 128
        nsl = len(STEPS) + 1
        chunk = max(1, (len(dq) + nsl - 1) // nsl)

        def drain(k):
            for _ in range(k):
                if dq:
                    dq.pop(0)()
        p.op("vector", lambda e, v=v: e.memset(v["tbm0"].ap[:], 0.0),
             writes=[v["tbm0"]])
        for step in STEPS:
            p.op("vector", lambda e, v=v, step=step: e.tensor_scalar(
                out=v["bias"].ap[:], in0=v["tbm0"].ap[:], scalar1=-2.0,
                scalar2=float(-2 * step + 130.5), op0=ALU.mult, op1=ALU.add),
                reads=[v["tbm0"]], writes=[v["bias"]])
            p.op("scalar", lambda e, v=v, ext=ext, simb=simb: e.activation(
                out=junk.ap[:, 0:ext], in_=simb.ap[:, 0:ext], func=AF.Sign,
                bias=v["bias"].ap[:], accum_out=v["acc"].ap[:]),
                reads=[simb, v["bias"]], writes=[junk, v["acc"]])
            drain(chunk)
            p.op("vector", lambda e, v=v, ext=ext, step=step: e.tensor_scalar(
                out=v["s2"].ap[:], in0=v["acc"].ap[:],
                scalar1=float(1640 - ext), scalar2=float(step),
                op0=ALU.is_ge, op1=ALU.mult),
                reads=[v["acc"]], writes=[v["s2"]])
            p.op("vector", lambda e, v=v: e.tensor_tensor(
                out=v["tbm0"].ap[:], in0=v["tbm0"].ap[:], in1=v["s2"].ap[:],
                op=ALU.add), reads=[v["tbm0"], v["s2"]], writes=[v["tbm0"]])
        p.op("vector", lambda e, v=v: e.tensor_scalar(
            out=v["tval"].ap[:], in0=v["tbm0"].ap[:], scalar1=2.0,
            scalar2=-130.0, op0=ALU.mult, op1=ALU.add),
            reads=[v["tbm0"]], writes=[v["tval"]])
        p.op("vector", lambda e, v=v: e.tensor_scalar(
            out=v["bias"].ap[:], in0=v["tval"].ap[:], scalar1=-1.0,
            scalar2=-0.5, op0=ALU.mult, op1=ALU.add),
            reads=[v["tval"]], writes=[v["bias"]])
        p.op("scalar", lambda e, v=v, ext=ext, simb=simb: e.activation(
            out=junk.ap[:, 0:ext], in_=simb.ap[:, 0:ext], func=AF.Sign,
            bias=v["bias"].ap[:], accum_out=v["acc"].ap[:]),
            reads=[simb, v["bias"]], writes=[junk, v["acc"]])
        drain(len(dq))
        p.op("vector", lambda e, v=v, ext=ext: e.tensor_scalar(
            out=v["rr"].ap[:], in0=v["acc"].ap[:], scalar1=-0.5,
            scalar2=float(820 - ext // 2), op0=ALU.mult, op1=ALU.add),
            reads=[v["acc"]], writes=[v["rr"]])

    def emit_post(j, qt):
        par = qt % 2
        simb, pmb = sim2[par], pm2[par]
        psB2, psSimb = psSc2[par], psSim2[par]
        v = sv[par]
        ctxs = ctxs2[par]
        ext = (qt + 1) * 128
        dq = []
        if qt >= 6:
            dq.append(lambda: p.op("vector", lambda e: e.tensor_scalar(
                out=e1.ap[:, 0:ext], in0=simb.ap[:, 0:ext],
                scalar1=v["tval"].ap[:], scalar2=None, op0=ALU.is_equal),
                reads=[simb, v["tval"]], writes=[e1]))
            dq.append(lambda: p.op("vector", lambda e: e.tensor_tensor_scan(
                out=scanb.ap[:, 0:ext], data0=onesb.ap[:, 0:ext],
                data1=e1.ap[:, 0:ext], initial=0.0, op0=ALU.mult, op1=ALU.add),
                reads=[onesb, e1], writes=[scanb]))
            dq.append(lambda: p.op("vector", lambda e: e.scalar_tensor_tensor(
                out=tieok.ap[:, 0:ext], in0=scanb.ap[:, 0:ext],
                scalar=v["rr"].ap[:], in1=e1.ap[:, 0:ext],
                op0=ALU.is_le, op1=ALU.mult),
                reads=[scanb, v["rr"], e1], writes=[tieok]))
            dq.append(lambda: p.op("vector", lambda e: e.scalar_tensor_tensor(
                out=kept.ap[:, 0:ext], in0=simb.ap[:, 0:ext],
                scalar=v["tval"].ap[:], in1=tieok.ap[:, 0:ext],
                op0=ALU.is_gt, op1=ALU.max),
                reads=[simb, v["tval"], tieok], writes=[kept]))
            dq.append(lambda: p.op("vector", lambda e: e.scalar_tensor_tensor(
                out=pmb.ap[:, 0:ext], in0=pmb.ap[:, 0:ext], scalar=1.0,
                in1=kept.ap[:, 0:ext], op0=ALU.mult, op1=ALU.mult,
                accum_out=v["zc"].ap[:]),
                reads=[pmb, kept], writes=[pmb, v["zc"]]))
        else:
            dq.append(lambda: p.op("vector", lambda e: e.tensor_tensor(
                out=pmb.ap[:, qt * 128:ext], in0=pmb.ap[:, qt * 128:ext],
                in1=causal01.ap[:], op=ALU.mult),
                reads=[pmb, causal01], writes=[pmb]))
            dq.append(lambda: p.op("vector", lambda e: e.tensor_scalar(
                out=junkV.ap[:, 0:ext], in0=pmb.ap[:, 0:ext], scalar1=1.0,
                scalar2=0.0, op0=ALU.mult, op1=ALU.add,
                accum_out=v["zc"].ap[:]),
                reads=[pmb], writes=[junkV, v["zc"]]))
        dq.append(lambda: p.op("vector", lambda e: e.reciprocal(
            out=v["rzc"].ap[:], in_=v["zc"].ap[:]),
            reads=[v["zc"]], writes=[v["rzc"]]))
        dq.append(lambda: p.op("tensor", lambda e: e.matmul(
            psSimb.ap[0:1, 0:128], v["rzc"].ap[:], identf.ap[:],
            start=True, stop=True), reads=[v["rzc"], identf], writes=[psSimb]))
        dq.append(lambda: p.op("vector", lambda e: e.tensor_copy(
            out=zrow.ap[:], in_=psSimb.ap[0:1, 0:128]),
            reads=[psSimb], writes=[zrow]))
        dq.append(lambda: p.op("tensor", lambda e: e.matmul(
            psB2.ap[:, 0:128], onesr.ap[:], zrow.ap[:], start=True, stop=True),
            reads=[onesr, zrow], writes=[psB2]))
        dq.append(lambda: p.op("vector", lambda e: e.tensor_copy(
            out=bz.ap[:], in_=psB2.ap[:, 0:128]), reads=[psB2], writes=[bz]))
        pts_ = [po.next_tp() for _ in range(qt + 1)]

        def mk_tp(kk):
            pt = pts_[kk]
            return lambda: p.op("tensor", lambda e: e.transpose(
                pt.ap[:], pmb.ap[:, kk * 128:(kk + 1) * 128], po.ident.ap[:]),
                reads=[pmb, po.ident], writes=[pt])
        dq.append(mk_tp(0))
        for kk in range(qt + 1):
            pt = pts_[kk]
            pb = ptb[kk % 2]
            dq.append(lambda pt=pt, pb=pb: p.op(
                "vector", lambda e: e.tensor_copy(out=pb.ap[:], in_=pt.ap[:]),
                reads=[pt], writes=[pb]))
            if kk + 1 <= qt:
                dq.append(mk_tp(kk + 1))
            vc0 = kk * 256 + (j // 4) * 128
            dq.append(lambda pb=pb, kk=kk, vc0=vc0: p.op(
                "tensor", lambda e: e.matmul(
                    psO.ap[:, 0:128], v_all.ap[:, vc0:vc0 + 128], pb.ap[:],
                    start=(kk == 0), stop=(kk == qt)),
                reads=[v_all, pb], writes=[psO]))
        dq.append(lambda: p.op("vector", lambda e: e.tensor_tensor(
            out=ctxs.ap[:], in0=psO.ap[:, 0:128], in1=bz.ap[:], op=ALU.mult),
            reads=[psO, bz], writes=[ctxs]))
        dq.append(lambda: p.store(ctx_out[j, :, qt * 128:(qt + 1) * 128], ctxs,
                                  queue="sync"))
        return dq

    for j in range(NPAIR):
        pj = j % 2
        p.load(kh_p[pj], khf[j, :, :], queue="sync")
        if j % 4 == 0:
            p.load(kr_p[(j // 4) % 2], krf[j // 4, :, :], queue="scalar")
        p.load(qh_p[pj], qhd[j, :, :], src_buf=qhd_b[j], queue="sync")
        p.load(qr_p[pj], qrd[j, :, :], src_buf=qrd_b[j], queue="scalar")
        dq = []
        emit_ss(j, pj, 0)
        for qt in range(NT):
            if qt >= 6:
                emit_search(qt, dq)
            else:
                for f in dq:
                    f()
                dq = []
            dq = emit_post(j, qt)
            if qt + 1 < NT:
                emit_ss(j, pj, qt + 1)
        for f in dq:
            f()
    env.pop_scope()


def build_pre_l1a(env):
    po = Pools(env, n_pf=4, n_pt=2, ln=True, wstream=True, rope=True)
    p = env.p
    x_in = env.dram_in("x", [NTILE, 128, D])
    wk = env.dram_in("wk", [32, 32, 128, 128])
    wv = env.dram_in("wv", [32, 8, 128, 512])
    kb_in = env.dram_in("kb", [128, 32])
    vb_in = env.dram_in("vbrow", [128, D])
    cosk = env.dram_in("cosk", [128, TOK])
    sink = env.dram_in("sink", [128, TOK])
    xnT_out = env.dram_out("xnT", [32, 128, TOK], BF16)
    kT_out = env.dram_out("kT", [32, 128, TOK], BF16)
    v_out = env.dram_out("v", [NTILE, 128, D], BF16)

    xnT = [env.sbuf(f"xnT{c}", [128, TOK], BF16) for c in range(32)]
    emit_ln_tp_phase(env, po, x_in, xnT, xnT_out=xnT_out)
    for c in xnT:
        c.frozen = True
    cos_b = env.sbuf("cos_b", [128, TOK], F32)
    sin_b = env.sbuf("sin_b", [128, TOK], F32)
    p.load(cos_b, cosk)
    p.load(sin_b, sink)
    kb = env.sbuf("kb_b", [128, 32], F32)
    p.load(kb, kb_in)
    vb = env.sbuf("vb_b", [128, D], F32)
    p.load(vb, vb_in)
    for b in (cos_b, sin_b, kb, vb):
        b.frozen = True
    kc2 = [env.sbuf(f"kc{i}", [128, TOK], BF16) for i in range(2)]
    for oc in range(32):
        kc = kc2[oc % 2]
        emit_proj_feat(env, po, wk, xnT, [kc], evac="rope",
                       rope=(cos_b, sin_b), oc_base=oc,
                       bias=lambda _oc, oc=oc: (kb, kb.ap[:, oc:oc + 1]))
        p.store(kT_out[oc, :, :], kc)
    vt = [env.sbuf(f"vt{t}", [128, 512], BF16) for t in range(2)]

    def evac_v(cb, ti, ps):
        vtb = vt[ti % 2]
        p.op("vector", lambda e, ps=ps, cb=cb, vtb=vtb: e.scalar_tensor_tensor(
            out=vtb.ap[:], in0=ps.ap[:], scalar=1.0,
            in1=vb.ap[:, cb * 512:(cb + 1) * 512], op0=ALU.mult, op1=ALU.add),
            reads=[ps, vb], writes=[vtb])
        p.store(v_out[ti, :, cb * 512:(cb + 1) * 512], vtb)

    emit_proj_tok(env, po, wv, xnT, ncolblk=8, evac_fn=evac_v, npsum=4)


def build_attn_l1a(env):
    po = Pools(env, n_pf=8, n_pt=0, ln=False, wstream=False, rope=True)
    p = env.p
    xnf = env.dram_in("xnf", [32, 128, S], BF16)
    wq = env.dram_in("wq", [128, 32 * NPAIR * 128])
    qb_in = env.dram_in("qb", [128, NPAIR])
    cosq = env.dram_in("cosq", [128, S], BF16)
    sinq = env.dram_in("sinq", [128, S], BF16)
    kTf = env.dram_in("kTf", [NPAIR, 128, S], BF16)
    vf = env.dram_in("vf", [NPAIR, 128, NT * 128], BF16)
    mvar = env.dram_in("mvar", [128, 4 * 512])
    ctx_out = env.dram_out("ctx", [NPAIR, 128, S], BF16)

    mv = env.sbuf("mv_b", [128, 4 * 512], F32)
    p.load(mv, mvar)
    mv.frozen = True
    qT_own = [env.sbuf(f"qT{j}", [128, S], BF16) for j in range(NPAIR)]

    env.push_scope()
    wq_all = env.sbuf("wq_all", [128, 32 * NPAIR * 128], BF16)
    p.load(wq_all, wq, queue="gpsimd")
    qb = env.sbuf("qb_b", [128, NPAIR], F32)
    p.load(qb, qb_in)
    cos_b = env.sbuf("cosb", [128, S], BF16)
    sin_b = env.sbuf("sinb", [128, S], BF16)
    p.load(cos_b, cosq)
    p.load(sin_b, sinq)
    for b in (wq_all, qb, cos_b, sin_b):
        b.frozen = True

    def evac_rope(j, tb, ps, dst, cols):
        emit_rope_evac(env, po, ps, dst, cols, cos_b, sin_b,
                       bias=(qb, qb.ap[:, j:j + 1]))

    emit_qproj_pairs(env, po, wq_all, xnf, qT_own, evac_fn=evac_rope)
    for b in qT_own:
        b.frozen = True
    env.pop_scope()

    k_p = [env.sbuf(f"k_p{i}", [128, S], BF16) for i in range(2)]
    v_p = [env.sbuf(f"v_p{i}", [128, NT * 128], BF16) for i in range(2)]
    tmp2 = [env.sbuf(f"at_tmp{i}", [128, 512], F32) for i in range(2)]
    cur = [-1]

    def kget(j):
        if cur[0] != j:
            p.load(k_p[j % 2], kTf[j, :, :], queue="sync")
            p.load(v_p[j % 2], vf[j, :, :], queue="scalar")
            cur[0] = j
        return (k_p[j % 2], 0)

    def exp_evac(j, qblk, kt, psS, ptile):
        tmp = tmp2[kt % 2]
        r = kt - 4 * qblk
        if 0 <= r <= 3:
            p.op("vector", lambda e, psS=psS, r=r, tmp=tmp: e.scalar_tensor_tensor(
                out=tmp.ap[:], in0=mv.ap[:, r * 512:(r + 1) * 512], scalar=1.0,
                in1=psS.ap[:], op0=ALU.mult, op1=ALU.add),
                reads=[mv, psS], writes=[tmp])
            p.op("scalar", lambda e, ptile=ptile, tmp=tmp: e.activation(
                out=ptile.ap[:], in_=tmp.ap[:], func=AF.Exp, scale=1.0 / SQ128),
                reads=[tmp], writes=[ptile])
        else:
            p.op("scalar", lambda e, psS=psS, ptile=ptile: e.activation(
                out=ptile.ap[:], in_=psS.ap[:], func=AF.Exp, scale=1.0 / SQ128),
                reads=[psS], writes=[ptile])

    emit_attn_pairs(env, po, qT_own, kget, lambda j: (v_p[j % 2], 0), ctx_out,
                    exp_evac=exp_evac)


def build_ffn1(env):
    po = Pools(env, n_pf=4, n_pt=2, ln=True, wstream=False, rope=False)
    p = env.p
    x_in = env.dram_in("x", [NTILE, 128, D])
    up_in = env.dram_in("up", [128, 16 * 2 * 8 * 128])
    dn_in = env.dram_in("dn", [128, 16 * 8 * 256])
    x_out = env.dram_out("xo", [NTILE, 128, D])
    up_all = env.sbuf("up_all", [128, 16 * 2 * 8 * 128], BF16)
    dn_all = env.sbuf("dn_all", [128, 16 * 8 * 256], BF16)
    p.load(up_all, up_in, queue="gpsimd")
    p.load(dn_all, dn_in, queue="gpsimd")
    up_all.frozen = True
    dn_all.frozen = True
    xw = env.sbuf("f1_x", [128, D], F32)
    xnw = env.sbuf("f1_xn", [128, D], BF16)
    xnTt = env.sbuf("f1_xnT", [128, D], BF16)
    gT = env.sbuf("f1_gT", [128, 8 * 128], BF16)
    xob = env.sbuf("f1_xo", [128, 256], F32)
    psG, psD = po.pf[0], po.pf[1]
    for ti in range(NTILE):
        p.load(xw, x_in[ti, :, :], queue="sync")
        emit_ln(env, po, xw, xnw)
        for c in range(32):
            emit_tp128(env, po, xnw, xnw.ap[:, c * 128:(c + 1) * 128],
                       xnTt, xnTt.ap[:, c * 128:(c + 1) * 128])
        for h in range(16):
            for mc in range(8):
                for dc in range(2):
                    c0 = ((h * 2 + dc) * 8 + mc) * 128
                    p.op("tensor", lambda e, h=h, mc=mc, dc=dc, c0=c0: e.matmul(
                        psG.ap[:, 0:128], up_all.ap[:, c0:c0 + 128],
                        xnTt.ap[:, (2 * h + dc) * 128:(2 * h + dc + 1) * 128],
                        start=(dc == 0), stop=(dc == 1)),
                        reads=[up_all, xnTt], writes=[psG])
                p.op("scalar", lambda e, mc=mc: e.activation(
                    out=gT.ap[:, mc * 128:(mc + 1) * 128], in_=psG.ap[:, 0:128],
                    func=AF.Gelu), reads=[psG], writes=[gT])
            for mc in range(8):
                c0 = (h * 8 + mc) * 256
                p.op("tensor", lambda e, h=h, mc=mc, c0=c0: e.matmul(
                    psD.ap[:, 0:256], gT.ap[:, mc * 128:(mc + 1) * 128],
                    dn_all.ap[:, c0:c0 + 256],
                    start=(mc == 0), stop=(mc == 7)),
                    reads=[gT, dn_all], writes=[psD])
            p.op("vector", lambda e, h=h: e.scalar_tensor_tensor(
                out=xob.ap[:], in0=psD.ap[:, 0:256], scalar=1.0,
                in1=xw.ap[:, h * 256:(h + 1) * 256], op0=ALU.mult, op1=ALU.add),
                reads=[psD, xw], writes=[xob])
            p.store(x_out[ti, :, h * 256:(h + 1) * 256], xob)


def build_pre_l1b(env):
    po = Pools(env, n_pf=4, n_pt=2, ln=True, wstream=True, rope=True)
    p = env.p
    x_in = env.dram_in("x", [NTILE, 128, D])
    wcq = env.dram_in("wcq", [32, 12, 128, 128])
    wckv = env.dram_in("wckv", [32, 4, 128, 128])
    wkr = env.dram_in("wkr", [32, 1, 128, 64])
    cosy = env.dram_in("cosy", [64, TOK])
    siny = env.dram_in("siny", [64, TOK])
    xnT_out = env.dram_out("xnT", [32, 128, TOK], BF16)
    cq_out = env.dram_out("cqT", [12, 128, TOK], BF16)
    ckv_out = env.dram_out("ckvT", [4, 128, TOK], BF16)
    kr_out = env.dram_out("krT", [64, TOK], BF16)

    xnT = [env.sbuf(f"xnT{c}", [128, TOK], BF16) for c in range(32)]
    emit_ln_tp_phase(env, po, x_in, xnT, xnT_out=xnT_out)
    for c in xnT:
        c.frozen = True
    cqT = [env.sbuf(f"cqT{i}", [128, TOK], BF16) for i in range(12)]
    emit_proj_feat(env, po, wcq, xnT, cqT, out_store=cq_out)
    ckvT = [env.sbuf(f"ckvT{i}", [128, TOK], BF16) for i in range(4)]
    emit_proj_feat(env, po, wckv, xnT, ckvT, out_store=ckv_out)
    cos_b = env.sbuf("cosy_b", [64, TOK], F32)
    sin_b = env.sbuf("siny_b", [64, TOK], F32)
    p.load(cos_b, cosy)
    p.load(sin_b, siny)
    cos_b.frozen = True
    sin_b.frozen = True
    krc = env.sbuf("krc", [64, TOK], BF16)
    ps = po.pf[0]
    for tb in range(TOK // 512):
        for dm in range(32):
            wb = po.next_wb()
            p.load(wb, wkr[dm, 0, :, 0:64], queue="gpsimd", dst_ap=wb.ap[:, 0:64])
            p.op("tensor", lambda e, dm=dm, tb=tb, wb=wb: e.matmul(
                ps.ap[0:64, :], wb.ap[:, 0:64],
                xnT[dm].ap[:, tb * 512:(tb + 1) * 512],
                start=(dm == 0), stop=(dm == 31)),
                reads=[wb, xnT[dm]], writes=[ps])
        emit_rope_evac64(env, po, ps, krc, slice(tb * 512, (tb + 1) * 512),
                         cos_b, sin_b, 0)
    p.store(kr_out, krc)


def emit_rope_evac64(env, po, ps, ob, cols, cos_b, sin_b, row0, bias=None):
    p = env.p
    t1, t2 = po.rope_t1, po.rope_t2
    src = ps
    sr = row0
    if bias is not None:
        bbuf, bap = bias
        tb = po.rope_tb
        p.op("scalar", lambda e: e.activation(out=tb.ap[sr:sr + 64, 0:512],
                                              in_=ps.ap[sr:sr + 64, 0:512],
                                              func=AF.Identity, bias=bap),
             reads=[ps, bbuf], writes=[tb])
        src = tb
    ncol = cols.stop - cols.start
    sa = src.ap[sr:sr + 32, 0:ncol]
    sb_ = src.ap[sr + 32:sr + 64, 0:ncol]
    ca = cos_b.ap[row0:row0 + 32, cols]
    sn = sin_b.ap[row0 + 32:row0 + 64, cols]
    cb = cos_b.ap[row0 + 32:row0 + 64, cols]
    snb = sin_b.ap[row0:row0 + 32, cols]
    p.op("vector", lambda e: e.tensor_tensor(out=t1.ap[0:32, 0:ncol], in0=sa, in1=ca, op=ALU.mult),
         reads=[src, cos_b], writes=[t1])
    p.op("vector", lambda e: e.tensor_tensor(out=t2.ap[0:32, 0:ncol], in0=sb_, in1=sn, op=ALU.mult),
         reads=[src, sin_b], writes=[t2])
    p.op("vector", lambda e: e.tensor_tensor(out=ob.ap[row0:row0 + 32, cols], in0=t1.ap[0:32, 0:ncol],
                                             in1=t2.ap[0:32, 0:ncol], op=ALU.subtract),
         reads=[t1, t2], writes=[ob])
    p.op("vector", lambda e: e.tensor_tensor(out=t1.ap[0:32, 0:ncol], in0=sb_, in1=cb, op=ALU.mult),
         reads=[src, cos_b], writes=[t1])
    p.op("vector", lambda e: e.tensor_tensor(out=t2.ap[0:32, 0:ncol], in0=sa, in1=snb, op=ALU.mult),
         reads=[src, sin_b], writes=[t2])
    p.op("vector", lambda e: e.tensor_tensor(out=ob.ap[row0 + 32:row0 + 64, cols], in0=t1.ap[0:32, 0:ncol],
                                             in1=t2.ap[0:32, 0:ncol], op=ALU.add),
         reads=[t1, t2], writes=[ob])


def build_attn_l1b(env):
    po = Pools(env, n_pf=7, n_pt=0, ln=False, wstream=False, rope=True)
    p = env.p
    cqf = env.dram_in("cqf", [12, 128, S], BF16)
    ckvf = env.dram_in("ckvf", [4, 128, S], BF16)
    krf = env.dram_in("krf", [64, S], BF16)
    wqn = env.dram_in("wqn", [128, 12 * NPAIR * 128])
    wqr = env.dram_in("wqr", [128, 12 * NPAIR * 64])
    wuk = env.dram_in("wuk", [128, 4 * NPAIR * 128])
    wuv = env.dram_in("wuv", [128, 4 * NPAIR * 128])
    cosy = env.dram_in("cosy", [64, S])
    siny = env.dram_in("siny", [64, S])
    mvar = env.dram_in("mvar", [128, 4 * 512])
    ctx_out = env.dram_out("ctx", [NPAIR, 128, S], BF16)

    ckvT = [env.sbuf(f"ckvT{i}", [128, S], BF16) for i in range(4)]
    for i in range(4):
        p.load(ckvT[i], ckvf[i, :, :])
        ckvT[i].frozen = True
    krT = env.sbuf("krT_b", [64, S], BF16)
    p.load(krT, krf)
    krT.frozen = True
    w_qn = env.sbuf("wqn_b", [128, 12 * NPAIR * 128], BF16)
    w_qr = env.sbuf("wqr_b", [128, 12 * NPAIR * 64], BF16)
    w_uk = env.sbuf("wuk_b", [128, 4 * NPAIR * 128], BF16)
    w_uv = env.sbuf("wuv_b", [128, 4 * NPAIR * 128], BF16)
    p.load(w_qn, wqn, queue="gpsimd")
    p.load(w_qr, wqr, queue="gpsimd")
    p.load(w_uk, wuk, queue="gpsimd")
    p.load(w_uv, wuv, queue="gpsimd")
    cos_b = env.sbuf("cosy_b", [64, S], F32)
    sin_b = env.sbuf("siny_b", [64, S], F32)
    p.load(cos_b, cosy)
    p.load(sin_b, siny)
    mv = env.sbuf("mv_b", [128, 4 * 512], F32)
    p.load(mv, mvar)
    for b in (w_qn, w_qr, w_uk, w_uv, cos_b, sin_b, mv):
        b.frozen = True

    cql = [env.sbuf(f"cq_xl{i}", [128, 6 * 512], BF16) for i in range(2)]
    cqh = [env.sbuf("cq_xh0", [128, 6 * 512], BF16)] * 2
    qnT = env.sbuf("qnT", [128, S], BF16)
    qroT = env.sbuf("qroT", [64, S], BF16)
    k_p = [env.sbuf(f"k_p{i}", [128, S], BF16) for i in range(2)]
    v_p = [env.sbuf("v_p0", [128, NT * 128], BF16)]
    tmp2 = [env.sbuf("at_tmp0", [128, 512], F32)] * 2
    SQ192 = float(np.sqrt(192.0))

    for j in range(NPAIR):
        kp, vp = k_p[j % 2], v_p[0]
        for tb in range(8):
            ps = po.pf[2 + tb % 2]
            for i in range(4):
                c0 = (i * NPAIR + j) * 128
                p.op("tensor", lambda e, i=i, c0=c0, tb=tb, ps=ps: e.matmul(
                    ps.ap[:], w_uk.ap[:, c0:c0 + 128],
                    ckvT[i].ap[:, tb * 512:(tb + 1) * 512],
                    start=(i == 0), stop=(i == 3)),
                    reads=[w_uk, ckvT[i]], writes=[ps])
            p.op("scalar", lambda e, tb=tb, ps=ps, kp=kp: e.copy(
                out=kp.ap[:, tb * 512:(tb + 1) * 512], in_=ps.ap[:]),
                reads=[ps], writes=[kp])
        for kk in range(NT):
            ps = po.pf[2 + kk % 2]
            for i in range(4):
                c0 = (i * NPAIR + j) * 128
                p.op("tensor", lambda e, i=i, c0=c0, kk=kk, ps=ps: e.matmul(
                    ps.ap[:, 0:128],
                    ckvT[i].ap[:, kk * 128:(kk + 1) * 128],
                    w_uv.ap[:, c0:c0 + 128],
                    start=(i == 0), stop=(i == 3)),
                    reads=[w_uv, ckvT[i]], writes=[ps])
            p.op("scalar", lambda e, kk=kk, ps=ps, vp=vp: e.copy(
                out=vp.ap[:, kk * 128:(kk + 1) * 128], in_=ps.ap[:, 0:128]),
                reads=[ps], writes=[vp])
        for tb in range(8):
            par = tb % 2
            cl, ch_ = cql[par], cqh[par]
            p.load(cl, cqf[0:6, :, tb * 512:(tb + 1) * 512]
                   .rearrange("a p m -> p a m"), queue="sync",
                   dst_ap=cl.ap[:].rearrange("p (a m) -> p a m", a=6))
            p.load(ch_, cqf[6:12, :, tb * 512:(tb + 1) * 512]
                   .rearrange("a p m -> p a m"), queue="scalar",
                   dst_ap=ch_.ap[:].rearrange("p (a m) -> p a m", a=6))
            psn, psr = po.pf[2], po.pf[3]
            for dm in range(12):
                src = cl if dm < 6 else ch_
                off = dm % 6
                c0 = (dm * NPAIR + j) * 128
                p.op("tensor", lambda e, dm=dm, c0=c0, src=src, off=off, psn=psn: e.matmul(
                    psn.ap[:], w_qn.ap[:, c0:c0 + 128],
                    src.ap[:, off * 512:(off + 1) * 512],
                    start=(dm == 0), stop=(dm == 11)),
                    reads=[w_qn, src], writes=[psn])
                c1 = (dm * NPAIR + j) * 64
                p.op("tensor", lambda e, dm=dm, c1=c1, src=src, off=off, psr=psr: e.matmul(
                    psr.ap[0:64, :], w_qr.ap[:, c1:c1 + 64],
                    src.ap[:, off * 512:(off + 1) * 512],
                    start=(dm == 0), stop=(dm == 11)),
                    reads=[w_qr, src], writes=[psr])
            cols = slice(tb * 512, (tb + 1) * 512)
            p.op("scalar", lambda e, cols=cols, psn=psn: e.copy(
                out=qnT.ap[:, cols], in_=psn.ap[:]),
                reads=[psn], writes=[qnT])
            emit_rope_evac64(env, po, psr, qroT, cols, cos_b, sin_b, 0)
        def score_extra(jj, qblk, kt, psS):
            p.op("tensor", lambda e, qblk=qblk, kt=kt, psS=psS: e.matmul(
                psS.ap[:], krT.ap[:, kt * 128:(kt + 1) * 128],
                qroT.ap[:, qblk * 512:(qblk + 1) * 512],
                start=False, stop=True),
                reads=[krT, qroT], writes=[psS])

        def exp_evac(jj, qblk, kt, psS, ptile):
            tmp = tmp2[kt % 2]
            r = kt - 4 * qblk
            if 0 <= r <= 3:
                p.op("vector", lambda e, psS=psS, r=r, tmp=tmp: e.scalar_tensor_tensor(
                    out=tmp.ap[:], in0=mv.ap[:, r * 512:(r + 1) * 512],
                    scalar=1.0, in1=psS.ap[:], op0=ALU.mult, op1=ALU.add),
                    reads=[mv, psS], writes=[tmp])
                p.op("scalar", lambda e, ptile=ptile, tmp=tmp: e.activation(
                    out=ptile.ap[:], in_=tmp.ap[:], func=AF.Exp, scale=1.0 / SQ192),
                    reads=[tmp], writes=[ptile])
            else:
                p.op("scalar", lambda e, psS=psS, ptile=ptile: e.activation(
                    out=ptile.ap[:], in_=psS.ap[:], func=AF.Exp, scale=1.0 / SQ192),
                    reads=[psS], writes=[ptile])

        emit_attn_pairs_single(env, po, j, qnT, kp, vp, ctx_out,
                               exp_evac=exp_evac, score_extra=score_extra)


def emit_attn_pairs_single(env, po, j, qT, kbuf, vbuf, ctx_out,
                           exp_evac, score_extra=None, seq=S):
    p = env.p
    nqb = seq // 512
    if not hasattr(env, "_aps_bufs"):
        pts = [env.sbuf(f"as_p{i}", [128, 512], BF16) for i in range(4)]
        zrow = env.sbuf("as_z", [1, 512], F32)
        onesr = env.sbuf("as_or", [1, 128], F32)
        bz = env.sbuf("as_bz", [128, 512], F32)
        ctxs = env.sbuf("as_ctx", [128, 512], BF16)
        p.op("vector", lambda e: e.memset(onesr.ap[:], 1.0), writes=[onesr])
        onesr.frozen = True
        env._aps_bufs = (pts, zrow, onesr, bz, ctxs)
    pts, zrow, onesr, bz, ctxs = env._aps_bufs
    psZ, psO, psB = po.pf[4], po.pf[5], po.pf[6]
    for qblk in range(nqb):
        nkt = 4 * qblk + 4

        def zpv(kt, nkt=nkt):
            ptile = pts[kt % 4]
            p.op("tensor", lambda e, kt=kt, ptile=ptile: e.matmul(
                psZ.ap[0:1, :], po.ones1.ap[:], ptile.ap[:],
                start=(kt == 0), stop=(kt == nkt - 1)),
                reads=[po.ones1, ptile], writes=[psZ])
            p.op("tensor", lambda e, kt=kt, ptile=ptile: e.matmul(
                psO.ap[:], vbuf.ap[:, kt * 128:(kt + 1) * 128], ptile.ap[:],
                start=(kt == 0), stop=(kt == nkt - 1)),
                reads=[vbuf, ptile], writes=[psO])

        for kt in range(nkt):
            psS = po.pf[kt % 2]
            def mm_s(e, qblk=qblk, kt=kt, psS=psS):
                return e.matmul(psS.ap[:], kbuf.ap[:, kt * 128:(kt + 1) * 128],
                                qT.ap[:, qblk * 512:(qblk + 1) * 512],
                                start=True, stop=(score_extra is None))
            p.op("tensor", mm_s, reads=[kbuf, qT], writes=[psS])
            if score_extra is not None:
                score_extra(j, qblk, kt, psS)
            exp_evac(j, qblk, kt, psS, pts[kt % 4])
            if kt >= 1:
                zpv(kt - 1)
        zpv(nkt - 1)
        p.op("vector", lambda e: e.tensor_copy(out=zrow.ap[:], in_=psZ.ap[0:1, :]),
             reads=[psZ], writes=[zrow])
        p.op("vector", lambda e: e.reciprocal(out=zrow.ap[:], in_=zrow.ap[:]),
             reads=[zrow], writes=[zrow])
        p.op("tensor", lambda e: e.matmul(psB.ap[:], onesr.ap[:], zrow.ap[:],
                                          start=True, stop=True),
             reads=[onesr, zrow], writes=[psB])
        p.op("vector", lambda e: e.tensor_copy(out=bz.ap[:], in_=psB.ap[:]),
             reads=[psB], writes=[bz])
        p.op("vector", lambda e: e.tensor_tensor(out=ctxs.ap[:], in0=psO.ap[:],
                                                 in1=bz.ap[:], op=ALU.mult),
             reads=[psO, bz], writes=[ctxs])
        p.store(ctx_out[j, :, qblk * 512:(qblk + 1) * 512], ctxs)


# version 07
# speedup vs baseline: 1.5822x; 1.1517x over previous
import numpy as np
import ml_dtypes
import concourse.bass as bass
import concourse.mybir as mybir
from concourse.bass_utils import run_bass_kernel_spmd
from contextlib import ExitStack

F32 = mybir.dt.float32
BF16 = mybir.dt.bfloat16
AF = mybir.ActivationFunctionType
ALU = mybir.AluOpType
AX = mybir.AxisListType

NCORES = 8
B = 2
S = 4096
D = 4096
NT = S // 128
TOK = 1024
NTILE = TOK // 128
BF = ml_dtypes.bfloat16
NPAIR = 8

EPS = 1e-5


class Buf:
    __slots__ = ("ap", "wtok", "rtoks", "frozen")

    def __init__(self, ap):
        self.ap = ap
        self.wtok = None
        self.rtoks = []
        self.frozen = False


class Prog:
    ENGS = ("sync", "vector", "scalar", "tensor", "gpsimd")

    NDMA = 8

    def __init__(self, nc):
        self.nc = nc
        self.qs = {e: [] for e in self.ENGS}
        self.sem_names = (["v", "s", "t", "g", "y"]
                          + [f"d{q}{i}" for q in "SPA" for i in range(self.NDMA)])
        self.sem_cnt = {n: 0 for n in self.sem_names}
        self.sems = {}
        self.wait_cache = {e: {} for e in self.ENGS}
        self.eng_sem = {"sync": "y", "gpsimd": "g", "vector": "v",
                        "scalar": "s", "tensor": "t"}
        self.dma_sem = {"sync": "S", "gpsimd": "P", "scalar": "A"}
        self.dma_idx = {"S": 0, "P": 0, "A": 0}
        self.barrier_toks = None

    def _emit_wait(self, eng_name, tok):
        if tok is None:
            return
        sem, val = tok
        if eng_name == "tensor" and sem == "t":
            return
        wc = self.wait_cache[eng_name]
        if wc.get(sem, 0) >= val:
            return
        wc[sem] = val
        self.qs[eng_name].append(("wait", sem, val))

    def _next(self, sem, step):
        self.sem_cnt[sem] += step
        return (sem, self.sem_cnt[sem])

    def op(self, eng_name, build, reads=(), writes=(), inc=True, dma=False,
           extra_waits=()):
        reads = [b for b in reads if b is not None]
        writes = [b for b in writes if b is not None]
        for b in reads:
            self._emit_wait(eng_name, b.wtok)
        for b in writes:
            self._emit_wait(eng_name, b.wtok)
            for t in b.rtoks:
                self._emit_wait(eng_name, t)
        for t in extra_waits:
            self._emit_wait(eng_name, t)
        if self.barrier_toks is not None:
            for b in writes:
                if b.wtok is None:
                    for t in self.barrier_toks:
                        self._emit_wait(eng_name, t)
                    break
        tok = None
        if inc:
            if dma:
                q = self.dma_sem[eng_name]
                i = self.dma_idx[q]
                self.dma_idx[q] = (i + 1) % self.NDMA
                tok = self._next(f"d{q}{i}", 16)
            else:
                tok = self._next(self.eng_sem[eng_name], 1)
        self.qs[eng_name].append(("inst", build, tok))
        if tok is not None:
            for b in reads:
                if not b.frozen:
                    b.rtoks.append(tok)
            for b in writes:
                b.rtoks = []
                b.wtok = tok
        return tok

    def load(self, dst, src_ap, queue="sync", dst_ap=None, src_buf=None):
        ap = dst.ap[:] if dst_ap is None else dst_ap
        return self.op(queue, lambda e: e.dma_start(out=ap, in_=src_ap),
                       reads=[src_buf] if src_buf is not None else (),
                       writes=[dst], dma=True)

    def store(self, dst_ap, src, queue="scalar", src_ap=None, dst_buf=None):
        ap = src.ap[:] if src_ap is None else src_ap
        return self.op(queue, lambda e: e.dma_start(out=dst_ap, in_=ap),
                       reads=[src],
                       writes=[dst_buf] if dst_buf is not None else (), dma=True)

    def all_tokens(self):
        return [(n, self.sem_cnt[n]) for n in self.sem_names if self.sem_cnt[n] > 0]

    def final_wait(self):
        toks = self.all_tokens()
        self.op("sync", lambda e: e.nop(), extra_waits=toks, inc=False)

    def finalize(self, block):
        def mk(eng_name):
            def body(eng):
                for item in self.qs[eng_name]:
                    if item[0] == "wait":
                        eng.wait_ge(self.sems[item[1]], item[2])
                    else:
                        inst = item[1](eng)
                        tok = item[2]
                        if tok is not None:
                            step = 16 if tok[0].startswith("d") else 1
                            inst.then_inc(self.sems[tok[0]], step)
            return body
        for e, deco in (("sync", block.sync), ("vector", block.vector),
                        ("scalar", block.scalar), ("tensor", block.tensor),
                        ("gpsimd", block.gpsimd)):
            if self.qs[e]:
                deco(mk(e))


class Env:

    def __init__(self, name):
        self.nc = bass.Bass("TRN2", target_bir_lowering=False)
        self.p = Prog(self.nc)
        self.ctx = ExitStack()
        self.scopes = []
        self._ins = {}
        self._outs = {}
        self.name = name

    def push_scope(self):
        self.scopes.append(ExitStack())

    def pop_scope(self):
        self.scopes.pop().close()
        self.p.barrier_toks = list(self.p.all_tokens())

    def dram_in(self, name, shape, dtype=F32):
        h = self.nc.dram_tensor(name, list(shape), dtype, kind="ExternalInput")
        self._ins[name] = h
        return h.ap()

    def dram_tmp(self, name, shape, dtype=F32):
        h = self.nc.dram_tensor(name, list(shape), dtype, kind="Internal")
        return h.ap()

    def dram_out(self, name, shape, dtype=F32):
        h = self.nc.dram_tensor(name, list(shape), dtype, kind="ExternalOutput")
        self._outs[name] = h
        return h.ap()

    def sbuf(self, name, shape, dtype):
        stack = self.scopes[-1] if self.scopes else self.ctx
        h = stack.enter_context(self.nc.sbuf_tensor(name, list(shape), dtype))
        return Buf(h)

    def psum(self, name, shape, dtype=F32):
        stack = self.scopes[-1] if self.scopes else self.ctx
        h = stack.enter_context(self.nc.psum_tensor(name, list(shape), dtype))
        return Buf(h)

    def build(self, emit_fn):
        with self.ctx:
            for n in self.p.sem_names:
                self.p.sems[n] = self.ctx.enter_context(self.nc.semaphore(n))
            block = self.ctx.enter_context(self.nc.Block())
            emit_fn(self)
            self.p.final_wait()
            self.p.finalize(block)
        return self.nc


def block_lhsT(w, kc=128, mc=128):
    K, M = w.shape
    return np.ascontiguousarray(
        w.reshape(K // kc, kc, M // mc, mc).transpose(0, 2, 1, 3))


def block_rhs(w, kc=128, nb=512):
    K, N = w.shape
    return np.ascontiguousarray(
        w.reshape(K // kc, kc, N // nb, nb).transpose(0, 2, 1, 3))


def rope_tables(positions, dim, base=10000.0):
    half = dim // 2
    inv = 1.0 / base ** (np.arange(half, dtype=np.float64) / half)
    ang = inv[:, None] * np.asarray(positions, np.float64)[None, :]
    cos = np.cos(ang)
    sin = np.sin(ang)
    cos2 = np.concatenate([cos, cos], 0).astype(np.float32)
    sin2 = np.concatenate([sin, sin], 0).astype(np.float32)
    return cos2, sin2


def yarn_tables(positions):
    DR = 64
    BASE, FACTOR, BFAST, BSLOW, OLD_CTX = 10000.0, 40.0, 32.0, 1.0, 4096.0
    freqs = 1.0 / BASE ** (np.arange(0, DR, 2, dtype=np.float64) / DR)
    wavelengths = 2.0 * np.pi / freqs
    ramp = np.clip((wavelengths / OLD_CTX - BSLOW) / (BFAST - BSLOW), 0.0, 1.0)
    scale = 1.0 - ramp + ramp * FACTOR
    inv = freqs / scale
    ang = inv[:, None] * np.asarray(positions, np.float64)[None, :]
    cos = np.cos(ang)
    sin = np.sin(ang)
    cos2 = np.concatenate([cos, cos], 0).astype(np.float32)
    sin2 = np.concatenate([sin, sin], 0).astype(np.float32)
    return cos2, sin2


def alibi_slopes(n_heads=32):
    import math
    closest = 2 ** int(math.floor(math.log2(n_heads)))
    base = 2.0 ** (-(2.0 ** (-(math.log2(closest) - 3))))
    slopes = [base ** i for i in range(1, closest + 1)]
    if closest != n_heads:
        eb = 2.0 ** (-(2.0 ** (-(math.log2(2 * closest) - 3))))
        n_extra = min(closest, n_heads - closest)
        slopes += [eb ** i for i in range(1, 2 * n_extra + 1, 2)]
    return np.array(slopes, np.float32)


class Pools:

    def __init__(self, env, n_pf=6, n_pt=2, ln=True, wstream=True, rope=True):
        self.env = env
        p = env.p
        self.pf = [env.psum(f"psf{i}", [128, 512], F32) for i in range(n_pf)]
        self.pt = [env.psum(f"pst{i}", [128, 128], BF16) for i in range(n_pt)]
        self._tp_i = 0
        self.ones1 = env.sbuf("sc_ones1", [128, 1], BF16)
        env.p.op("vector", lambda e: e.memset(self.ones1.ap[:], 1.0), writes=[self.ones1])
        if ln or n_pt:
            self.ident = env.sbuf("sc_ident", [128, 128], BF16)
            ident_in = env.dram_in("c_ident", [128, 128], BF16)
            env.p.load(self.ident, ident_in)
        if ln:
            self.mu = env.sbuf("sc_mu", [128, 1], F32)
            self.sq = env.sbuf("sc_sq", [128, 1], F32)
            self.rstd = env.sbuf("sc_rstd", [128, 1], F32)
            self.nmur = env.sbuf("sc_nmur", [128, 1], F32)
            self.dump = env.sbuf("sc_dump", [128, 4096], F32)
            self.eps = env.sbuf("sc_eps", [128, 1], F32)
            env.p.op("vector", lambda e: e.memset(self.eps.ap[:], EPS), writes=[self.eps])
        if wstream:
            self.wb = [env.sbuf(f"sc_wb{i}", [128, 512], BF16) for i in range(4)]
        self._wb_i = 0
        if rope:
            self.rope_t1 = env.sbuf("sc_ropet1", [128, 512], BF16)
            self.rope_t2 = env.sbuf("sc_ropet2", [128, 512], BF16)
            self.rope_tb = env.sbuf("sc_ropetb", [128, 512], F32)

    def next_wb(self):
        b = self.wb[self._wb_i % len(self.wb)]
        self._wb_i += 1
        return b

    def next_tp(self):
        b = self.pt[self._tp_i % len(self.pt)]
        self._tp_i += 1
        return b


def emit_ln(env, po, x_buf, xn_buf):
    p = env.p
    inv = 1.0 / D
    mu, sq, rstd, nmur, dump, eps = po.mu, po.sq, po.rstd, po.nmur, po.dump, po.eps
    p.op("vector", lambda e: e.reduce_sum(mu.ap[:], x_buf.ap[:], axis=AX.X),
         reads=[x_buf], writes=[mu])
    p.op("scalar", lambda e: e.activation(out=dump.ap[:], in_=x_buf.ap[:],
                                          func=AF.Square, accum_out=sq.ap[:]),
         reads=[x_buf], writes=[sq, dump])
    p.op("vector", lambda e: e.tensor_scalar_mul(mu.ap[:], mu.ap[:], inv),
         reads=[mu], writes=[mu])
    p.op("vector", lambda e: e.tensor_scalar_mul(sq.ap[:], sq.ap[:], inv),
         reads=[sq], writes=[sq])
    p.op("vector", lambda e: e.tensor_mul(rstd.ap[:], mu.ap[:], mu.ap[:]),
         reads=[mu], writes=[rstd])
    p.op("vector", lambda e: e.tensor_sub(rstd.ap[:], sq.ap[:], rstd.ap[:]),
         reads=[sq, rstd], writes=[rstd])
    p.op("scalar", lambda e: e.activation(out=rstd.ap[:], in_=rstd.ap[:],
                                          func=AF.Ln, bias=eps.ap[:]),
         reads=[rstd, eps], writes=[rstd])
    p.op("scalar", lambda e: e.activation(out=rstd.ap[:], in_=rstd.ap[:],
                                          func=AF.Exp, scale=-0.5),
         reads=[rstd], writes=[rstd])
    p.op("scalar", lambda e: e.mul(nmur.ap[:], mu.ap[:], -1.0),
         reads=[mu], writes=[nmur])
    p.op("scalar", lambda e: e.mul(nmur.ap[:], nmur.ap[:], rstd.ap[:]),
         reads=[nmur, rstd], writes=[nmur])
    p.op("scalar", lambda e: e.activation(out=xn_buf.ap[:], in_=x_buf.ap[:],
                                          func=AF.Identity, bias=nmur.ap[:],
                                          scale=rstd.ap[:]),
         reads=[x_buf, nmur, rstd], writes=[xn_buf])


def emit_tp128(env, po, src_buf, src_ap, dst_buf, dst_ap):
    p = env.p
    ps = po.next_tp()
    p.op("tensor", lambda e: e.transpose(ps.ap[:], src_ap, po.ident.ap[:]),
         reads=[src_buf, po.ident], writes=[ps])
    p.op("vector", lambda e: e.tensor_copy(out=dst_ap, in_=ps.ap[:]),
         reads=[ps], writes=[dst_buf])


def emit_ln_tp_phase(env, po, x_dram, xnT, xnT_out=None, x_tiles_keep=None):
    p = env.p
    xw = env.sbuf("lt_x", [128, D], F32)
    xnw = env.sbuf("lt_xn", [128, D], BF16)
    for tile in range(NTILE):
        xbuf = x_tiles_keep[tile] if x_tiles_keep is not None else xw
        p.load(xbuf, x_dram[tile, :, :])
        emit_ln(env, po, xbuf, xnw)
        for c in range(D // 128):
            emit_tp128(env, po, xnw, xnw.ap[:, c * 128:(c + 1) * 128],
                       xnT[c], xnT[c].ap[:, tile * 128:(tile + 1) * 128])
    if xnT_out is not None:
        for c in range(D // 128):
            p.store(xnT_out[c, :, :], xnT[c])


def emit_proj_feat(env, po, w_dram, xnT, out_chunks, ndm=32, evac="copy",
                   bias=None, rope=None, out_store=None, scale=1.0, oc_base=0):
    p = env.p
    noc = len(out_chunks)
    ntb = TOK // 512
    for oc in range(noc):
        pss = [po.pf[oc % 2 * 2], po.pf[oc % 2 * 2 + 1]][:ntb] if ntb <= 2 else None
        for dm in range(ndm):
            wb = po.next_wb()
            p.load(wb, w_dram[dm, oc_base + oc, :, :], queue="gpsimd",
                   dst_ap=wb.ap[:, 0:128])
            for tb in range(ntb):
                def mk(e, wb=wb, dm=dm, tb=tb, oc=oc, ps=pss[tb]):
                    return e.matmul(ps.ap[:], wb.ap[:, 0:128],
                                    xnT[dm].ap[:, tb * 512:(tb + 1) * 512],
                                    start=(dm == 0), stop=(dm == ndm - 1))
                p.op("tensor", mk, reads=[wb, xnT[dm]], writes=[pss[tb]])
        ob = out_chunks[oc]
        for tb in range(ntb):
            ps = pss[tb]
            cols = slice(tb * 512, (tb + 1) * 512)
            if evac == "copy":
                if bias is not None:
                    bbuf, bap = bias(oc)
                    p.op("scalar", lambda e, ps=ps, ob=ob, cols=cols, bap=bap:
                         e.activation(out=ob.ap[:, cols], in_=ps.ap[:],
                                      func=AF.Identity, bias=bap, scale=scale),
                         reads=[ps, bbuf], writes=[ob])
                elif scale != 1.0:
                    p.op("scalar", lambda e, ps=ps, ob=ob, cols=cols:
                         e.activation(out=ob.ap[:, cols], in_=ps.ap[:],
                                      func=AF.Identity, scale=scale),
                         reads=[ps], writes=[ob])
                else:
                    p.op("scalar", lambda e, ps=ps, ob=ob, cols=cols:
                         e.copy(out=ob.ap[:, cols], in_=ps.ap[:]),
                         reads=[ps], writes=[ob])
            elif evac == "rope":
                cos_b, sin_b = rope
                emit_rope_evac(env, po, ps, ob, cols, cos_b, sin_b, half=64,
                               bias=None if bias is None else bias(oc))
        if out_store is not None:
            p.store(out_store[oc, :, :], ob)


def emit_rope_evac(env, po, ps, ob, cols, cos_b, sin_b, half=64, bias=None):
    p = env.p
    t1, t2 = po.rope_t1, po.rope_t2
    src = ps
    if bias is not None:
        bbuf, bap = bias
        tb = po.rope_tb
        p.op("scalar", lambda e: e.activation(out=tb.ap[:, 0:512], in_=ps.ap[:],
                                              func=AF.Identity, bias=bap),
             reads=[ps, bbuf], writes=[tb])
        src = tb
    h = half
    sa = src.ap[0:h, 0:512]
    sb_ = src.ap[h:2 * h, 0:512]
    ca = cos_b.ap[0:h, cols]
    sn = sin_b.ap[h:2 * h, cols]
    cb = cos_b.ap[h:2 * h, cols]
    snb = sin_b.ap[0:h, cols]
    p.op("vector", lambda e: e.tensor_tensor(out=t1.ap[0:h, 0:512], in0=sa, in1=ca, op=ALU.mult),
         reads=[src, cos_b], writes=[t1])
    p.op("vector", lambda e: e.tensor_tensor(out=t2.ap[0:h, 0:512], in0=sb_, in1=sn, op=ALU.mult),
         reads=[src, sin_b], writes=[t2])
    p.op("vector", lambda e: e.tensor_tensor(out=ob.ap[0:h, cols], in0=t1.ap[0:h, 0:512],
                                             in1=t2.ap[0:h, 0:512], op=ALU.subtract),
         reads=[t1, t2], writes=[ob])
    p.op("vector", lambda e: e.tensor_tensor(out=t1.ap[0:h, 0:512], in0=sb_, in1=cb, op=ALU.mult),
         reads=[src, cos_b], writes=[t1])
    p.op("vector", lambda e: e.tensor_tensor(out=t2.ap[0:h, 0:512], in0=sa, in1=snb, op=ALU.mult),
         reads=[src, sin_b], writes=[t2])
    p.op("vector", lambda e: e.tensor_tensor(out=ob.ap[h:2 * h, cols], in0=t1.ap[0:h, 0:512],
                                             in1=t2.ap[0:h, 0:512], op=ALU.add),
         reads=[t1, t2], writes=[ob])


def emit_proj_tok(env, po, w_dram, inT, ncolblk, ndm=32, colw=512,
                  out_tiles=None, out_store=None, evac_fn=None, npsum=6,
                  inT_cols=None):
    p = env.p
    if inT_cols is None:
        def inT_cols(dm, ti):
            return inT[dm].ap[:, ti * 128:(ti + 1) * 128]
    for cb in range(ncolblk):
        for grp in range(0, NTILE, npsum):
            tiles = list(range(grp, min(grp + npsum, NTILE)))
            for dm in range(ndm):
                wb = po.next_wb()
                p.load(wb, w_dram[dm, cb, :, 0:colw], queue="gpsimd",
                       dst_ap=wb.ap[:, 0:colw])
                for ti in tiles:
                    ps = po.pf[ti % npsum]
                    def mk(e, wb=wb, dm=dm, ti=ti, ps=ps):
                        return e.matmul(ps.ap[:, 0:colw], inT_cols(dm, ti),
                                        wb.ap[:, 0:colw],
                                        start=(dm == 0), stop=(dm == ndm - 1))
                    p.op("tensor", mk, reads=[wb, inT[dm]], writes=[ps])
            for ti in tiles:
                ps = po.pf[ti % npsum]
                if evac_fn is not None:
                    evac_fn(cb, ti, ps)
                else:
                    ob = out_tiles[ti]
                    cols = slice(cb * colw, (cb + 1) * colw)
                    p.op("scalar", lambda e, ps=ps, ob=ob, cols=cols:
                         e.copy(out=ob.ap[:, cols], in_=ps.ap[:, 0:colw]),
                         reads=[ps], writes=[ob])
    if out_store is not None:
        for ti in range(NTILE):
            p.store(out_store[ti, :, :], out_tiles[ti])


SQ128 = float(np.sqrt(128.0))


def emit_qproj_pairs(env, po, wq_all, xnf_dram, qT_own, ndm=32, seq=S,
                     evac_fn=None):
    p = env.p
    assert ndm == 32
    xql = [env.sbuf(f"qp_xl{i}", [128, 8 * 512], BF16) for i in range(2)]
    xqh = [env.sbuf(f"qp_xh{i}", [128, 8 * 512], BF16) for i in range(2)]
    ntb = seq // 512
    for tb in range(ntb):
      for half in range(2):
        d0 = half * 16
        par = (tb * 2 + half) % 2
        xl, xh = xql[par], xqh[par]
        p.load(xl, xnf_dram[d0:d0 + 8, :, tb * 512:(tb + 1) * 512]
               .rearrange("a p m -> p a m"), queue="sync",
               dst_ap=xl.ap[:].rearrange("p (a m) -> p a m", a=8))
        p.load(xh, xnf_dram[d0 + 8:d0 + 16, :, tb * 512:(tb + 1) * 512]
               .rearrange("a p m -> p a m"), queue="scalar",
               dst_ap=xh.ap[:].rearrange("p (a m) -> p a m", a=8))
        for dm in range(d0, d0 + 16):
            src = xl if dm - d0 < 8 else xh
            off = (dm - d0) % 8
            for j in range(NPAIR):
                c0 = (dm * NPAIR + j) * 128
                def mk(e, dm=dm, j=j, c0=c0, off=off, src=src):
                    return e.matmul(po.pf[j].ap[:], wq_all.ap[:, c0:c0 + 128],
                                    src.ap[:, off * 512:(off + 1) * 512],
                                    start=(dm == 0), stop=(dm == ndm - 1))
                p.op("tensor", mk, reads=[wq_all, src], writes=[po.pf[j]])
      for j in range(NPAIR):
            ps = po.pf[j]
            cols = slice(tb * 512, (tb + 1) * 512)
            if evac_fn is not None:
                evac_fn(j, tb, ps, qT_own[j], cols)
            else:
                p.op("scalar", lambda e, ps=ps, j=j, cols=cols:
                     e.copy(out=qT_own[j].ap[:, cols], in_=ps.ap[:]),
                     reads=[ps], writes=[qT_own[j]])


def emit_attn_pairs(env, po, qT_own, kT_getter, v_getter, ctx_out, seq=S,
                    exp_evac=None, score_extra=None, ctx_sb=None):
    p = env.p
    nqb = seq // 512
    pts = [env.sbuf(f"at_p{i}", [128, 512], BF16) for i in range(4)]
    zrow = env.sbuf("at_z", [1, 512], F32)
    onesr = env.sbuf("at_or", [1, 128], F32)
    bz = env.sbuf("at_bz", [128, 512], F32)
    ctxs = env.sbuf("at_ctx", [128, 512], BF16)
    p.op("vector", lambda e: e.memset(onesr.ap[:], 1.0), writes=[onesr])
    psZ, psO, psB = po.pf[4], po.pf[5], po.pf[6]
    for j in range(NPAIR):
        kbuf, kc0 = kT_getter(j)
        vbuf, vc0 = v_getter(j)
        for qblk in range(nqb):
            nkt = 4 * qblk + 4

            def zpv(kt, nkt=nkt, vbuf=vbuf, vc0=vc0):
                ptile = pts[kt % 4]
                def mm_z(e, kt=kt, ptile=ptile):
                    return e.matmul(psZ.ap[0:1, :], po.ones1.ap[:],
                                    ptile.ap[:], start=(kt == 0), stop=(kt == nkt - 1))
                p.op("tensor", mm_z, reads=[po.ones1, ptile], writes=[psZ])
                def mm_o(e, kt=kt, ptile=ptile):
                    return e.matmul(psO.ap[:], vbuf.ap[:, vc0 + kt * 128:vc0 + (kt + 1) * 128],
                                    ptile.ap[:], start=(kt == 0), stop=(kt == nkt - 1))
                p.op("tensor", mm_o, reads=[vbuf, ptile], writes=[psO])

            for kt in range(nkt):
                psS = po.pf[kt % 4]
                def mm_s(e, j=j, qblk=qblk, kt=kt, psS=psS, kbuf=kbuf, kc0=kc0):
                    return e.matmul(psS.ap[:], kbuf.ap[:, kc0 + kt * 128:kc0 + (kt + 1) * 128],
                                    qT_own[j].ap[:, qblk * 512:(qblk + 1) * 512],
                                    start=True, stop=(score_extra is None))
                p.op("tensor", mm_s, reads=[kbuf, qT_own[j]], writes=[psS])
                if score_extra is not None:
                    score_extra(j, qblk, kt, psS)
                exp_evac(j, qblk, kt, psS, pts[kt % 4])
                if kt >= 1:
                    zpv(kt - 1)
            zpv(nkt - 1)
            p.op("vector", lambda e: e.tensor_copy(out=zrow.ap[:], in_=psZ.ap[0:1, :]),
                 reads=[psZ], writes=[zrow])
            p.op("vector", lambda e: e.reciprocal(out=zrow.ap[:], in_=zrow.ap[:]),
                 reads=[zrow], writes=[zrow])
            p.op("tensor", lambda e: e.matmul(psB.ap[:], onesr.ap[:], zrow.ap[:],
                                              start=True, stop=True),
                 reads=[onesr, zrow], writes=[psB])
            p.op("vector", lambda e: e.tensor_copy(out=bz.ap[:], in_=psB.ap[:]),
                 reads=[psB], writes=[bz])
            p.op("vector", lambda e: e.tensor_tensor(out=ctxs.ap[:], in0=psO.ap[:],
                                                     in1=bz.ap[:], op=ALU.mult),
                 reads=[psO, bz], writes=[ctxs])
            if ctx_sb is not None:
                p.op("vector", lambda e, j=j, qblk=qblk:
                     e.tensor_copy(out=ctx_sb[j].ap[:, qblk * 512:(qblk + 1) * 512],
                                   in_=ctxs.ap[:]),
                     reads=[ctxs], writes=[ctx_sb[j]])
            if ctx_out is not None:
                p.store(ctx_out[j, :, qblk * 512:(qblk + 1) * 512], ctxs)


def build_attn0(env):
    po = Pools(env, n_pf=8, n_pt=0, ln=False, wstream=False, rope=False)
    p = env.p
    xnf = env.dram_in("xnf", [32, 128, S], BF16)
    kTf = env.dram_in("kTf", [1, 128, S], BF16)
    vf = env.dram_in("vf", [128, NT * 128], BF16)
    wq = env.dram_in("wq", [128, 32 * NPAIR * 128])
    biasKQ = env.dram_in("biasKQ", [128, NPAIR * NT * 8])
    qoffm = env.dram_in("qoffm", [128, 5 * 512])
    slopes = env.dram_in("slopes", [128, NPAIR])
    ctx_out = env.dram_out("ctx", [NPAIR, 128, S], BF16)

    wq_all = env.sbuf("wq_all", [128, 32 * NPAIR * 128], BF16)
    p.load(wq_all, wq, queue="gpsimd")
    wq_all.frozen = True
    kT = env.sbuf("kT", [128, S], BF16)
    p.load(kT, kTf[0, :, :])
    kT.frozen = True
    v_all = env.sbuf("v_all", [128, NT * 128], BF16)
    p.load(v_all, vf)
    v_all.frozen = True
    bkq = env.sbuf("bkq", [128, NPAIR * NT * 8], F32)
    p.load(bkq, biasKQ)
    bkq.frozen = True
    qoff = env.sbuf("qoff", [128, 5 * 512], F32)
    p.load(qoff, qoffm)
    qoff.frozen = True
    slp = env.sbuf("slp", [128, NPAIR], F32)
    p.load(slp, slopes)
    slp.frozen = True

    qT_own = [env.sbuf(f"qT{j}", [128, S], BF16) for j in range(NPAIR)]
    emit_qproj_pairs(env, po, wq_all, xnf, qT_own)
    for b in qT_own:
        b.frozen = True

    tmp2 = [env.sbuf(f"at_tmp{i}", [128, 512], F32) for i in range(2)]

    def exp_evac(j, qblk, kt, psS, ptile):
        tmp = tmp2[kt % 2]
        r = kt - 4 * qblk
        var = (r + 1) if 0 <= r <= 3 else 0
        qsl = qoff.ap[:, var * 512:(var + 1) * 512]
        p.op("vector", lambda e, tmp=tmp: e.scalar_tensor_tensor(
            out=tmp.ap[:], in0=qsl, scalar=slp.ap[:, j:j + 1],
            in1=psS.ap[:], op0=ALU.mult, op1=ALU.add),
            reads=[qoff, slp, psS], writes=[tmp])
        col = (j * NT + kt) * 8 + qblk
        p.op("scalar", lambda e, col=col, tmp=tmp: e.activation(
            out=ptile.ap[:], in_=tmp.ap[:], func=AF.Exp,
            bias=bkq.ap[:, col:col + 1], scale=1.0 / SQ128),
            reads=[tmp, bkq], writes=[ptile])

    emit_attn_pairs(env, po, qT_own, lambda j: (kT, 0), lambda j: (v_all, 0),
                    ctx_out, exp_evac=exp_evac)


def emit_wo_res(env, po, ctxT, wo_dram, x_in_dram, x_out_dram, bias_dram=None,
                x_keep=None):
    p = env.p
    xsl = [env.sbuf(f"wr_x{i}", [128, 512], F32) for i in range(2)]
    xo = [env.sbuf(f"wr_o{i}", [128, 512], F32) for i in range(2)]
    bias_b = None
    if bias_dram is not None:
        bias_b = env.sbuf("wr_b", [128, D], F32)
        p.load(bias_b, bias_dram)
        bias_b.frozen = True

    def evac(cb, ti, ps):
        xs = xsl[ti % 2]
        xob = xo[ti % 2]
        p.load(xs, x_in_dram[ti, :, cb * 512:(cb + 1) * 512], queue="sync")
        if bias_b is not None:
            p.op("vector", lambda e, xs=xs, cb=cb: e.tensor_tensor(
                out=xs.ap[:], in0=xs.ap[:],
                in1=bias_b.ap[:, cb * 512:(cb + 1) * 512], op=ALU.add),
                reads=[xs, bias_b], writes=[xs])
        p.op("vector", lambda e, ps=ps, xs=xs, xob=xob: e.scalar_tensor_tensor(
            out=xob.ap[:], in0=ps.ap[:], scalar=1.0, in1=xs.ap[:],
            op0=ALU.mult, op1=ALU.add),
            reads=[ps, xs], writes=[xob])
        if x_keep is not None:
            p.op("vector", lambda e, xob=xob, ti=ti, cb=cb: e.tensor_copy(
                out=x_keep[ti].ap[:, cb * 512:(cb + 1) * 512], in_=xob.ap[:]),
                reads=[xob], writes=[x_keep[ti]])
        p.store(x_out_dram[ti, :, cb * 512:(cb + 1) * 512], xob)

    emit_proj_tok(env, po, wo_dram, ctxT, ncolblk=8, evac_fn=evac, npsum=8)


def mk_build_wo_res(with_bias):
    def build(env):
        po = Pools(env, n_pf=8, n_pt=0, ln=False, wstream=True, rope=False)
        p = env.p
        ctx_in = env.dram_in("ctxT", [32, 128, TOK], BF16)
        wo = env.dram_in("wo", [32, 8, 128, 512])
        x_in = env.dram_in("x", [NTILE, 128, D])
        bias = env.dram_in("bias", [128, D]) if with_bias else None
        x_out = env.dram_out("xo", [NTILE, 128, D])
        ctxT = [env.sbuf(f"ctxT{c}", [128, TOK], BF16) for c in range(32)]
        for c in range(32):
            p.load(ctxT[c], ctx_in[c, :, :])
            ctxT[c].frozen = True
        emit_wo_res(env, po, ctxT, wo, x_in, x_out, bias_dram=bias)
    return build


build_wo_res = mk_build_wo_res(False)
build_wo_res_bias = mk_build_wo_res(True)


def emit_ffn0(env, po, x_in_dram, x_out_dram, ab_bufs, prr, x_tiles=None):
    p = env.p
    A_rhs, B_rhs, WqR, keysT = (ab_bufs["A"], ab_bufs["B"], ab_bufs["WqR"],
                                ab_bufs["keysT"])
    xw = env.sbuf("f0_x", [128, D], F32)
    xnw = env.sbuf("f0_xn", [128, D], BF16)
    xnTt = env.sbuf("f0_xnT", [128, D], BF16)
    z_bf = env.sbuf("f0_z", [128, 512], BF16)
    qTf = env.sbuf("f0_qT", [16, 128], F32)
    scT = env.sbuf("f0_scT", [64, 128], F32)
    sc = env.sbuf("f0_sc", [128, 64], F32)
    m8a = env.sbuf("f0_m8a", [128, 8], F32)
    m8b = env.sbuf("f0_m8b", [128, 8], F32)
    ind8 = env.sbuf("f0_ind8", [128, 64], F32)
    scm = env.sbuf("f0_scm", [128, 64], F32)
    kept = env.sbuf("f0_kept", [128, 64], F32)
    negm1 = env.sbuf("f0_negm1", [128, 1], F32)
    gexp = env.sbuf("f0_gexp", [128, 64], F32)
    zs = env.sbuf("f0_zs", [128, 1], F32)
    gsc = env.sbuf("f0_gsc", [128, 1], F32)
    gate_bf = env.sbuf("f0_gate", [128, 64], BF16)
    zg = env.sbuf("f0_zg", [128, 512], BF16)
    zgTt = env.sbuf("f0_zgT", [128, 512], BF16)
    xsl = env.sbuf("f0_xsl", [128, 512], F32)
    xob = env.sbuf("f0_xo", [128, 512], F32)
    psZz, psQ, psSc, psD = po.pf[0], po.pf[1], po.pf[2], po.pf[3]

    for ti in range(NTILE):
        if x_tiles is not None:
            xbuf = x_tiles[ti]
        else:
            xbuf = xw
            p.load(xbuf, x_in_dram[ti, :, :], queue="sync")
        emit_ln(env, po, xbuf, xnw)
        for c in range(32):
            emit_tp128(env, po, xnw, xnw.ap[:, c * 128:(c + 1) * 128],
                       xnTt, xnTt.ap[:, c * 128:(c + 1) * 128])
        for dm in range(32):
            p.op("tensor", lambda e, dm=dm: e.matmul(
                psZz.ap[:], xnTt.ap[:, dm * 128:(dm + 1) * 128],
                A_rhs.ap[:, dm * 512:(dm + 1) * 512],
                start=(dm == 0), stop=(dm == 31)),
                reads=[xnTt, A_rhs], writes=[psZz])
        p.op("scalar", lambda e: e.copy(out=z_bf.ap[:], in_=psZz.ap[:]),
             reads=[psZz], writes=[z_bf])
        for dm in range(32):
            p.op("tensor", lambda e, dm=dm: e.matmul(
                psQ.ap[0:16, 0:128], WqR.ap[:, dm * 16:(dm + 1) * 16],
                xnTt.ap[:, dm * 128:(dm + 1) * 128],
                start=(dm == 0), stop=(dm == 31)),
                reads=[xnTt, WqR], writes=[psQ])
        p.op("vector", lambda e: e.tensor_copy(out=qTf.ap[:], in_=psQ.ap[0:16, 0:128]),
             reads=[psQ], writes=[qTf])
        p.op("tensor", lambda e: e.matmul(psSc.ap[0:64, 0:128], keysT.ap[:],
                                          qTf.ap[:], start=True, stop=True),
             reads=[keysT, qTf], writes=[psSc])
        p.op("vector", lambda e: e.tensor_copy(out=scT.ap[:], in_=psSc.ap[0:64, 0:128]),
             reads=[psSc], writes=[scT])
        p.op("tensor", lambda e: e.transpose(prr.ap[0:128, 0:64], scT.ap[:],
                                             po.identf.ap[0:64, 0:64]),
             reads=[scT, po.identf], writes=[prr])
        p.op("vector", lambda e: e.tensor_copy(out=sc.ap[:], in_=prr.ap[0:128, 0:64]),
             reads=[prr], writes=[sc])
        p.op("vector", lambda e: e.max(out=m8a.ap[:], in_=sc.ap[:]),
             reads=[sc], writes=[m8a])
        p.op("vector", lambda e: e.tensor_scalar(
            out=ind8.ap[:], in0=sc.ap[:], scalar1=m8a.ap[:, 7:8], scalar2=None,
            op0=ALU.is_ge), reads=[sc, m8a], writes=[ind8])
        p.op("vector", lambda e: e.scalar_tensor_tensor(
            out=scm.ap[:], in0=ind8.ap[:], scalar=-1e30, in1=sc.ap[:],
            op0=ALU.mult, op1=ALU.add), reads=[ind8, sc], writes=[scm])
        p.op("vector", lambda e: e.max(out=m8b.ap[:], in_=scm.ap[:]),
             reads=[scm], writes=[m8b])
        p.op("vector", lambda e: e.tensor_scalar(
            out=kept.ap[:], in0=sc.ap[:], scalar1=m8b.ap[:, 7:8], scalar2=None,
            op0=ALU.is_ge), reads=[sc, m8b], writes=[kept])
        p.op("scalar", lambda e: e.mul(negm1.ap[:], m8a.ap[:, 0:1], -1.0),
             reads=[m8a], writes=[negm1])
        p.op("scalar", lambda e: e.activation(out=gexp.ap[:], in_=sc.ap[:],
                                              func=AF.Exp, bias=negm1.ap[:]),
             reads=[sc, negm1], writes=[gexp])
        p.op("vector", lambda e: e.tensor_tensor(out=gexp.ap[:], in0=gexp.ap[:],
                                                 in1=kept.ap[:], op=ALU.mult),
             reads=[gexp, kept], writes=[gexp])
        p.op("vector", lambda e: e.reduce_sum(zs.ap[:], gexp.ap[:], axis=AX.X),
             reads=[gexp], writes=[zs])
        p.op("vector", lambda e: e.reciprocal(out=gsc.ap[:], in_=zs.ap[:]),
             reads=[zs], writes=[gsc])
        p.op("vector", lambda e: e.tensor_scalar_mul(gsc.ap[:], gsc.ap[:], 2.0),
             reads=[gsc], writes=[gsc])
        p.op("vector", lambda e: e.tensor_scalar_mul(gexp.ap[:], gexp.ap[:], gsc.ap[:]),
             reads=[gexp, gsc], writes=[gexp])
        p.op("vector", lambda e: e.tensor_copy(out=gate_bf.ap[:], in_=gexp.ap[:]),
             reads=[gexp], writes=[gate_bf])
        if hasattr(env, "dbg_gate"):
            p.store(env.dbg_gate[ti, :, :], gexp)
            p.store(env.dbg_z[ti, :, :], z_bf)
            p.store(env.dbg_sc[ti, :, :], sc)
        p.op("vector", lambda e: e.tensor_tensor(
            out=zg.ap[:].rearrange("p (a r) -> p a r", a=64),
            in0=z_bf.ap[:].rearrange("p (a r) -> p a r", a=64),
            in1=gate_bf.ap[:].unsqueeze(2).to_broadcast([128, 64, 8]),
            op=ALU.mult), reads=[z_bf, gate_bf], writes=[zg])
        if hasattr(env, "dbg_zg"):
            p.store(env.dbg_zg[ti, :, :], zg)
        for i in range(4):
            emit_tp128(env, po, zg, zg.ap[:, i * 128:(i + 1) * 128],
                       zgTt, zgTt.ap[:, i * 128:(i + 1) * 128])
        if hasattr(env, "dbg_zgT"):
            p.store(env.dbg_zgT[ti, :, :], zgTt)
        for cb in range(8):
            for i in range(4):
                p.op("tensor", lambda e, i=i, cb=cb: e.matmul(
                    psD.ap[:], zgTt.ap[:, i * 128:(i + 1) * 128],
                    B_rhs.ap[:, (i * 8 + cb) * 512:(i * 8 + cb + 1) * 512],
                    start=(i == 0), stop=(i == 3)),
                    reads=[zgTt, B_rhs], writes=[psD])
            if x_tiles is not None:
                p.op("vector", lambda e, ti=ti, cb=cb: e.scalar_tensor_tensor(
                    out=xob.ap[:], in0=psD.ap[:], scalar=1.0,
                    in1=x_tiles[ti].ap[:, cb * 512:(cb + 1) * 512],
                    op0=ALU.mult, op1=ALU.add),
                    reads=[psD, x_tiles[ti]], writes=[xob])
            else:
                p.load(xsl, x_in_dram[ti, :, cb * 512:(cb + 1) * 512], queue="sync")
                p.op("vector", lambda e: e.scalar_tensor_tensor(
                    out=xob.ap[:], in0=psD.ap[:], scalar=1.0, in1=xsl.ap[:],
                    op0=ALU.mult, op1=ALU.add),
                    reads=[psD, xsl], writes=[xob])
            p.store(x_out_dram[ti, :, cb * 512:(cb + 1) * 512], xob)


def build_ffn0(env):
    po = Pools(env, n_pf=4, n_pt=2, ln=True, wstream=False, rope=False)
    po.identf = env.sbuf("sc_identf", [128, 128], F32)
    idf_in = env.dram_in("c_identf", [128, 128], F32)
    env.p.load(po.identf, idf_in)
    po.identf.frozen = True
    prr = env.psum("psrr", [128, 64], F32)
    p = env.p
    x_in = env.dram_in("x", [NTILE, 128, D])
    a_in = env.dram_in("Arhs", [128, 32 * 512])
    b_in = env.dram_in("Brhs", [128, 32 * 512])
    wqr_in = env.dram_in("WqR", [128, 32 * 16])
    keys_in = env.dram_in("keysT", [16, 64])
    x_out = env.dram_out("xo", [NTILE, 128, D])
    A = env.sbuf("A_rhs", [128, 32 * 512], BF16)
    Bb = env.sbuf("B_rhs", [128, 32 * 512], BF16)
    Wq = env.sbuf("WqR_b", [128, 32 * 16], BF16)
    ks = env.sbuf("keysT_b", [16, 64], F32)
    p.load(A, a_in, queue="gpsimd")
    p.load(Bb, b_in, queue="gpsimd")
    p.load(Wq, wqr_in, queue="gpsimd")
    p.load(ks, keys_in, queue="sync")
    for b in (A, Bb, Wq, ks):
        b.frozen = True
    emit_ffn0(env, po, x_in, x_out,
              {"A": A, "B": Bb, "WqR": Wq, "keysT": ks}, prr)


def build_pre0(env):
    po = Pools(env)
    x_in = env.dram_in("x", [NTILE, 128, D])
    wk = env.dram_in("wk", [32, 1, 128, 128])
    wv = env.dram_in("wv", [32, 1, 128, 128])
    xnT_out = env.dram_out("xnT", [32, 128, TOK], BF16)
    kT_out = env.dram_out("kT", [1, 128, TOK], BF16)
    v_out = env.dram_out("v", [NTILE, 128, 128], BF16)
    xnT = [env.sbuf(f"xnT{c}", [128, TOK], BF16) for c in range(32)]
    emit_ln_tp_phase(env, po, x_in, xnT, xnT_out=xnT_out)
    for c in xnT:
        c.frozen = True
    kT = [env.sbuf("kT0", [128, TOK], BF16)]
    emit_proj_feat(env, po, wk, xnT, kT, out_store=kT_out)
    vt = [env.sbuf(f"vt{t}", [128, 128], BF16) for t in range(NTILE)]
    emit_proj_tok(env, po, wv, xnT, ncolblk=1, colw=128,
                  out_tiles=vt, out_store=v_out)


_IDENT = np.eye(128, dtype=BF)
_IDENTF = np.eye(128, dtype=np.float32)


_HOOK = None


def run_launch(build_fn, in_maps, name="l", trace=False):
    env = Env(name)
    nc = env.build(build_fn)
    names = set(env._ins)
    for m in in_maps:
        if "c_ident" in names:
            m["c_ident"] = _IDENT
        if "c_identf" in names:
            m["c_identf"] = _IDENTF
    res = run_bass_kernel_spmd(nc, in_maps, core_ids=list(range(NCORES)),
                               trace=trace)
    if _HOOK is not None:
        _HOOK(name, res.results if not trace else res.results)
    return res.results if not trace else res


MBIG = 2.33e10


def _pm(a):
    a = np.asarray(a)
    if a.ndim == 2:
        return np.ascontiguousarray(a)
    if a.ndim == 3:
        assert a.shape[1] == 128, a.shape
        return np.ascontiguousarray(a.transpose(1, 0, 2).reshape(a.shape[1], -1))
    raise ValueError(a.shape)


def _row(v):
    return np.broadcast_to(np.asarray(v, np.float32)[None, :], (128, len(v))).copy()


def _gather_batch(res, key, b, axis=2):
    return np.concatenate([np.asarray(res[4 * b + i][key]) for i in range(4)],
                          axis=axis)


def _ctx_for_core(res, c):
    b = c // 4
    tc = c % 4
    ctx = np.stack([np.asarray(res[4 * b + j // 8]["ctx"][j % 8])
                    for j in range(32)])
    return np.ascontiguousarray(ctx[:, :, tc * 1024:(tc + 1) * 1024])


def kernel(**I):
    perm = np.asarray(I["L1_ffn_perm"]).astype(np.int64)
    x = np.asarray(I["L0_attn_a_hidden_states"])
    xp = np.ascontiguousarray(x[:, :, perm])
    shards = [np.ascontiguousarray(
        xp[c // 4, (c % 4) * TOK:(c % 4 + 1) * TOK].reshape(NTILE, 128, D))
        for c in range(NCORES)]

    slopes = alibi_slopes(32)
    cosS, sinS = rope_tables(np.arange(S), 128)
    cosyS, sinyS = yarn_tables(np.arange(S))
    qoffm = np.zeros((5, 128, 512), np.float32)
    qoffm[0] = np.broadcast_to(np.arange(512, dtype=np.float32)[None, :], (128, 512))
    for r in range(4):
        qoffm[r + 1] = qoffm[0] + MBIG * (
            np.arange(512)[None, :] < 128 * r + np.arange(128)[:, None])
    qoffm_pm = _pm(qoffm)
    mdiag = np.where(np.arange(128)[:, None] < np.arange(128)[None, :],
                     -428.0, 0.0).astype(np.float32)
    mdiagw = np.zeros((128, 4 * 512), np.float32)
    for r_ in range(4):
        mdiagw[:, r_ * 512 + r_ * 128:r_ * 512 + (r_ + 1) * 128] = mdiag
    mvar = np.zeros((4, 128, 512), np.float32)
    for r in range(4):
        mvar[r] = -1e9 * (np.arange(512)[None, :] < 128 * r + np.arange(128)[:, None])
    mvar_pm = _pm(mvar)
    biasKQ = {}
    for h0 in (0, 8, 16, 24):
        t = np.zeros((128, 8 * 32 * 8), np.float32)
        p_ = np.arange(128, dtype=np.float32)
        for j in range(8):
            sl = slopes[h0 + j]
            for kt in range(32):
                for qb_ in range(8):
                    t[:, (j * 32 + kt) * 8 + qb_] = sl * (kt * 128 + p_ - qb_ * 512)
        biasKQ[h0] = t
    slp = {h0: _row(-slopes[h0:h0 + 8] * SQ128) for h0 in (0, 8, 16, 24)}
    percore_pos = [np.arange((c % 4) * TOK, (c % 4 + 1) * TOK) for c in range(NCORES)]

    Wq0 = np.asarray(I["L0_attn_a_Wq"])[perm, :]
    Wkv0 = np.asarray(I["L0_attn_a_Wkv"])[perm, :]
    Wo0 = np.asarray(I["L0_attn_a_Wo"])[:, perm]
    A_p = np.asarray(I["L0_ffn_A_w"])[:, perm]
    B_p = np.asarray(I["L0_ffn_B_w"])[perm, :]
    WqR_p = np.asarray(I["L0_ffn_Wq_w"])[:, perm]
    keys0 = np.asarray(I["L0_ffn_keys"])
    Wqb = np.asarray(I["L0_attn_b_Wq"])[:, perm]
    Wkb = np.asarray(I["L0_attn_b_Wk"])[:, perm]
    Wvb = np.asarray(I["L0_attn_b_Wv"])[:, perm]
    Wob = np.asarray(I["L0_attn_b_Wo"])[perm, :]
    hp1 = np.asarray(I["L0_attn_b_hash_proj1"])
    hb1 = np.asarray(I["L0_attn_b_hash_bias1"])
    hp2 = np.asarray(I["L0_attn_b_hash_proj2"])
    hb2 = np.asarray(I["L0_attn_b_hash_bias2"])
    qkvw = np.asarray(I["L1_attn_a_qkv_w"])[perm, :]
    qkvb = np.asarray(I["L1_attn_a_qkv_b"])
    outw = np.asarray(I["L1_attn_a_out_w"])[:, perm]
    outb = np.asarray(I["L1_attn_a_out_b"])[perm]
    wqd = np.asarray(I["L1_attn_b_w_q_down"])[perm, :]
    wqun = np.asarray(I["L1_attn_b_w_q_up_nope"])
    wqur = np.asarray(I["L1_attn_b_w_q_up_rope"])
    wkvd = np.asarray(I["L1_attn_b_w_kv_down"])[perm, :]
    wkrp = np.asarray(I["L1_attn_b_w_k_rope"])[perm, :]
    wuk = np.asarray(I["L1_attn_b_w_uk"])
    wuv = np.asarray(I["L1_attn_b_w_uv"])
    wout1b = np.asarray(I["L1_attn_b_w_out"])[:, perm]
    up = np.asarray(I["L1_ffn_up"])
    down = np.asarray(I["L1_ffn_down"])

    wk0_blk = block_lhsT(Wkv0[:, :128])
    wv0_rhs = block_rhs(Wkv0[:, 128:], nb=128)
    wq0_4 = Wq0.reshape(32, 128, 32, 128).transpose(0, 2, 1, 3)
    wo0_blk = block_rhs(Wo0)
    Arhs = _pm(A_p.T.reshape(32, 128, 512))
    Brhs = np.ascontiguousarray(
        B_p.T.reshape(4, 128, 8, 512).transpose(1, 0, 2, 3)).reshape(128, -1)
    WqRb = _pm(WqR_p.T.reshape(32, 128, 16))
    keysT = np.ascontiguousarray(keys0.T)
    wkb_blk = block_lhsT(Wkb.T)
    wvb_rhs = block_rhs(Wvb.T)
    wqb_4 = Wqb.T.reshape(32, 128, 32, 128).transpose(0, 2, 1, 3)
    wob_blk = block_rhs(Wob.T)
    hp1_pm = _pm(hp1)
    hp2_pm = _pm(hp2)
    hb1_pm = np.ascontiguousarray(hb1[:, 0, :].T)
    hb2_pm = np.ascontiguousarray(hb2[:, 0, :].T)
    wk1_blk = block_lhsT(qkvw[:, D:2 * D])
    wv1_rhs = block_rhs(qkvw[:, 2 * D:])
    wq1_4 = qkvw[:, :D].reshape(32, 128, 32, 128).transpose(0, 2, 1, 3)
    kb1 = np.ascontiguousarray(qkvb[D:2 * D].reshape(32, 128).T)
    vb1_row = _row(qkvb[2 * D:])
    qb1 = np.ascontiguousarray(qkvb[:D].reshape(32, 128).T)
    wo1_blk = block_rhs(outw)
    ob1_row = _row(outb)
    up_pm = _pm(up.reshape(16, 2, 128, 8, 128).transpose(2, 0, 1, 3, 4)
                .reshape(128, -1))
    dn_pm = _pm(down.reshape(16, 8, 128, 256).transpose(2, 0, 1, 3)
                .reshape(128, -1))
    wcq_blk = block_lhsT(wqd)
    wckv_blk = block_lhsT(wkvd)
    wkr_blk = block_lhsT(wkrp, mc=64)
    wqn_blk = block_lhsT(wqun)
    wqr_blk = block_lhsT(wqur, mc=64)
    wuk4 = wuk.reshape(32, 128, 4, 128).transpose(0, 2, 3, 1)
    wuv4 = wuv.reshape(32, 128, 4, 128).transpose(0, 2, 3, 1)
    wo1b_blk = block_rhs(wout1b)

    def wq_slice(w4, h0):
        return np.ascontiguousarray(
            w4[:, h0:h0 + 8].reshape(-1, 128, 128).transpose(1, 0, 2)
        ).reshape(128, -1)

    in1 = [{"x": shards[c], "wk": wk0_blk, "wv": wv0_rhs} for c in range(NCORES)]
    r1 = run_launch(build_pre0, in1, "pre0")
    in2 = []
    for c in range(NCORES):
        b, h0 = c // 4, 8 * (c % 4)
        in2.append({"xnf": _gather_batch(r1, "xnT", b),
                    "kTf": _gather_batch(r1, "kT", b),
                    "vf": _pm(_gather_batch(r1, "v", b, axis=0)),
                    "wq": wq_slice(wq0_4, h0),
                    "biasKQ": biasKQ[h0], "qoffm": qoffm_pm, "slopes": slp[h0]})
    r2 = run_launch(build_attn0, in2, "attn0")
    in3 = [{"ctxT": _ctx_for_core(r2, c), "wo": wo0_blk, "x": shards[c]}
           for c in range(NCORES)]
    r3 = run_launch(build_wo_res, in3, "wores0")
    in4 = [{"x": np.asarray(r3[c]["xo"]), "Arhs": Arhs, "Brhs": Brhs,
            "WqR": WqRb, "keysT": keysT} for c in range(NCORES)]
    r4 = run_launch(build_ffn0, in4, "ffn0a")

    in5 = []
    for c in range(NCORES):
        pos = percore_pos[c]
        in5.append({"x": np.asarray(r4[c]["xo"]), "wk": wkb_blk, "wv": wvb_rhs,
                    "cosk": cosS[:, pos], "sink": sinS[:, pos],
                    "hp1": hp1_pm, "hb1": hb1_pm, "hp2": hp2_pm, "hb2": hb2_pm})
    r5 = run_launch(build_pre_l0b, in5, "prel0b")
    in6 = []
    for c in range(NCORES):
        b, h0 = c // 4, 8 * (c % 4)
        g0 = h0 // 4
        vfull = _gather_batch(r5, "v", b, axis=0)
        in6.append({"xnf": _gather_batch(r5, "xnT", b),
                    "wq": np.ascontiguousarray(
                        wqb_4[:, h0:h0 + 8].transpose(0, 2, 1, 3)
                        .reshape(32, 128, -1)),
                    "cosq": cosS.astype(BF), "sinq": sinS.astype(BF),
                    "hp1": hp1_pm[:, h0 * 128:(h0 + 8) * 128],
                    "hb1": hb1_pm[:, h0:h0 + 8],
                    "hp2": hp2_pm[:, h0 * 128:(h0 + 8) * 128],
                    "hb2": hb2_pm[:, h0:h0 + 8],
                    "khf": _gather_batch(r5, "kh", b)[h0:h0 + 8],
                    "krf": _gather_batch(r5, "kT", b)[g0:g0 + 2],
                    "vf": _pm(vfull[:, :, g0 * 128:(g0 + 2) * 128]),
                    "mdiag": mdiag, "mdiagw": mdiagw})
    r6 = run_launch(build_attn_l0b, in6, "attnl0b")
    in7 = [{"ctxT": _ctx_for_core(r6, c), "wo": wob_blk,
            "x": np.asarray(r4[c]["xo"])} for c in range(NCORES)]
    r7 = run_launch(build_wo_res, in7, "woresl0b")
    in8 = [{"x": np.asarray(r7[c]["xo"]), "Arhs": Arhs, "Brhs": Brhs,
            "WqR": WqRb, "keysT": keysT} for c in range(NCORES)]
    r8 = run_launch(build_ffn0, in8, "ffn0b")

    in9 = []
    for c in range(NCORES):
        pos = percore_pos[c]
        in9.append({"x": np.asarray(r8[c]["xo"]), "wk": wk1_blk, "wv": wv1_rhs,
                    "kb": kb1, "vbrow": vb1_row,
                    "cosk": cosS[:, pos], "sink": sinS[:, pos]})
    r9 = run_launch(build_pre_l1a, in9, "prel1a")
    in10 = []
    for c in range(NCORES):
        b, h0 = c // 4, 8 * (c % 4)
        vfull = _gather_batch(r9, "v", b, axis=0)
        vf = np.stack([_pm(vfull[:, :, (h0 + j) * 128:(h0 + j + 1) * 128])
                       for j in range(8)])
        in10.append({"xnf": _gather_batch(r9, "xnT", b),
                     "wq": wq_slice(wq1_4, h0),
                     "qb": qb1[:, h0:h0 + 8],
                     "cosq": cosS.astype(BF), "sinq": sinS.astype(BF),
                     "kTf": _gather_batch(r9, "kT", b)[h0:h0 + 8],
                     "vf": vf, "mvar": mvar_pm})
    r10 = run_launch(build_attn_l1a, in10, "attnl1a")
    in11 = [{"ctxT": _ctx_for_core(r10, c), "wo": wo1_blk,
             "x": np.asarray(r8[c]["xo"]), "bias": ob1_row}
            for c in range(NCORES)]
    r11 = run_launch(build_wo_res_bias, in11, "woresl1a")
    in12 = [{"x": np.asarray(r11[c]["xo"]), "up": up_pm, "dn": dn_pm}
            for c in range(NCORES)]
    r12 = run_launch(build_ffn1, in12, "ffn1a")

    in13 = []
    for c in range(NCORES):
        pos = percore_pos[c]
        cosy, siny = cosyS[:, pos], sinyS[:, pos]
        in13.append({"x": np.asarray(r12[c]["xo"]), "wcq": wcq_blk,
                     "wckv": wckv_blk, "wkr": wkr_blk,
                     "cosy": cosy, "siny": siny})
    r13 = run_launch(build_pre_l1b, in13, "prel1b")
    in14 = []
    for c in range(NCORES):
        b, h0 = c // 4, 8 * (c % 4)
        in14.append({"cqf": _gather_batch(r13, "cqT", b),
                     "ckvf": _gather_batch(r13, "ckvT", b),
                     "krf": _gather_batch(r13, "krT", b, axis=1),
                     "wqn": np.ascontiguousarray(
                         wqn_blk[:, h0:h0 + 8].transpose(2, 0, 1, 3)).reshape(128, -1),
                     "wqr": np.ascontiguousarray(
                         wqr_blk[:, h0:h0 + 8].transpose(2, 0, 1, 3)).reshape(128, -1),
                     "wuk": np.ascontiguousarray(
                         wuk4[h0:h0 + 8].transpose(2, 1, 0, 3)).reshape(128, -1),
                     "wuv": np.ascontiguousarray(
                         wuv4[h0:h0 + 8].transpose(2, 1, 0, 3)).reshape(128, -1),
                     "cosy": cosyS, "siny": sinyS, "mvar": mvar_pm})
    r14 = run_launch(build_attn_l1b, in14, "attnl1b")
    in15 = [{"ctxT": _ctx_for_core(r14, c), "wo": wo1b_blk,
             "x": np.asarray(r12[c]["xo"])} for c in range(NCORES)]
    r15 = run_launch(build_wo_res, in15, "woresl1b")
    in16 = [{"x": np.asarray(r15[c]["xo"]), "up": up_pm, "dn": dn_pm}
            for c in range(NCORES)]
    r16 = run_launch(build_ffn1, in16, "ffn1b")

    out = np.empty((B, S, D), np.float32)
    for c in range(NCORES):
        out[c // 4, (c % 4) * TOK:(c % 4 + 1) * TOK, :] = \
            np.asarray(r16[c]["xo"]).reshape(TOK, D)
    res = np.empty_like(out)
    res[:, :, perm] = out
    return res


def emit_hash(env, po, src_chunks, out_chunks, hp1, hb1, hp2, hb2, nh, seq,
              psA, psB_, out_dram=None, out_dram_bufs=None):
    p = env.p
    t1 = env.sbuf("hh_t1", [128, 512], BF16)
    x1h = env.sbuf("hh_x1", [128, 512], BF16)
    t2 = env.sbuf("hh_t2", [128, 512], BF16)
    t3 = env.sbuf("hh_t3", [128, 512], BF16)
    stg = env.sbuf("hh_stg", [128, 512], BF16) if out_dram is not None else None
    for h in range(nh):
        src = src_chunks(h)
        for tb in range(seq // 512):
            cols = slice(tb * 512, (tb + 1) * 512)
            p.op("tensor", lambda e, h=h, cols=cols, src=src: e.matmul(
                psA.ap[:], hp1.ap[:, h * 128:(h + 1) * 128], src.ap[:, cols],
                start=True, stop=True), reads=[hp1, src], writes=[psA])
            p.op("scalar", lambda e, h=h: e.activation(
                out=t1.ap[:], in_=psA.ap[:], func=AF.Silu,
                bias=hb1.ap[:, h:h + 1]), reads=[psA, hb1], writes=[t1])
            p.op("vector", lambda e, cols=cols, src=src: e.tensor_tensor(
                out=x1h.ap[:], in0=t1.ap[:], in1=src.ap[:, cols], op=ALU.add),
                reads=[t1, src], writes=[x1h])
            p.op("tensor", lambda e, h=h: e.matmul(
                psB_.ap[:], hp2.ap[:, h * 128:(h + 1) * 128], x1h.ap[:],
                start=True, stop=True), reads=[hp2, x1h], writes=[psB_])
            p.op("scalar", lambda e, h=h: e.activation(
                out=t2.ap[:], in_=psB_.ap[:], func=AF.Identity,
                bias=hb2.ap[:, h:h + 1]), reads=[psB_, hb2], writes=[t2])
            p.op("vector", lambda e: e.tensor_tensor(
                out=t3.ap[:], in0=t2.ap[:], in1=x1h.ap[:], op=ALU.add),
                reads=[t2, x1h], writes=[t3])
            if out_dram is None:
                p.op("scalar", lambda e, h=h, cols=cols: e.activation(
                    out=out_chunks[h].ap[:, cols], in_=t3.ap[:], func=AF.Sign),
                    reads=[t3], writes=[out_chunks[h]])
            else:
                p.op("scalar", lambda e: e.activation(
                    out=stg.ap[:], in_=t3.ap[:], func=AF.Sign),
                    reads=[t3], writes=[stg])
                p.store(out_dram[h, :, cols], stg,
                        dst_buf=None if out_dram_bufs is None else out_dram_bufs[h])


def build_pre_l0b(env):
    po = Pools(env, n_pf=4, n_pt=2, ln=True, wstream=True, rope=True)
    p = env.p
    x_in = env.dram_in("x", [NTILE, 128, D])
    wk = env.dram_in("wk", [32, 8, 128, 128])
    wv = env.dram_in("wv", [32, 2, 128, 512])
    cosk = env.dram_in("cosk", [128, TOK])
    sink = env.dram_in("sink", [128, TOK])
    hp1_in = env.dram_in("hp1", [128, 32 * 128])
    hb1_in = env.dram_in("hb1", [128, 32])
    hp2_in = env.dram_in("hp2", [128, 32 * 128])
    hb2_in = env.dram_in("hb2", [128, 32])
    xnT_out = env.dram_out("xnT", [32, 128, TOK], BF16)
    kT_out = env.dram_out("kT", [8, 128, TOK], BF16)
    kh_out = env.dram_out("kh", [32, 128, TOK], BF16)
    v_out = env.dram_out("v", [NTILE, 128, 1024], BF16)

    xnT = [env.sbuf(f"xnT{c}", [128, TOK], BF16) for c in range(32)]
    emit_ln_tp_phase(env, po, x_in, xnT, xnT_out=xnT_out)
    for c in xnT:
        c.frozen = True
    cos_b = env.sbuf("cos_b", [128, TOK], F32)
    sin_b = env.sbuf("sin_b", [128, TOK], F32)
    p.load(cos_b, cosk)
    p.load(sin_b, sink)
    cos_b.frozen = True
    sin_b.frozen = True
    kT = [env.sbuf(f"kTc{cgrp}", [128, TOK], BF16) for cgrp in range(8)]
    emit_proj_feat(env, po, wk, xnT, kT, evac="rope", rope=(cos_b, sin_b),
                   out_store=kT_out)
    for b in kT:
        b.frozen = True
    vt = [env.sbuf(f"vt{t}", [128, 1024], BF16) for t in range(NTILE)]
    emit_proj_tok(env, po, wv, xnT, ncolblk=2, colw=512, out_tiles=vt,
                  out_store=v_out, npsum=2)
    hp1 = env.sbuf("hp1b", [128, 32 * 128], BF16)
    hp2 = env.sbuf("hp2b", [128, 32 * 128], BF16)
    hb1 = env.sbuf("hb1b", [128, 32], F32)
    hb2 = env.sbuf("hb2b", [128, 32], F32)
    p.load(hp1, hp1_in, queue="gpsimd")
    p.load(hp2, hp2_in, queue="gpsimd")
    p.load(hb1, hb1_in)
    p.load(hb2, hb2_in)
    for b in (hp1, hp2, hb1, hb2):
        b.frozen = True
    emit_hash(env, po, lambda h: kT[h // 4], None, hp1, hb1, hp2, hb2, 32, TOK,
              po.pf[2], po.pf[3], out_dram=kh_out)


def build_attn_l0b(env):
    po = Pools(env, n_pf=0, n_pt=2, ln=False, wstream=False, rope=True)
    p = env.p
    xnf = env.dram_in("xnf", [32, 128, S], BF16)
    wq_in = env.dram_in("wq", [32, 128, NPAIR * 128])
    cosq = env.dram_in("cosq", [128, S], BF16)
    sinq = env.dram_in("sinq", [128, S], BF16)
    hp1_in = env.dram_in("hp1", [128, NPAIR * 128])
    hb1_in = env.dram_in("hb1", [128, NPAIR])
    hp2_in = env.dram_in("hp2", [128, NPAIR * 128])
    hb2_in = env.dram_in("hb2", [128, NPAIR])
    khf = env.dram_in("khf", [NPAIR, 128, S], BF16)
    krf = env.dram_in("krf", [2, 128, S], BF16)
    vf = env.dram_in("vf", [128, NT * 256], BF16)
    mdiag_in = env.dram_in("mdiag", [128, 128])
    mdw_in = env.dram_in("mdiagw", [128, 4 * 512])
    idf_in = env.dram_in("c_identf", [128, 128], F32)
    ctx_out = env.dram_out("ctx", [NPAIR, 128, S], BF16)

    v_all = env.sbuf("v_all", [128, NT * 256], BF16)
    p.load(v_all, vf)
    identf = env.sbuf("identf_b", [128, 128], F32)
    p.load(identf, idf_in)
    identf.frozen = True
    mdiag = env.sbuf("mdiag_b", [128, 128], F32)
    p.load(mdiag, mdiag_in)
    causal01 = env.sbuf("caus01", [128, 128], BF16)
    p.op("vector", lambda e: e.tensor_scalar(
        out=causal01.ap[:], in0=mdiag.ap[:], scalar1=-1.0, scalar2=None,
        op0=ALU.is_ge), reads=[mdiag], writes=[causal01])
    for b in (v_all, mdiag, causal01):
        b.frozen = True

    qrd = env.dram_tmp("qrd", [NPAIR, 128, S], BF16)
    qhd = env.dram_tmp("qhd", [NPAIR, 128, S], BF16)
    qrd_b = [Buf(None) for _ in range(NPAIR)]
    qhd_b = [Buf(None) for _ in range(NPAIR)]
    env.push_scope()
    pfA = [env.psum(f"pa{i}", [128, 512], F32) for i in range(4)]
    qrT = [env.sbuf(f"qrT{j}", [128, S], BF16) for j in range(NPAIR)]
    cos_b = env.sbuf("cosb", [128, S], BF16)
    sin_b = env.sbuf("sinb", [128, S], BF16)
    p.load(cos_b, cosq)
    p.load(sin_b, sinq)
    hp1 = env.sbuf("hp1b", [128, NPAIR * 128], BF16)
    hp2 = env.sbuf("hp2b", [128, NPAIR * 128], BF16)
    hb1 = env.sbuf("hb1b", [128, NPAIR], F32)
    hb2 = env.sbuf("hb2b", [128, NPAIR], F32)
    p.load(hp1, hp1_in, queue="gpsimd")
    p.load(hp2, hp2_in, queue="gpsimd")
    p.load(hb1, hb1_in)
    p.load(hb2, hb2_in)
    for b in (cos_b, sin_b, hp1, hp2, hb1, hb2):
        b.frozen = True
    xtl = [env.sbuf(f"qp_xl{i}", [128, 8 * 512], BF16) for i in range(2)]
    xth = [env.sbuf(f"qp_xh{i}", [128, 8 * 512], BF16) for i in range(2)]
    wqb = [env.sbuf(f"wqb{i}", [128, NPAIR * 128], BF16) for i in range(2)]
    for grp in range(2):
        pj = range(grp * 4, grp * 4 + 4)
        for tb in range(8):
            for half in range(2):
                d0 = half * 16
                par = (tb * 2 + half) % 2
                xl, xh = xtl[par], xth[par]
                p.load(xl, xnf[d0:d0 + 8, :, tb * 512:(tb + 1) * 512]
                       .rearrange("a p m -> p a m"), queue="sync",
                       dst_ap=xl.ap[:].rearrange("p (a m) -> p a m", a=8))
                p.load(xh, xnf[d0 + 8:d0 + 16, :, tb * 512:(tb + 1) * 512]
                       .rearrange("a p m -> p a m"), queue="scalar",
                       dst_ap=xh.ap[:].rearrange("p (a m) -> p a m", a=8))
                for dm in range(d0, d0 + 16):
                    wb = wqb[dm % 2]
                    p.load(wb, wq_in[dm, :, :], queue="gpsimd")
                    src = xl if dm - d0 < 8 else xh
                    off = (dm - d0) % 8
                    for j in pj:
                        def mk(e, j=j, dm=dm, off=off, wb=wb, src=src):
                            return e.matmul(
                                pfA[j % 4].ap[:], wb.ap[:, j * 128:(j + 1) * 128],
                                src.ap[:, off * 512:(off + 1) * 512],
                                start=(dm == 0), stop=(dm == 31))
                        p.op("tensor", mk, reads=[wb, src], writes=[pfA[j % 4]])
            for j in pj:
                cols = slice(tb * 512, (tb + 1) * 512)
                emit_rope_evac(env, po, pfA[j % 4], qrT[j], cols, cos_b, sin_b)
    for b in qrT:
        b.frozen = True
    emit_hash(env, po, lambda j: qrT[j], None, hp1, hb1, hp2, hb2, NPAIR, S,
              pfA[0], pfA[1], out_dram=qhd, out_dram_bufs=qhd_b)
    for j in range(NPAIR):
        p.store(qrd[j, :, :], qrT[j], dst_buf=qrd_b[j])
    env.pop_scope()

    env.push_scope()
    pfC = [env.psum(f"pc{i}", [128, 512], F32) for i in range(6)]
    psSim2, psSc2 = pfC[0:2], pfC[2:4]
    psO, psZ = pfC[4], pfC[5]
    kh_p = [env.sbuf(f"kh_p{i}", [128, S], BF16) for i in range(2)]
    kr_p = [env.sbuf(f"kr_p{i}", [128, S], BF16) for i in range(2)]
    qh_p = [env.sbuf(f"qh_p{i}", [128, S], BF16) for i in range(2)]
    qr_p = [env.sbuf(f"qr_p{i}", [128, S], BF16) for i in range(2)]
    sim2 = [env.sbuf(f"sim{i}", [128, S], BF16) for i in range(2)]
    pm2 = [env.sbuf(f"pmb{i}", [128, S], BF16) for i in range(2)]
    junk = env.sbuf("junkb", [128, S], BF16)
    onesb = env.sbuf("onesb", [128, S], BF16)
    p.op("vector", lambda e: e.memset(onesb.ap[:], 1.0), writes=[onesb])
    onesb.frozen = True
    scanb = env.sbuf("scanb", [128, S], F32)
    e1 = env.sbuf("e1b", [128, S], BF16)
    tieok = env.sbuf("tieokb", [128, S], BF16)
    kept = env.sbuf("keptb", [128, S], BF16)
    ptb = [env.sbuf(f"ptb{i}", [128, 128], BF16) for i in range(2)]
    sv = [{n: env.sbuf(f"sv_{n}{i}", [128, 1], F32)
           for n in ("tbm0", "acc", "s2", "bias", "tval", "rr", "zc", "rzc")}
          for i in range(2)]
    zrow = env.sbuf("zrowb", [1, 128], F32)
    bz = env.sbuf("bzb", [128, 128], F32)
    onesr = env.sbuf("onesrb", [1, 128], F32)
    p.op("vector", lambda e: e.memset(onesr.ap[:], 1.0), writes=[onesr])
    onesr.frozen = True
    ctxs2 = [env.sbuf(f"ctxsb{i}", [128, 128], BF16) for i in range(2)]
    junkV = env.sbuf("junkV", [128, 768], BF16)
    STEPS = (128, 64, 32, 16, 8, 4, 2, 1)
    mdiagw = env.sbuf("mdiagw_b", [128, 4 * 512], F32)
    p.load(mdiagw, mdw_in)
    mdiagw.frozen = True

    def emit_ss(j, pj, qt):
        par = qt % 2
        simb, pmb = sim2[par], pm2[par]
        psA, psB2 = psSim2[par], psSc2[par]
        ext = (qt + 1) * 128
        nch = (ext + 511) // 512
        r = qt % 4
        for ch in range(nch):
            n = min(512, ext - ch * 512)
            p.op("tensor", lambda e, qt=qt, ch=ch, n=n, psA=psA: e.matmul(
                psA.ap[:, 0:n], qh_p[pj].ap[:, qt * 128:(qt + 1) * 128],
                kh_p[pj].ap[:, ch * 512:ch * 512 + n], start=True, stop=True),
                reads=[qh_p[pj], kh_p[pj]], writes=[psA])
            if ch == nch - 1:
                p.op("vector", lambda e, ch=ch, n=n, psA=psA, simb=simb, r=r:
                     e.scalar_tensor_tensor(
                         out=simb.ap[:, ch * 512:ch * 512 + n],
                         in0=psA.ap[:, 0:n], scalar=1.0,
                         in1=mdiagw.ap[:, r * 512:r * 512 + n],
                         op0=ALU.mult, op1=ALU.add),
                     reads=[psA, mdiagw], writes=[simb])
            else:
                p.op("scalar", lambda e, ch=ch, n=n, psA=psA, simb=simb: e.copy(
                    out=simb.ap[:, ch * 512:ch * 512 + n], in_=psA.ap[:, 0:n]),
                    reads=[psA], writes=[simb])
        krb = kr_p[(j // 4) % 2]
        for ch in range(nch):
            n = min(512, ext - ch * 512)
            p.op("tensor", lambda e, qt=qt, ch=ch, n=n, psB2=psB2, krb=krb: e.matmul(
                psB2.ap[:, 0:n], qr_p[pj].ap[:, qt * 128:(qt + 1) * 128],
                krb.ap[:, ch * 512:ch * 512 + n], start=True, stop=True),
                reads=[qr_p[pj], krb], writes=[psB2])
            p.op("scalar", lambda e, ch=ch, n=n, psB2=psB2, pmb=pmb: e.activation(
                out=pmb.ap[:, ch * 512:ch * 512 + n], in_=psB2.ap[:, 0:n],
                func=AF.Exp, scale=1.0 / SQ128),
                reads=[psB2], writes=[pmb])

    def emit_search(qt, dq):
        par = qt % 2
        simb = sim2[par]
        v = sv[par]
        ext = (qt + 1) * 128
        chunk = 3

        def drain(k):
            for _ in range(k):
                if dq:
                    dq.pop(0)()
        p.op("vector", lambda e, v=v: e.memset(v["tbm0"].ap[:], 0.0),
             writes=[v["tbm0"]])
        for step in STEPS:
            p.op("vector", lambda e, v=v, step=step: e.tensor_scalar(
                out=v["bias"].ap[:], in0=v["tbm0"].ap[:], scalar1=-2.0,
                scalar2=float(-2 * step + 130.5), op0=ALU.mult, op1=ALU.add),
                reads=[v["tbm0"]], writes=[v["bias"]])
            p.op("scalar", lambda e, v=v, ext=ext, simb=simb: e.activation(
                out=junk.ap[:, 0:ext], in_=simb.ap[:, 0:ext], func=AF.Sign,
                bias=v["bias"].ap[:], accum_out=v["acc"].ap[:]),
                reads=[simb, v["bias"]], writes=[junk, v["acc"]])
            drain(chunk)
            p.op("vector", lambda e, v=v, ext=ext, step=step: e.tensor_scalar(
                out=v["s2"].ap[:], in0=v["acc"].ap[:],
                scalar1=float(1640 - ext), scalar2=float(step),
                op0=ALU.is_ge, op1=ALU.mult),
                reads=[v["acc"]], writes=[v["s2"]])
            p.op("vector", lambda e, v=v: e.tensor_tensor(
                out=v["tbm0"].ap[:], in0=v["tbm0"].ap[:], in1=v["s2"].ap[:],
                op=ALU.add), reads=[v["tbm0"], v["s2"]], writes=[v["tbm0"]])
        p.op("vector", lambda e, v=v: e.tensor_scalar(
            out=v["tval"].ap[:], in0=v["tbm0"].ap[:], scalar1=2.0,
            scalar2=-130.0, op0=ALU.mult, op1=ALU.add),
            reads=[v["tbm0"]], writes=[v["tval"]])
        p.op("vector", lambda e, v=v: e.tensor_scalar(
            out=v["bias"].ap[:], in0=v["tval"].ap[:], scalar1=-1.0,
            scalar2=-0.5, op0=ALU.mult, op1=ALU.add),
            reads=[v["tval"]], writes=[v["bias"]])
        p.op("scalar", lambda e, v=v, ext=ext, simb=simb: e.activation(
            out=junk.ap[:, 0:ext], in_=simb.ap[:, 0:ext], func=AF.Sign,
            bias=v["bias"].ap[:], accum_out=v["acc"].ap[:]),
            reads=[simb, v["bias"]], writes=[junk, v["acc"]])
        drain(chunk)
        p.op("vector", lambda e, v=v, ext=ext: e.tensor_scalar(
            out=v["rr"].ap[:], in0=v["acc"].ap[:], scalar1=-0.5,
            scalar2=float(820 - ext // 2), op0=ALU.mult, op1=ALU.add),
            reads=[v["acc"]], writes=[v["rr"]])
        drain(len(dq))

    def emit_post(j, qt):
        par = qt % 2
        simb, pmb = sim2[par], pm2[par]
        psB2, psSimb = psSc2[par], psSim2[par]
        v = sv[par]
        ctxs = ctxs2[par]
        ext = (qt + 1) * 128
        dq = []
        if qt >= 6:
            dq.append(lambda: p.op("vector", lambda e: e.tensor_scalar(
                out=e1.ap[:, 0:ext], in0=simb.ap[:, 0:ext],
                scalar1=v["tval"].ap[:], scalar2=None, op0=ALU.is_equal),
                reads=[simb, v["tval"]], writes=[e1]))
            dq.append(lambda: p.op("vector", lambda e: e.tensor_tensor_scan(
                out=scanb.ap[:, 0:ext], data0=onesb.ap[:, 0:ext],
                data1=e1.ap[:, 0:ext], initial=0.0, op0=ALU.mult, op1=ALU.add),
                reads=[onesb, e1], writes=[scanb]))
            dq.append(lambda: p.op("vector", lambda e: e.scalar_tensor_tensor(
                out=tieok.ap[:, 0:ext], in0=scanb.ap[:, 0:ext],
                scalar=v["rr"].ap[:], in1=e1.ap[:, 0:ext],
                op0=ALU.is_le, op1=ALU.mult),
                reads=[scanb, v["rr"], e1], writes=[tieok]))
            dq.append(lambda: p.op("vector", lambda e: e.scalar_tensor_tensor(
                out=kept.ap[:, 0:ext], in0=simb.ap[:, 0:ext],
                scalar=v["tval"].ap[:], in1=tieok.ap[:, 0:ext],
                op0=ALU.is_gt, op1=ALU.max),
                reads=[simb, v["tval"], tieok], writes=[kept]))
            dq.append(lambda: p.op("vector", lambda e: e.scalar_tensor_tensor(
                out=pmb.ap[:, 0:ext], in0=pmb.ap[:, 0:ext], scalar=1.0,
                in1=kept.ap[:, 0:ext], op0=ALU.mult, op1=ALU.mult,
                accum_out=v["zc"].ap[:]),
                reads=[pmb, kept], writes=[pmb, v["zc"]]))
        else:
            dq.append(lambda: p.op("vector", lambda e: e.tensor_tensor(
                out=pmb.ap[:, qt * 128:ext], in0=pmb.ap[:, qt * 128:ext],
                in1=causal01.ap[:], op=ALU.mult),
                reads=[pmb, causal01], writes=[pmb]))
            dq.append(lambda: p.op("vector", lambda e: e.tensor_scalar(
                out=junkV.ap[:, 0:ext], in0=pmb.ap[:, 0:ext], scalar1=1.0,
                scalar2=0.0, op0=ALU.mult, op1=ALU.add,
                accum_out=v["zc"].ap[:]),
                reads=[pmb], writes=[junkV, v["zc"]]))
        dq.append(lambda: p.op("vector", lambda e: e.reciprocal(
            out=v["rzc"].ap[:], in_=v["zc"].ap[:]),
            reads=[v["zc"]], writes=[v["rzc"]]))
        dq.append(lambda: p.op("tensor", lambda e: e.matmul(
            psSimb.ap[0:1, 0:128], v["rzc"].ap[:], identf.ap[:],
            start=True, stop=True), reads=[v["rzc"], identf], writes=[psSimb]))
        dq.append(lambda: p.op("vector", lambda e: e.tensor_copy(
            out=zrow.ap[:], in_=psSimb.ap[0:1, 0:128]),
            reads=[psSimb], writes=[zrow]))
        dq.append(lambda: p.op("tensor", lambda e: e.matmul(
            psB2.ap[:, 0:128], onesr.ap[:], zrow.ap[:], start=True, stop=True),
            reads=[onesr, zrow], writes=[psB2]))
        dq.append(lambda: p.op("vector", lambda e: e.tensor_copy(
            out=bz.ap[:], in_=psB2.ap[:, 0:128]), reads=[psB2], writes=[bz]))
        pts_ = [po.next_tp() for _ in range(qt + 1)]

        def mk_tp(kk):
            pt = pts_[kk]
            return lambda: p.op("tensor", lambda e: e.transpose(
                pt.ap[:], pmb.ap[:, kk * 128:(kk + 1) * 128], po.ident.ap[:]),
                reads=[pmb, po.ident], writes=[pt])
        dq.append(mk_tp(0))
        for kk in range(qt + 1):
            pt = pts_[kk]
            pb = ptb[kk % 2]
            dq.append(lambda pt=pt, pb=pb: p.op(
                "vector", lambda e: e.tensor_copy(out=pb.ap[:], in_=pt.ap[:]),
                reads=[pt], writes=[pb]))
            if kk + 1 <= qt:
                dq.append(mk_tp(kk + 1))
            vc0 = kk * 256 + (j // 4) * 128
            dq.append(lambda pb=pb, kk=kk, vc0=vc0: p.op(
                "tensor", lambda e: e.matmul(
                    psO.ap[:, 0:128], v_all.ap[:, vc0:vc0 + 128], pb.ap[:],
                    start=(kk == 0), stop=(kk == qt)),
                reads=[v_all, pb], writes=[psO]))
        dq.append(lambda: p.op("vector", lambda e: e.tensor_tensor(
            out=ctxs.ap[:], in0=psO.ap[:, 0:128], in1=bz.ap[:], op=ALU.mult),
            reads=[psO, bz], writes=[ctxs]))
        dq.append(lambda: p.store(ctx_out[j, :, qt * 128:(qt + 1) * 128], ctxs,
                                  queue="sync"))
        return dq

    for j in range(NPAIR):
        pj = j % 2
        p.load(kh_p[pj], khf[j, :, :], queue="sync")
        if j % 4 == 0:
            p.load(kr_p[(j // 4) % 2], krf[j // 4, :, :], queue="scalar")
        p.load(qh_p[pj], qhd[j, :, :], src_buf=qhd_b[j], queue="sync")
        p.load(qr_p[pj], qrd[j, :, :], src_buf=qrd_b[j], queue="scalar")
        dq = []
        emit_ss(j, pj, 0)
        for qt in range(NT):
            if qt >= 6:
                emit_search(qt, dq)
            else:
                for f in dq:
                    f()
                dq = []
            dq = emit_post(j, qt)
            if qt + 1 < NT:
                emit_ss(j, pj, qt + 1)
        for f in dq:
            f()
    env.pop_scope()


def build_pre_l1a(env):
    po = Pools(env, n_pf=4, n_pt=2, ln=True, wstream=True, rope=True)
    p = env.p
    x_in = env.dram_in("x", [NTILE, 128, D])
    wk = env.dram_in("wk", [32, 32, 128, 128])
    wv = env.dram_in("wv", [32, 8, 128, 512])
    kb_in = env.dram_in("kb", [128, 32])
    vb_in = env.dram_in("vbrow", [128, D])
    cosk = env.dram_in("cosk", [128, TOK])
    sink = env.dram_in("sink", [128, TOK])
    xnT_out = env.dram_out("xnT", [32, 128, TOK], BF16)
    kT_out = env.dram_out("kT", [32, 128, TOK], BF16)
    v_out = env.dram_out("v", [NTILE, 128, D], BF16)

    xnT = [env.sbuf(f"xnT{c}", [128, TOK], BF16) for c in range(32)]
    emit_ln_tp_phase(env, po, x_in, xnT, xnT_out=xnT_out)
    for c in xnT:
        c.frozen = True
    cos_b = env.sbuf("cos_b", [128, TOK], F32)
    sin_b = env.sbuf("sin_b", [128, TOK], F32)
    p.load(cos_b, cosk)
    p.load(sin_b, sink)
    kb = env.sbuf("kb_b", [128, 32], F32)
    p.load(kb, kb_in)
    vb = env.sbuf("vb_b", [128, D], F32)
    p.load(vb, vb_in)
    for b in (cos_b, sin_b, kb, vb):
        b.frozen = True
    kc2 = [env.sbuf(f"kc{i}", [128, TOK], BF16) for i in range(2)]
    for oc in range(32):
        kc = kc2[oc % 2]
        emit_proj_feat(env, po, wk, xnT, [kc], evac="rope",
                       rope=(cos_b, sin_b), oc_base=oc,
                       bias=lambda _oc, oc=oc: (kb, kb.ap[:, oc:oc + 1]))
        p.store(kT_out[oc, :, :], kc)
    vt = [env.sbuf(f"vt{t}", [128, 512], BF16) for t in range(2)]

    def evac_v(cb, ti, ps):
        vtb = vt[ti % 2]
        p.op("vector", lambda e, ps=ps, cb=cb, vtb=vtb: e.scalar_tensor_tensor(
            out=vtb.ap[:], in0=ps.ap[:], scalar=1.0,
            in1=vb.ap[:, cb * 512:(cb + 1) * 512], op0=ALU.mult, op1=ALU.add),
            reads=[ps, vb], writes=[vtb])
        p.store(v_out[ti, :, cb * 512:(cb + 1) * 512], vtb)

    emit_proj_tok(env, po, wv, xnT, ncolblk=8, evac_fn=evac_v, npsum=4)


def build_attn_l1a(env):
    po = Pools(env, n_pf=8, n_pt=0, ln=False, wstream=False, rope=True)
    p = env.p
    xnf = env.dram_in("xnf", [32, 128, S], BF16)
    wq = env.dram_in("wq", [128, 32 * NPAIR * 128])
    qb_in = env.dram_in("qb", [128, NPAIR])
    cosq = env.dram_in("cosq", [128, S], BF16)
    sinq = env.dram_in("sinq", [128, S], BF16)
    kTf = env.dram_in("kTf", [NPAIR, 128, S], BF16)
    vf = env.dram_in("vf", [NPAIR, 128, NT * 128], BF16)
    mvar = env.dram_in("mvar", [128, 4 * 512])
    ctx_out = env.dram_out("ctx", [NPAIR, 128, S], BF16)

    mv = env.sbuf("mv_b", [128, 4 * 512], F32)
    p.load(mv, mvar)
    mv.frozen = True
    qT_own = [env.sbuf(f"qT{j}", [128, S], BF16) for j in range(NPAIR)]

    env.push_scope()
    wq_all = env.sbuf("wq_all", [128, 32 * NPAIR * 128], BF16)
    p.load(wq_all, wq, queue="gpsimd")
    qb = env.sbuf("qb_b", [128, NPAIR], F32)
    p.load(qb, qb_in)
    cos_b = env.sbuf("cosb", [128, S], BF16)
    sin_b = env.sbuf("sinb", [128, S], BF16)
    p.load(cos_b, cosq)
    p.load(sin_b, sinq)
    for b in (wq_all, qb, cos_b, sin_b):
        b.frozen = True

    def evac_rope(j, tb, ps, dst, cols):
        emit_rope_evac(env, po, ps, dst, cols, cos_b, sin_b,
                       bias=(qb, qb.ap[:, j:j + 1]))

    emit_qproj_pairs(env, po, wq_all, xnf, qT_own, evac_fn=evac_rope)
    for b in qT_own:
        b.frozen = True
    env.pop_scope()

    k_p = [env.sbuf(f"k_p{i}", [128, S], BF16) for i in range(2)]
    v_p = [env.sbuf(f"v_p{i}", [128, NT * 128], BF16) for i in range(2)]
    tmp2 = [env.sbuf(f"at_tmp{i}", [128, 512], F32) for i in range(2)]
    cur = [-1]

    def kget(j):
        if cur[0] != j:
            p.load(k_p[j % 2], kTf[j, :, :], queue="sync")
            p.load(v_p[j % 2], vf[j, :, :], queue="scalar")
            cur[0] = j
        return (k_p[j % 2], 0)

    def exp_evac(j, qblk, kt, psS, ptile):
        tmp = tmp2[kt % 2]
        r = kt - 4 * qblk
        if 0 <= r <= 3:
            p.op("vector", lambda e, psS=psS, r=r, tmp=tmp: e.scalar_tensor_tensor(
                out=tmp.ap[:], in0=mv.ap[:, r * 512:(r + 1) * 512], scalar=1.0,
                in1=psS.ap[:], op0=ALU.mult, op1=ALU.add),
                reads=[mv, psS], writes=[tmp])
            p.op("scalar", lambda e, ptile=ptile, tmp=tmp: e.activation(
                out=ptile.ap[:], in_=tmp.ap[:], func=AF.Exp, scale=1.0 / SQ128),
                reads=[tmp], writes=[ptile])
        else:
            p.op("scalar", lambda e, psS=psS, ptile=ptile: e.activation(
                out=ptile.ap[:], in_=psS.ap[:], func=AF.Exp, scale=1.0 / SQ128),
                reads=[psS], writes=[ptile])

    emit_attn_pairs(env, po, qT_own, kget, lambda j: (v_p[j % 2], 0), ctx_out,
                    exp_evac=exp_evac)


def build_ffn1(env):
    po = Pools(env, n_pf=4, n_pt=2, ln=True, wstream=False, rope=False)
    p = env.p
    x_in = env.dram_in("x", [NTILE, 128, D])
    up_in = env.dram_in("up", [128, 16 * 2 * 8 * 128])
    dn_in = env.dram_in("dn", [128, 16 * 8 * 256])
    x_out = env.dram_out("xo", [NTILE, 128, D])
    up_all = env.sbuf("up_all", [128, 16 * 2 * 8 * 128], BF16)
    dn_all = env.sbuf("dn_all", [128, 16 * 8 * 256], BF16)
    p.load(up_all, up_in, queue="gpsimd")
    p.load(dn_all, dn_in, queue="gpsimd")
    up_all.frozen = True
    dn_all.frozen = True
    xw = env.sbuf("f1_x", [128, D], F32)
    xnw = env.sbuf("f1_xn", [128, D], BF16)
    xnTt = env.sbuf("f1_xnT", [128, D], BF16)
    gT = env.sbuf("f1_gT", [128, 8 * 128], BF16)
    xob = env.sbuf("f1_xo", [128, 256], F32)
    psG, psD = po.pf[0], po.pf[1]
    for ti in range(NTILE):
        p.load(xw, x_in[ti, :, :], queue="sync")
        emit_ln(env, po, xw, xnw)
        for c in range(32):
            emit_tp128(env, po, xnw, xnw.ap[:, c * 128:(c + 1) * 128],
                       xnTt, xnTt.ap[:, c * 128:(c + 1) * 128])
        for h in range(16):
            for mc in range(8):
                for dc in range(2):
                    c0 = ((h * 2 + dc) * 8 + mc) * 128
                    p.op("tensor", lambda e, h=h, mc=mc, dc=dc, c0=c0: e.matmul(
                        psG.ap[:, 0:128], up_all.ap[:, c0:c0 + 128],
                        xnTt.ap[:, (2 * h + dc) * 128:(2 * h + dc + 1) * 128],
                        start=(dc == 0), stop=(dc == 1)),
                        reads=[up_all, xnTt], writes=[psG])
                p.op("scalar", lambda e, mc=mc: e.activation(
                    out=gT.ap[:, mc * 128:(mc + 1) * 128], in_=psG.ap[:, 0:128],
                    func=AF.Gelu), reads=[psG], writes=[gT])
            for mc in range(8):
                c0 = (h * 8 + mc) * 256
                p.op("tensor", lambda e, h=h, mc=mc, c0=c0: e.matmul(
                    psD.ap[:, 0:256], gT.ap[:, mc * 128:(mc + 1) * 128],
                    dn_all.ap[:, c0:c0 + 256],
                    start=(mc == 0), stop=(mc == 7)),
                    reads=[gT, dn_all], writes=[psD])
            p.op("vector", lambda e, h=h: e.scalar_tensor_tensor(
                out=xob.ap[:], in0=psD.ap[:, 0:256], scalar=1.0,
                in1=xw.ap[:, h * 256:(h + 1) * 256], op0=ALU.mult, op1=ALU.add),
                reads=[psD, xw], writes=[xob])
            p.store(x_out[ti, :, h * 256:(h + 1) * 256], xob)


def build_pre_l1b(env):
    po = Pools(env, n_pf=4, n_pt=2, ln=True, wstream=True, rope=True)
    p = env.p
    x_in = env.dram_in("x", [NTILE, 128, D])
    wcq = env.dram_in("wcq", [32, 12, 128, 128])
    wckv = env.dram_in("wckv", [32, 4, 128, 128])
    wkr = env.dram_in("wkr", [32, 1, 128, 64])
    cosy = env.dram_in("cosy", [64, TOK])
    siny = env.dram_in("siny", [64, TOK])
    xnT_out = env.dram_out("xnT", [32, 128, TOK], BF16)
    cq_out = env.dram_out("cqT", [12, 128, TOK], BF16)
    ckv_out = env.dram_out("ckvT", [4, 128, TOK], BF16)
    kr_out = env.dram_out("krT", [64, TOK], BF16)

    xnT = [env.sbuf(f"xnT{c}", [128, TOK], BF16) for c in range(32)]
    emit_ln_tp_phase(env, po, x_in, xnT, xnT_out=xnT_out)
    for c in xnT:
        c.frozen = True
    cqT = [env.sbuf(f"cqT{i}", [128, TOK], BF16) for i in range(12)]
    emit_proj_feat(env, po, wcq, xnT, cqT, out_store=cq_out)
    ckvT = [env.sbuf(f"ckvT{i}", [128, TOK], BF16) for i in range(4)]
    emit_proj_feat(env, po, wckv, xnT, ckvT, out_store=ckv_out)
    cos_b = env.sbuf("cosy_b", [64, TOK], F32)
    sin_b = env.sbuf("siny_b", [64, TOK], F32)
    p.load(cos_b, cosy)
    p.load(sin_b, siny)
    cos_b.frozen = True
    sin_b.frozen = True
    krc = env.sbuf("krc", [64, TOK], BF16)
    ps = po.pf[0]
    for tb in range(TOK // 512):
        for dm in range(32):
            wb = po.next_wb()
            p.load(wb, wkr[dm, 0, :, 0:64], queue="gpsimd", dst_ap=wb.ap[:, 0:64])
            p.op("tensor", lambda e, dm=dm, tb=tb, wb=wb: e.matmul(
                ps.ap[0:64, :], wb.ap[:, 0:64],
                xnT[dm].ap[:, tb * 512:(tb + 1) * 512],
                start=(dm == 0), stop=(dm == 31)),
                reads=[wb, xnT[dm]], writes=[ps])
        emit_rope_evac64(env, po, ps, krc, slice(tb * 512, (tb + 1) * 512),
                         cos_b, sin_b, 0)
    p.store(kr_out, krc)


def emit_rope_evac64(env, po, ps, ob, cols, cos_b, sin_b, row0, bias=None):
    p = env.p
    t1, t2 = po.rope_t1, po.rope_t2
    src = ps
    sr = row0
    if bias is not None:
        bbuf, bap = bias
        tb = po.rope_tb
        p.op("scalar", lambda e: e.activation(out=tb.ap[sr:sr + 64, 0:512],
                                              in_=ps.ap[sr:sr + 64, 0:512],
                                              func=AF.Identity, bias=bap),
             reads=[ps, bbuf], writes=[tb])
        src = tb
    ncol = cols.stop - cols.start
    sa = src.ap[sr:sr + 32, 0:ncol]
    sb_ = src.ap[sr + 32:sr + 64, 0:ncol]
    ca = cos_b.ap[row0:row0 + 32, cols]
    sn = sin_b.ap[row0 + 32:row0 + 64, cols]
    cb = cos_b.ap[row0 + 32:row0 + 64, cols]
    snb = sin_b.ap[row0:row0 + 32, cols]
    p.op("vector", lambda e: e.tensor_tensor(out=t1.ap[0:32, 0:ncol], in0=sa, in1=ca, op=ALU.mult),
         reads=[src, cos_b], writes=[t1])
    p.op("vector", lambda e: e.tensor_tensor(out=t2.ap[0:32, 0:ncol], in0=sb_, in1=sn, op=ALU.mult),
         reads=[src, sin_b], writes=[t2])
    p.op("vector", lambda e: e.tensor_tensor(out=ob.ap[row0:row0 + 32, cols], in0=t1.ap[0:32, 0:ncol],
                                             in1=t2.ap[0:32, 0:ncol], op=ALU.subtract),
         reads=[t1, t2], writes=[ob])
    p.op("vector", lambda e: e.tensor_tensor(out=t1.ap[0:32, 0:ncol], in0=sb_, in1=cb, op=ALU.mult),
         reads=[src, cos_b], writes=[t1])
    p.op("vector", lambda e: e.tensor_tensor(out=t2.ap[0:32, 0:ncol], in0=sa, in1=snb, op=ALU.mult),
         reads=[src, sin_b], writes=[t2])
    p.op("vector", lambda e: e.tensor_tensor(out=ob.ap[row0 + 32:row0 + 64, cols], in0=t1.ap[0:32, 0:ncol],
                                             in1=t2.ap[0:32, 0:ncol], op=ALU.add),
         reads=[t1, t2], writes=[ob])


def build_attn_l1b(env):
    po = Pools(env, n_pf=7, n_pt=0, ln=False, wstream=False, rope=True)
    p = env.p
    cqf = env.dram_in("cqf", [12, 128, S], BF16)
    ckvf = env.dram_in("ckvf", [4, 128, S], BF16)
    krf = env.dram_in("krf", [64, S], BF16)
    wqn = env.dram_in("wqn", [128, 12 * NPAIR * 128])
    wqr = env.dram_in("wqr", [128, 12 * NPAIR * 64])
    wuk = env.dram_in("wuk", [128, 4 * NPAIR * 128])
    wuv = env.dram_in("wuv", [128, 4 * NPAIR * 128])
    cosy = env.dram_in("cosy", [64, S])
    siny = env.dram_in("siny", [64, S])
    mvar = env.dram_in("mvar", [128, 4 * 512])
    ctx_out = env.dram_out("ctx", [NPAIR, 128, S], BF16)

    ckvT = [env.sbuf(f"ckvT{i}", [128, S], BF16) for i in range(4)]
    for i in range(4):
        p.load(ckvT[i], ckvf[i, :, :])
        ckvT[i].frozen = True
    krT = env.sbuf("krT_b", [64, S], BF16)
    p.load(krT, krf)
    krT.frozen = True
    w_qn = env.sbuf("wqn_b", [128, 12 * NPAIR * 128], BF16)
    w_qr = env.sbuf("wqr_b", [128, 12 * NPAIR * 64], BF16)
    w_uk = env.sbuf("wuk_b", [128, 4 * NPAIR * 128], BF16)
    w_uv = env.sbuf("wuv_b", [128, 4 * NPAIR * 128], BF16)
    p.load(w_qn, wqn, queue="gpsimd")
    p.load(w_qr, wqr, queue="gpsimd")
    p.load(w_uk, wuk, queue="gpsimd")
    p.load(w_uv, wuv, queue="gpsimd")
    cos_b = env.sbuf("cosy_b", [64, S], F32)
    sin_b = env.sbuf("siny_b", [64, S], F32)
    p.load(cos_b, cosy)
    p.load(sin_b, siny)
    mv = env.sbuf("mv_b", [128, 4 * 512], F32)
    p.load(mv, mvar)
    for b in (w_qn, w_qr, w_uk, w_uv, cos_b, sin_b, mv):
        b.frozen = True

    cql = [env.sbuf(f"cq_xl{i}", [128, 6 * 512], BF16) for i in range(2)]
    cqh = [env.sbuf("cq_xh0", [128, 6 * 512], BF16)] * 2
    qnT = env.sbuf("qnT", [128, S], BF16)
    qroT = env.sbuf("qroT", [64, S], BF16)
    k_p = [env.sbuf(f"k_p{i}", [128, S], BF16) for i in range(2)]
    v_p = [env.sbuf("v_p0", [128, NT * 128], BF16)]
    tmp2 = [env.sbuf("at_tmp0", [128, 512], F32)] * 2
    SQ192 = float(np.sqrt(192.0))

    for j in range(NPAIR):
        kp, vp = k_p[j % 2], v_p[0]
        for tb in range(8):
            ps = po.pf[2 + tb % 2]
            for i in range(4):
                c0 = (i * NPAIR + j) * 128
                p.op("tensor", lambda e, i=i, c0=c0, tb=tb, ps=ps: e.matmul(
                    ps.ap[:], w_uk.ap[:, c0:c0 + 128],
                    ckvT[i].ap[:, tb * 512:(tb + 1) * 512],
                    start=(i == 0), stop=(i == 3)),
                    reads=[w_uk, ckvT[i]], writes=[ps])
            p.op("scalar", lambda e, tb=tb, ps=ps, kp=kp: e.copy(
                out=kp.ap[:, tb * 512:(tb + 1) * 512], in_=ps.ap[:]),
                reads=[ps], writes=[kp])
        for kk in range(NT):
            ps = po.pf[2 + kk % 2]
            for i in range(4):
                c0 = (i * NPAIR + j) * 128
                p.op("tensor", lambda e, i=i, c0=c0, kk=kk, ps=ps: e.matmul(
                    ps.ap[:, 0:128],
                    ckvT[i].ap[:, kk * 128:(kk + 1) * 128],
                    w_uv.ap[:, c0:c0 + 128],
                    start=(i == 0), stop=(i == 3)),
                    reads=[w_uv, ckvT[i]], writes=[ps])
            p.op("scalar", lambda e, kk=kk, ps=ps, vp=vp: e.copy(
                out=vp.ap[:, kk * 128:(kk + 1) * 128], in_=ps.ap[:, 0:128]),
                reads=[ps], writes=[vp])
        for tb in range(8):
            par = tb % 2
            cl, ch_ = cql[par], cqh[par]
            p.load(cl, cqf[0:6, :, tb * 512:(tb + 1) * 512]
                   .rearrange("a p m -> p a m"), queue="sync",
                   dst_ap=cl.ap[:].rearrange("p (a m) -> p a m", a=6))
            p.load(ch_, cqf[6:12, :, tb * 512:(tb + 1) * 512]
                   .rearrange("a p m -> p a m"), queue="scalar",
                   dst_ap=ch_.ap[:].rearrange("p (a m) -> p a m", a=6))
            psn, psr = po.pf[2], po.pf[3]
            for dm in range(12):
                src = cl if dm < 6 else ch_
                off = dm % 6
                c0 = (dm * NPAIR + j) * 128
                p.op("tensor", lambda e, dm=dm, c0=c0, src=src, off=off, psn=psn: e.matmul(
                    psn.ap[:], w_qn.ap[:, c0:c0 + 128],
                    src.ap[:, off * 512:(off + 1) * 512],
                    start=(dm == 0), stop=(dm == 11)),
                    reads=[w_qn, src], writes=[psn])
                c1 = (dm * NPAIR + j) * 64
                p.op("tensor", lambda e, dm=dm, c1=c1, src=src, off=off, psr=psr: e.matmul(
                    psr.ap[0:64, :], w_qr.ap[:, c1:c1 + 64],
                    src.ap[:, off * 512:(off + 1) * 512],
                    start=(dm == 0), stop=(dm == 11)),
                    reads=[w_qr, src], writes=[psr])
            cols = slice(tb * 512, (tb + 1) * 512)
            p.op("scalar", lambda e, cols=cols, psn=psn: e.copy(
                out=qnT.ap[:, cols], in_=psn.ap[:]),
                reads=[psn], writes=[qnT])
            emit_rope_evac64(env, po, psr, qroT, cols, cos_b, sin_b, 0)
        def score_extra(jj, qblk, kt, psS):
            p.op("tensor", lambda e, qblk=qblk, kt=kt, psS=psS: e.matmul(
                psS.ap[:], krT.ap[:, kt * 128:(kt + 1) * 128],
                qroT.ap[:, qblk * 512:(qblk + 1) * 512],
                start=False, stop=True),
                reads=[krT, qroT], writes=[psS])

        def exp_evac(jj, qblk, kt, psS, ptile):
            tmp = tmp2[kt % 2]
            r = kt - 4 * qblk
            if 0 <= r <= 3:
                p.op("vector", lambda e, psS=psS, r=r, tmp=tmp: e.scalar_tensor_tensor(
                    out=tmp.ap[:], in0=mv.ap[:, r * 512:(r + 1) * 512],
                    scalar=1.0, in1=psS.ap[:], op0=ALU.mult, op1=ALU.add),
                    reads=[mv, psS], writes=[tmp])
                p.op("scalar", lambda e, ptile=ptile, tmp=tmp: e.activation(
                    out=ptile.ap[:], in_=tmp.ap[:], func=AF.Exp, scale=1.0 / SQ192),
                    reads=[tmp], writes=[ptile])
            else:
                p.op("scalar", lambda e, psS=psS, ptile=ptile: e.activation(
                    out=ptile.ap[:], in_=psS.ap[:], func=AF.Exp, scale=1.0 / SQ192),
                    reads=[psS], writes=[ptile])

        emit_attn_pairs_single(env, po, j, qnT, kp, vp, ctx_out,
                               exp_evac=exp_evac, score_extra=score_extra)


def emit_attn_pairs_single(env, po, j, qT, kbuf, vbuf, ctx_out,
                           exp_evac, score_extra=None, seq=S):
    p = env.p
    nqb = seq // 512
    if not hasattr(env, "_aps_bufs"):
        pts = [env.sbuf(f"as_p{i}", [128, 512], BF16) for i in range(4)]
        zrow = env.sbuf("as_z", [1, 512], F32)
        onesr = env.sbuf("as_or", [1, 128], F32)
        bz = env.sbuf("as_bz", [128, 512], F32)
        ctxs = env.sbuf("as_ctx", [128, 512], BF16)
        p.op("vector", lambda e: e.memset(onesr.ap[:], 1.0), writes=[onesr])
        onesr.frozen = True
        env._aps_bufs = (pts, zrow, onesr, bz, ctxs)
    pts, zrow, onesr, bz, ctxs = env._aps_bufs
    psZ, psO, psB = po.pf[4], po.pf[5], po.pf[6]
    for qblk in range(nqb):
        nkt = 4 * qblk + 4

        def zpv(kt, nkt=nkt):
            ptile = pts[kt % 4]
            p.op("tensor", lambda e, kt=kt, ptile=ptile: e.matmul(
                psZ.ap[0:1, :], po.ones1.ap[:], ptile.ap[:],
                start=(kt == 0), stop=(kt == nkt - 1)),
                reads=[po.ones1, ptile], writes=[psZ])
            p.op("tensor", lambda e, kt=kt, ptile=ptile: e.matmul(
                psO.ap[:], vbuf.ap[:, kt * 128:(kt + 1) * 128], ptile.ap[:],
                start=(kt == 0), stop=(kt == nkt - 1)),
                reads=[vbuf, ptile], writes=[psO])

        for kt in range(nkt):
            psS = po.pf[kt % 2]
            def mm_s(e, qblk=qblk, kt=kt, psS=psS):
                return e.matmul(psS.ap[:], kbuf.ap[:, kt * 128:(kt + 1) * 128],
                                qT.ap[:, qblk * 512:(qblk + 1) * 512],
                                start=True, stop=(score_extra is None))
            p.op("tensor", mm_s, reads=[kbuf, qT], writes=[psS])
            if score_extra is not None:
                score_extra(j, qblk, kt, psS)
            exp_evac(j, qblk, kt, psS, pts[kt % 4])
            if kt >= 1:
                zpv(kt - 1)
        zpv(nkt - 1)
        p.op("vector", lambda e: e.tensor_copy(out=zrow.ap[:], in_=psZ.ap[0:1, :]),
             reads=[psZ], writes=[zrow])
        p.op("vector", lambda e: e.reciprocal(out=zrow.ap[:], in_=zrow.ap[:]),
             reads=[zrow], writes=[zrow])
        p.op("tensor", lambda e: e.matmul(psB.ap[:], onesr.ap[:], zrow.ap[:],
                                          start=True, stop=True),
             reads=[onesr, zrow], writes=[psB])
        p.op("vector", lambda e: e.tensor_copy(out=bz.ap[:], in_=psB.ap[:]),
             reads=[psB], writes=[bz])
        p.op("vector", lambda e: e.tensor_tensor(out=ctxs.ap[:], in0=psO.ap[:],
                                                 in1=bz.ap[:], op=ALU.mult),
             reads=[psO, bz], writes=[ctxs])
        p.store(ctx_out[j, :, qblk * 512:(qblk + 1) * 512], ctxs)
